# version 12
# baseline (speedup 1.0000x reference)
import numpy as np
from contextlib import ExitStack

import concourse.bass as bass
import concourse.mybir as mybir
from concourse.bass_utils import run_bass_kernel_spmd

F32 = mybir.dt.float32
BF16 = mybir.dt.bfloat16
F16 = mybir.dt.float16
AF = mybir.ActivationFunctionType
ALU = mybir.AluOpType
AX = mybir.AxisListType

NCORES = 8
TOK = 2048
TT = TOK // 128
DM = 1024
HC = DM // 128
DI = 2816
IC = DI // 128
S = 4096
ST = S // 128
QMAX = 127.0
RC = 1.5 * 2.0 ** 23
EPS = 1e-5
DK = 64

ENGS = ("tensor", "vector", "scalar", "gpsimd", "sync")


class Sem:
    def __init__(self, es, nc, name):
        self.h = es.enter_context(nc.semaphore(name))
        self.n = 0


class InstRec:

    def __init__(self, inst):
        self.i = inst
        self.updated = False

    def then_inc(self, sem, k=None):
        self.updated = True
        if k is None:
            return self.i.then_inc(sem)
        return self.i.then_inc(sem, k)


class EngRec:
    def __init__(self, eng, store, key):
        object.__setattr__(self, "_eng", eng)
        object.__setattr__(self, "_store", store)
        object.__setattr__(self, "_key", key)

    def __getattr__(self, name):
        attr = getattr(self._eng, name)
        if not callable(attr):
            return attr

        def wrap(*a, **kw):
            r = attr(*a, **kw)
            if isinstance(r, bass.BassInstruction):
                r = InstRec(r)
                self._store[self._key] = r
            return r
        return wrap


class Ctx:
    def __init__(self, nc, block, es):
        self.nc = nc
        self.block = block
        self.es = es
        self.last = {}
        self.sems = []
        self.s_bar = Sem(es, nc, "bar")

    def sec(self, eng_name, fn):
        def g(eng):
            fn(EngRec(eng, self.last, eng_name))
        getattr(self.block, eng_name)(g)

    def tensor(self, fn): self.sec("tensor", fn)
    def vector(self, fn): self.sec("vector", fn)
    def scalar(self, fn): self.sec("scalar", fn)
    def gpsimd(self, fn): self.sec("gpsimd", fn)
    def sync(self, fn): self.sec("sync", fn)

    def barrier(self):
        target = self.s_bar.n
        for e in ENGS:
            li = self.last.pop(e, None)
            if li is not None and not li.updated:
                try:
                    li.i.then_inc(self.s_bar.h, 1)
                    target += 1
                except AssertionError:
                    li.i.then_inc(self.s_bar.h, 16)
                    target += 16
        waits = [(s.h, s.n) for s in self.sems if s.n > 0]
        if target > self.s_bar.n:
            waits.append((self.s_bar.h, target))

        for e in ENGS:
            def f(eng, ws=tuple(waits)):
                for h, n in ws:
                    eng.wait_ge(h, n)
            self.sec(e, f)
        self.s_bar.n = target
        self.last = {}



def build_program(upto="full", debug=()):
    nc = bass.Bass("TRN2", target_bir_lowering=False)
    dbg = set(debug)
    d = {}
    d["upto"] = upto

    for nm, shape, dt in [
        ("x", [TOK, DM], F32), ("wq", [DM, 512], F32), ("wk", [DM, 512], F32),
        ("wv", [DM, 512], F32), ("wo", [DM, DM], F32), ("wg", [DI, DM], F32),
        ("wu", [DI, DM], F32), ("wd", [DM, DI], F32), ("clip", [1, 2], F32),
    ]:
        d[nm + "_in"] = nc.dram_tensor(nm, shape, dt, kind="ExternalInput").ap()
    d["out"] = nc.dram_tensor("out", [TOK, DM], F32, kind="ExternalOutput").ap()

    d["dbg"] = dbg
    d["dbg_out"] = {}

    def mkdbg(name, shape, dt=F32):
        if name in dbg:
            d["dbg_out"][name] = nc.dram_tensor(
                "dbg_" + name, shape, dt, kind="ExternalOutput").ap()
    d["mkdbg"] = mkdbg
    mkdbg("xq", [TOK, DM], BF16)
    mkdbg("h", [DI, TOK], F16)
    mkdbg("hq", [DI, TOK], BF16)
    mkdbg("gmax", [1, 4], F32)
    mkdbg("wgq", [DI, DM], BF16)
    mkdbg("qf", [128, 4 * S], BF16)
    mkdbg("kf", [128, 4 * S], BF16)
    mkdbg("vt", [128, (S // 128) * 520], BF16)
    mkdbg("avf", [128, 4 * S], BF16)
    mkdbg("xln", [S, DM], BF16)
    mkdbg("battn", [128, 4], F32)
    mkdbg("cos", [128, S], BF16)
    mkdbg("xfs0", [128, HC * 512], BF16)
    mkdbg("rotb", [128, 128], BF16)
    mkdbg("avall", [128, 8 * TOK], BF16)
    mkdbg("qraw0", [128, 512], BF16)
    mkdbg("sin", [128, S], BF16)
    mkdbg("ivf", [128, 1], F32)

    d["wgq_nat"] = nc.dram_tensor("wgq_nat", [DI, DM], BF16)
    d["wuq_nat"] = nc.dram_tensor("wuq_nat", [DI, DM], BF16)
    d["wdq_nat"] = nc.dram_tensor("wdq_nat", [DM, DI], BF16)
    d["wgq_t"] = nc.dram_tensor("wgq_t", [DM, DI], BF16)
    d["wuq_t"] = nc.dram_tensor("wuq_t", [DM, DI], BF16)
    d["wdq_t"] = nc.dram_tensor("wdq_t", [DI, DM], BF16)
    d["xq_nat"] = nc.dram_tensor("xq_nat", [TOK, DM], BF16)
    d["h_nat"] = nc.dram_tensor("h_nat", [DI, TOK], F16)
    d["hq_nat"] = nc.dram_tensor("hq_nat", [DI, TOK], BF16)
    d["mx_in"] = nc.dram_tensor("mx_in", [1, 16], F32)
    d["mx_out"] = nc.dram_tensor("mx_out", [1, 16], F32)
    d["ag_xln_in"] = nc.dram_tensor("ag_xln_in", [TOK, DM], BF16)
    d["ag_xln_out"] = nc.dram_tensor("ag_xln_out", [S, DM], BF16)
    d["ag_av_in"] = nc.dram_tensor("ag_av_in", [512, S], BF16)
    d["ag_av_out"] = nc.dram_tensor("ag_av_out", [1024, S], BF16)

    es = ExitStack()
    d["es_root"] = es

    _cnt = [0]

    def sb(name, shape, dt, scope=None):
        _cnt[0] += 1
        return (scope or es).enter_context(
            nc.sbuf_tensor(f"{name}_{_cnt[0]}", shape, dt))

    def psum(name, shape, dt, scope=None):
        return (scope or es).enter_context(nc.psum_tensor(name, shape, dt))
    d["sb"] = sb
    d["psum"] = psum

    d["xr"] = sb("xr", [128, TT * DM], F32)
    d["iden_f"] = sb("iden_f", [128, 128], F32)
    d["tri_b"] = sb("tri_b", [128, 128], BF16)
    d["rot_b"] = sb("rot_b", [128, 128], BF16)
    d["ones_row"] = sb("ones_row", [1, 128], F32)
    d["iota_p"] = sb("iota_p", [128, 1], F32)
    d["sclip"] = sb("sclip", [1, 2], F32)
    d["ssig"] = sb("ssig", [1, 2], F32)
    d["s_tmp"] = sb("s_tmp", [1, 8], F32)
    d["gmax"] = sb("gmax", [1, 4], F32)
    d["binv"] = sb("binv", [128, 8], F32)
    d["hcol"] = sb("hcol", [128, 2], F32)
    d["lmax_col"] = sb("lmax_col", [128, 1], F32)
    for nm in ("st_m", "st_r", "st_a", "st_b", "st_c", "st_d"):
        d[nm] = sb(nm, [128, TT], F32)
    d["wsc_g"] = sb("wsc_g", [128, IC], F32)
    d["wsc_u"] = sb("wsc_u", [128, IC], F32)
    d["wsc_d"] = sb("wsc_d", [128, HC], F32)
    d["sqb"] = sb("sqb", [128, DM], F32)

    d["pall"] = psum("pall", [128, 8 * 512], F32)

    with nc.Block() as block:
        ctx = Ctx(nc, block, es)
        for s in ("s_ld", "s_sc", "s_gld", "s_st", "s_pe", "s_dve",
                  "s_act", "s_gp", "s_cc"):
            d[s] = Sem(es, nc, s)
            ctx.sems.append(d[s])
        emit_all(ctx, d)
    es.close()
    return nc, d["dbg_out"]


def mk_psems(ctx, d, name):
    import itertools
    es = ctx.es
    sems = [Sem(es, ctx.nc, f"{name}{i}") for i in range(2)]
    for s in sems:
        ctx.sems.append(s)
    return sems


def emit_all(ctx, d):
    upto = d["upto"]
    with ExitStack() as pscope:
        d["iota_j"] = d["sb"]("iota_j", [128, 128], F32, pscope)
        d["scr1"] = d["sb"]("scr1", [128, 128], F32, pscope)
        d["scr2"] = d["sb"]("scr2", [128, 128], F32, pscope)
        emit_prologue(ctx, d)
        ctx.barrier()
    with ExitStack() as wscope:
        emit_weight_quant(ctx, d, wscope)
    emit_ln_stats(ctx, d, want_max=True)
    ctx.barrier()
    emit_maxreduce(ctx, d, slot=0)
    emit_scale_chain(ctx, d, slot=0, sig_idx=0)
    ctx.barrier()
    with ExitStack() as fscope:
        emit_ffn(ctx, d, fscope, app=0)
    if upto == "ffn1":
        emit_output(ctx, d)
        return
    with ExitStack() as ascope:
        emit_attention(ctx, d, ascope)
    if upto == "attn":
        emit_output(ctx, d)
        return
    emit_ln_stats(ctx, d, want_max=True)
    ctx.barrier()
    emit_maxreduce(ctx, d, slot=2)
    emit_scale_chain(ctx, d, slot=2, sig_idx=0)
    ctx.barrier()
    with ExitStack() as fscope:
        emit_ffn(ctx, d, fscope, app=1)
    emit_output(ctx, d)


def emit_ffn(ctx, d, scope, app):
    sb = d["sb"]
    nc = ctx.nc
    d["hmax_cols"] = sb("hmax_cols", [128, IC * 4], F32, scope)
    d["qstage"] = sb("qstage", [128, 2 * DM], F32, scope)
    d["qout"] = sb("qout", [128, 2 * DM], BF16, scope)
    d["srs"] = sb("srs", [128, TT], F32, scope)
    d["bsd"] = sb("bsd", [128, DM], F32, scope)
    d["brow"] = sb("brow", [1, 128], F32, scope)
    with ExitStack() as gscope:
        d["xq_fm"] = sb("xq_fm", [128, HC * TOK], BF16, gscope)
        emit_xq(ctx, d, slot=2 * app)
        ctx.barrier()
        emit_gateup(ctx, d, gscope, slot=2 * app)
        ctx.barrier()
    emit_hmax_finish(ctx, d)
    emit_maxreduce(ctx, d, slot=2 * app + 1)
    emit_scale_chain(ctx, d, slot=2 * app + 1, sig_idx=1)
    emit_bsd(ctx, d)
    ctx.barrier()
    with ExitStack() as dscope:
        emit_down(ctx, d, dscope, app=app)
    ctx.barrier()
    for k in ("xq_fm", "hmax_cols", "qstage", "qout", "srs", "wgb", "wub", "sgt", "upt",
              "hft", "hbt", "svg", "svu", "hst", "hqo", "hfs", "hqb", "wdb",
              "wdr", "hstage", "dst1", "bsd", "brow"):
        d.pop(k, None)



def emit_prologue(ctx, d):
    s_ld, s_gp, s_dve, s_act = d["s_ld"], d["s_gp"], d["s_dve"], d["s_act"]
    x_t = d["x_in"].rearrange("(t p) m -> t p m", p=128)
    xr = d["xr"]

    def fsync(e):
        for t in range(TT):
            e.dma_start(out=xr[:, t * DM:(t + 1) * DM], in_=x_t[t]).then_inc(s_ld.h, 16)
        e.dma_start(out=d["sclip"][:, :], in_=d["clip_in"][:, :]).then_inc(s_ld.h, 16)
        s_ld.n += 16 * (TT + 1)
    ctx.sync(fsync)

    def fgp(e):
        e.iota(d["iota_j"][:, :], [[1, 128]], channel_multiplier=0,
               allow_small_or_imprecise_dtypes=True)
        e.iota(d["iota_p"][:, :], [[1, 1]], channel_multiplier=1,
               allow_small_or_imprecise_dtypes=True)
        e.memset(d["ones_row"][:, :], 1.0).then_inc(s_gp.h, 1)
        s_gp.n += 1
    ctx.gpsimd(fgp)

    def fv(e):
        e.wait_ge(s_gp.h, s_gp.n)
        jt, pt = d["iota_j"], d["iota_p"]
        e.tensor_scalar(d["iden_f"][:, :], jt[:, :], pt[:, :], None, ALU.is_equal)
        e.tensor_scalar(d["tri_b"][:, :], jt[:, :], pt[:, :], 0.0,
                        ALU.subtract, ALU.is_ge)
        pcol = d["lmax_col"]
        e.tensor_scalar(pcol[:, :], pt[:, :], 0.5, -0.25, ALU.mult, ALU.add) \
            .then_inc(s_dve.h, 1)
        s_dve.n += 1
        e.wait_ge(s_dve.h, s_dve.n)
        e.tensor_scalar(pcol[:, :], pcol[:, :], RC, RC, ALU.add, ALU.subtract) \
            .then_inc(s_dve.h, 1)
        s_dve.n += 1
        e.wait_ge(s_dve.h, s_dve.n)
        e.scalar_tensor_tensor(pcol[:, :], pcol[:, :], -2.0, pt[:, :],
                               ALU.mult, ALU.add).then_inc(s_dve.h, 1)
        s_dve.n += 1
        e.wait_ge(s_dve.h, s_dve.n)
        scr1, scr2 = d["scr1"], d["scr2"]
        e.tensor_scalar(scr2[:, :], jt[:, :], pt[:, :], 1.0, ALU.subtract,
                        ALU.is_equal).then_inc(s_dve.h, 1)
        s_dve.n += 1
        e.wait_ge(s_dve.h, s_dve.n)
        e.tensor_scalar(scr1[:, :], scr2[:, :], pcol[:, :], None, ALU.mult) \
            .then_inc(s_dve.h, 1)
        s_dve.n += 1
        e.wait_ge(s_dve.h, s_dve.n)
        e.tensor_sub(scr2[:, :], scr2[:, :], scr1[:, :]).then_inc(s_dve.h, 1)
        s_dve.n += 1
        e.wait_ge(s_dve.h, s_dve.n)
        e.tensor_scalar(scr1[:, :], jt[:, :], pt[:, :], -1.0, ALU.subtract,
                        ALU.is_equal).then_inc(s_dve.h, 1)
        s_dve.n += 1
        e.wait_ge(s_dve.h, s_dve.n)
        e.tensor_scalar(scr1[:, :], scr1[:, :], pcol[:, :], None, ALU.mult) \
            .then_inc(s_dve.h, 1)
        s_dve.n += 1
        e.wait_ge(s_dve.h, s_dve.n)
        e.tensor_sub(d["rot_b"][:, :], scr2[:, :], scr1[:, :]).then_inc(s_dve.h, 1)
        s_dve.n += 1
    ctx.vector(fv)

    def fs(e):
        e.wait_ge(s_ld.h, s_ld.n)
        e.activation(d["ssig"][:, :], d["sclip"][:, :], AF.Sigmoid) \
            .then_inc(s_act.h, 1)
        s_act.n += 1
    ctx.scalar(fs)


def emit_weight_quant(ctx, d, scope):
    sb = d["sb"]
    s_ld, s_sc, s_dve, s_act, s_st = d["s_ld"], d["s_sc"], d["s_dve"], d["s_act"], d["s_st"]

    wst = sb("wst", [128, 2 * DI], F32, scope)
    wsq = sb("wsq", [128, 2 * DI], F32, scope)
    wqb = sb("wqb", [128, 2 * DI], BF16, scope)
    rcol = sb("rcol", [128, 4], F32, scope)
    tb = sb("tb", [128, 2 * DI], BF16, scope)

    jobs = []
    for (src, dst, wsc, nch, W) in [
        (d["wg_in"], d["wgq_nat"], d["wsc_g"], IC, DM),
        (d["wu_in"], d["wuq_nat"], d["wsc_u"], IC, DM),
        (d["wd_in"], d["wdq_nat"], d["wsc_d"], HC, DI),
    ]:
        for k in range(nch):
            jobs.append((src, dst, wsc, k, W))
    nj = len(jobs)
    dve0, act0, st0 = s_dve.n, s_act.n, s_st.n
    p_ld = mk_psems(ctx, d, "wql")
    p_st = mk_psems(ctx, d, "wqs")

    def fsync(e):
        for j, (src, dst, wsc, k, W) in enumerate(jobs):
            if j >= 2:
                e.wait_ge(s_act.h, act0 + j - 1)
            srcv = src.rearrange("(c p) m -> c p m", p=128)
            e.dma_start(out=wst[:, (j % 2) * DI:(j % 2) * DI + W],
                        in_=srcv[k]).then_inc(p_ld[j % 2].h, 16)
            p_ld[j % 2].n += 16
    ctx.sync(fsync)

    def fv(e):
        for j, (src, dst, wsc, k, W) in enumerate(jobs):
            sl = slice((j % 2) * DI, (j % 2) * DI + W)
            e.wait_ge(p_ld[j % 2].h, 16 * (j // 2 + 1))
            e.reduce_max(wsc[:, k:k + 1], wst[:, sl], AX.X,
                         apply_absolute_value=True).then_inc(s_dve.h, 1)
            e.wait_ge(s_dve.h, dve0 + 5 * j + 1)
            e.reciprocal(rcol[:, 2 * (j % 2):2 * (j % 2) + 1], wsc[:, k:k + 1]) \
                .then_inc(s_dve.h, 1)
            e.wait_ge(s_dve.h, dve0 + 5 * j + 2)
            e.tensor_scalar(rcol[:, 2 * (j % 2) + 1:2 * (j % 2) + 2],
                            rcol[:, 2 * (j % 2):2 * (j % 2) + 1], QMAX, None,
                            ALU.mult).then_inc(s_dve.h, 1)
            e.wait_ge(s_act.h, act0 + j + 1)
            if j >= 2:
                e.wait_ge(p_st[j % 2].h, 16 * (j // 2))
            e.tensor_scalar(wsq[:, sl], wsq[:, sl], QMAX, -QMAX - 1.0,
                            ALU.min, ALU.max).then_inc(s_dve.h, 1)
            e.wait_ge(s_dve.h, dve0 + 5 * j + 4)
            e.tensor_scalar(wqb[:, sl], wsq[:, sl], RC, RC, ALU.add,
                            ALU.subtract).then_inc(s_dve.h, 1)
        s_dve.n += 5 * nj
    ctx.vector(fv)

    def fs(e):
        for j, (src, dst, wsc, k, W) in enumerate(jobs):
            sl = slice((j % 2) * DI, (j % 2) * DI + W)
            e.wait_ge(s_dve.h, dve0 + 5 * j + 3)
            e.activation(wsq[:, sl], wst[:, sl], AF.Copy,
                         scale=rcol[:, 2 * (j % 2) + 1:2 * (j % 2) + 2]) \
                .then_inc(s_act.h, 1)
        s_act.n += nj
    ctx.scalar(fs)

    def fst(e):
        for j, (src, dst, wsc, k, W) in enumerate(jobs):
            sl = slice((j % 2) * DI, (j % 2) * DI + W)
            e.wait_ge(s_dve.h, dve0 + 5 * (j + 1))
            dstv = dst.rearrange("(c p) m -> c p m", p=128)
            e.dma_start(out=dstv[k], in_=wqb[:, sl]).then_inc(p_st[j % 2].h, 16)
            p_st[j % 2].n += 16
    ctx.gpsimd(fst)

    ctx.barrier()

    p_tr = mk_psems(ctx, d, "bnt")
    p_bw = mk_psems(ctx, d, "bnc")
    bounce_jobs = []
    for (natt, tt_, nch, W) in [
        (d["wgq_nat"], d["wgq_t"], HC, DI),
        (d["wuq_nat"], d["wuq_t"], HC, DI),
        (d["wdq_nat"], d["wdq_t"], IC, DM),
    ]:
        for k in range(nch):
            bounce_jobs.append((natt, tt_, k, W))

    def fb(e):
        for n, (natt, tt_, k, W) in enumerate(bounce_jobs):
            buf = tb[:, (n % 2) * DI:(n % 2) * DI + W]
            if n >= 2:
                e.wait_ge(p_bw[n % 2].h, 16 * (n // 2))
            e.dma_start_transpose(buf, natt[:, k * 128:(k + 1) * 128]) \
                .then_inc(p_tr[n % 2].h, 16)
            p_tr[n % 2].n += 16
    ctx.sync(fb)

    def fbw(e):
        for n, (natt, tt_, k, W) in enumerate(bounce_jobs):
            buf = tb[:, (n % 2) * DI:(n % 2) * DI + W]
            e.wait_ge(p_tr[n % 2].h, 16 * (n // 2 + 1))
            e.dma_start(out=tt_[k * 128:(k + 1) * 128, :], in_=buf) \
                .then_inc(p_bw[n % 2].h, 16)
            p_bw[n % 2].n += 16
        e.wait_ge(p_bw[0].h, p_bw[0].n)
        e.wait_ge(p_bw[1].h, p_bw[1].n)
    ctx.gpsimd(fbw)

    if "wgq" in d["dbg"]:
        s_gld = d["s_gld"]

        def fdbg(e):
            e.wait_ge(s_st.h, s_st.n)
            e.dma_start(out=d["dbg_out"]["wgq"][:, :], in_=d["wgq_nat"][:, :]) \
                .then_inc(s_gld.h, 16)
            s_gld.n += 16
            e.wait_ge(s_gld.h, s_gld.n)
        ctx.gpsimd(fdbg)
    ctx.barrier()


def emit_ln_stats(ctx, d, want_max):
    s_ld, s_dve, s_act = d["s_ld"], d["s_dve"], d["s_act"]
    xr = d["xr"]
    sqb = d["sqb"]
    act0 = s_act.n
    dve0 = s_dve.n

    xr_dve = dict(d.get("xr_dve", {}))

    def fs(e):
        for t in range(TT):
            if t in xr_dve:
                e.wait_ge(s_dve.h, xr_dve[t])
            if t >= 1:
                e.wait_ge(s_dve.h, dve0 + t)
            sl = slice(0, DM)
            e.activation(sqb[:, sl], xr[:, t * DM:(t + 1) * DM], AF.Square) \
                .then_inc(s_act.h, 1)
        s_act.n += TT
    ctx.scalar(fs)

    def fv(e):
        for t in range(TT):
            xt = xr[:, t * DM:(t + 1) * DM]
            sl = slice(0, DM)
            e.reduce_sum(d["st_a"][:, t:t + 1], xt, AX.X)
            if want_max:
                e.reduce_max(d["st_c"][:, t:t + 1], xt, AX.X)
                e.tensor_reduce(d["st_d"][:, t:t + 1], xt, AX.X, ALU.min)
            e.wait_ge(s_act.h, act0 + t + 1)
            e.reduce_sum(d["st_b"][:, t:t + 1], sqb[:, sl], AX.X) \
                .then_inc(s_dve.h, 1)
        s_dve.n += TT
        e.wait_ge(s_dve.h, s_dve.n)
        e.tensor_scalar(d["st_m"][:, :], d["st_a"][:, :], 1.0 / DM, None,
                        ALU.mult).then_inc(s_dve.h, 1)
        s_dve.n += 1
        e.tensor_scalar(d["st_b"][:, :], d["st_b"][:, :], 1.0 / DM, None,
                        ALU.mult).then_inc(s_dve.h, 1)
        s_dve.n += 1
        e.wait_ge(s_dve.h, s_dve.n)
        e.tensor_mul(d["st_a"][:, :], d["st_m"][:, :], d["st_m"][:, :]) \
            .then_inc(s_dve.h, 1)
        s_dve.n += 1
        e.wait_ge(s_dve.h, s_dve.n)
        e.scalar_tensor_tensor(d["st_b"][:, :], d["st_b"][:, :], EPS,
                               d["st_a"][:, :], ALU.add, ALU.subtract) \
            .then_inc(s_dve.h, 1)
        s_dve.n += 1
    ctx.vector(fv)

    def fs2(e):
        e.wait_ge(s_dve.h, s_dve.n)
        e.activation(d["st_r"][:, :], d["st_b"][:, :], AF.Sqrt).then_inc(s_act.h, 1)
        s_act.n += 1
    ctx.scalar(fs2)

    def fv2(e):
        e.wait_ge(s_act.h, s_act.n)
        e.reciprocal(d["st_r"][:, :], d["st_r"][:, :]).then_inc(s_dve.h, 1)
        s_dve.n += 1
        if want_max:
            e.wait_ge(s_dve.h, s_dve.n)
            e.tensor_sub(d["st_c"][:, :], d["st_c"][:, :], d["st_m"][:, :]) \
                .then_inc(s_dve.h, 1)
            s_dve.n += 1
            e.tensor_sub(d["st_d"][:, :], d["st_m"][:, :], d["st_d"][:, :]) \
                .then_inc(s_dve.h, 1)
            s_dve.n += 1
            e.wait_ge(s_dve.h, s_dve.n)
            e.tensor_max(d["st_c"][:, :], d["st_c"][:, :], d["st_d"][:, :]) \
                .then_inc(s_dve.h, 1)
            s_dve.n += 1
            e.wait_ge(s_dve.h, s_dve.n)
            e.tensor_mul(d["st_c"][:, :], d["st_c"][:, :], d["st_r"][:, :]) \
                .then_inc(s_dve.h, 1)
            s_dve.n += 1
            e.wait_ge(s_dve.h, s_dve.n)
            e.reduce_max(d["lmax_col"][:, :], d["st_c"][:, :], AX.X) \
                .then_inc(s_dve.h, 1)
            s_dve.n += 1
    ctx.vector(fv2)


def emit_maxreduce(ctx, d, slot):
    s_pe, s_dve, s_gld, s_cc, s_gp = d["s_pe"], d["s_dve"], d["s_gld"], d["s_cc"], d["s_gp"]
    pall = d["pall"]

    def ft(e):
        e.wait_ge(s_dve.h, s_dve.n)
        e.transpose(pall[0:1, 7 * 512:7 * 512 + 128], d["lmax_col"][:, :],
                    d["iden_f"][:, :]).then_inc(s_pe.h, 1)
        s_pe.n += 1
    ctx.tensor(ft)

    def fv(e):
        e.wait_ge(s_pe.h, s_pe.n)
        e.reduce_max(d["s_tmp"][:, 0:1], pall[0:1, 7 * 512:7 * 512 + 128], AX.X) \
            .then_inc(s_dve.h, 1)
        s_dve.n += 1
    ctx.vector(fv)

    def fg(e):
        e.memset(d["s_tmp"][:, 1:4], 0.0).then_inc(s_gp.h, 1)
        s_gp.n += 1
        e.wait_ge(s_gp.h, s_gp.n)
        e.wait_ge(s_dve.h, s_dve.n)
        e.dma_start(out=d["mx_in"][:, 0:4], in_=d["s_tmp"][:, 0:4]) \
            .then_inc(s_gld.h, 16)
        s_gld.n += 16
        e.wait_ge(s_gld.h, s_gld.n)
        e.collective_compute(
            "AllReduce", ALU.max, replica_groups=[list(range(NCORES))],
            ins=[d["mx_in"].ap()], outs=[d["mx_out"].ap()],
        ).then_inc(s_cc.h, 1)
        s_cc.n += 1
        e.wait_ge(s_cc.h, s_cc.n)
        e.dma_start(out=d["gmax"][:, slot:slot + 1], in_=d["mx_out"][:, 0:1]) \
            .then_inc(s_gld.h, 16)
        s_gld.n += 16
        e.wait_ge(s_gld.h, s_gld.n)
    ctx.gpsimd(fg)


def emit_scale_chain(ctx, d, slot, sig_idx):
    s_dve, s_pe, s_gld = d["s_dve"], d["s_pe"], d["s_gld"]
    pall = d["pall"]

    def fv(e):
        e.wait_ge(s_gld.h, s_gld.n)
        e.tensor_scalar(d["s_tmp"][:, 4:5], d["gmax"][:, slot:slot + 1],
                        d["ssig"][:, sig_idx:sig_idx + 1], 1.0 / QMAX,
                        ALU.mult, ALU.mult).then_inc(s_dve.h, 1)
        s_dve.n += 1
        e.wait_ge(s_dve.h, s_dve.n)
        e.reciprocal(d["s_tmp"][:, 5:6], d["s_tmp"][:, 4:5]).then_inc(s_dve.h, 1)
        s_dve.n += 1
    ctx.vector(fv)

    def ft(e):
        e.wait_ge(s_dve.h, s_dve.n)
        e.matmul(pall[:, 7 * 512:7 * 512 + 2], d["ones_row"][:, :],
                 d["s_tmp"][0:1, 4:6], start=True, stop=True).then_inc(s_pe.h, 1)
        s_pe.n += 1
    ctx.tensor(ft)

    def fv2(e):
        e.wait_ge(s_pe.h, s_pe.n)
        e.tensor_copy(d["binv"][:, 2 * slot:2 * slot + 1],
                      pall[:, 7 * 512 + 1:7 * 512 + 2])
        e.tensor_copy(d["binv"][:, 2 * slot + 1:2 * slot + 2],
                      pall[:, 7 * 512:7 * 512 + 1]).then_inc(s_dve.h, 1)
        s_dve.n += 1
        if slot % 2 == 1:
            e.wait_ge(s_dve.h, s_dve.n)
            e.tensor_scalar(d["hcol"][:, slot // 2:slot // 2 + 1],
                            d["binv"][:, 2 * slot + 1:2 * slot + 2], 0.5, None,
                            ALU.mult).then_inc(s_dve.h, 1)
            s_dve.n += 1
        if "gmax" in d["dbg"]:
            e.wait_ge(s_dve.h, s_dve.n)

            def fdbg(e2):
                e2.wait_ge(s_dve.h, s_dve.n)
                e2.dma_start(out=d["dbg_out"]["gmax"][:, :], in_=d["gmax"][:, :]) \
                    .then_inc(d["s_st"].h, 16)
                d["s_st"].n += 16
            ctx.gpsimd(fdbg)
    ctx.vector(fv2)


def emit_xq(ctx, d, slot):
    s_dve, s_st, s_sc = d["s_dve"], d["s_st"], d["s_sc"]
    qstage, qout, srs, xq_fm = d["qstage"], d["qout"], d["srs"], d["xq_fm"]
    dve0 = s_dve.n
    p_xs = mk_psems(ctx, d, "xqs%d" % slot)

    def fv(e):
        e.tensor_scalar(srs[:, :], d["st_r"][:, :],
                        d["binv"][:, 2 * slot:2 * slot + 1], None, ALU.mult) \
            .then_inc(s_dve.h, 1)
        e.wait_ge(s_dve.h, dve0 + 1)
        for t in range(TT):
            sl = slice((t % 2) * DM, (t % 2) * DM + DM)
            if t >= 2:
                e.wait_ge(p_xs[t % 2].h, 16 * (t // 2))
            xt = d["xr"][:, t * DM:(t + 1) * DM]
            e.tensor_scalar(qstage[:, sl], xt, d["st_m"][:, t:t + 1],
                            srs[:, t:t + 1], ALU.subtract, ALU.mult)
            e.tensor_scalar(qstage[:, sl], qstage[:, sl], QMAX, -QMAX - 1.0,
                            ALU.min, ALU.max)
            e.tensor_scalar(qout[:, sl], qstage[:, sl], RC, RC, ALU.add,
                            ALU.subtract).then_inc(s_dve.h, 1)
        s_dve.n += 1 + TT
    ctx.vector(fv)

    xq_t = d["xq_nat"].rearrange("(t p) m -> t p m", p=128)
    sc0 = s_sc.n

    def fst(e):
        for t in range(TT):
            sl = slice((t % 2) * DM, (t % 2) * DM + DM)
            e.wait_ge(s_dve.h, dve0 + 2 + t)
            e.dma_start(out=xq_t[t], in_=qout[:, sl]).then_inc(p_xs[t % 2].h, 16)
            p_xs[t % 2].n += 16
        e.wait_ge(p_xs[0].h, p_xs[0].n)
        e.wait_ge(p_xs[1].h, p_xs[1].n)
        for hc in range(HC):
            e.dma_start_transpose(xq_fm[:, hc * TOK:(hc + 1) * TOK],
                                  d["xq_nat"][:, hc * 128:(hc + 1) * 128]) \
                .then_inc(s_sc.h, 16)
        s_sc.n += 16 * HC
        e.wait_ge(s_sc.h, s_sc.n)
        if "xq" in d["dbg"]:
            e.dma_start(out=d["dbg_out"]["xq"][:, :], in_=d["xq_nat"][:, :]) \
                .then_inc(s_st.h, 16)
            s_st.n += 16
            e.wait_ge(s_st.h, s_st.n)
    ctx.sync(fst)


def emit_gateup(ctx, d, scope, slot):
    sb = d["sb"]
    s_ld, s_pe, s_dve, s_act, s_st = d["s_ld"], d["s_pe"], d["s_dve"], d["s_act"], d["s_st"]
    pall, xq_fm = d["pall"], d["xq_fm"]

    wgb = d.get("wgb") or sb("wgb", [128, 2 * HC * 128], BF16, scope)
    wub = d.get("wub") or sb("wub", [128, 2 * HC * 128], BF16, scope)
    sgt = d.get("sgt") or sb("sgt", [128, 2 * 512], F32, scope)
    upt = d.get("upt") or sb("upt", [128, 2 * 512], F32, scope)
    hft = d.get("hft") or sb("hft", [128, 2 * 512], F32, scope)
    hbt = d.get("hbt") or sb("hbt", [128, 2 * 512], F16, scope)
    svg = d.get("svg") or sb("svg", [128, IC], F32, scope)
    svu = d.get("svu") or sb("svu", [128, IC], F32, scope)
    for nm, v in [("wgb", wgb), ("wub", wub), ("sgt", sgt), ("upt", upt),
                  ("hft", hft), ("hbt", hbt), ("svg", svg), ("svu", svu)]:
        d[nm] = v

    NS = 4
    nu = IC * NS
    pe0, act0, dve0 = s_pe.n, s_act.n, s_dve.n
    p_wl = mk_psems(ctx, d, "gwl%d" % slot)
    p_hs = mk_psems(ctx, d, "ghs%d" % slot)
    wg_t = d["wgq_t"].rearrange("(c p) i -> p c i", p=128)
    wu_t = d["wuq_t"].rearrange("(c p) i -> p c i", p=128)

    def fsync(e):
        for ic in range(IC):
            if ic >= 2:
                e.wait_ge(s_pe.h, pe0 + (ic - 1) * NS)
            b = (ic % 2) * HC * 128
            e.dma_start(out=wgb[:, b:b + HC * 128].rearrange("p (c n) -> p c n", c=HC),
                        in_=wg_t[:, :, ic * 128:(ic + 1) * 128]).then_inc(p_wl[ic % 2].h, 16)
            e.dma_start(out=wub[:, b:b + HC * 128].rearrange("p (c n) -> p c n", c=HC),
                        in_=wu_t[:, :, ic * 128:(ic + 1) * 128]).then_inc(p_wl[ic % 2].h, 16)
            p_wl[ic % 2].n += 32
    ctx.sync(fsync)

    def fv0(e):
        e.tensor_scalar(svg[:, :], d["wsc_g"][:, :], d["binv"][:, 2 * slot + 1:2 * slot + 2],
                        1.0 / QMAX, ALU.mult, ALU.mult)
        e.tensor_scalar(svu[:, :], d["wsc_u"][:, :], d["binv"][:, 2 * slot + 1:2 * slot + 2],
                        1.0 / QMAX, ALU.mult, ALU.mult).then_inc(s_dve.h, 1)
        s_dve.n += 1
    ctx.vector(fv0)
    dve_pre = s_dve.n

    def ft(e):
        e.wait_ge(s_dve.h, dve_pre)
        u = 0
        for ic in range(IC):
            e.wait_ge(p_wl[ic % 2].h, 32 * (ic // 2 + 1))
            b = (ic % 2) * HC * 128
            for s in range(NS):
                if u >= 2:
                    e.wait_ge(s_dve.h, dve_pre + (u - 1) * 2)
                    e.wait_ge(s_act.h, act0 + u - 1)
                gb, ub = (u % 2) * 2, (u % 2) * 2 + 1
                for hc in range(HC):
                    e.matmul(pall[:, gb * 512:gb * 512 + 512],
                             wgb[:, b + hc * 128:b + hc * 128 + 128],
                             xq_fm[:, hc * TOK + s * 512:hc * TOK + (s + 1) * 512],
                             start=(hc == 0), stop=(hc == HC - 1))
                for hc in range(HC):
                    inst = e.matmul(
                        pall[:, ub * 512:ub * 512 + 512],
                        wub[:, b + hc * 128:b + hc * 128 + 128],
                        xq_fm[:, hc * TOK + s * 512:hc * TOK + (s + 1) * 512],
                        start=(hc == 0), stop=(hc == HC - 1))
                inst.then_inc(s_pe.h, 1)
                u += 1
        s_pe.n += nu
    ctx.tensor(ft)

    def fs(e):
        u = 0
        for ic in range(IC):
            for s in range(NS):
                e.wait_ge(s_pe.h, pe0 + u + 1)
                if u >= 2:
                    e.wait_ge(s_dve.h, dve_pre + (u - 1) * 2)
                gb = (u % 2) * 2
                e.activation(sgt[:, (u % 2) * 512:(u % 2) * 512 + 512],
                             pall[:, gb * 512:gb * 512 + 512], AF.Silu,
                             scale=svg[:, ic:ic + 1]).then_inc(s_act.h, 1)
                u += 1
        s_act.n += nu
    ctx.scalar(fs)

    def fv(e):
        u = 0
        for ic in range(IC):
            for s in range(NS):
                e.wait_ge(s_pe.h, pe0 + u + 1)
                ub = (u % 2) * 2 + 1
                if u >= 2:
                    e.wait_ge(p_hs[u % 2].h, 16 * (u // 2))
                e.tensor_scalar(upt[:, (u % 2) * 512:(u % 2) * 512 + 512],
                                pall[:, ub * 512:ub * 512 + 512],
                                svu[:, ic:ic + 1], None, ALU.mult)
                e.wait_ge(s_act.h, act0 + u + 1)
                e.tensor_mul(hft[:, (u % 2) * 512:(u % 2) * 512 + 512],
                             sgt[:, (u % 2) * 512:(u % 2) * 512 + 512],
                             upt[:, (u % 2) * 512:(u % 2) * 512 + 512])
                e.tensor_copy(hbt[:, (u % 2) * 512:(u % 2) * 512 + 512],
                              hft[:, (u % 2) * 512:(u % 2) * 512 + 512]) \
                    .then_inc(s_dve.h, 1)
                e.reduce_max(d["hmax_cols"][:, u:u + 1],
                             hft[:, (u % 2) * 512:(u % 2) * 512 + 512], AX.X,
                             apply_absolute_value=True).then_inc(s_dve.h, 1)
                u += 1
        s_dve.n += 2 * nu
    ctx.vector(fv)

    h_t = d["h_nat"].rearrange("(c p) n -> c p n", p=128)

    def fst(e):
        u = 0
        for ic in range(IC):
            for s in range(NS):
                e.wait_ge(s_dve.h, dve_pre + 2 * u + 2)
                e.dma_start(out=h_t[ic, :, s * 512:(s + 1) * 512],
                            in_=hbt[:, (u % 2) * 512:(u % 2) * 512 + 512]) \
                    .then_inc(p_hs[u % 2].h, 16)
                p_hs[u % 2].n += 16
                u += 1
        e.wait_ge(p_hs[0].h, p_hs[0].n)
        e.wait_ge(p_hs[1].h, p_hs[1].n)
        if "h" in d["dbg"]:
            e.dma_start(out=d["dbg_out"]["h"][:, :], in_=d["h_nat"][:, :]) \
                .then_inc(s_st.h, 16)
            s_st.n += 16
            e.wait_ge(s_st.h, s_st.n)
    ctx.gpsimd(fst)


def emit_hmax_finish(ctx, d):
    s_dve = d["s_dve"]

    def fv(e):
        e.reduce_max(d["lmax_col"][:, :], d["hmax_cols"][:, :], AX.X) \
            .then_inc(s_dve.h, 1)
        s_dve.n += 1
    ctx.vector(fv)


def emit_hquant(ctx, d, scope, slot):
    sb = d["sb"]
    s_ld, s_dve, s_act, s_st = d["s_ld"], d["s_dve"], d["s_act"], d["s_st"]
    hst = d.get("hst") or sb("hst", [128, 2 * TOK], F16, scope)
    hqo = d.get("hqo") or sb("hqo", [128, 2 * TOK], BF16, scope)
    hfs = d.get("hfs") or sb("hfs", [128, 2 * TOK], F32, scope)
    d["hst"], d["hqo"], d["hfs"] = hst, hqo, hfs

    h_t = d["h_nat"].rearrange("(c p) n -> c p n", p=128)
    hq_t = d["hq_nat"].rearrange("(c p) n -> c p n", p=128)
    act0, dve0 = s_act.n, s_dve.n
    p_hl = mk_psems(ctx, d, "hql%d" % slot)
    p_qs = mk_psems(ctx, d, "hqs%d" % slot)

    def fsync(e):
        for ic in range(IC):
            if ic >= 2:
                e.wait_ge(p_qs[ic % 2].h, 16 * (ic // 2))
            e.dma_start(out=hst[:, (ic % 2) * TOK:(ic % 2 + 1) * TOK],
                        in_=h_t[ic]).then_inc(p_hl[ic % 2].h, 16)
            p_hl[ic % 2].n += 16
    ctx.sync(fsync)

    def fs(e):
        for ic in range(IC):
            sl = slice((ic % 2) * TOK, (ic % 2) * TOK + TOK)
            e.wait_ge(p_hl[ic % 2].h, 16 * (ic // 2 + 1))
            if ic >= 2:
                e.wait_ge(s_dve.h, dve0 + 2 * (ic - 1))
            e.activation(hfs[:, sl], hst[:, sl], AF.Copy,
                         scale=d["binv"][:, 2 * slot:2 * slot + 1]) \
                .then_inc(s_act.h, 1)
        s_act.n += IC
    ctx.scalar(fs)

    def fv(e):
        for ic in range(IC):
            sl = slice((ic % 2) * TOK, (ic % 2) * TOK + TOK)
            e.wait_ge(s_act.h, act0 + ic + 1)
            if ic >= 2:
                e.wait_ge(p_qs[ic % 2].h, 16 * (ic // 2))
            e.tensor_scalar(hfs[:, sl], hfs[:, sl], QMAX, -QMAX - 1.0,
                            ALU.min, ALU.max)
            e.tensor_scalar(hqo[:, sl], hfs[:, sl], RC, RC, ALU.add,
                            ALU.subtract).then_inc(s_dve.h, 2)
        s_dve.n += 2 * IC
    ctx.vector(fv)

    def fst(e):
        for ic in range(IC):
            sl = slice((ic % 2) * TOK, (ic % 2) * TOK + TOK)
            e.wait_ge(s_dve.h, dve0 + 2 * (ic + 1))
            e.dma_start(out=hq_t[ic], in_=hqo[:, sl]).then_inc(p_qs[ic % 2].h, 16)
            p_qs[ic % 2].n += 16
        e.wait_ge(p_qs[0].h, p_qs[0].n)
        e.wait_ge(p_qs[1].h, p_qs[1].n)
        if "hq" in d["dbg"]:
            e.dma_start(out=d["dbg_out"]["hq"][:, :], in_=d["hq_nat"][:, :]) \
                .then_inc(s_st.h, 16)
            s_st.n += 16
            e.wait_ge(s_st.h, s_st.n)
    ctx.gpsimd(fst)


def emit_bsd(ctx, d):
    s_pe, s_dve = d["s_pe"], d["s_dve"]
    pall = d["pall"]

    for hc in range(HC):
        def ft(e, hc=hc):
            e.wait_ge(s_dve.h, s_dve.n)
            e.transpose(pall[0:1, 6 * 512:6 * 512 + 128], d["wsc_d"][:, hc:hc + 1],
                        d["iden_f"][:, :]).then_inc(s_pe.h, 1)
            s_pe.n += 1
        ctx.tensor(ft)

        def fv(e, hc=hc):
            e.wait_ge(s_pe.h, s_pe.n)
            e.tensor_copy(d["brow"][:, :], pall[0:1, 6 * 512:6 * 512 + 128]) \
                .then_inc(s_dve.h, 1)
            s_dve.n += 1
        ctx.vector(fv)

        def ft2(e, hc=hc):
            e.wait_ge(s_dve.h, s_dve.n)
            e.matmul(pall[:, 7 * 512:7 * 512 + 128], d["ones_row"][:, :],
                     d["brow"][:, :], start=True, stop=True).then_inc(s_pe.h, 1)
            s_pe.n += 1
        ctx.tensor(ft2)

        def fv2(e, hc=hc):
            e.wait_ge(s_pe.h, s_pe.n)
            e.tensor_scalar(d["bsd"][:, hc * 128:(hc + 1) * 128],
                            pall[:, 7 * 512:7 * 512 + 128], 1.0 / QMAX, None,
                            ALU.mult).then_inc(s_dve.h, 1)
            s_dve.n += 1
        ctx.vector(fv2)


def emit_down(ctx, d, scope, app):
    sb = d["sb"]
    s_ld, s_pe, s_dve, s_act = d["s_ld"], d["s_pe"], d["s_dve"], d["s_act"]
    pall = d["pall"]
    slot = 2 * app + 1

    wdr = sb("wdr", [128, IC * DM], BF16, scope)
    hstage = sb("hstage", [128, 2 * IC * 256], F16, scope)
    hfs = sb("hfs", [128, IC * 256], F32, scope)
    hqb = sb("hqb", [128, 2 * IC * 256], BF16, scope)
    dst1 = sb("dst1", [128, DM], F32, scope)

    NTP = TT // 2
    CW = IC * 256
    h_v = d["h_nat"].rearrange("(c p) n -> p c n", p=128)
    wd_v = d["wdq_t"].rearrange("(c p) m -> p c m", p=128)
    p_wd = mk_psems(ctx, d, "dwd%d" % app)
    p_hl = mk_psems(ctx, d, "dhl%d" % app)
    pe0, dve0, act0 = s_pe.n, s_dve.n, s_act.n

    def fw(e):
        e.dma_start(out=wdr[:, :].rearrange("p (c m) -> p c m", c=IC),
                    in_=wd_v).then_inc(p_wd[0].h, 16)
        p_wd[0].n += 16
    ctx.sync(fw)

    def fhl(e):
        for tp in range(NTP):
            if tp >= 2:
                e.wait_ge(s_act.h, act0 + tp - 1)
            e.dma_start(out=hstage[:, (tp % 2) * CW:(tp % 2 + 1) * CW]
                        .rearrange("p (c n) -> p c n", c=IC),
                        in_=h_v[:, :, tp * 256:(tp + 1) * 256]) \
                .then_inc(p_hl[tp % 2].h, 16)
            p_hl[tp % 2].n += 16
    ctx.gpsimd(fhl)

    def fa(e):
        for tp in range(NTP):
            e.wait_ge(p_hl[tp % 2].h, 16 * (tp // 2 + 1))
            if tp >= 1:
                e.wait_ge(s_dve.h, dve0 + 5 * (tp - 1) + 1)
            e.activation(hfs[:, :], hstage[:, (tp % 2) * CW:(tp % 2 + 1) * CW],
                         AF.Copy, scale=d["binv"][:, 2 * slot:2 * slot + 1]) \
                .then_inc(s_act.h, 1)
        s_act.n += NTP
    ctx.scalar(fa)

    cnt_quant = {}
    c = dve0
    for tp in range(NTP):
        c += 1
        cnt_quant[tp] = c
        if tp >= 1:
            c += 4

    def emit_evac(e, tp):
        e.wait_ge(s_pe.h, pe0 + (tp + 1) * IC * 4)
        for ti in range(2):
            t = tp * 2 + ti
            bank0 = (tp % 2) * 4 + ti * 2
            e.tensor_mul(dst1[:, :],
                         pall[:, bank0 * 512:bank0 * 512 + 1024], d["bsd"][:, :])
            e.scalar_tensor_tensor(
                d["xr"][:, t * DM:(t + 1) * DM],
                dst1[:, :],
                d["hcol"][:, app:app + 1],
                d["xr"][:, t * DM:(t + 1) * DM],
                ALU.mult, ALU.add).then_inc(s_dve.h, 2)
            s_dve.n += 2
            d.setdefault("xr_dve", {})[t] = s_dve.n

    def fqv(e):
        for tp in range(NTP):
            e.wait_ge(s_act.h, act0 + tp + 1)
            if tp >= 2:
                e.wait_ge(s_pe.h, pe0 + 88 * (tp - 1))
            e.tensor_scalar(hfs[:, :], hfs[:, :], QMAX, -QMAX - 1.0,
                            ALU.min, ALU.max)
            e.tensor_scalar(hqb[:, (tp % 2) * CW:(tp % 2 + 1) * CW], hfs[:, :],
                            RC, RC, ALU.add, ALU.subtract).then_inc(s_dve.h, 1)
            s_dve.n += 1
            if tp >= 1:
                emit_evac(e, tp - 1)
        emit_evac(e, NTP - 1)
    ctx.vector(fqv)

    def ft(e):
        e.wait_ge(p_wd[0].h, 16)
        for tp in range(NTP):
            e.wait_ge(s_dve.h, cnt_quant[tp])
            for ic in range(IC):
                hb = (tp % 2) * CW
                for ti in range(2):
                    for half in range(2):
                        bank = (tp % 2) * 4 + ti * 2 + half
                        ii = e.matmul(
                            pall[:, bank * 512:bank * 512 + 512],
                            hqb[:, hb + ic * 256 + ti * 128:hb + ic * 256 + ti * 128 + 128],
                            wdr[:, ic * DM + half * 512:ic * DM + half * 512 + 512],
                            start=(ic == 0), stop=(ic == IC - 1))
                ii.then_inc(s_pe.h, 4)
        s_pe.n += 4 * NTP * IC
    ctx.tensor(ft)


PI = 3.14159265358979323846


def emit_attention(ctx, d, scope):
    nc = ctx.nc
    sb = d["sb"]
    s_pe, s_dve, s_act, s_gp = d["s_pe"], d["s_dve"], d["s_act"], d["s_gp"]
    s_ld, s_sc, s_gld, s_st, s_cc = d["s_ld"], d["s_sc"], d["s_gld"], d["s_st"], d["s_cc"]
    pall = d["pall"]

    emit_ln_stats(ctx, d, want_max=False)
    ctx.barrier()

    battn = sb("battn", [128, 4], F32, scope)
    ones_col = sb("ones_col", [128, 1], BF16, scope)
    core_scope = ExitStack()
    qf = sb("qf", [128, 4 * S], BF16, core_scope)
    kf = sb("kf", [128, 4 * S], BF16, core_scope)

    xst_scope = ExitStack()
    xst = sb("xst", [128, DM], BF16, xst_scope)
    p_xl = mk_psems(ctx, d, "axl")
    dve0 = s_dve.n

    def fv_ln(e):
        e.memset(ones_col[:, :], 1.0)
        for t in range(TT):
            sl = slice(0, DM)
            if t >= 1:
                e.wait_ge(p_xl[0].h, 16 * t)
            e.tensor_scalar(xst[:, sl], d["xr"][:, t * DM:(t + 1) * DM],
                            d["st_m"][:, t:t + 1], d["st_r"][:, t:t + 1],
                            ALU.subtract, ALU.mult).then_inc(s_dve.h, 1)
        s_dve.n += TT
    ctx.vector(fv_ln)

    ag_in_t = d["ag_xln_in"].rearrange("(t p) m -> t p m", p=128)

    def fst_ln(e):
        for t in range(TT):
            e.wait_ge(s_dve.h, dve0 + t + 1)
            e.dma_start(out=ag_in_t[t], in_=xst[:, :]).then_inc(p_xl[0].h, 16)
            p_xl[0].n += 16
    ctx.sync(fst_ln)

    def fcc(e):
        e.wait_ge(p_xl[0].h, p_xl[0].n)
        e.wait_ge(p_xl[1].h, p_xl[1].n)
        e.collective_compute(
            "AllGather", ALU.bypass,
            replica_groups=[[2 * i, 2 * i + 1] for i in range(NCORES // 2)],
            ins=[d["ag_xln_in"].ap().opt()], outs=[d["ag_xln_out"].ap().opt()],
        ).then_inc(s_cc.h, 1)
        s_cc.n += 1
        if "xln" in d["dbg"]:
            e.wait_ge(s_cc.h, s_cc.n)
            e.dma_start(out=d["dbg_out"]["xln"][:, :], in_=d["ag_xln_out"][:, :]) \
                .then_inc(s_gld.h, 16)
            s_gld.n += 16
            e.wait_ge(s_gld.h, s_gld.n)
    ctx.gpsimd(fcc)
    xst_scope.close()

    with ExitStack() as qscope:
        xfs = sb("xfs", [128, 2 * HC * 512], BF16, qscope)
        cos_t = sb("cos_t", [128, S], BF16, qscope)
        sin_t = sb("sin_t", [128, S], BF16, qscope)
        tmpf = sb("tmpf", [128, 2 * 512], F32, qscope)
        pio2 = sb("pio2", [128, 1], F32, qscope)
        qraw = sb("qraw", [128, 2 * 512], BF16, qscope)
        posrow = sb("posrow", [1, 512], F32, qscope)
        ivf = sb("ivf", [128, 1], F32, qscope)
        dummy_split_marker = None

        def fv_ivf(e):
            e.memset(pio2[:, :], PI / 2)
            pt = d["iota_p"]
            for step in range(5):
                ops = [
                    lambda: e.tensor_scalar(ivf[:, :], pt[:, :], 1.0 / 64,
                                            -63.0 / 128, ALU.mult, ALU.add),
                    lambda: e.tensor_scalar(ivf[:, :], ivf[:, :], RC, RC,
                                            ALU.add, ALU.subtract),
                    lambda: e.scalar_tensor_tensor(ivf[:, :], ivf[:, :], -64.0,
                                                   pt[:, :], ALU.mult, ALU.add),
                    lambda: e.tensor_scalar(ivf[:, :], ivf[:, :], 0.5, -0.25,
                                            ALU.mult, ALU.add),
                    lambda: e.tensor_scalar(ivf[:, :], ivf[:, :], RC, RC,
                                            ALU.add, ALU.subtract),
                ]
                ops[step]().then_inc(s_dve.h, 1)
                s_dve.n += 1
                e.wait_ge(s_dve.h, s_dve.n)
        ctx.vector(fv_ivf)

        def fs_ivf(e):
            e.wait_ge(s_dve.h, s_dve.n)
            import math
            e.activation(ivf[:, :], ivf[:, :], AF.Exp, scale=-math.log(10000.0) / 32) \
                .then_inc(s_act.h, 1)
            s_act.n += 1
        ctx.scalar(fs_ivf)

        for sp in range(8):
            def fgp(e, sp=sp):
                if sp == 0:
                    e.wait_ge(s_act.h, s_act.n)
                else:
                    e.wait_ge(s_dve.h, s_dve.n)
                e.iota(posrow[:, :], [[1, 512]], base=sp * 512,
                       channel_multiplier=0,
                       allow_small_or_imprecise_dtypes=True).then_inc(s_gp.h, 1)
                s_gp.n += 1
            ctx.gpsimd(fgp)

            def ft(e, sp=sp):
                e.wait_ge(s_gp.h, s_gp.n)
                e.wait_ge(s_dve.h, s_dve.n)
                e.matmul(pall[:, 6 * 512:7 * 512], d["ones_row"][:, :],
                         posrow[:, :], start=True, stop=True).then_inc(s_pe.h, 1)
                s_pe.n += 1
            ctx.tensor(ft)

            def fv(e, sp=sp):
                e.wait_ge(s_pe.h, s_pe.n)
                ang = tmpf[:, 0:512]
                r = tmpf[:, 512:1024]
                e.tensor_scalar(ang[:, :], pall[:, 6 * 512:7 * 512], ivf[:, :],
                                None, ALU.mult).then_inc(s_dve.h, 1)
                s_dve.n += 1
                e.wait_ge(s_dve.h, s_dve.n)
                e.tensor_scalar(r[:, :], ang[:, :], 1.0 / (2 * PI), None,
                                ALU.mult).then_inc(s_dve.h, 1)
                s_dve.n += 1
                e.wait_ge(s_dve.h, s_dve.n)
                e.tensor_scalar(r[:, :], r[:, :], RC, RC, ALU.add, ALU.subtract) \
                    .then_inc(s_dve.h, 1)
                s_dve.n += 1
                e.wait_ge(s_dve.h, s_dve.n)
                e.scalar_tensor_tensor(r[:, :], r[:, :], -2 * PI, ang[:, :],
                                       ALU.mult, ALU.add).then_inc(s_dve.h, 1)
                s_dve.n += 1
            ctx.vector(fv)

            def fs(e, sp=sp):
                e.wait_ge(s_dve.h, s_dve.n)
                e.activation(sin_t[:, sp * 512:(sp + 1) * 512], tmpf[:, 512:1024],
                             AF.Sin).then_inc(s_act.h, 1)
                s_act.n += 1
            ctx.scalar(fs)

            def fv2(e, sp=sp):
                ang = tmpf[:, 0:512]
                r = tmpf[:, 512:1024]
                e.wait_ge(s_act.h, s_act.n)
                e.tensor_scalar(r[:, :], ang[:, :], 1.0 / (2 * PI), 0.25,
                                ALU.mult, ALU.add).then_inc(s_dve.h, 1)
                s_dve.n += 1
                e.wait_ge(s_dve.h, s_dve.n)
                e.tensor_scalar(r[:, :], r[:, :], RC, RC, ALU.add, ALU.subtract) \
                    .then_inc(s_dve.h, 1)
                s_dve.n += 1
                e.wait_ge(s_dve.h, s_dve.n)
                e.scalar_tensor_tensor(r[:, :], r[:, :], -2 * PI, ang[:, :],
                                       ALU.mult, ALU.add).then_inc(s_dve.h, 1)
                s_dve.n += 1
            ctx.vector(fv2)

            def fs2(e, sp=sp):
                e.wait_ge(s_dve.h, s_dve.n)
                e.activation(cos_t[:, sp * 512:(sp + 1) * 512], tmpf[:, 512:1024],
                             AF.Sin, bias=pio2[:, :]).then_inc(s_act.h, 1)
                s_act.n += 1
            ctx.scalar(fs2)

        if "rotb" in d["dbg"]:
            def fdr(e):
                e.dma_start(out=d["dbg_out"]["rotb"][:, :], in_=d["rot_b"][:, :]) \
                    .then_inc(s_gld.h, 16)
                s_gld.n += 16
                e.wait_ge(s_gld.h, s_gld.n)
            ctx.gpsimd(fdr)
        if d["dbg"] & {"cos", "sin", "ivf"}:
            ctx.barrier()
            def fdbg_t(e):
                for nm, buf in (("cos", cos_t), ("sin", sin_t), ("ivf", ivf)):
                    if nm in d["dbg"]:
                        e.dma_start(out=d["dbg_out"][nm][:, :], in_=buf[:, :]) \
                            .then_inc(s_gld.h, 16)
                        s_gld.n += 16
                e.wait_ge(s_gld.h, s_gld.n)
            ctx.gpsimd(fdbg_t)
            ctx.barrier()
        for (w_in, dstf, pname) in ((d["wq_in"], qf, "aq"), (d["wk_in"], kf, "ak")):
            with ExitStack() as wsc:
                wqa = sb("w" + pname, [128, HC * 512], BF16, wsc)
                wq_v = w_in.rearrange("(c p) m -> p c m", p=128)

                def fwl(e, wqa=wqa, wq_v=wq_v):
                    e.wait_ge(s_pe.h, s_pe.n)
                    e.dma_start(out=wqa[:, :].rearrange("p (c m) -> p c m", c=HC),
                                in_=wq_v).then_inc(s_gld.h, 16)
                    s_gld.n += 16
                    e.wait_ge(s_cc.h, s_cc.n)
                ctx.gpsimd(fwl)
                ctx.barrier()

                p_tr = mk_psems(ctx, d, "atr" + pname)
                dve_pass0 = s_dve.n
                pe_pass0 = s_pe.n
                for sp in range(8):
                    def ftr(e, sp=sp, p_tr=p_tr):
                        if sp >= 2:
                            e.wait_ge(s_pe.h, pe_pass0 + 8 * (sp - 1))
                        xb = (sp % 2) * HC * 512
                        for hc in range(HC):
                            e.dma_start_transpose(
                                xfs[:, xb + hc * 512:xb + (hc + 1) * 512],
                                d["ag_xln_out"][sp * 512:(sp + 1) * 512,
                                                hc * 128:(hc + 1) * 128]) \
                                .then_inc(p_tr[sp % 2].h, 16)
                            p_tr[sp % 2].n += 16
                    ctx.sync(ftr)

                    for mc in range(4):
                        n_u = sp * 4 + mc
                        st_ = (n_u % 2) * 4
                        qb_, rq_ = st_, st_ + 2
                        qsl = slice((n_u % 2) * 512, (n_u % 2) * 512 + 512)
                        xb = (sp % 2) * HC * 512

                        def ft(e, sp=sp, mc=mc, qb_=qb_, wqa=wqa, p_tr=p_tr,
                               n_u=n_u, xb=xb):
                            e.wait_ge(p_tr[sp % 2].h, 128 * (sp // 2 + 1))
                            if n_u >= 2:
                                e.wait_ge(s_dve.h, dve_pass0 + 3 * (n_u - 1))
                            for hc in range(HC):
                                inst = e.matmul(
                                    pall[:, qb_ * 512:qb_ * 512 + 512],
                                    wqa[:, hc * 512 + mc * 128:hc * 512 + mc * 128 + 128],
                                    xfs[:, xb + hc * 512:xb + (hc + 1) * 512],
                                    start=(hc == 0), stop=(hc == HC - 1))
                            inst.then_inc(s_pe.h, 1)
                            s_pe.n += 1
                        ctx.tensor(ft)

                        def fv_cp(e, qb_=qb_, qsl=qsl):
                            e.wait_ge(s_pe.h, s_pe.n)
                            e.tensor_copy(qraw[:, qsl],
                                          pall[:, qb_ * 512:qb_ * 512 + 512]) \
                                .then_inc(s_dve.h, 1)
                            s_dve.n += 1
                        ctx.vector(fv_cp)
                        if sp == 0 and mc == 3 and pname == "aq" and \
                                "qraw0" in d["dbg"]:
                            def fdq(e):
                                e.wait_ge(s_dve.h, s_dve.n)
                                e.dma_start(out=d["dbg_out"]["qraw0"][:, :],
                                            in_=qraw[:, :]).then_inc(s_gld.h, 16)
                                s_gld.n += 16
                                e.wait_ge(s_gld.h, s_gld.n)
                            ctx.gpsimd(fdq)
                            def fdx(e):
                                e.dma_start(out=d["dbg_out"]["xfs0"][:, :],
                                            in_=xfs[:, :]).then_inc(s_gld.h, 16)
                                s_gld.n += 16
                                e.wait_ge(s_gld.h, s_gld.n)
                            ctx.gpsimd(fdx)

                        def ft_rot(e, rq_=rq_, qsl=qsl):
                            e.wait_ge(s_dve.h, s_dve.n)
                            e.matmul(pall[:, rq_ * 512:rq_ * 512 + 512],
                                     d["rot_b"][:, :], qraw[:, qsl],
                                     start=True, stop=True).then_inc(s_pe.h, 1)
                            s_pe.n += 1
                        ctx.tensor(ft_rot)

                        def fv_rope(e, sp=sp, mc=mc, rq_=rq_, dstf=dstf,
                                    qsl=qsl):
                            cs = cos_t[:, sp * 512:(sp + 1) * 512]
                            sn = sin_t[:, sp * 512:(sp + 1) * 512]
                            e.tensor_mul(tmpf[:, 0:512], qraw[:, qsl], cs)
                            e.wait_ge(s_pe.h, s_pe.n)
                            e.tensor_mul(tmpf[:, 512:1024],
                                         pall[:, rq_ * 512:rq_ * 512 + 512], sn) \
                                .then_inc(s_dve.h, 1)
                            s_dve.n += 1
                            e.wait_ge(s_dve.h, s_dve.n)
                            e.tensor_add(dstf[:, mc * S + sp * 512:mc * S + sp * 512 + 512],
                                         tmpf[:, 0:512], tmpf[:, 512:1024]) \
                                .then_inc(s_dve.h, 1)
                            s_dve.n += 1
                        ctx.vector(fv_rope)

        dummy_end_qk = None
    ctx.barrier()
    vt = sb("vt", [128, ST * 520], BF16, core_scope)

    def fv_vt(e):
        e.memset(vt[:, :], 1.0).then_inc(s_dve.h, 1)
        s_dve.n += 1
    ctx.vector(fv_vt)

    with ExitStack() as qscope:
        xfs = sb("xfs", [128, 2 * HC * 512], BF16, qscope)
        qraw = sb("qraw", [128, 512], BF16, qscope)
        nrm = sb("nrm", [1, 64], F32, qscope)
        nfin = sb("nfin", [1, 8], F32, qscope)
        with ExitStack() as wsc:
            wva = sb("wva", [128, HC * 512], BF16, wsc)
            wv_v = d["wv_in"].rearrange("(c p) m -> p c m", p=128)

            def fwl(e):
                e.wait_ge(s_pe.h, s_pe.n)
                e.wait_ge(s_dve.h, s_dve.n)
                e.dma_start(out=wva[:, :].rearrange("p (c m) -> p c m", c=HC),
                            in_=wv_v).then_inc(s_gld.h, 16)
                s_gld.n += 16
            ctx.gpsimd(fwl)
            ctx.barrier()

            p_tr2 = mk_psems(ctx, d, "atr2")
            pe_v0 = s_pe.n
            for sp in range(8):
                def ftr(e, sp=sp):
                    if sp >= 2:
                        e.wait_ge(s_pe.h, pe_v0 + 4 * (sp - 1))
                    xb = (sp % 2) * HC * 512
                    for hc in range(HC):
                        e.dma_start_transpose(
                            xfs[:, xb + hc * 512:xb + (hc + 1) * 512],
                            d["ag_xln_out"][sp * 512:(sp + 1) * 512,
                                            hc * 128:(hc + 1) * 128]) \
                            .then_inc(p_tr2[sp % 2].h, 16)
                        p_tr2[sp % 2].n += 16
                ctx.sync(ftr)

                for ti4 in range(4):
                    st_ = ((sp * 4 + ti4) % 2) * 4
                    xb = (sp % 2) * HC * 512
                    def ft(e, sp=sp, ti4=ti4, st_=st_, xb=xb):
                        e.wait_ge(p_tr2[sp % 2].h, 128 * (sp // 2 + 1))
                        if sp * 4 + ti4 >= 2:
                            e.wait_ge(s_dve.h, s_dve.n - 1)
                        for hc in range(HC):
                            inst = e.matmul(
                                pall[:, st_ * 512:st_ * 512 + 512],
                                xfs[:, xb + hc * 512 + ti4 * 128:xb + hc * 512 + ti4 * 128 + 128],
                                wva[:, hc * 512:(hc + 1) * 512],
                                start=(hc == 0), stop=(hc == HC - 1))
                        inst.then_inc(s_pe.h, 1)
                        s_pe.n += 1
                    ctx.tensor(ft)

                    def fv(e, sp=sp, ti4=ti4, st_=st_):
                        kb = sp * 4 + ti4
                        e.wait_ge(s_pe.h, s_pe.n)
                        e.tensor_copy(
                            vt[:, kb * 520:(kb + 1) * 520]
                            .rearrange("p (h c) -> p h c", h=8)[:, :, 0:64],
                            pall[:, st_ * 512:st_ * 512 + 512]
                            .rearrange("p (h c) -> p h c", h=8)) \
                            .then_inc(s_dve.h, 1)
                        s_dve.n += 1
                    ctx.vector(fv)

            def fvb(e):
                e.memset(battn[:, :], -32.0).then_inc(s_dve.h, 1)
                s_dve.n += 1
            ctx.vector(fvb)
    ctx.barrier()
    if d["dbg"] & {"qf", "kf", "vt", "battn"}:
        def fdbg(e):
            for nm, buf in (("qf", qf), ("kf", kf), ("vt", vt), ("battn", battn)):
                if nm in d["dbg"]:
                    e.dma_start(out=d["dbg_out"][nm][:, :], in_=buf[:, :]) \
                        .then_inc(s_gld.h, 16)
                    s_gld.n += 16
            e.wait_ge(s_gld.h, s_gld.n)
        ctx.gpsimd(fdbg)
        ctx.barrier()

    with ExitStack() as sscope:
        av_fm = sb("av_fm", [128, 4 * S], BF16, sscope)
        ebAB = sb("ebAB", [128, 3 * 1024], BF16, sscope)
        nrow = d["sqb"][0:1, 0:512]
        bcs = d["sqb"][0:64, 512:768].bitcast(BF16)

        uidx = 0
        evac_end = {}
        for hp in range(4):
            for qb in range(8):
                nkb = 4 * qb + 4
                pe0 = s_pe.n
                act0 = s_act.n
                dve_u0 = s_dve.n
                avb = 4 + 2 * (uidx % 2)

                pe_sc = {}
                pe_av = {}
                _c = pe0
                for kb in range(nkb):
                    _c += 1
                    pe_sc[kb] = _c
                    if kb >= 2:
                        _c += 1
                        pe_av[kb - 2] = _c
                _c += 1
                pe_av[nkb - 2] = _c
                _c += 1
                pe_av[nkb - 1] = _c

                def emit_av(e, kb, hp, nkb, act0, dve_u0, qb, avb):
                    e.wait_ge(s_act.h, act0 + kb + 1)
                    if kb >= 4 * qb:
                        e.wait_ge(s_dve.h, dve_u0 + (kb - 4 * qb + 1))
                    ebs = (kb % 3) * 1024
                    e.matmul(pall[:, avb * 512:avb * 512 + 512][0:65, :],
                             vt[:, kb * 520 + (2 * hp) * 65:kb * 520 + (2 * hp) * 65 + 65],
                             ebAB[:, ebs:ebs + 512],
                             start=(kb == 0), stop=(kb == nkb - 1))
                    e.matmul(pall[:, (avb + 1) * 512:(avb + 1) * 512 + 512][0:65, :],
                             vt[:, kb * 520 + (2 * hp + 1) * 65:kb * 520 + (2 * hp + 1) * 65 + 65],
                             ebAB[:, ebs + 512:ebs + 1024],
                             start=(kb == 0), stop=(kb == nkb - 1)) \
                        .then_inc(s_pe.h, 1)

                w_evac = evac_end.get(uidx - 2, 0)

                def ft(e, hp=hp, qb=qb, nkb=nkb, pe0=pe0, act0=act0,
                       dve_u0=dve_u0, avb=avb, w_evac=w_evac):
                    if w_evac:
                        e.wait_ge(s_dve.h, w_evac)
                    for kb in range(nkb):
                        scs = (kb % 2) * 2
                        if kb >= 2:
                            e.wait_ge(s_act.h, act0 + kb - 1)
                        e.matmul(pall[:, scs * 512:scs * 512 + 512],
                                 kf[0:64, hp * S + kb * 128:hp * S + kb * 128 + 128],
                                 qf[0:64, hp * S + qb * 512:hp * S + qb * 512 + 512],
                                 start=True, stop=True)
                        e.matmul(pall[:, (scs + 1) * 512:(scs + 1) * 512 + 512],
                                 kf[64:128, hp * S + kb * 128:hp * S + kb * 128 + 128],
                                 qf[64:128, hp * S + qb * 512:hp * S + qb * 512 + 512],
                                 start=True, stop=True).then_inc(s_pe.h, 1)
                        if kb >= 2:
                            emit_av(e, kb - 2, hp, nkb, act0, dve_u0, qb, avb)
                    emit_av(e, nkb - 2, hp, nkb, act0, dve_u0, qb, avb)
                    emit_av(e, nkb - 1, hp, nkb, act0, dve_u0, qb, avb)
                    s_pe.n += 2 * nkb
                ctx.tensor(ft)

                def fs(e, nkb=nkb, pe0=pe0, hp=hp, pe_sc=pe_sc, pe_av=pe_av):
                    for kb in range(nkb):
                        scs = (kb % 2) * 2
                        ebs = (kb % 3) * 1024
                        e.wait_ge(s_pe.h, pe_sc[kb])
                        if kb >= 3:
                            e.wait_ge(s_pe.h, pe_av[kb - 3])
                        e.activation(ebAB[:, ebs:ebs + 1024],
                                     pall[:, scs * 512:scs * 512 + 1024], AF.Exp,
                                     bias=battn[:, hp:hp + 1], scale=0.125) \
                            .then_inc(s_act.h, 1)
                    s_act.n += nkb
                ctx.scalar(fs)

                def fvm(e, qb=qb, nkb=nkb, act0=act0):
                    for kb in range(4 * qb, nkb):
                        off = (kb - 4 * qb) * 128
                        ebs = (kb % 3) * 1024
                        e.wait_ge(s_act.h, act0 + kb + 1)
                        for hof in (0, 512):
                            if off > 0:
                                e.memset(ebAB[:, ebs + hof:ebs + hof + off], 0.0)
                            e.tensor_mul(ebAB[:, ebs + hof + off:ebs + hof + off + 128],
                                         ebAB[:, ebs + hof + off:ebs + hof + off + 128],
                                         d["tri_b"][:, :])
                        e.tensor_scalar(nrow[:, 0:1], nrow[:, 0:1], 1.0, None,
                                        ALU.mult).then_inc(s_dve.h, 1)
                        s_dve.n += 1
                ctx.vector(fvm)

                pe_mm_end = s_pe.n
                for hloc in (0, 1):
                    def fve1(e, hloc=hloc, avb=avb, pe_mm_end=pe_mm_end):
                        e.wait_ge(s_pe.h, pe_mm_end + hloc)
                        bank = avb + hloc
                        e.tensor_scalar(nrow[:, :],
                                        pall[64:65, bank * 512:bank * 512 + 512],
                                        1e-30, None, ALU.max).then_inc(s_dve.h, 1)
                        s_dve.n += 1
                        e.wait_ge(s_dve.h, s_dve.n)
                        e.reciprocal(nrow[:, :], nrow[:, :]) \
                            .then_inc(s_dve.h, 1)
                        s_dve.n += 1
                    ctx.vector(fve1)

                    def fte(e, hloc=hloc, avb=avb):
                        e.wait_ge(s_dve.h, s_dve.n)
                        bank = avb + hloc
                        e.matmul(pall[:, bank * 512:bank * 512 + 512][64:128, :],
                                 d["ones_row"][:, 0:64],
                                 nrow[:, :], start=True, stop=True) \
                            .then_inc(s_pe.h, 1)
                        s_pe.n += 1
                    ctx.tensor(fte)

                def fve2(e, hp=hp, qb=qb, avb=avb):
                    e.wait_ge(s_pe.h, s_pe.n)
                    for hloc in (0, 1):
                        bank = avb + hloc
                        e.tensor_copy(bcs[0:64, :],
                                      pall[64:128, bank * 512:bank * 512 + 512]) \
                            .then_inc(s_dve.h, 1)
                        s_dve.n += 1
                        e.tensor_mul(
                            av_fm[hloc * 64:hloc * 64 + 64,
                                  hp * S + qb * 512:hp * S + qb * 512 + 512],
                            pall[0:64, bank * 512:bank * 512 + 512],
                            bcs[0:64, :]).then_inc(s_dve.h, 1)
                        s_dve.n += 1
                ctx.vector(fve2)
                evac_end[uidx] = s_dve.n
                uidx += 1
        ctx.barrier()

        ag_av_in_v = d["ag_av_in"].rearrange("(c p) n -> p c n", p=128)

        def fxa(e):
            if "avf" in d["dbg"]:
                e.dma_start(out=d["dbg_out"]["avf"][:, :], in_=av_fm[:, :]) \
                    .then_inc(s_gld.h, 16)
                s_gld.n += 16
                e.wait_ge(s_gld.h, s_gld.n)
            e.dma_start(out=ag_av_in_v,
                        in_=av_fm[:, :].rearrange("p (c n) -> p c n", c=4)) \
                .then_inc(s_gld.h, 16)
            s_gld.n += 16
            e.wait_ge(s_gld.h, s_gld.n)
            e.collective_compute(
                "AllGather", ALU.bypass,
                replica_groups=[[2 * i, 2 * i + 1] for i in range(NCORES // 2)],
                ins=[d["ag_av_in"].ap().opt()], outs=[d["ag_av_out"].ap().opt()],
            ).then_inc(s_cc.h, 1)
            s_cc.n += 1
            e.wait_ge(s_cc.h, s_cc.n)
        ctx.gpsimd(fxa)
    ctx.barrier()

    core_scope.close()
    with ExitStack() as oscope:
        avall = sb("avall", [128, 8 * TOK], BF16, oscope)
        wob = sb("wob", [128, HC * DM], BF16, oscope)
        wo_v = d["wo_in"].rearrange("(c p) m -> p c m", p=128)

        def fol(e):
            e.dma_start(out=wob[:, :].rearrange("p (c m) -> p c m", c=HC),
                        in_=wo_v).then_inc(s_gld.h, 16)
            s_gld.n += 16
            with e.register("rk") as rk:
                e.cc_rank_ld(rk, replica_groups=[[2 * i, 2 * i + 1]
                                                 for i in range(NCORES // 2)])
                with e.register("ofs") as ofs:
                    e.reg_mov(ofs, rk)
                    e.reg_mul(ofs, ofs, TOK)
                    e.dma_start(
                        out=avall[:, :].rearrange("p (c n) -> p c n", c=8),
                        in_=bass.AP(d["ag_av_out"], ofs,
                                    [[S, 128], [128 * S, 8], [1, TOK]])) \
                        .then_inc(s_gld.h, 16)
                    s_gld.n += 16
            e.wait_ge(s_gld.h, s_gld.n)
            if "avall" in d["dbg"]:
                e.dma_start(out=d["dbg_out"]["avall"][:, :], in_=avall[:, :]) \
                    .then_inc(s_gld.h, 16)
                s_gld.n += 16
                e.wait_ge(s_gld.h, s_gld.n)
        ctx.gpsimd(fol)
        ctx.barrier()

        pe0 = s_pe.n
        dve0 = s_dve.n

        def ft(e):
            for t in range(TT):
                st_ = (t % 2) * 2
                if t >= 2:
                    e.wait_ge(s_dve.h, dve0 + t - 1)
                for half in range(2):
                    for dc in range(HC):
                        inst = e.matmul(
                            pall[:, (st_ + half) * 512:(st_ + half) * 512 + 512],
                            avall[:, dc * TOK + t * 128:dc * TOK + t * 128 + 128],
                            wob[:, dc * DM + half * 512:dc * DM + half * 512 + 512],
                            start=(dc == 0), stop=(dc == HC - 1))
                inst.then_inc(s_pe.h, 1)
            s_pe.n += TT
        ctx.tensor(ft)

        def fv(e):
            for t in range(TT):
                st_ = (t % 2) * 2
                e.wait_ge(s_pe.h, pe0 + t + 1)
                e.tensor_add(d["xr"][:, t * DM:(t + 1) * DM],
                             pall[:, st_ * 512:st_ * 512 + 1024],
                             d["xr"][:, t * DM:(t + 1) * DM]).then_inc(s_dve.h, 1)
                s_dve.n += 1
                d.setdefault("xr_dve", {})[t] = s_dve.n
        ctx.vector(fv)
    ctx.barrier()


def emit_output(ctx, d):
    s_st, s_dve = d["s_st"], d["s_dve"]
    xr_dve = dict(d.get("xr_dve", {}))
    out_t = d["out"].rearrange("(t p) m -> t p m", p=128)

    def fst(e):
        for t in range(TT):
            if t in xr_dve:
                e.wait_ge(s_dve.h, xr_dve[t])
            e.dma_start(out=out_t[t], in_=d["xr"][:, t * DM:(t + 1) * DM]) \
                .then_inc(s_st.h, 16)
        s_st.n += 16 * TT
        e.wait_ge(s_st.h, s_st.n)
    ctx.sync(fst)



_BUILD_KW = {}


def kernel(L0_attn_x, L0_attn_Wq, L0_attn_Wk, L0_attn_Wv, L0_attn_Wo,
           L0_ffn_clip_x, L0_ffn_clip_down, L0_ffn_w_gate, L0_ffn_w_up,
           L0_ffn_w_down):
    x = np.ascontiguousarray(np.asarray(L0_attn_x, dtype=np.float32))
    wq = np.ascontiguousarray(np.asarray(L0_attn_Wq, dtype=np.float32))
    wk = np.ascontiguousarray(np.asarray(L0_attn_Wk, dtype=np.float32))
    wv = np.ascontiguousarray(np.asarray(L0_attn_Wv, dtype=np.float32))
    wo = np.ascontiguousarray(np.asarray(L0_attn_Wo, dtype=np.float32))
    wg = np.ascontiguousarray(np.asarray(L0_ffn_w_gate, dtype=np.float32))
    wu = np.ascontiguousarray(np.asarray(L0_ffn_w_up, dtype=np.float32))
    wd = np.ascontiguousarray(np.asarray(L0_ffn_w_down, dtype=np.float32))
    clip = np.array([[np.asarray(L0_ffn_clip_x, np.float32).ravel()[0],
                      np.asarray(L0_ffn_clip_down, np.float32).ravel()[0]]],
                    dtype=np.float32)

    nc, dbg_out = build_program(**_BUILD_KW)
    in_maps = []
    for c in range(NCORES):
        b, h2 = c // 2, c % 2
        in_maps.append({
            "x": np.ascontiguousarray(x[b, h2 * TOK:(h2 + 1) * TOK, :]),
            "wq": np.ascontiguousarray(wq[:, h2 * 512:(h2 + 1) * 512]),
            "wk": np.ascontiguousarray(wk[:, h2 * 512:(h2 + 1) * 512]),
            "wv": np.ascontiguousarray(wv[:, h2 * 512:(h2 + 1) * 512]),
            "wo": wo, "wg": wg, "wu": wu, "wd": wd, "clip": clip,
        })
    res = run_bass_kernel_spmd(nc, in_maps, core_ids=list(range(NCORES)))
    out = np.empty((4, S, DM), np.float32)
    for c in range(NCORES):
        b, h2 = c // 2, c % 2
        out[b, h2 * TOK:(h2 + 1) * TOK, :] = res.results[c]["out"]
    if _BUILD_KW.get("debug"):
        kernel.last_results = res
    return out


# version 14
# speedup vs baseline: 1.0322x; 1.0322x over previous
import numpy as np
from contextlib import ExitStack

import concourse.bass as bass
import concourse.mybir as mybir
from concourse.bass_utils import run_bass_kernel_spmd

F32 = mybir.dt.float32
BF16 = mybir.dt.bfloat16
F16 = mybir.dt.float16
AF = mybir.ActivationFunctionType
ALU = mybir.AluOpType
AX = mybir.AxisListType

NCORES = 8
TOK = 2048
TT = TOK // 128
DM = 1024
HC = DM // 128
DI = 2816
IC = DI // 128
S = 4096
ST = S // 128
QMAX = 127.0
RC = 1.5 * 2.0 ** 23
EPS = 1e-5
DK = 64

ENGS = ("tensor", "vector", "scalar", "gpsimd", "sync")


class Sem:
    def __init__(self, es, nc, name):
        self.h = es.enter_context(nc.semaphore(name))
        self.n = 0


class InstRec:

    def __init__(self, inst):
        self.i = inst
        self.updated = False

    def then_inc(self, sem, k=None):
        self.updated = True
        if k is None:
            return self.i.then_inc(sem)
        return self.i.then_inc(sem, k)


class EngRec:
    def __init__(self, eng, store, key):
        object.__setattr__(self, "_eng", eng)
        object.__setattr__(self, "_store", store)
        object.__setattr__(self, "_key", key)

    def __getattr__(self, name):
        attr = getattr(self._eng, name)
        if not callable(attr):
            return attr

        def wrap(*a, **kw):
            r = attr(*a, **kw)
            if isinstance(r, bass.BassInstruction):
                r = InstRec(r)
                self._store[self._key] = r
            return r
        return wrap


class Ctx:
    def __init__(self, nc, block, es):
        self.nc = nc
        self.block = block
        self.es = es
        self.last = {}
        self.sems = []
        self.s_bar = Sem(es, nc, "bar")

    def sec(self, eng_name, fn):
        def g(eng):
            fn(EngRec(eng, self.last, eng_name))
        getattr(self.block, eng_name)(g)

    def tensor(self, fn): self.sec("tensor", fn)
    def vector(self, fn): self.sec("vector", fn)
    def scalar(self, fn): self.sec("scalar", fn)
    def gpsimd(self, fn): self.sec("gpsimd", fn)
    def sync(self, fn): self.sec("sync", fn)

    def barrier(self):
        target = self.s_bar.n
        for e in ENGS:
            li = self.last.pop(e, None)
            if li is not None and not li.updated:
                try:
                    li.i.then_inc(self.s_bar.h, 1)
                    target += 1
                except AssertionError:
                    li.i.then_inc(self.s_bar.h, 16)
                    target += 16
        waits = [(s.h, s.n) for s in self.sems if s.n > 0]
        if target > self.s_bar.n:
            waits.append((self.s_bar.h, target))

        for e in ENGS:
            def f(eng, ws=tuple(waits)):
                for h, n in ws:
                    eng.wait_ge(h, n)
            self.sec(e, f)
        self.s_bar.n = target
        self.last = {}



def build_program(upto="full", debug=()):
    nc = bass.Bass("TRN2", target_bir_lowering=False)
    dbg = set(debug)
    d = {}
    d["upto"] = upto

    for nm, shape, dt in [
        ("x", [TOK, DM], F32), ("wq", [DM, 512], F32), ("wk", [DM, 512], F32),
        ("wv", [DM, 512], F32), ("wo", [DM, DM], F32), ("wg", [DI, DM], F32),
        ("wu", [DI, DM], F32), ("wd", [DM, DI], F32), ("clip", [1, 2], F32),
    ]:
        d[nm + "_in"] = nc.dram_tensor(nm, shape, dt, kind="ExternalInput").ap()
    d["out"] = nc.dram_tensor("out", [TOK, DM], F32, kind="ExternalOutput").ap()

    d["dbg"] = dbg
    d["dbg_out"] = {}

    def mkdbg(name, shape, dt=F32):
        if name in dbg:
            d["dbg_out"][name] = nc.dram_tensor(
                "dbg_" + name, shape, dt, kind="ExternalOutput").ap()
    d["mkdbg"] = mkdbg
    mkdbg("xq", [TOK, DM], BF16)
    mkdbg("h", [DI, TOK], F16)
    mkdbg("hq", [DI, TOK], BF16)
    mkdbg("gmax", [1, 4], F32)
    mkdbg("wgq", [DI, DM], BF16)
    mkdbg("qf", [128, 4 * S], BF16)
    mkdbg("kf", [128, 4 * S], BF16)
    mkdbg("vt", [128, (S // 128) * 520], BF16)
    mkdbg("avf", [128, 4 * S], BF16)
    mkdbg("xln", [S, DM], BF16)
    mkdbg("battn", [128, 4], F32)
    mkdbg("cos", [128, S], BF16)
    mkdbg("xfs0", [128, HC * 512], BF16)
    mkdbg("rotb", [128, 128], BF16)
    mkdbg("avall", [128, 8 * TOK], BF16)
    mkdbg("qraw0", [128, 512], BF16)
    mkdbg("sin", [128, S], BF16)
    mkdbg("ivf", [128, 1], F32)

    d["wgq_nat"] = nc.dram_tensor("wgq_nat", [DI, DM], BF16)
    d["wuq_nat"] = nc.dram_tensor("wuq_nat", [DI, DM], BF16)
    d["wdq_nat"] = nc.dram_tensor("wdq_nat", [DM, DI], BF16)
    d["wgq_t"] = nc.dram_tensor("wgq_t", [DM, DI], BF16)
    d["wuq_t"] = nc.dram_tensor("wuq_t", [DM, DI], BF16)
    d["wdq_t"] = nc.dram_tensor("wdq_t", [DI, DM], BF16)
    d["xq_nat"] = nc.dram_tensor("xq_nat", [TOK, DM], BF16)
    d["h_nat"] = nc.dram_tensor("h_nat", [DI, TOK], F16)
    d["hq_nat"] = nc.dram_tensor("hq_nat", [DI, TOK], BF16)
    d["mx_in"] = nc.dram_tensor("mx_in", [1, 16], F32)
    d["mx_out"] = nc.dram_tensor("mx_out", [1, 16], F32)
    d["ag_xln_in"] = nc.dram_tensor("ag_xln_in", [TOK, DM], BF16)
    d["ag_xln_out"] = nc.dram_tensor("ag_xln_out", [S, DM], BF16)
    d["ag_av_in"] = nc.dram_tensor("ag_av_in", [512, S], BF16)
    d["ag_av_out"] = nc.dram_tensor("ag_av_out", [1024, S], BF16)

    es = ExitStack()
    d["es_root"] = es

    _cnt = [0]

    def sb(name, shape, dt, scope=None):
        _cnt[0] += 1
        return (scope or es).enter_context(
            nc.sbuf_tensor(f"{name}_{_cnt[0]}", shape, dt))

    def psum(name, shape, dt, scope=None):
        return (scope or es).enter_context(nc.psum_tensor(name, shape, dt))
    d["sb"] = sb
    d["psum"] = psum

    d["xr"] = sb("xr", [128, TT * DM], F32)
    d["iden_f"] = sb("iden_f", [128, 128], F32)
    d["tri_b"] = sb("tri_b", [128, 128], BF16)
    d["rot_b"] = sb("rot_b", [128, 128], BF16)
    d["ones_row"] = sb("ones_row", [1, 128], F32)
    d["iota_p"] = sb("iota_p", [128, 1], F32)
    d["sclip"] = sb("sclip", [1, 2], F32)
    d["ssig"] = sb("ssig", [1, 2], F32)
    d["s_tmp"] = sb("s_tmp", [1, 8], F32)
    d["gmax"] = sb("gmax", [1, 4], F32)
    d["binv"] = sb("binv", [128, 8], F32)
    d["hcol"] = sb("hcol", [128, 2], F32)
    d["lmax_col"] = sb("lmax_col", [128, 1], F32)
    for nm in ("st_m", "st_r", "st_a", "st_b", "st_c", "st_d"):
        d[nm] = sb(nm, [128, TT], F32)
    d["wsc_g"] = sb("wsc_g", [128, IC], F32)
    d["wsc_u"] = sb("wsc_u", [128, IC], F32)
    d["wsc_d"] = sb("wsc_d", [128, HC], F32)
    d["sqb"] = sb("sqb", [128, DM], F32)

    d["pall"] = psum("pall", [128, 8 * 512], F32)

    with nc.Block() as block:
        ctx = Ctx(nc, block, es)
        for s in ("s_ld", "s_sc", "s_gld", "s_st", "s_pe", "s_dve",
                  "s_act", "s_gp", "s_cc"):
            d[s] = Sem(es, nc, s)
            ctx.sems.append(d[s])
        emit_all(ctx, d)
    es.close()
    return nc, d["dbg_out"]


def mk_psems(ctx, d, name):
    import itertools
    es = ctx.es
    sems = [Sem(es, ctx.nc, f"{name}{i}") for i in range(2)]
    for s in sems:
        ctx.sems.append(s)
    return sems


def emit_all(ctx, d):
    upto = d["upto"]
    with ExitStack() as pscope:
        d["iota_j"] = d["sb"]("iota_j", [128, 128], F32, pscope)
        d["scr1"] = d["sb"]("scr1", [128, 128], F32, pscope)
        d["scr2"] = d["sb"]("scr2", [128, 128], F32, pscope)
        emit_prologue(ctx, d)
        ctx.barrier()
    with ExitStack() as wscope:
        emit_weight_quant(ctx, d, wscope)
    emit_ln_stats(ctx, d, want_max=True)
    ctx.barrier()
    emit_maxreduce(ctx, d, slot=0)
    emit_scale_chain(ctx, d, slot=0, sig_idx=0)
    ctx.barrier()
    with ExitStack() as fscope:
        emit_ffn(ctx, d, fscope, app=0)
    if upto == "ffn1":
        emit_output(ctx, d)
        return
    with ExitStack() as ascope:
        emit_attention(ctx, d, ascope)
    if upto == "attn":
        emit_output(ctx, d)
        return
    emit_ln_stats(ctx, d, want_max=True)
    ctx.barrier()
    emit_maxreduce(ctx, d, slot=2)
    emit_scale_chain(ctx, d, slot=2, sig_idx=0)
    ctx.barrier()
    with ExitStack() as fscope:
        emit_ffn(ctx, d, fscope, app=1)
    emit_output(ctx, d)


def emit_ffn(ctx, d, scope, app):
    sb = d["sb"]
    nc = ctx.nc
    d["hmax_cols"] = sb("hmax_cols", [128, IC * 4], F32, scope)
    d["qstage"] = sb("qstage", [128, 2 * DM], F32, scope)
    d["qout"] = sb("qout", [128, 2 * DM], BF16, scope)
    d["srs"] = sb("srs", [128, TT], F32, scope)
    d["bsd"] = sb("bsd", [128, DM], F32, scope)
    d["brow"] = sb("brow", [1, 128], F32, scope)
    with ExitStack() as gscope:
        d["xq_fm"] = sb("xq_fm", [128, HC * TOK], BF16, gscope)
        emit_xq(ctx, d, slot=2 * app)
        ctx.barrier()
        emit_gateup(ctx, d, gscope, slot=2 * app)
        ctx.barrier()
    emit_hmax_finish(ctx, d)
    emit_maxreduce(ctx, d, slot=2 * app + 1)
    emit_scale_chain(ctx, d, slot=2 * app + 1, sig_idx=1)
    emit_bsd(ctx, d)
    ctx.barrier()
    with ExitStack() as dscope:
        emit_down(ctx, d, dscope, app=app)
    ctx.barrier()
    for k in ("xq_fm", "hmax_cols", "qstage", "qout", "srs", "wgb", "wub", "sgt", "upt",
              "hft", "hbt", "svg", "svu", "hst", "hqo", "hfs", "hqb", "wdb",
              "wdr", "hstage", "dst1", "bsd", "brow"):
        d.pop(k, None)



def emit_prologue(ctx, d):
    s_ld, s_gp, s_dve, s_act = d["s_ld"], d["s_gp"], d["s_dve"], d["s_act"]
    x_t = d["x_in"].rearrange("(t p) m -> t p m", p=128)
    xr = d["xr"]

    def fsync(e):
        for t in range(TT):
            e.dma_start(out=xr[:, t * DM:(t + 1) * DM], in_=x_t[t]).then_inc(s_ld.h, 16)
        e.dma_start(out=d["sclip"][:, :], in_=d["clip_in"][:, :]).then_inc(s_ld.h, 16)
        s_ld.n += 16 * (TT + 1)
    ctx.sync(fsync)

    def fgp(e):
        e.iota(d["iota_j"][:, :], [[1, 128]], channel_multiplier=0,
               allow_small_or_imprecise_dtypes=True)
        e.iota(d["iota_p"][:, :], [[1, 1]], channel_multiplier=1,
               allow_small_or_imprecise_dtypes=True)
        e.memset(d["ones_row"][:, :], 1.0).then_inc(s_gp.h, 1)
        s_gp.n += 1
    ctx.gpsimd(fgp)

    def fv(e):
        e.wait_ge(s_gp.h, s_gp.n)
        jt, pt = d["iota_j"], d["iota_p"]
        e.tensor_scalar(d["iden_f"][:, :], jt[:, :], pt[:, :], None, ALU.is_equal)
        e.tensor_scalar(d["tri_b"][:, :], jt[:, :], pt[:, :], 0.0,
                        ALU.subtract, ALU.is_ge)
        pcol = d["lmax_col"]
        e.tensor_scalar(pcol[:, :], pt[:, :], 0.5, -0.25, ALU.mult, ALU.add) \
            .then_inc(s_dve.h, 1)
        s_dve.n += 1
        e.wait_ge(s_dve.h, s_dve.n)
        e.tensor_scalar(pcol[:, :], pcol[:, :], RC, RC, ALU.add, ALU.subtract) \
            .then_inc(s_dve.h, 1)
        s_dve.n += 1
        e.wait_ge(s_dve.h, s_dve.n)
        e.scalar_tensor_tensor(pcol[:, :], pcol[:, :], -2.0, pt[:, :],
                               ALU.mult, ALU.add).then_inc(s_dve.h, 1)
        s_dve.n += 1
        e.wait_ge(s_dve.h, s_dve.n)
        scr1, scr2 = d["scr1"], d["scr2"]
        e.tensor_scalar(scr2[:, :], jt[:, :], pt[:, :], 1.0, ALU.subtract,
                        ALU.is_equal).then_inc(s_dve.h, 1)
        s_dve.n += 1
        e.wait_ge(s_dve.h, s_dve.n)
        e.tensor_scalar(scr1[:, :], scr2[:, :], pcol[:, :], None, ALU.mult) \
            .then_inc(s_dve.h, 1)
        s_dve.n += 1
        e.wait_ge(s_dve.h, s_dve.n)
        e.tensor_sub(scr2[:, :], scr2[:, :], scr1[:, :]).then_inc(s_dve.h, 1)
        s_dve.n += 1
        e.wait_ge(s_dve.h, s_dve.n)
        e.tensor_scalar(scr1[:, :], jt[:, :], pt[:, :], -1.0, ALU.subtract,
                        ALU.is_equal).then_inc(s_dve.h, 1)
        s_dve.n += 1
        e.wait_ge(s_dve.h, s_dve.n)
        e.tensor_scalar(scr1[:, :], scr1[:, :], pcol[:, :], None, ALU.mult) \
            .then_inc(s_dve.h, 1)
        s_dve.n += 1
        e.wait_ge(s_dve.h, s_dve.n)
        e.tensor_sub(d["rot_b"][:, :], scr2[:, :], scr1[:, :]).then_inc(s_dve.h, 1)
        s_dve.n += 1
    ctx.vector(fv)

    def fs(e):
        e.wait_ge(s_ld.h, s_ld.n)
        e.activation(d["ssig"][:, :], d["sclip"][:, :], AF.Sigmoid) \
            .then_inc(s_act.h, 1)
        s_act.n += 1
    ctx.scalar(fs)


def emit_weight_quant(ctx, d, scope):
    sb = d["sb"]
    s_ld, s_sc, s_dve, s_act, s_st = d["s_ld"], d["s_sc"], d["s_dve"], d["s_act"], d["s_st"]

    wst = sb("wst", [128, 2 * DI], F32, scope)
    wsq = sb("wsq", [128, 2 * DI], F32, scope)
    wqb = sb("wqb", [128, 2 * DI], BF16, scope)
    rcol = sb("rcol", [128, 4], F32, scope)
    tb = sb("tb", [128, 2 * DI], BF16, scope)

    jobs = []
    for (src, dst, wsc, nch, W) in [
        (d["wg_in"], d["wgq_nat"], d["wsc_g"], IC, DM),
        (d["wu_in"], d["wuq_nat"], d["wsc_u"], IC, DM),
        (d["wd_in"], d["wdq_nat"], d["wsc_d"], HC, DI),
    ]:
        for k in range(nch):
            jobs.append((src, dst, wsc, k, W))
    nj = len(jobs)
    dve0, act0, st0 = s_dve.n, s_act.n, s_st.n
    p_ld = mk_psems(ctx, d, "wql")
    p_st = mk_psems(ctx, d, "wqs")

    def fsync(e):
        for j, (src, dst, wsc, k, W) in enumerate(jobs):
            if j >= 2:
                e.wait_ge(s_act.h, act0 + j - 1)
            srcv = src.rearrange("(c p) m -> c p m", p=128)
            e.dma_start(out=wst[:, (j % 2) * DI:(j % 2) * DI + W],
                        in_=srcv[k]).then_inc(p_ld[j % 2].h, 16)
            p_ld[j % 2].n += 16
    ctx.sync(fsync)

    def fv(e):
        for j, (src, dst, wsc, k, W) in enumerate(jobs):
            sl = slice((j % 2) * DI, (j % 2) * DI + W)
            e.wait_ge(p_ld[j % 2].h, 16 * (j // 2 + 1))
            e.reduce_max(wsc[:, k:k + 1], wst[:, sl], AX.X,
                         apply_absolute_value=True).then_inc(s_dve.h, 1)
            e.wait_ge(s_dve.h, dve0 + 5 * j + 1)
            e.reciprocal(rcol[:, 2 * (j % 2):2 * (j % 2) + 1], wsc[:, k:k + 1]) \
                .then_inc(s_dve.h, 1)
            e.wait_ge(s_dve.h, dve0 + 5 * j + 2)
            e.tensor_scalar(rcol[:, 2 * (j % 2) + 1:2 * (j % 2) + 2],
                            rcol[:, 2 * (j % 2):2 * (j % 2) + 1], QMAX, None,
                            ALU.mult).then_inc(s_dve.h, 1)
            e.wait_ge(s_act.h, act0 + j + 1)
            if j >= 2:
                e.wait_ge(p_st[j % 2].h, 16 * (j // 2))
            e.tensor_scalar(wsq[:, sl], wsq[:, sl], QMAX, -QMAX - 1.0,
                            ALU.min, ALU.max).then_inc(s_dve.h, 1)
            e.wait_ge(s_dve.h, dve0 + 5 * j + 4)
            e.tensor_scalar(wqb[:, sl], wsq[:, sl], RC, RC, ALU.add,
                            ALU.subtract).then_inc(s_dve.h, 1)
        s_dve.n += 5 * nj
    ctx.vector(fv)

    def fs(e):
        for j, (src, dst, wsc, k, W) in enumerate(jobs):
            sl = slice((j % 2) * DI, (j % 2) * DI + W)
            e.wait_ge(s_dve.h, dve0 + 5 * j + 3)
            e.activation(wsq[:, sl], wst[:, sl], AF.Copy,
                         scale=rcol[:, 2 * (j % 2) + 1:2 * (j % 2) + 2]) \
                .then_inc(s_act.h, 1)
        s_act.n += nj
    ctx.scalar(fs)

    def fst(e):
        for j, (src, dst, wsc, k, W) in enumerate(jobs):
            sl = slice((j % 2) * DI, (j % 2) * DI + W)
            e.wait_ge(s_dve.h, dve0 + 5 * (j + 1))
            dstv = dst.rearrange("(c p) m -> c p m", p=128)
            e.dma_start(out=dstv[k], in_=wqb[:, sl]).then_inc(p_st[j % 2].h, 16)
            p_st[j % 2].n += 16
    ctx.gpsimd(fst)

    ctx.barrier()

    p_tr = mk_psems(ctx, d, "bnt")
    p_bw = mk_psems(ctx, d, "bnc")
    bounce_jobs = []
    for (natt, tt_, nch, W) in [
        (d["wgq_nat"], d["wgq_t"], HC, DI),
        (d["wuq_nat"], d["wuq_t"], HC, DI),
        (d["wdq_nat"], d["wdq_t"], IC, DM),
    ]:
        for k in range(nch):
            bounce_jobs.append((natt, tt_, k, W))

    def fb(e):
        for n, (natt, tt_, k, W) in enumerate(bounce_jobs):
            buf = tb[:, (n % 2) * DI:(n % 2) * DI + W]
            if n >= 2:
                e.wait_ge(p_bw[n % 2].h, 16 * (n // 2))
            e.dma_start_transpose(buf, natt[:, k * 128:(k + 1) * 128]) \
                .then_inc(p_tr[n % 2].h, 16)
            p_tr[n % 2].n += 16
    ctx.sync(fb)

    def fbw(e):
        for n, (natt, tt_, k, W) in enumerate(bounce_jobs):
            buf = tb[:, (n % 2) * DI:(n % 2) * DI + W]
            e.wait_ge(p_tr[n % 2].h, 16 * (n // 2 + 1))
            e.dma_start(out=tt_[k * 128:(k + 1) * 128, :], in_=buf) \
                .then_inc(p_bw[n % 2].h, 16)
            p_bw[n % 2].n += 16
        e.wait_ge(p_bw[0].h, p_bw[0].n)
        e.wait_ge(p_bw[1].h, p_bw[1].n)
    ctx.gpsimd(fbw)

    if "wgq" in d["dbg"]:
        s_gld = d["s_gld"]

        def fdbg(e):
            e.wait_ge(s_st.h, s_st.n)
            e.dma_start(out=d["dbg_out"]["wgq"][:, :], in_=d["wgq_nat"][:, :]) \
                .then_inc(s_gld.h, 16)
            s_gld.n += 16
            e.wait_ge(s_gld.h, s_gld.n)
        ctx.gpsimd(fdbg)
    ctx.barrier()


def emit_ln_stats(ctx, d, want_max):
    s_ld, s_dve, s_act = d["s_ld"], d["s_dve"], d["s_act"]
    xr = d["xr"]
    sqb = d["sqb"]
    act0 = s_act.n
    dve0 = s_dve.n

    xr_dve = dict(d.get("xr_dve", {}))

    def fs(e):
        for t in range(TT):
            if t in xr_dve:
                e.wait_ge(s_dve.h, xr_dve[t])
            if t >= 1:
                e.wait_ge(s_dve.h, dve0 + t)
            sl = slice(0, DM)
            e.activation(sqb[:, sl], xr[:, t * DM:(t + 1) * DM], AF.Square) \
                .then_inc(s_act.h, 1)
        s_act.n += TT
    ctx.scalar(fs)

    def fv(e):
        for t in range(TT):
            xt = xr[:, t * DM:(t + 1) * DM]
            sl = slice(0, DM)
            e.reduce_sum(d["st_a"][:, t:t + 1], xt, AX.X)
            if want_max:
                e.reduce_max(d["st_c"][:, t:t + 1], xt, AX.X)
                e.tensor_reduce(d["st_d"][:, t:t + 1], xt, AX.X, ALU.min)
            e.wait_ge(s_act.h, act0 + t + 1)
            e.reduce_sum(d["st_b"][:, t:t + 1], sqb[:, sl], AX.X) \
                .then_inc(s_dve.h, 1)
        s_dve.n += TT
        e.wait_ge(s_dve.h, s_dve.n)
        e.tensor_scalar(d["st_m"][:, :], d["st_a"][:, :], 1.0 / DM, None,
                        ALU.mult).then_inc(s_dve.h, 1)
        s_dve.n += 1
        e.tensor_scalar(d["st_b"][:, :], d["st_b"][:, :], 1.0 / DM, None,
                        ALU.mult).then_inc(s_dve.h, 1)
        s_dve.n += 1
        e.wait_ge(s_dve.h, s_dve.n)
        e.tensor_mul(d["st_a"][:, :], d["st_m"][:, :], d["st_m"][:, :]) \
            .then_inc(s_dve.h, 1)
        s_dve.n += 1
        e.wait_ge(s_dve.h, s_dve.n)
        e.scalar_tensor_tensor(d["st_b"][:, :], d["st_b"][:, :], EPS,
                               d["st_a"][:, :], ALU.add, ALU.subtract) \
            .then_inc(s_dve.h, 1)
        s_dve.n += 1
    ctx.vector(fv)

    def fs2(e):
        e.wait_ge(s_dve.h, s_dve.n)
        e.activation(d["st_r"][:, :], d["st_b"][:, :], AF.Sqrt).then_inc(s_act.h, 1)
        s_act.n += 1
    ctx.scalar(fs2)

    def fv2(e):
        e.wait_ge(s_act.h, s_act.n)
        e.reciprocal(d["st_r"][:, :], d["st_r"][:, :]).then_inc(s_dve.h, 1)
        s_dve.n += 1
        if want_max:
            e.wait_ge(s_dve.h, s_dve.n)
            e.tensor_sub(d["st_c"][:, :], d["st_c"][:, :], d["st_m"][:, :]) \
                .then_inc(s_dve.h, 1)
            s_dve.n += 1
            e.tensor_sub(d["st_d"][:, :], d["st_m"][:, :], d["st_d"][:, :]) \
                .then_inc(s_dve.h, 1)
            s_dve.n += 1
            e.wait_ge(s_dve.h, s_dve.n)
            e.tensor_max(d["st_c"][:, :], d["st_c"][:, :], d["st_d"][:, :]) \
                .then_inc(s_dve.h, 1)
            s_dve.n += 1
            e.wait_ge(s_dve.h, s_dve.n)
            e.tensor_mul(d["st_c"][:, :], d["st_c"][:, :], d["st_r"][:, :]) \
                .then_inc(s_dve.h, 1)
            s_dve.n += 1
            e.wait_ge(s_dve.h, s_dve.n)
            e.reduce_max(d["lmax_col"][:, :], d["st_c"][:, :], AX.X) \
                .then_inc(s_dve.h, 1)
            s_dve.n += 1
    ctx.vector(fv2)


def emit_maxreduce(ctx, d, slot):
    s_pe, s_dve, s_gld, s_cc, s_gp = d["s_pe"], d["s_dve"], d["s_gld"], d["s_cc"], d["s_gp"]
    pall = d["pall"]

    def ft(e):
        e.wait_ge(s_dve.h, s_dve.n)
        e.transpose(pall[0:1, 7 * 512:7 * 512 + 128], d["lmax_col"][:, :],
                    d["iden_f"][:, :]).then_inc(s_pe.h, 1)
        s_pe.n += 1
    ctx.tensor(ft)

    def fv(e):
        e.wait_ge(s_pe.h, s_pe.n)
        e.reduce_max(d["s_tmp"][:, 0:1], pall[0:1, 7 * 512:7 * 512 + 128], AX.X) \
            .then_inc(s_dve.h, 1)
        s_dve.n += 1
    ctx.vector(fv)

    def fg(e):
        e.memset(d["s_tmp"][:, 1:4], 0.0).then_inc(s_gp.h, 1)
        s_gp.n += 1
        e.wait_ge(s_gp.h, s_gp.n)
        e.wait_ge(s_dve.h, s_dve.n)
        e.dma_start(out=d["mx_in"][:, 0:4], in_=d["s_tmp"][:, 0:4]) \
            .then_inc(s_gld.h, 16)
        s_gld.n += 16
        e.wait_ge(s_gld.h, s_gld.n)
        e.collective_compute(
            "AllReduce", ALU.max, replica_groups=[list(range(NCORES))],
            ins=[d["mx_in"].ap()], outs=[d["mx_out"].ap()],
        ).then_inc(s_cc.h, 1)
        s_cc.n += 1
        e.wait_ge(s_cc.h, s_cc.n)
        e.dma_start(out=d["gmax"][:, slot:slot + 1], in_=d["mx_out"][:, 0:1]) \
            .then_inc(s_gld.h, 16)
        s_gld.n += 16
        e.wait_ge(s_gld.h, s_gld.n)
    ctx.gpsimd(fg)


def emit_scale_chain(ctx, d, slot, sig_idx):
    s_dve, s_pe, s_gld = d["s_dve"], d["s_pe"], d["s_gld"]
    pall = d["pall"]

    def fv(e):
        e.wait_ge(s_gld.h, s_gld.n)
        e.tensor_scalar(d["s_tmp"][:, 4:5], d["gmax"][:, slot:slot + 1],
                        d["ssig"][:, sig_idx:sig_idx + 1], 1.0 / QMAX,
                        ALU.mult, ALU.mult).then_inc(s_dve.h, 1)
        s_dve.n += 1
        e.wait_ge(s_dve.h, s_dve.n)
        e.reciprocal(d["s_tmp"][:, 5:6], d["s_tmp"][:, 4:5]).then_inc(s_dve.h, 1)
        s_dve.n += 1
    ctx.vector(fv)

    def ft(e):
        e.wait_ge(s_dve.h, s_dve.n)
        e.matmul(pall[:, 7 * 512:7 * 512 + 2], d["ones_row"][:, :],
                 d["s_tmp"][0:1, 4:6], start=True, stop=True).then_inc(s_pe.h, 1)
        s_pe.n += 1
    ctx.tensor(ft)

    def fv2(e):
        e.wait_ge(s_pe.h, s_pe.n)
        e.tensor_copy(d["binv"][:, 2 * slot:2 * slot + 1],
                      pall[:, 7 * 512 + 1:7 * 512 + 2])
        e.tensor_copy(d["binv"][:, 2 * slot + 1:2 * slot + 2],
                      pall[:, 7 * 512:7 * 512 + 1]).then_inc(s_dve.h, 1)
        s_dve.n += 1
        if slot % 2 == 1:
            e.wait_ge(s_dve.h, s_dve.n)
            e.tensor_scalar(d["hcol"][:, slot // 2:slot // 2 + 1],
                            d["binv"][:, 2 * slot + 1:2 * slot + 2], 0.5, None,
                            ALU.mult).then_inc(s_dve.h, 1)
            s_dve.n += 1
        if "gmax" in d["dbg"]:
            e.wait_ge(s_dve.h, s_dve.n)

            def fdbg(e2):
                e2.wait_ge(s_dve.h, s_dve.n)
                e2.dma_start(out=d["dbg_out"]["gmax"][:, :], in_=d["gmax"][:, :]) \
                    .then_inc(d["s_st"].h, 16)
                d["s_st"].n += 16
            ctx.gpsimd(fdbg)
    ctx.vector(fv2)


def emit_xq(ctx, d, slot):
    s_dve, s_st, s_sc = d["s_dve"], d["s_st"], d["s_sc"]
    qstage, qout, srs, xq_fm = d["qstage"], d["qout"], d["srs"], d["xq_fm"]
    dve0 = s_dve.n
    p_xs = mk_psems(ctx, d, "xqs%d" % slot)

    def fv(e):
        e.tensor_scalar(srs[:, :], d["st_r"][:, :],
                        d["binv"][:, 2 * slot:2 * slot + 1], None, ALU.mult) \
            .then_inc(s_dve.h, 1)
        e.wait_ge(s_dve.h, dve0 + 1)
        for t in range(TT):
            sl = slice((t % 2) * DM, (t % 2) * DM + DM)
            if t >= 2:
                e.wait_ge(p_xs[t % 2].h, 16 * (t // 2))
            xt = d["xr"][:, t * DM:(t + 1) * DM]
            e.tensor_scalar(qstage[:, sl], xt, d["st_m"][:, t:t + 1],
                            srs[:, t:t + 1], ALU.subtract, ALU.mult)
            e.tensor_scalar(qstage[:, sl], qstage[:, sl], QMAX, -QMAX - 1.0,
                            ALU.min, ALU.max)
            e.tensor_scalar(qout[:, sl], qstage[:, sl], RC, RC, ALU.add,
                            ALU.subtract).then_inc(s_dve.h, 1)
        s_dve.n += 1 + TT
    ctx.vector(fv)

    xq_t = d["xq_nat"].rearrange("(t p) m -> t p m", p=128)
    sc0 = s_sc.n

    def fst(e):
        for t in range(TT):
            sl = slice((t % 2) * DM, (t % 2) * DM + DM)
            e.wait_ge(s_dve.h, dve0 + 2 + t)
            e.dma_start(out=xq_t[t], in_=qout[:, sl]).then_inc(p_xs[t % 2].h, 16)
            p_xs[t % 2].n += 16
        e.wait_ge(p_xs[0].h, p_xs[0].n)
        e.wait_ge(p_xs[1].h, p_xs[1].n)
        for hc in range(HC):
            e.dma_start_transpose(xq_fm[:, hc * TOK:(hc + 1) * TOK],
                                  d["xq_nat"][:, hc * 128:(hc + 1) * 128]) \
                .then_inc(s_sc.h, 16)
        s_sc.n += 16 * HC
        e.wait_ge(s_sc.h, s_sc.n)
        if "xq" in d["dbg"]:
            e.dma_start(out=d["dbg_out"]["xq"][:, :], in_=d["xq_nat"][:, :]) \
                .then_inc(s_st.h, 16)
            s_st.n += 16
            e.wait_ge(s_st.h, s_st.n)
    ctx.sync(fst)


def emit_gateup(ctx, d, scope, slot):
    sb = d["sb"]
    s_ld, s_pe, s_dve, s_act, s_st = d["s_ld"], d["s_pe"], d["s_dve"], d["s_act"], d["s_st"]
    pall, xq_fm = d["pall"], d["xq_fm"]

    wgb = d.get("wgb") or sb("wgb", [128, 2 * HC * 128], BF16, scope)
    wub = d.get("wub") or sb("wub", [128, 2 * HC * 128], BF16, scope)
    sgt = d.get("sgt") or sb("sgt", [128, 2 * 512], F32, scope)
    upt = d.get("upt") or sb("upt", [128, 2 * 512], F32, scope)
    hft = d.get("hft") or sb("hft", [128, 2 * 512], F32, scope)
    hbt = d.get("hbt") or sb("hbt", [128, 2 * 512], F16, scope)
    svg = d.get("svg") or sb("svg", [128, IC], F32, scope)
    svu = d.get("svu") or sb("svu", [128, IC], F32, scope)
    for nm, v in [("wgb", wgb), ("wub", wub), ("sgt", sgt), ("upt", upt),
                  ("hft", hft), ("hbt", hbt), ("svg", svg), ("svu", svu)]:
        d[nm] = v

    NS = 4
    nu = IC * NS
    pe0, act0, dve0 = s_pe.n, s_act.n, s_dve.n
    p_wl = mk_psems(ctx, d, "gwl%d" % slot)
    p_hs = mk_psems(ctx, d, "ghs%d" % slot)
    wg_t = d["wgq_t"].rearrange("(c p) i -> p c i", p=128)
    wu_t = d["wuq_t"].rearrange("(c p) i -> p c i", p=128)

    def fsync(e):
        for ic in range(IC):
            if ic >= 2:
                e.wait_ge(s_pe.h, pe0 + (ic - 1) * NS)
            b = (ic % 2) * HC * 128
            e.dma_start(out=wgb[:, b:b + HC * 128].rearrange("p (c n) -> p c n", c=HC),
                        in_=wg_t[:, :, ic * 128:(ic + 1) * 128]).then_inc(p_wl[ic % 2].h, 16)
            e.dma_start(out=wub[:, b:b + HC * 128].rearrange("p (c n) -> p c n", c=HC),
                        in_=wu_t[:, :, ic * 128:(ic + 1) * 128]).then_inc(p_wl[ic % 2].h, 16)
            p_wl[ic % 2].n += 32
    ctx.sync(fsync)

    def fv0(e):
        e.tensor_scalar(svg[:, :], d["wsc_g"][:, :], d["binv"][:, 2 * slot + 1:2 * slot + 2],
                        1.0 / QMAX, ALU.mult, ALU.mult)
        e.tensor_scalar(svu[:, :], d["wsc_u"][:, :], d["binv"][:, 2 * slot + 1:2 * slot + 2],
                        1.0 / QMAX, ALU.mult, ALU.mult).then_inc(s_dve.h, 1)
        s_dve.n += 1
    ctx.vector(fv0)
    dve_pre = s_dve.n

    def ft(e):
        e.wait_ge(s_dve.h, dve_pre)
        u = 0
        for ic in range(IC):
            e.wait_ge(p_wl[ic % 2].h, 32 * (ic // 2 + 1))
            b = (ic % 2) * HC * 128
            for s in range(NS):
                if u >= 2:
                    e.wait_ge(s_dve.h, dve_pre + (u - 1) * 2)
                    e.wait_ge(s_act.h, act0 + u - 1)
                gb, ub = (u % 2) * 2, (u % 2) * 2 + 1
                for hc in range(HC):
                    e.matmul(pall[:, gb * 512:gb * 512 + 512],
                             wgb[:, b + hc * 128:b + hc * 128 + 128],
                             xq_fm[:, hc * TOK + s * 512:hc * TOK + (s + 1) * 512],
                             start=(hc == 0), stop=(hc == HC - 1))
                for hc in range(HC):
                    inst = e.matmul(
                        pall[:, ub * 512:ub * 512 + 512],
                        wub[:, b + hc * 128:b + hc * 128 + 128],
                        xq_fm[:, hc * TOK + s * 512:hc * TOK + (s + 1) * 512],
                        start=(hc == 0), stop=(hc == HC - 1))
                inst.then_inc(s_pe.h, 1)
                u += 1
        s_pe.n += nu
    ctx.tensor(ft)

    def fs(e):
        u = 0
        for ic in range(IC):
            for s in range(NS):
                e.wait_ge(s_pe.h, pe0 + u + 1)
                if u >= 2:
                    e.wait_ge(s_dve.h, dve_pre + (u - 1) * 2)
                gb = (u % 2) * 2
                e.activation(sgt[:, (u % 2) * 512:(u % 2) * 512 + 512],
                             pall[:, gb * 512:gb * 512 + 512], AF.Silu,
                             scale=svg[:, ic:ic + 1]).then_inc(s_act.h, 1)
                u += 1
        s_act.n += nu
    ctx.scalar(fs)

    def fv(e):
        u = 0
        for ic in range(IC):
            for s in range(NS):
                e.wait_ge(s_pe.h, pe0 + u + 1)
                ub = (u % 2) * 2 + 1
                if u >= 2:
                    e.wait_ge(p_hs[u % 2].h, 16 * (u // 2))
                e.tensor_scalar(upt[:, (u % 2) * 512:(u % 2) * 512 + 512],
                                pall[:, ub * 512:ub * 512 + 512],
                                svu[:, ic:ic + 1], None, ALU.mult)
                e.wait_ge(s_act.h, act0 + u + 1)
                e.tensor_mul(hft[:, (u % 2) * 512:(u % 2) * 512 + 512],
                             sgt[:, (u % 2) * 512:(u % 2) * 512 + 512],
                             upt[:, (u % 2) * 512:(u % 2) * 512 + 512])
                e.tensor_copy(hbt[:, (u % 2) * 512:(u % 2) * 512 + 512],
                              hft[:, (u % 2) * 512:(u % 2) * 512 + 512]) \
                    .then_inc(s_dve.h, 1)
                e.reduce_max(d["hmax_cols"][:, u:u + 1],
                             hft[:, (u % 2) * 512:(u % 2) * 512 + 512], AX.X,
                             apply_absolute_value=True).then_inc(s_dve.h, 1)
                u += 1
        s_dve.n += 2 * nu
    ctx.vector(fv)

    h_t = d["h_nat"].rearrange("(c p) n -> c p n", p=128)

    def fst(e):
        u = 0
        for ic in range(IC):
            for s in range(NS):
                e.wait_ge(s_dve.h, dve_pre + 2 * u + 2)
                e.dma_start(out=h_t[ic, :, s * 512:(s + 1) * 512],
                            in_=hbt[:, (u % 2) * 512:(u % 2) * 512 + 512]) \
                    .then_inc(p_hs[u % 2].h, 16)
                p_hs[u % 2].n += 16
                u += 1
        e.wait_ge(p_hs[0].h, p_hs[0].n)
        e.wait_ge(p_hs[1].h, p_hs[1].n)
        if "h" in d["dbg"]:
            e.dma_start(out=d["dbg_out"]["h"][:, :], in_=d["h_nat"][:, :]) \
                .then_inc(s_st.h, 16)
            s_st.n += 16
            e.wait_ge(s_st.h, s_st.n)
    ctx.gpsimd(fst)


def emit_hmax_finish(ctx, d):
    s_dve = d["s_dve"]

    def fv(e):
        e.reduce_max(d["lmax_col"][:, :], d["hmax_cols"][:, :], AX.X) \
            .then_inc(s_dve.h, 1)
        s_dve.n += 1
    ctx.vector(fv)


def emit_hquant(ctx, d, scope, slot):
    sb = d["sb"]
    s_ld, s_dve, s_act, s_st = d["s_ld"], d["s_dve"], d["s_act"], d["s_st"]
    hst = d.get("hst") or sb("hst", [128, 2 * TOK], F16, scope)
    hqo = d.get("hqo") or sb("hqo", [128, 2 * TOK], BF16, scope)
    hfs = d.get("hfs") or sb("hfs", [128, 2 * TOK], F32, scope)
    d["hst"], d["hqo"], d["hfs"] = hst, hqo, hfs

    h_t = d["h_nat"].rearrange("(c p) n -> c p n", p=128)
    hq_t = d["hq_nat"].rearrange("(c p) n -> c p n", p=128)
    act0, dve0 = s_act.n, s_dve.n
    p_hl = mk_psems(ctx, d, "hql%d" % slot)
    p_qs = mk_psems(ctx, d, "hqs%d" % slot)

    def fsync(e):
        for ic in range(IC):
            if ic >= 2:
                e.wait_ge(p_qs[ic % 2].h, 16 * (ic // 2))
            e.dma_start(out=hst[:, (ic % 2) * TOK:(ic % 2 + 1) * TOK],
                        in_=h_t[ic]).then_inc(p_hl[ic % 2].h, 16)
            p_hl[ic % 2].n += 16
    ctx.sync(fsync)

    def fs(e):
        for ic in range(IC):
            sl = slice((ic % 2) * TOK, (ic % 2) * TOK + TOK)
            e.wait_ge(p_hl[ic % 2].h, 16 * (ic // 2 + 1))
            if ic >= 2:
                e.wait_ge(s_dve.h, dve0 + 2 * (ic - 1))
            e.activation(hfs[:, sl], hst[:, sl], AF.Copy,
                         scale=d["binv"][:, 2 * slot:2 * slot + 1]) \
                .then_inc(s_act.h, 1)
        s_act.n += IC
    ctx.scalar(fs)

    def fv(e):
        for ic in range(IC):
            sl = slice((ic % 2) * TOK, (ic % 2) * TOK + TOK)
            e.wait_ge(s_act.h, act0 + ic + 1)
            if ic >= 2:
                e.wait_ge(p_qs[ic % 2].h, 16 * (ic // 2))
            e.tensor_scalar(hfs[:, sl], hfs[:, sl], QMAX, -QMAX - 1.0,
                            ALU.min, ALU.max)
            e.tensor_scalar(hqo[:, sl], hfs[:, sl], RC, RC, ALU.add,
                            ALU.subtract).then_inc(s_dve.h, 2)
        s_dve.n += 2 * IC
    ctx.vector(fv)

    def fst(e):
        for ic in range(IC):
            sl = slice((ic % 2) * TOK, (ic % 2) * TOK + TOK)
            e.wait_ge(s_dve.h, dve0 + 2 * (ic + 1))
            e.dma_start(out=hq_t[ic], in_=hqo[:, sl]).then_inc(p_qs[ic % 2].h, 16)
            p_qs[ic % 2].n += 16
        e.wait_ge(p_qs[0].h, p_qs[0].n)
        e.wait_ge(p_qs[1].h, p_qs[1].n)
        if "hq" in d["dbg"]:
            e.dma_start(out=d["dbg_out"]["hq"][:, :], in_=d["hq_nat"][:, :]) \
                .then_inc(s_st.h, 16)
            s_st.n += 16
            e.wait_ge(s_st.h, s_st.n)
    ctx.gpsimd(fst)


def emit_bsd(ctx, d):
    s_pe, s_dve = d["s_pe"], d["s_dve"]
    pall = d["pall"]

    for hc in range(HC):
        def ft(e, hc=hc):
            e.wait_ge(s_dve.h, s_dve.n)
            e.transpose(pall[0:1, 6 * 512:6 * 512 + 128], d["wsc_d"][:, hc:hc + 1],
                        d["iden_f"][:, :]).then_inc(s_pe.h, 1)
            s_pe.n += 1
        ctx.tensor(ft)

        def fv(e, hc=hc):
            e.wait_ge(s_pe.h, s_pe.n)
            e.tensor_copy(d["brow"][:, :], pall[0:1, 6 * 512:6 * 512 + 128]) \
                .then_inc(s_dve.h, 1)
            s_dve.n += 1
        ctx.vector(fv)

        def ft2(e, hc=hc):
            e.wait_ge(s_dve.h, s_dve.n)
            e.matmul(pall[:, 7 * 512:7 * 512 + 128], d["ones_row"][:, :],
                     d["brow"][:, :], start=True, stop=True).then_inc(s_pe.h, 1)
            s_pe.n += 1
        ctx.tensor(ft2)

        def fv2(e, hc=hc):
            e.wait_ge(s_pe.h, s_pe.n)
            e.tensor_scalar(d["bsd"][:, hc * 128:(hc + 1) * 128],
                            pall[:, 7 * 512:7 * 512 + 128], 1.0 / QMAX, None,
                            ALU.mult).then_inc(s_dve.h, 1)
            s_dve.n += 1
        ctx.vector(fv2)


def emit_down(ctx, d, scope, app):
    sb = d["sb"]
    s_ld, s_pe, s_dve, s_act = d["s_ld"], d["s_pe"], d["s_dve"], d["s_act"]
    pall = d["pall"]
    slot = 2 * app + 1

    wdr = sb("wdr", [128, IC * DM], BF16, scope)
    hstage = sb("hstage", [128, 2 * IC * 256], F16, scope)
    hfs = sb("hfs", [128, IC * 256], F32, scope)
    hqb = sb("hqb", [128, 2 * IC * 256], BF16, scope)
    dst1 = sb("dst1", [128, DM], F32, scope)

    NTP = TT // 2
    CW = IC * 256
    h_v = d["h_nat"].rearrange("(c p) n -> p c n", p=128)
    wd_v = d["wdq_t"].rearrange("(c p) m -> p c m", p=128)
    p_wd = mk_psems(ctx, d, "dwd%d" % app)
    p_hl = mk_psems(ctx, d, "dhl%d" % app)
    pe0, dve0, act0 = s_pe.n, s_dve.n, s_act.n

    def fw(e):
        e.dma_start(out=wdr[:, :].rearrange("p (c m) -> p c m", c=IC),
                    in_=wd_v).then_inc(p_wd[0].h, 16)
        p_wd[0].n += 16
    ctx.sync(fw)

    def fhl(e):
        for tp in range(NTP):
            if tp >= 2:
                e.wait_ge(s_act.h, act0 + tp - 1)
            e.dma_start(out=hstage[:, (tp % 2) * CW:(tp % 2 + 1) * CW]
                        .rearrange("p (c n) -> p c n", c=IC),
                        in_=h_v[:, :, tp * 256:(tp + 1) * 256]) \
                .then_inc(p_hl[tp % 2].h, 16)
            p_hl[tp % 2].n += 16
    ctx.gpsimd(fhl)

    def fa(e):
        for tp in range(NTP):
            e.wait_ge(p_hl[tp % 2].h, 16 * (tp // 2 + 1))
            if tp >= 1:
                e.wait_ge(s_dve.h, dve0 + 5 * (tp - 1) + 1)
            e.activation(hfs[:, :], hstage[:, (tp % 2) * CW:(tp % 2 + 1) * CW],
                         AF.Copy, scale=d["binv"][:, 2 * slot:2 * slot + 1]) \
                .then_inc(s_act.h, 1)
        s_act.n += NTP
    ctx.scalar(fa)

    cnt_quant = {}
    c = dve0
    for tp in range(NTP):
        c += 1
        cnt_quant[tp] = c
        if tp >= 1:
            c += 4

    def emit_evac(e, tp):
        e.wait_ge(s_pe.h, pe0 + (tp + 1) * IC * 4)
        for ti in range(2):
            t = tp * 2 + ti
            bank0 = (tp % 2) * 4 + ti * 2
            e.tensor_mul(dst1[:, :],
                         pall[:, bank0 * 512:bank0 * 512 + 1024], d["bsd"][:, :])
            e.scalar_tensor_tensor(
                d["xr"][:, t * DM:(t + 1) * DM],
                dst1[:, :],
                d["hcol"][:, app:app + 1],
                d["xr"][:, t * DM:(t + 1) * DM],
                ALU.mult, ALU.add).then_inc(s_dve.h, 2)
            s_dve.n += 2
            d.setdefault("xr_dve", {})[t] = s_dve.n

    def fqv(e):
        for tp in range(NTP):
            e.wait_ge(s_act.h, act0 + tp + 1)
            if tp >= 2:
                e.wait_ge(s_pe.h, pe0 + 88 * (tp - 1))
            e.tensor_scalar(hfs[:, :], hfs[:, :], QMAX, -QMAX - 1.0,
                            ALU.min, ALU.max)
            e.tensor_scalar(hqb[:, (tp % 2) * CW:(tp % 2 + 1) * CW], hfs[:, :],
                            RC, RC, ALU.add, ALU.subtract).then_inc(s_dve.h, 1)
            s_dve.n += 1
            if tp >= 1:
                emit_evac(e, tp - 1)
        emit_evac(e, NTP - 1)
    ctx.vector(fqv)

    def ft(e):
        e.wait_ge(p_wd[0].h, 16)
        for tp in range(NTP):
            e.wait_ge(s_dve.h, cnt_quant[tp])
            for ic in range(IC):
                hb = (tp % 2) * CW
                for ti in range(2):
                    for half in range(2):
                        bank = (tp % 2) * 4 + ti * 2 + half
                        ii = e.matmul(
                            pall[:, bank * 512:bank * 512 + 512],
                            hqb[:, hb + ic * 256 + ti * 128:hb + ic * 256 + ti * 128 + 128],
                            wdr[:, ic * DM + half * 512:ic * DM + half * 512 + 512],
                            start=(ic == 0), stop=(ic == IC - 1))
                ii.then_inc(s_pe.h, 4)
        s_pe.n += 4 * NTP * IC
    ctx.tensor(ft)


PI = 3.14159265358979323846


def emit_attention(ctx, d, scope):
    nc = ctx.nc
    sb = d["sb"]
    s_pe, s_dve, s_act, s_gp = d["s_pe"], d["s_dve"], d["s_act"], d["s_gp"]
    s_ld, s_sc, s_gld, s_st, s_cc = d["s_ld"], d["s_sc"], d["s_gld"], d["s_st"], d["s_cc"]
    pall = d["pall"]

    emit_ln_stats(ctx, d, want_max=False)
    ctx.barrier()

    battn = sb("battn", [128, 4], F32, scope)
    ones_col = sb("ones_col", [128, 1], BF16, scope)
    core_scope = ExitStack()
    qf = sb("qf", [128, 4 * S], BF16, core_scope)
    kf = sb("kf", [128, 4 * S], BF16, core_scope)

    xst_scope = ExitStack()
    xst = sb("xst", [128, DM], BF16, xst_scope)
    p_xl = mk_psems(ctx, d, "axl")
    dve0 = s_dve.n

    def fv_ln(e):
        e.memset(ones_col[:, :], 1.0)
        for t in range(TT):
            sl = slice(0, DM)
            if t >= 1:
                e.wait_ge(p_xl[0].h, 16 * t)
            e.tensor_scalar(xst[:, sl], d["xr"][:, t * DM:(t + 1) * DM],
                            d["st_m"][:, t:t + 1], d["st_r"][:, t:t + 1],
                            ALU.subtract, ALU.mult).then_inc(s_dve.h, 1)
        s_dve.n += TT
    ctx.vector(fv_ln)

    ag_in_t = d["ag_xln_in"].rearrange("(t p) m -> t p m", p=128)

    def fst_ln(e):
        for t in range(TT):
            e.wait_ge(s_dve.h, dve0 + t + 1)
            e.dma_start(out=ag_in_t[t], in_=xst[:, :]).then_inc(p_xl[0].h, 16)
            p_xl[0].n += 16
    ctx.sync(fst_ln)

    def fcc(e):
        e.wait_ge(p_xl[0].h, p_xl[0].n)
        e.wait_ge(p_xl[1].h, p_xl[1].n)
        e.collective_compute(
            "AllGather", ALU.bypass,
            replica_groups=[[2 * i, 2 * i + 1] for i in range(NCORES // 2)],
            ins=[d["ag_xln_in"].ap().opt()], outs=[d["ag_xln_out"].ap().opt()],
        ).then_inc(s_cc.h, 1)
        s_cc.n += 1
        if "xln" in d["dbg"]:
            e.wait_ge(s_cc.h, s_cc.n)
            e.dma_start(out=d["dbg_out"]["xln"][:, :], in_=d["ag_xln_out"][:, :]) \
                .then_inc(s_gld.h, 16)
            s_gld.n += 16
            e.wait_ge(s_gld.h, s_gld.n)
    ctx.gpsimd(fcc)
    xst_scope.close()

    with ExitStack() as qscope:
        xfs = sb("xfs", [128, 2 * HC * 512], BF16, qscope)
        cos_t = sb("cos_t", [128, S], BF16, qscope)
        sin_t = sb("sin_t", [128, S], BF16, qscope)
        tmpf = sb("tmpf", [128, 2 * 512], F32, qscope)
        pio2 = sb("pio2", [128, 1], F32, qscope)
        qraw = sb("qraw", [128, 2 * 512], BF16, qscope)
        posrow = sb("posrow", [1, 512], F32, qscope)
        ivf = sb("ivf", [128, 1], F32, qscope)
        dummy_split_marker = None

        def fv_ivf(e):
            e.memset(pio2[:, :], PI / 2)
            pt = d["iota_p"]
            for step in range(5):
                ops = [
                    lambda: e.tensor_scalar(ivf[:, :], pt[:, :], 1.0 / 64,
                                            -63.0 / 128, ALU.mult, ALU.add),
                    lambda: e.tensor_scalar(ivf[:, :], ivf[:, :], RC, RC,
                                            ALU.add, ALU.subtract),
                    lambda: e.scalar_tensor_tensor(ivf[:, :], ivf[:, :], -64.0,
                                                   pt[:, :], ALU.mult, ALU.add),
                    lambda: e.tensor_scalar(ivf[:, :], ivf[:, :], 0.5, -0.25,
                                            ALU.mult, ALU.add),
                    lambda: e.tensor_scalar(ivf[:, :], ivf[:, :], RC, RC,
                                            ALU.add, ALU.subtract),
                ]
                ops[step]().then_inc(s_dve.h, 1)
                s_dve.n += 1
                e.wait_ge(s_dve.h, s_dve.n)
        ctx.vector(fv_ivf)

        def fs_ivf(e):
            e.wait_ge(s_dve.h, s_dve.n)
            import math
            e.activation(ivf[:, :], ivf[:, :], AF.Exp, scale=-math.log(10000.0) / 32) \
                .then_inc(s_act.h, 1)
            s_act.n += 1
        ctx.scalar(fs_ivf)

        for sp in range(8):
            def fgp(e, sp=sp):
                if sp == 0:
                    e.wait_ge(s_act.h, s_act.n)
                else:
                    e.wait_ge(s_dve.h, s_dve.n)
                e.iota(posrow[:, :], [[1, 512]], base=sp * 512,
                       channel_multiplier=0,
                       allow_small_or_imprecise_dtypes=True).then_inc(s_gp.h, 1)
                s_gp.n += 1
            ctx.gpsimd(fgp)

            def ft(e, sp=sp):
                e.wait_ge(s_gp.h, s_gp.n)
                e.wait_ge(s_dve.h, s_dve.n)
                e.matmul(pall[:, 6 * 512:7 * 512], d["ones_row"][:, :],
                         posrow[:, :], start=True, stop=True).then_inc(s_pe.h, 1)
                s_pe.n += 1
            ctx.tensor(ft)

            def fv(e, sp=sp):
                e.wait_ge(s_pe.h, s_pe.n)
                ang = tmpf[:, 0:512]
                r = tmpf[:, 512:1024]
                e.tensor_scalar(ang[:, :], pall[:, 6 * 512:7 * 512], ivf[:, :],
                                None, ALU.mult).then_inc(s_dve.h, 1)
                s_dve.n += 1
                e.wait_ge(s_dve.h, s_dve.n)
                e.tensor_scalar(r[:, :], ang[:, :], 1.0 / (2 * PI), None,
                                ALU.mult).then_inc(s_dve.h, 1)
                s_dve.n += 1
                e.wait_ge(s_dve.h, s_dve.n)
                e.tensor_scalar(r[:, :], r[:, :], RC, RC, ALU.add, ALU.subtract) \
                    .then_inc(s_dve.h, 1)
                s_dve.n += 1
                e.wait_ge(s_dve.h, s_dve.n)
                e.scalar_tensor_tensor(r[:, :], r[:, :], -2 * PI, ang[:, :],
                                       ALU.mult, ALU.add).then_inc(s_dve.h, 1)
                s_dve.n += 1
            ctx.vector(fv)

            def fs(e, sp=sp):
                e.wait_ge(s_dve.h, s_dve.n)
                e.activation(sin_t[:, sp * 512:(sp + 1) * 512], tmpf[:, 512:1024],
                             AF.Sin).then_inc(s_act.h, 1)
                s_act.n += 1
            ctx.scalar(fs)

            def fv2(e, sp=sp):
                ang = tmpf[:, 0:512]
                r = tmpf[:, 512:1024]
                e.wait_ge(s_act.h, s_act.n)
                e.tensor_scalar(r[:, :], ang[:, :], 1.0 / (2 * PI), 0.25,
                                ALU.mult, ALU.add).then_inc(s_dve.h, 1)
                s_dve.n += 1
                e.wait_ge(s_dve.h, s_dve.n)
                e.tensor_scalar(r[:, :], r[:, :], RC, RC, ALU.add, ALU.subtract) \
                    .then_inc(s_dve.h, 1)
                s_dve.n += 1
                e.wait_ge(s_dve.h, s_dve.n)
                e.scalar_tensor_tensor(r[:, :], r[:, :], -2 * PI, ang[:, :],
                                       ALU.mult, ALU.add).then_inc(s_dve.h, 1)
                s_dve.n += 1
            ctx.vector(fv2)

            def fs2(e, sp=sp):
                e.wait_ge(s_dve.h, s_dve.n)
                e.activation(cos_t[:, sp * 512:(sp + 1) * 512], tmpf[:, 512:1024],
                             AF.Sin, bias=pio2[:, :]).then_inc(s_act.h, 1)
                s_act.n += 1
            ctx.scalar(fs2)

        if "rotb" in d["dbg"]:
            def fdr(e):
                e.dma_start(out=d["dbg_out"]["rotb"][:, :], in_=d["rot_b"][:, :]) \
                    .then_inc(s_gld.h, 16)
                s_gld.n += 16
                e.wait_ge(s_gld.h, s_gld.n)
            ctx.gpsimd(fdr)
        if d["dbg"] & {"cos", "sin", "ivf"}:
            ctx.barrier()
            def fdbg_t(e):
                for nm, buf in (("cos", cos_t), ("sin", sin_t), ("ivf", ivf)):
                    if nm in d["dbg"]:
                        e.dma_start(out=d["dbg_out"][nm][:, :], in_=buf[:, :]) \
                            .then_inc(s_gld.h, 16)
                        s_gld.n += 16
                e.wait_ge(s_gld.h, s_gld.n)
            ctx.gpsimd(fdbg_t)
            ctx.barrier()
        for (w_in, dstf, pname) in ((d["wq_in"], qf, "aq"), (d["wk_in"], kf, "ak")):
            with ExitStack() as wsc:
                wqa = sb("w" + pname, [128, HC * 512], BF16, wsc)
                wq_v = w_in.rearrange("(c p) m -> p c m", p=128)

                def fwl(e, wqa=wqa, wq_v=wq_v):
                    e.wait_ge(s_pe.h, s_pe.n)
                    e.dma_start(out=wqa[:, :].rearrange("p (c m) -> p c m", c=HC),
                                in_=wq_v).then_inc(s_gld.h, 16)
                    s_gld.n += 16
                    e.wait_ge(s_cc.h, s_cc.n)
                ctx.gpsimd(fwl)
                ctx.barrier()

                p_tr = mk_psems(ctx, d, "atr" + pname)
                dve_pass0 = s_dve.n
                pe_pass0 = s_pe.n
                for sp in range(8):
                    def ftr(e, sp=sp, p_tr=p_tr):
                        if sp >= 2:
                            e.wait_ge(s_pe.h, pe_pass0 + 8 * (sp - 1))
                        xb = (sp % 2) * HC * 512
                        for hc in range(HC):
                            e.dma_start_transpose(
                                xfs[:, xb + hc * 512:xb + (hc + 1) * 512],
                                d["ag_xln_out"][sp * 512:(sp + 1) * 512,
                                                hc * 128:(hc + 1) * 128]) \
                                .then_inc(p_tr[sp % 2].h, 16)
                            p_tr[sp % 2].n += 16
                    ctx.sync(ftr)

                    for mc in range(4):
                        n_u = sp * 4 + mc
                        st_ = (n_u % 2) * 4
                        qb_, rq_ = st_, st_ + 2
                        qsl = slice((n_u % 2) * 512, (n_u % 2) * 512 + 512)
                        xb = (sp % 2) * HC * 512

                        def ft(e, sp=sp, mc=mc, qb_=qb_, wqa=wqa, p_tr=p_tr,
                               n_u=n_u, xb=xb):
                            e.wait_ge(p_tr[sp % 2].h, 128 * (sp // 2 + 1))
                            if n_u >= 2:
                                e.wait_ge(s_dve.h, dve_pass0 + 3 * (n_u - 1))
                            for hc in range(HC):
                                inst = e.matmul(
                                    pall[:, qb_ * 512:qb_ * 512 + 512],
                                    wqa[:, hc * 512 + mc * 128:hc * 512 + mc * 128 + 128],
                                    xfs[:, xb + hc * 512:xb + (hc + 1) * 512],
                                    start=(hc == 0), stop=(hc == HC - 1))
                            inst.then_inc(s_pe.h, 1)
                            s_pe.n += 1
                        ctx.tensor(ft)

                        def fv_cp(e, qb_=qb_, qsl=qsl):
                            e.wait_ge(s_pe.h, s_pe.n)
                            e.tensor_copy(qraw[:, qsl],
                                          pall[:, qb_ * 512:qb_ * 512 + 512]) \
                                .then_inc(s_dve.h, 1)
                            s_dve.n += 1
                        ctx.vector(fv_cp)
                        if sp == 0 and mc == 3 and pname == "aq" and \
                                "qraw0" in d["dbg"]:
                            def fdq(e):
                                e.wait_ge(s_dve.h, s_dve.n)
                                e.dma_start(out=d["dbg_out"]["qraw0"][:, :],
                                            in_=qraw[:, :]).then_inc(s_gld.h, 16)
                                s_gld.n += 16
                                e.wait_ge(s_gld.h, s_gld.n)
                            ctx.gpsimd(fdq)
                            def fdx(e):
                                e.dma_start(out=d["dbg_out"]["xfs0"][:, :],
                                            in_=xfs[:, :]).then_inc(s_gld.h, 16)
                                s_gld.n += 16
                                e.wait_ge(s_gld.h, s_gld.n)
                            ctx.gpsimd(fdx)

                        def ft_rot(e, rq_=rq_, qsl=qsl):
                            e.wait_ge(s_dve.h, s_dve.n)
                            e.matmul(pall[:, rq_ * 512:rq_ * 512 + 512],
                                     d["rot_b"][:, :], qraw[:, qsl],
                                     start=True, stop=True).then_inc(s_pe.h, 1)
                            s_pe.n += 1
                        ctx.tensor(ft_rot)

                        def fv_rope(e, sp=sp, mc=mc, rq_=rq_, dstf=dstf,
                                    qsl=qsl):
                            cs = cos_t[:, sp * 512:(sp + 1) * 512]
                            sn = sin_t[:, sp * 512:(sp + 1) * 512]
                            e.tensor_mul(tmpf[:, 0:512], qraw[:, qsl], cs)
                            e.wait_ge(s_pe.h, s_pe.n)
                            e.tensor_mul(tmpf[:, 512:1024],
                                         pall[:, rq_ * 512:rq_ * 512 + 512], sn) \
                                .then_inc(s_dve.h, 1)
                            s_dve.n += 1
                            e.wait_ge(s_dve.h, s_dve.n)
                            e.tensor_add(dstf[:, mc * S + sp * 512:mc * S + sp * 512 + 512],
                                         tmpf[:, 0:512], tmpf[:, 512:1024]) \
                                .then_inc(s_dve.h, 1)
                            s_dve.n += 1
                        ctx.vector(fv_rope)

        dummy_end_qk = None
    ctx.barrier()
    vt = sb("vt", [128, ST * 520], BF16, core_scope)

    def fv_vt(e):
        e.memset(vt[:, :], 1.0).then_inc(s_dve.h, 1)
        s_dve.n += 1
    ctx.vector(fv_vt)

    with ExitStack() as qscope:
        xfs = sb("xfs", [128, 2 * HC * 512], BF16, qscope)
        qraw = sb("qraw", [128, 512], BF16, qscope)
        nrm = sb("nrm", [1, 64], F32, qscope)
        nfin = sb("nfin", [1, 8], F32, qscope)
        with ExitStack() as wsc:
            wva = sb("wva", [128, HC * 512], BF16, wsc)
            wv_v = d["wv_in"].rearrange("(c p) m -> p c m", p=128)

            def fwl(e):
                e.wait_ge(s_pe.h, s_pe.n)
                e.wait_ge(s_dve.h, s_dve.n)
                e.dma_start(out=wva[:, :].rearrange("p (c m) -> p c m", c=HC),
                            in_=wv_v).then_inc(s_gld.h, 16)
                s_gld.n += 16
            ctx.gpsimd(fwl)
            ctx.barrier()

            p_tr2 = mk_psems(ctx, d, "atr2")
            pe_v0 = s_pe.n
            for sp in range(8):
                def ftr(e, sp=sp):
                    if sp >= 2:
                        e.wait_ge(s_pe.h, pe_v0 + 4 * (sp - 1))
                    xb = (sp % 2) * HC * 512
                    for hc in range(HC):
                        e.dma_start_transpose(
                            xfs[:, xb + hc * 512:xb + (hc + 1) * 512],
                            d["ag_xln_out"][sp * 512:(sp + 1) * 512,
                                            hc * 128:(hc + 1) * 128]) \
                            .then_inc(p_tr2[sp % 2].h, 16)
                        p_tr2[sp % 2].n += 16
                ctx.sync(ftr)

                for ti4 in range(4):
                    st_ = ((sp * 4 + ti4) % 2) * 4
                    xb = (sp % 2) * HC * 512
                    def ft(e, sp=sp, ti4=ti4, st_=st_, xb=xb):
                        e.wait_ge(p_tr2[sp % 2].h, 128 * (sp // 2 + 1))
                        if sp * 4 + ti4 >= 2:
                            e.wait_ge(s_dve.h, s_dve.n - 1)
                        for hc in range(HC):
                            inst = e.matmul(
                                pall[:, st_ * 512:st_ * 512 + 512],
                                xfs[:, xb + hc * 512 + ti4 * 128:xb + hc * 512 + ti4 * 128 + 128],
                                wva[:, hc * 512:(hc + 1) * 512],
                                start=(hc == 0), stop=(hc == HC - 1))
                        inst.then_inc(s_pe.h, 1)
                        s_pe.n += 1
                    ctx.tensor(ft)

                    def fv(e, sp=sp, ti4=ti4, st_=st_):
                        kb = sp * 4 + ti4
                        e.wait_ge(s_pe.h, s_pe.n)
                        e.tensor_copy(
                            vt[:, kb * 520:(kb + 1) * 520]
                            .rearrange("p (h c) -> p h c", h=8)[:, :, 0:64],
                            pall[:, st_ * 512:st_ * 512 + 512]
                            .rearrange("p (h c) -> p h c", h=8)) \
                            .then_inc(s_dve.h, 1)
                        s_dve.n += 1
                    ctx.vector(fv)

            def fvb(e):
                e.memset(battn[:, :], -32.0).then_inc(s_dve.h, 1)
                s_dve.n += 1
            ctx.vector(fvb)
    ctx.barrier()
    if d["dbg"] & {"qf", "kf", "vt", "battn"}:
        def fdbg(e):
            for nm, buf in (("qf", qf), ("kf", kf), ("vt", vt), ("battn", battn)):
                if nm in d["dbg"]:
                    e.dma_start(out=d["dbg_out"][nm][:, :], in_=buf[:, :]) \
                        .then_inc(s_gld.h, 16)
                    s_gld.n += 16
            e.wait_ge(s_gld.h, s_gld.n)
        ctx.gpsimd(fdbg)
        ctx.barrier()

    with ExitStack() as sscope:
        av_fm = sb("av_fm", [128, 4 * S], BF16, sscope)
        ebAB = sb("ebAB", [128, 3 * 1024], BF16, sscope)
        nrow = d["sqb"][32:33, 0:1]
        bcs = d["sqb"][64:128, 0:256].bitcast(BF16)

        uidx = 0
        evac_end = {}
        pend_bcast = []
        unit_meta = {}
        for hp in range(4):
            for qb in range(8):
                nkb = 4 * qb + 4
                pe0 = s_pe.n
                act0 = s_act.n
                dve_u0 = s_dve.n
                avb = 4 + 2 * (uidx % 2)

                pe_sc = {}
                pe_av = {}
                _c = pe0
                for kb in range(nkb):
                    _c += 1
                    pe_sc[kb] = _c
                    if kb >= 2:
                        _c += 1
                        pe_av[kb - 2] = _c
                _c += 1
                pe_av[nkb - 2] = _c
                _c += 1
                pe_av[nkb - 1] = _c

                def emit_av(e, kb, hp, nkb, act0, dve_u0, qb, avb):
                    e.wait_ge(s_act.h, act0 + kb + 1)
                    if kb >= 4 * qb:
                        e.wait_ge(s_dve.h, dve_u0 + (kb - 4 * qb + 1))
                    ebs = (kb % 3) * 1024
                    e.matmul(pall[:, avb * 512:avb * 512 + 512][0:65, :],
                             vt[:, kb * 520 + (2 * hp) * 65:kb * 520 + (2 * hp) * 65 + 65],
                             ebAB[:, ebs:ebs + 512],
                             start=(kb == 0), stop=(kb == nkb - 1))
                    e.matmul(pall[:, (avb + 1) * 512:(avb + 1) * 512 + 512][0:65, :],
                             vt[:, kb * 520 + (2 * hp + 1) * 65:kb * 520 + (2 * hp + 1) * 65 + 65],
                             ebAB[:, ebs + 512:ebs + 1024],
                             start=(kb == 0), stop=(kb == nkb - 1)) \
                        .then_inc(s_pe.h, 1)

                w_evac = evac_end.get(uidx - 2, 0)
                my_pend = list(pend_bcast)
                pend_bcast.clear()
                nb = len(my_pend)
                if nb:
                    unit_meta[uidx - 1] = unit_meta[uidx - 1][:3] + (s_pe.n + nb,)
                pe0 += nb
                for kk in list(pe_sc):
                    pe_sc[kk] += nb
                for kk in list(pe_av):
                    pe_av[kk] += nb

                def ft(e, hp=hp, qb=qb, nkb=nkb, pe0=pe0, act0=act0,
                       dve_u0=dve_u0, avb=avb, w_evac=w_evac, my_pend=my_pend):
                    for (pavb, hloc, wdve, nsl) in my_pend:
                        e.wait_ge(s_dve.h, wdve)
                        bank = pavb + hloc
                        e.matmul(pall[:, bank * 512:bank * 512 + 512][64:128, :],
                                 d["ones_row"][:, 0:64],
                                 d["sqb"][0:1, nsl], start=True, stop=True) \
                            .then_inc(s_pe.h, 1)
                        s_pe.n += 1
                    if w_evac:
                        e.wait_ge(s_dve.h, w_evac)
                    for kb in range(nkb):
                        scs = (kb % 2) * 2
                        if kb >= 2:
                            e.wait_ge(s_act.h, act0 + kb - 1)
                        e.matmul(pall[:, scs * 512:scs * 512 + 512],
                                 kf[0:64, hp * S + kb * 128:hp * S + kb * 128 + 128],
                                 qf[0:64, hp * S + qb * 512:hp * S + qb * 512 + 512],
                                 start=True, stop=True)
                        e.matmul(pall[:, (scs + 1) * 512:(scs + 1) * 512 + 512],
                                 kf[64:128, hp * S + kb * 128:hp * S + kb * 128 + 128],
                                 qf[64:128, hp * S + qb * 512:hp * S + qb * 512 + 512],
                                 start=True, stop=True).then_inc(s_pe.h, 1)
                        if kb >= 2:
                            emit_av(e, kb - 2, hp, nkb, act0, dve_u0, qb, avb)
                    emit_av(e, nkb - 2, hp, nkb, act0, dve_u0, qb, avb)
                    emit_av(e, nkb - 1, hp, nkb, act0, dve_u0, qb, avb)
                    s_pe.n += 2 * nkb
                ctx.tensor(ft)

                def fs(e, nkb=nkb, pe0=pe0, hp=hp, pe_sc=pe_sc, pe_av=pe_av):
                    for kb in range(nkb):
                        scs = (kb % 2) * 2
                        ebs = (kb % 3) * 1024
                        e.wait_ge(s_pe.h, pe_sc[kb])
                        if kb >= 3:
                            e.wait_ge(s_pe.h, pe_av[kb - 3])
                        e.activation(ebAB[:, ebs:ebs + 1024],
                                     pall[:, scs * 512:scs * 512 + 1024], AF.Exp,
                                     bias=battn[:, hp:hp + 1], scale=0.125) \
                            .then_inc(s_act.h, 1)
                    s_act.n += nkb
                ctx.scalar(fs)

                def fvm(e, qb=qb, nkb=nkb, act0=act0):
                    for kb in range(4 * qb, nkb):
                        off = (kb - 4 * qb) * 128
                        ebs = (kb % 3) * 1024
                        e.wait_ge(s_act.h, act0 + kb + 1)
                        for hof in (0, 512):
                            if off > 0:
                                e.memset(ebAB[:, ebs + hof:ebs + hof + off], 0.0)
                            e.tensor_mul(ebAB[:, ebs + hof + off:ebs + hof + off + 128],
                                         ebAB[:, ebs + hof + off:ebs + hof + off + 128],
                                         d["tri_b"][:, :])
                        e.tensor_scalar(nrow[:, 0:1], nrow[:, 0:1], 1.0, None,
                                        ALU.mult).then_inc(s_dve.h, 1)
                        s_dve.n += 1
                ctx.vector(fvm)

                pe_mm_end = s_pe.n
                for hloc in (0, 1):
                    nsl = slice(hloc * 512, hloc * 512 + 512)

                    def fve1(e, hloc=hloc, avb=avb, pe_mm_end=pe_mm_end,
                             nsl=nsl):
                        e.wait_ge(s_pe.h, pe_mm_end)
                        bank = avb + hloc
                        e.tensor_scalar(d["sqb"][0:1, nsl],
                                        pall[64:65, bank * 512:bank * 512 + 512],
                                        1e-30, None, ALU.max).then_inc(s_dve.h, 1)
                        s_dve.n += 1
                        e.wait_ge(s_dve.h, s_dve.n)
                        e.reciprocal(d["sqb"][0:1, nsl], d["sqb"][0:1, nsl]) \
                            .then_inc(s_dve.h, 1)
                        s_dve.n += 1
                        pend_bcast.append((avb, hloc, s_dve.n, nsl))
                    ctx.vector(fve1)

                if uidx >= 1:
                    phl, pqb, pavb, ppe = unit_meta[uidx - 1]
                    assert ppe is not None

                    def fve2(e, phl=phl, pqb=pqb, pavb=pavb, ppe=ppe):
                        e.wait_ge(s_pe.h, ppe)
                        for hloc in (0, 1):
                            bank = pavb + hloc
                            e.tensor_copy(bcs[0:64, :],
                                          pall[64:128, bank * 512:bank * 512 + 512]) \
                                .then_inc(s_dve.h, 1)
                            s_dve.n += 1
                            e.tensor_mul(
                                av_fm[hloc * 64:hloc * 64 + 64,
                                      phl * S + pqb * 512:phl * S + pqb * 512 + 512],
                                pall[0:64, bank * 512:bank * 512 + 512],
                                bcs[0:64, :]).then_inc(s_dve.h, 1)
                            s_dve.n += 1
                    ctx.vector(fve2)
                    evac_end[uidx - 1] = s_dve.n
                unit_meta[uidx] = (hp, qb, avb, None)
                uidx += 1
        my_pend = list(pend_bcast)
        pend_bcast.clear()
        lhl, lqb, lavb, _ = unit_meta[uidx - 1]
        bcast_done = s_pe.n + len(my_pend)

        def ft_tail(e, my_pend=my_pend):
            for (pavb, hloc, wdve, nsl) in my_pend:
                e.wait_ge(s_dve.h, wdve)
                bank = pavb + hloc
                e.matmul(pall[:, bank * 512:bank * 512 + 512][64:128, :],
                         d["ones_row"][:, 0:64],
                         d["sqb"][0:1, nsl], start=True, stop=True) \
                    .then_inc(s_pe.h, 1)
                s_pe.n += 1
        ctx.tensor(ft_tail)

        def fve2_tail(e, lhl=lhl, lqb=lqb, lavb=lavb, bcast_done=bcast_done):
            e.wait_ge(s_pe.h, bcast_done)
            for hloc in (0, 1):
                bank = lavb + hloc
                e.tensor_copy(bcs[0:64, :],
                              pall[64:128, bank * 512:bank * 512 + 512]) \
                    .then_inc(s_dve.h, 1)
                s_dve.n += 1
                e.tensor_mul(
                    av_fm[hloc * 64:hloc * 64 + 64,
                          lhl * S + lqb * 512:lhl * S + lqb * 512 + 512],
                    pall[0:64, bank * 512:bank * 512 + 512],
                    bcs[0:64, :]).then_inc(s_dve.h, 1)
                s_dve.n += 1
        ctx.vector(fve2_tail)
        ctx.barrier()

        ag_av_in_v = d["ag_av_in"].rearrange("(c p) n -> p c n", p=128)

        def fxa(e):
            if "avf" in d["dbg"]:
                e.dma_start(out=d["dbg_out"]["avf"][:, :], in_=av_fm[:, :]) \
                    .then_inc(s_gld.h, 16)
                s_gld.n += 16
                e.wait_ge(s_gld.h, s_gld.n)
            e.dma_start(out=ag_av_in_v,
                        in_=av_fm[:, :].rearrange("p (c n) -> p c n", c=4)) \
                .then_inc(s_gld.h, 16)
            s_gld.n += 16
            e.wait_ge(s_gld.h, s_gld.n)
            e.collective_compute(
                "AllGather", ALU.bypass,
                replica_groups=[[2 * i, 2 * i + 1] for i in range(NCORES // 2)],
                ins=[d["ag_av_in"].ap().opt()], outs=[d["ag_av_out"].ap().opt()],
            ).then_inc(s_cc.h, 1)
            s_cc.n += 1
            e.wait_ge(s_cc.h, s_cc.n)
        ctx.gpsimd(fxa)
    ctx.barrier()

    core_scope.close()
    with ExitStack() as oscope:
        avall = sb("avall", [128, 8 * TOK], BF16, oscope)
        wob = sb("wob", [128, HC * DM], BF16, oscope)
        wo_v = d["wo_in"].rearrange("(c p) m -> p c m", p=128)

        def fol(e):
            e.dma_start(out=wob[:, :].rearrange("p (c m) -> p c m", c=HC),
                        in_=wo_v).then_inc(s_gld.h, 16)
            s_gld.n += 16
            with e.register("rk") as rk:
                e.cc_rank_ld(rk, replica_groups=[[2 * i, 2 * i + 1]
                                                 for i in range(NCORES // 2)])
                with e.register("ofs") as ofs:
                    e.reg_mov(ofs, rk)
                    e.reg_mul(ofs, ofs, TOK)
                    e.dma_start(
                        out=avall[:, :].rearrange("p (c n) -> p c n", c=8),
                        in_=bass.AP(d["ag_av_out"], ofs,
                                    [[S, 128], [128 * S, 8], [1, TOK]])) \
                        .then_inc(s_gld.h, 16)
                    s_gld.n += 16
            e.wait_ge(s_gld.h, s_gld.n)
            if "avall" in d["dbg"]:
                e.dma_start(out=d["dbg_out"]["avall"][:, :], in_=avall[:, :]) \
                    .then_inc(s_gld.h, 16)
                s_gld.n += 16
                e.wait_ge(s_gld.h, s_gld.n)
        ctx.gpsimd(fol)
        ctx.barrier()

        pe0 = s_pe.n
        dve0 = s_dve.n

        def ft(e):
            for t in range(TT):
                st_ = (t % 2) * 2
                if t >= 2:
                    e.wait_ge(s_dve.h, dve0 + t - 1)
                for half in range(2):
                    for dc in range(HC):
                        inst = e.matmul(
                            pall[:, (st_ + half) * 512:(st_ + half) * 512 + 512],
                            avall[:, dc * TOK + t * 128:dc * TOK + t * 128 + 128],
                            wob[:, dc * DM + half * 512:dc * DM + half * 512 + 512],
                            start=(dc == 0), stop=(dc == HC - 1))
                inst.then_inc(s_pe.h, 1)
            s_pe.n += TT
        ctx.tensor(ft)

        def fv(e):
            for t in range(TT):
                st_ = (t % 2) * 2
                e.wait_ge(s_pe.h, pe0 + t + 1)
                e.tensor_add(d["xr"][:, t * DM:(t + 1) * DM],
                             pall[:, st_ * 512:st_ * 512 + 1024],
                             d["xr"][:, t * DM:(t + 1) * DM]).then_inc(s_dve.h, 1)
                s_dve.n += 1
                d.setdefault("xr_dve", {})[t] = s_dve.n
        ctx.vector(fv)
    ctx.barrier()


def emit_output(ctx, d):
    s_st, s_dve = d["s_st"], d["s_dve"]
    xr_dve = dict(d.get("xr_dve", {}))
    out_t = d["out"].rearrange("(t p) m -> t p m", p=128)

    def fst(e):
        for t in range(TT):
            if t in xr_dve:
                e.wait_ge(s_dve.h, xr_dve[t])
            e.dma_start(out=out_t[t], in_=d["xr"][:, t * DM:(t + 1) * DM]) \
                .then_inc(s_st.h, 16)
        s_st.n += 16 * TT
        e.wait_ge(s_st.h, s_st.n)
    ctx.sync(fst)



_BUILD_KW = {}


def kernel(L0_attn_x, L0_attn_Wq, L0_attn_Wk, L0_attn_Wv, L0_attn_Wo,
           L0_ffn_clip_x, L0_ffn_clip_down, L0_ffn_w_gate, L0_ffn_w_up,
           L0_ffn_w_down):
    x = np.ascontiguousarray(np.asarray(L0_attn_x, dtype=np.float32))
    wq = np.ascontiguousarray(np.asarray(L0_attn_Wq, dtype=np.float32))
    wk = np.ascontiguousarray(np.asarray(L0_attn_Wk, dtype=np.float32))
    wv = np.ascontiguousarray(np.asarray(L0_attn_Wv, dtype=np.float32))
    wo = np.ascontiguousarray(np.asarray(L0_attn_Wo, dtype=np.float32))
    wg = np.ascontiguousarray(np.asarray(L0_ffn_w_gate, dtype=np.float32))
    wu = np.ascontiguousarray(np.asarray(L0_ffn_w_up, dtype=np.float32))
    wd = np.ascontiguousarray(np.asarray(L0_ffn_w_down, dtype=np.float32))
    clip = np.array([[np.asarray(L0_ffn_clip_x, np.float32).ravel()[0],
                      np.asarray(L0_ffn_clip_down, np.float32).ravel()[0]]],
                    dtype=np.float32)

    nc, dbg_out = build_program(**_BUILD_KW)
    in_maps = []
    for c in range(NCORES):
        b, h2 = c // 2, c % 2
        in_maps.append({
            "x": np.ascontiguousarray(x[b, h2 * TOK:(h2 + 1) * TOK, :]),
            "wq": np.ascontiguousarray(wq[:, h2 * 512:(h2 + 1) * 512]),
            "wk": np.ascontiguousarray(wk[:, h2 * 512:(h2 + 1) * 512]),
            "wv": np.ascontiguousarray(wv[:, h2 * 512:(h2 + 1) * 512]),
            "wo": wo, "wg": wg, "wu": wu, "wd": wd, "clip": clip,
        })
    res = run_bass_kernel_spmd(nc, in_maps, core_ids=list(range(NCORES)))
    out = np.empty((4, S, DM), np.float32)
    for c in range(NCORES):
        b, h2 = c // 2, c % 2
        out[b, h2 * TOK:(h2 + 1) * TOK, :] = res.results[c]["out"]
    if _BUILD_KW.get("debug"):
        kernel.last_results = res
    return out


# version 15
# speedup vs baseline: 1.0455x; 1.0129x over previous
import numpy as np
from contextlib import ExitStack

import concourse.bass as bass
import concourse.mybir as mybir
from concourse.bass_utils import run_bass_kernel_spmd

F32 = mybir.dt.float32
BF16 = mybir.dt.bfloat16
F16 = mybir.dt.float16
AF = mybir.ActivationFunctionType
ALU = mybir.AluOpType
AX = mybir.AxisListType

NCORES = 8
TOK = 2048
TT = TOK // 128
DM = 1024
HC = DM // 128
DI = 2816
IC = DI // 128
S = 4096
ST = S // 128
QMAX = 127.0
RC = 1.5 * 2.0 ** 23
EPS = 1e-5
DK = 64

ENGS = ("tensor", "vector", "scalar", "gpsimd", "sync")


class Sem:
    def __init__(self, es, nc, name):
        self.h = es.enter_context(nc.semaphore(name))
        self.n = 0


class InstRec:

    def __init__(self, inst):
        self.i = inst
        self.updated = False

    def then_inc(self, sem, k=None):
        self.updated = True
        if k is None:
            return self.i.then_inc(sem)
        return self.i.then_inc(sem, k)


class EngRec:
    def __init__(self, eng, store, key):
        object.__setattr__(self, "_eng", eng)
        object.__setattr__(self, "_store", store)
        object.__setattr__(self, "_key", key)

    def __getattr__(self, name):
        attr = getattr(self._eng, name)
        if not callable(attr):
            return attr

        def wrap(*a, **kw):
            r = attr(*a, **kw)
            if isinstance(r, bass.BassInstruction):
                r = InstRec(r)
                self._store[self._key] = r
            return r
        return wrap


class Ctx:
    def __init__(self, nc, block, es):
        self.nc = nc
        self.block = block
        self.es = es
        self.last = {}
        self.sems = []
        self.s_bar = Sem(es, nc, "bar")

    def sec(self, eng_name, fn):
        def g(eng):
            fn(EngRec(eng, self.last, eng_name))
        getattr(self.block, eng_name)(g)

    def tensor(self, fn): self.sec("tensor", fn)
    def vector(self, fn): self.sec("vector", fn)
    def scalar(self, fn): self.sec("scalar", fn)
    def gpsimd(self, fn): self.sec("gpsimd", fn)
    def sync(self, fn): self.sec("sync", fn)

    def barrier(self):
        target = self.s_bar.n
        for e in ENGS:
            li = self.last.pop(e, None)
            if li is not None and not li.updated:
                try:
                    li.i.then_inc(self.s_bar.h, 1)
                    target += 1
                except AssertionError:
                    li.i.then_inc(self.s_bar.h, 16)
                    target += 16
        waits = [(s.h, s.n) for s in self.sems if s.n > 0]
        if target > self.s_bar.n:
            waits.append((self.s_bar.h, target))

        for e in ENGS:
            def f(eng, ws=tuple(waits)):
                for h, n in ws:
                    eng.wait_ge(h, n)
            self.sec(e, f)
        self.s_bar.n = target
        self.last = {}



def build_program(upto="full", debug=()):
    nc = bass.Bass("TRN2", target_bir_lowering=False)
    dbg = set(debug)
    d = {}
    d["upto"] = upto

    for nm, shape, dt in [
        ("x", [TOK, DM], F32), ("wq", [DM, 512], F32), ("wk", [DM, 512], F32),
        ("wv", [DM, 512], F32), ("wo", [DM, DM], F32), ("wg", [DI, DM], F32),
        ("wu", [DI, DM], F32), ("wd", [DM, DI], F32), ("clip", [1, 2], F32),
    ]:
        d[nm + "_in"] = nc.dram_tensor(nm, shape, dt, kind="ExternalInput").ap()
    d["out"] = nc.dram_tensor("out", [TOK, DM], F32, kind="ExternalOutput").ap()

    d["dbg"] = dbg
    d["dbg_out"] = {}

    def mkdbg(name, shape, dt=F32):
        if name in dbg:
            d["dbg_out"][name] = nc.dram_tensor(
                "dbg_" + name, shape, dt, kind="ExternalOutput").ap()
    d["mkdbg"] = mkdbg
    mkdbg("xq", [TOK, DM], BF16)
    mkdbg("h", [DI, TOK], F16)
    mkdbg("hq", [DI, TOK], BF16)
    mkdbg("gmax", [1, 4], F32)
    mkdbg("wgq", [DI, DM], BF16)
    mkdbg("qf", [128, 4 * S], BF16)
    mkdbg("kf", [128, 4 * S], BF16)
    mkdbg("vt", [128, (S // 128) * 520], BF16)
    mkdbg("avf", [128, 4 * S], BF16)
    mkdbg("xln", [S, DM], BF16)
    mkdbg("battn", [128, 4], F32)
    mkdbg("cos", [128, S], BF16)
    mkdbg("xfs0", [128, HC * 512], BF16)
    mkdbg("rotb", [128, 128], BF16)
    mkdbg("avall", [128, 8 * TOK], BF16)
    mkdbg("qraw0", [128, 512], BF16)
    mkdbg("sin", [128, S], BF16)
    mkdbg("ivf", [128, 1], F32)

    d["wgq_nat"] = nc.dram_tensor("wgq_nat", [DI, DM], BF16)
    d["wuq_nat"] = nc.dram_tensor("wuq_nat", [DI, DM], BF16)
    d["wdq_nat"] = nc.dram_tensor("wdq_nat", [DM, DI], BF16)
    d["wgq_t"] = nc.dram_tensor("wgq_t", [DM, DI], BF16)
    d["wuq_t"] = nc.dram_tensor("wuq_t", [DM, DI], BF16)
    d["wdq_t"] = nc.dram_tensor("wdq_t", [DI, DM], BF16)
    d["xq_nat"] = nc.dram_tensor("xq_nat", [TOK, DM], BF16)
    d["h_nat"] = nc.dram_tensor("h_nat", [DI, TOK], F16)
    d["hq_nat"] = nc.dram_tensor("hq_nat", [DI, TOK], BF16)
    d["mx_in"] = nc.dram_tensor("mx_in", [1, 16], F32)
    d["mx_out"] = nc.dram_tensor("mx_out", [1, 16], F32)
    d["ag_xln_in"] = nc.dram_tensor("ag_xln_in", [TOK, DM], BF16)
    d["ag_xln_out"] = nc.dram_tensor("ag_xln_out", [S, DM], BF16)
    for _h in range(4):
        d[f"ag_avi{_h}"] = nc.dram_tensor(f"ag_avi{_h}", [128, S], BF16)
        d[f"ag_avo{_h}"] = nc.dram_tensor(f"ag_avo{_h}", [256, S], BF16)

    es = ExitStack()
    d["es_root"] = es

    _cnt = [0]

    def sb(name, shape, dt, scope=None):
        _cnt[0] += 1
        return (scope or es).enter_context(
            nc.sbuf_tensor(f"{name}_{_cnt[0]}", shape, dt))

    def psum(name, shape, dt, scope=None):
        return (scope or es).enter_context(nc.psum_tensor(name, shape, dt))
    d["sb"] = sb
    d["psum"] = psum

    d["xr"] = sb("xr", [128, TT * DM], F32)
    d["iden_f"] = sb("iden_f", [128, 128], F32)
    d["tri_b"] = sb("tri_b", [128, 128], BF16)
    d["rot_b"] = sb("rot_b", [128, 128], BF16)
    d["ones_row"] = sb("ones_row", [1, 128], F32)
    d["iota_p"] = sb("iota_p", [128, 1], F32)
    d["sclip"] = sb("sclip", [1, 2], F32)
    d["ssig"] = sb("ssig", [1, 2], F32)
    d["s_tmp"] = sb("s_tmp", [1, 8], F32)
    d["gmax"] = sb("gmax", [1, 4], F32)
    d["binv"] = sb("binv", [128, 8], F32)
    d["hcol"] = sb("hcol", [128, 2], F32)
    d["lmax_col"] = sb("lmax_col", [128, 1], F32)
    for nm in ("st_m", "st_r", "st_a", "st_b", "st_c", "st_d"):
        d[nm] = sb(nm, [128, TT], F32)
    d["wsc_g"] = sb("wsc_g", [128, IC], F32)
    d["wsc_u"] = sb("wsc_u", [128, IC], F32)
    d["wsc_d"] = sb("wsc_d", [128, HC], F32)
    d["sqb"] = sb("sqb", [128, DM], F32)

    d["pall"] = psum("pall", [128, 8 * 512], F32)

    with nc.Block() as block:
        ctx = Ctx(nc, block, es)
        for s in ("s_ld", "s_sc", "s_gld", "s_st", "s_pe", "s_dve",
                  "s_act", "s_gp", "s_cc"):
            d[s] = Sem(es, nc, s)
            ctx.sems.append(d[s])
        emit_all(ctx, d)
    es.close()
    return nc, d["dbg_out"]


def mk_psems(ctx, d, name):
    import itertools
    es = ctx.es
    sems = [Sem(es, ctx.nc, f"{name}{i}") for i in range(2)]
    for s in sems:
        ctx.sems.append(s)
    return sems


def emit_all(ctx, d):
    upto = d["upto"]
    with ExitStack() as pscope:
        d["iota_j"] = d["sb"]("iota_j", [128, 128], F32, pscope)
        d["scr1"] = d["sb"]("scr1", [128, 128], F32, pscope)
        d["scr2"] = d["sb"]("scr2", [128, 128], F32, pscope)
        emit_prologue(ctx, d)
        ctx.barrier()
    with ExitStack() as wscope:
        emit_weight_quant(ctx, d, wscope)
    emit_ln_stats(ctx, d, want_max=True)
    ctx.barrier()
    emit_maxreduce(ctx, d, slot=0)
    emit_scale_chain(ctx, d, slot=0, sig_idx=0)
    ctx.barrier()
    with ExitStack() as fscope:
        emit_ffn(ctx, d, fscope, app=0)
    if upto == "ffn1":
        emit_output(ctx, d)
        return
    with ExitStack() as ascope:
        emit_attention(ctx, d, ascope)
    if upto == "attn":
        emit_output(ctx, d)
        return
    emit_ln_stats(ctx, d, want_max=True)
    ctx.barrier()
    emit_maxreduce(ctx, d, slot=2)
    emit_scale_chain(ctx, d, slot=2, sig_idx=0)
    ctx.barrier()
    with ExitStack() as fscope:
        emit_ffn(ctx, d, fscope, app=1)
    emit_output(ctx, d)


def emit_ffn(ctx, d, scope, app):
    sb = d["sb"]
    nc = ctx.nc
    d["hmax_cols"] = sb("hmax_cols", [128, IC * 4], F32, scope)
    d["qstage"] = sb("qstage", [128, 2 * DM], F32, scope)
    d["qout"] = sb("qout", [128, 2 * DM], BF16, scope)
    d["srs"] = sb("srs", [128, TT], F32, scope)
    d["bsd"] = sb("bsd", [128, DM], F32, scope)
    d["brow"] = sb("brow", [1, 128], F32, scope)
    with ExitStack() as gscope:
        d["xq_fm"] = sb("xq_fm", [128, HC * TOK], BF16, gscope)
        emit_xq(ctx, d, slot=2 * app)
        ctx.barrier()
        emit_gateup(ctx, d, gscope, slot=2 * app)
        ctx.barrier()
    emit_hmax_finish(ctx, d)
    emit_maxreduce(ctx, d, slot=2 * app + 1)
    emit_scale_chain(ctx, d, slot=2 * app + 1, sig_idx=1)
    emit_bsd(ctx, d)
    ctx.barrier()
    with ExitStack() as dscope:
        emit_down(ctx, d, dscope, app=app)
    ctx.barrier()
    for k in ("xq_fm", "hmax_cols", "qstage", "qout", "srs", "wgb", "wub", "sgt", "upt",
              "hft", "hbt", "svg", "svu", "hst", "hqo", "hfs", "hqb", "wdb",
              "wdr", "hstage", "dst1", "bsd", "brow"):
        d.pop(k, None)



def emit_prologue(ctx, d):
    s_ld, s_gp, s_dve, s_act = d["s_ld"], d["s_gp"], d["s_dve"], d["s_act"]
    x_t = d["x_in"].rearrange("(t p) m -> t p m", p=128)
    xr = d["xr"]

    def fsync(e):
        for t in range(TT):
            e.dma_start(out=xr[:, t * DM:(t + 1) * DM], in_=x_t[t]).then_inc(s_ld.h, 16)
        e.dma_start(out=d["sclip"][:, :], in_=d["clip_in"][:, :]).then_inc(s_ld.h, 16)
        s_ld.n += 16 * (TT + 1)
    ctx.sync(fsync)

    def fgp(e):
        e.iota(d["iota_j"][:, :], [[1, 128]], channel_multiplier=0,
               allow_small_or_imprecise_dtypes=True)
        e.iota(d["iota_p"][:, :], [[1, 1]], channel_multiplier=1,
               allow_small_or_imprecise_dtypes=True)
        e.memset(d["ones_row"][:, :], 1.0).then_inc(s_gp.h, 1)
        s_gp.n += 1
    ctx.gpsimd(fgp)

    def fv(e):
        e.wait_ge(s_gp.h, s_gp.n)
        jt, pt = d["iota_j"], d["iota_p"]
        e.tensor_scalar(d["iden_f"][:, :], jt[:, :], pt[:, :], None, ALU.is_equal)
        e.tensor_scalar(d["tri_b"][:, :], jt[:, :], pt[:, :], 0.0,
                        ALU.subtract, ALU.is_ge)
        pcol = d["lmax_col"]
        e.tensor_scalar(pcol[:, :], pt[:, :], 0.5, -0.25, ALU.mult, ALU.add) \
            .then_inc(s_dve.h, 1)
        s_dve.n += 1
        e.wait_ge(s_dve.h, s_dve.n)
        e.tensor_scalar(pcol[:, :], pcol[:, :], RC, RC, ALU.add, ALU.subtract) \
            .then_inc(s_dve.h, 1)
        s_dve.n += 1
        e.wait_ge(s_dve.h, s_dve.n)
        e.scalar_tensor_tensor(pcol[:, :], pcol[:, :], -2.0, pt[:, :],
                               ALU.mult, ALU.add).then_inc(s_dve.h, 1)
        s_dve.n += 1
        e.wait_ge(s_dve.h, s_dve.n)
        scr1, scr2 = d["scr1"], d["scr2"]
        e.tensor_scalar(scr2[:, :], jt[:, :], pt[:, :], 1.0, ALU.subtract,
                        ALU.is_equal).then_inc(s_dve.h, 1)
        s_dve.n += 1
        e.wait_ge(s_dve.h, s_dve.n)
        e.tensor_scalar(scr1[:, :], scr2[:, :], pcol[:, :], None, ALU.mult) \
            .then_inc(s_dve.h, 1)
        s_dve.n += 1
        e.wait_ge(s_dve.h, s_dve.n)
        e.tensor_sub(scr2[:, :], scr2[:, :], scr1[:, :]).then_inc(s_dve.h, 1)
        s_dve.n += 1
        e.wait_ge(s_dve.h, s_dve.n)
        e.tensor_scalar(scr1[:, :], jt[:, :], pt[:, :], -1.0, ALU.subtract,
                        ALU.is_equal).then_inc(s_dve.h, 1)
        s_dve.n += 1
        e.wait_ge(s_dve.h, s_dve.n)
        e.tensor_scalar(scr1[:, :], scr1[:, :], pcol[:, :], None, ALU.mult) \
            .then_inc(s_dve.h, 1)
        s_dve.n += 1
        e.wait_ge(s_dve.h, s_dve.n)
        e.tensor_sub(d["rot_b"][:, :], scr2[:, :], scr1[:, :]).then_inc(s_dve.h, 1)
        s_dve.n += 1
    ctx.vector(fv)

    def fs(e):
        e.wait_ge(s_ld.h, s_ld.n)
        e.activation(d["ssig"][:, :], d["sclip"][:, :], AF.Sigmoid) \
            .then_inc(s_act.h, 1)
        s_act.n += 1
    ctx.scalar(fs)


def emit_weight_quant(ctx, d, scope):
    sb = d["sb"]
    s_ld, s_sc, s_dve, s_act, s_st = d["s_ld"], d["s_sc"], d["s_dve"], d["s_act"], d["s_st"]

    wst = sb("wst", [128, 2 * DI], F32, scope)
    wsq = sb("wsq", [128, 2 * DI], F32, scope)
    wqb = sb("wqb", [128, 2 * DI], BF16, scope)
    rcol = sb("rcol", [128, 4], F32, scope)
    tb = sb("tb", [128, 2 * DI], BF16, scope)

    jobs = []
    for (src, dst, wsc, nch, W) in [
        (d["wg_in"], d["wgq_nat"], d["wsc_g"], IC, DM),
        (d["wu_in"], d["wuq_nat"], d["wsc_u"], IC, DM),
        (d["wd_in"], d["wdq_nat"], d["wsc_d"], HC, DI),
    ]:
        for k in range(nch):
            jobs.append((src, dst, wsc, k, W))
    nj = len(jobs)
    dve0, act0, st0 = s_dve.n, s_act.n, s_st.n
    p_ld = mk_psems(ctx, d, "wql")
    p_st = mk_psems(ctx, d, "wqs")

    def fsync(e):
        for j, (src, dst, wsc, k, W) in enumerate(jobs):
            if j >= 2:
                e.wait_ge(s_act.h, act0 + j - 1)
            srcv = src.rearrange("(c p) m -> c p m", p=128)
            e.dma_start(out=wst[:, (j % 2) * DI:(j % 2) * DI + W],
                        in_=srcv[k]).then_inc(p_ld[j % 2].h, 16)
            p_ld[j % 2].n += 16
    ctx.sync(fsync)

    def fv(e):
        for j, (src, dst, wsc, k, W) in enumerate(jobs):
            sl = slice((j % 2) * DI, (j % 2) * DI + W)
            e.wait_ge(p_ld[j % 2].h, 16 * (j // 2 + 1))
            e.reduce_max(wsc[:, k:k + 1], wst[:, sl], AX.X,
                         apply_absolute_value=True).then_inc(s_dve.h, 1)
            e.wait_ge(s_dve.h, dve0 + 5 * j + 1)
            e.reciprocal(rcol[:, 2 * (j % 2):2 * (j % 2) + 1], wsc[:, k:k + 1]) \
                .then_inc(s_dve.h, 1)
            e.wait_ge(s_dve.h, dve0 + 5 * j + 2)
            e.tensor_scalar(rcol[:, 2 * (j % 2) + 1:2 * (j % 2) + 2],
                            rcol[:, 2 * (j % 2):2 * (j % 2) + 1], QMAX, None,
                            ALU.mult).then_inc(s_dve.h, 1)
            e.wait_ge(s_act.h, act0 + j + 1)
            if j >= 2:
                e.wait_ge(p_st[j % 2].h, 16 * (j // 2))
            e.tensor_scalar(wsq[:, sl], wsq[:, sl], QMAX, -QMAX - 1.0,
                            ALU.min, ALU.max).then_inc(s_dve.h, 1)
            e.wait_ge(s_dve.h, dve0 + 5 * j + 4)
            e.tensor_scalar(wqb[:, sl], wsq[:, sl], RC, RC, ALU.add,
                            ALU.subtract).then_inc(s_dve.h, 1)
        s_dve.n += 5 * nj
    ctx.vector(fv)

    def fs(e):
        for j, (src, dst, wsc, k, W) in enumerate(jobs):
            sl = slice((j % 2) * DI, (j % 2) * DI + W)
            e.wait_ge(s_dve.h, dve0 + 5 * j + 3)
            e.activation(wsq[:, sl], wst[:, sl], AF.Copy,
                         scale=rcol[:, 2 * (j % 2) + 1:2 * (j % 2) + 2]) \
                .then_inc(s_act.h, 1)
        s_act.n += nj
    ctx.scalar(fs)

    def fst(e):
        for j, (src, dst, wsc, k, W) in enumerate(jobs):
            sl = slice((j % 2) * DI, (j % 2) * DI + W)
            e.wait_ge(s_dve.h, dve0 + 5 * (j + 1))
            dstv = dst.rearrange("(c p) m -> c p m", p=128)
            e.dma_start(out=dstv[k], in_=wqb[:, sl]).then_inc(p_st[j % 2].h, 16)
            p_st[j % 2].n += 16
    ctx.gpsimd(fst)

    ctx.barrier()

    p_tr = mk_psems(ctx, d, "bnt")
    p_bw = mk_psems(ctx, d, "bnc")
    bounce_jobs = []
    for (natt, tt_, nch, W) in [
        (d["wgq_nat"], d["wgq_t"], HC, DI),
        (d["wuq_nat"], d["wuq_t"], HC, DI),
        (d["wdq_nat"], d["wdq_t"], IC, DM),
    ]:
        for k in range(nch):
            bounce_jobs.append((natt, tt_, k, W))

    def fb(e):
        for n, (natt, tt_, k, W) in enumerate(bounce_jobs):
            buf = tb[:, (n % 2) * DI:(n % 2) * DI + W]
            if n >= 2:
                e.wait_ge(p_bw[n % 2].h, 16 * (n // 2))
            e.dma_start_transpose(buf, natt[:, k * 128:(k + 1) * 128]) \
                .then_inc(p_tr[n % 2].h, 16)
            p_tr[n % 2].n += 16
    ctx.sync(fb)

    def fbw(e):
        for n, (natt, tt_, k, W) in enumerate(bounce_jobs):
            buf = tb[:, (n % 2) * DI:(n % 2) * DI + W]
            e.wait_ge(p_tr[n % 2].h, 16 * (n // 2 + 1))
            e.dma_start(out=tt_[k * 128:(k + 1) * 128, :], in_=buf) \
                .then_inc(p_bw[n % 2].h, 16)
            p_bw[n % 2].n += 16
        e.wait_ge(p_bw[0].h, p_bw[0].n)
        e.wait_ge(p_bw[1].h, p_bw[1].n)
    ctx.gpsimd(fbw)

    if "wgq" in d["dbg"]:
        s_gld = d["s_gld"]

        def fdbg(e):
            e.wait_ge(s_st.h, s_st.n)
            e.dma_start(out=d["dbg_out"]["wgq"][:, :], in_=d["wgq_nat"][:, :]) \
                .then_inc(s_gld.h, 16)
            s_gld.n += 16
            e.wait_ge(s_gld.h, s_gld.n)
        ctx.gpsimd(fdbg)
    ctx.barrier()


def emit_ln_stats(ctx, d, want_max):
    s_ld, s_dve, s_act = d["s_ld"], d["s_dve"], d["s_act"]
    xr = d["xr"]
    sqb = d["sqb"]
    act0 = s_act.n
    dve0 = s_dve.n

    xr_dve = dict(d.get("xr_dve", {}))

    def fs(e):
        for t in range(TT):
            if t in xr_dve:
                e.wait_ge(s_dve.h, xr_dve[t])
            if t >= 1:
                e.wait_ge(s_dve.h, dve0 + t)
            sl = slice(0, DM)
            e.activation(sqb[:, sl], xr[:, t * DM:(t + 1) * DM], AF.Square) \
                .then_inc(s_act.h, 1)
        s_act.n += TT
    ctx.scalar(fs)

    def fv(e):
        for t in range(TT):
            xt = xr[:, t * DM:(t + 1) * DM]
            sl = slice(0, DM)
            e.reduce_sum(d["st_a"][:, t:t + 1], xt, AX.X)
            if want_max:
                e.reduce_max(d["st_c"][:, t:t + 1], xt, AX.X)
                e.tensor_reduce(d["st_d"][:, t:t + 1], xt, AX.X, ALU.min)
            e.wait_ge(s_act.h, act0 + t + 1)
            e.reduce_sum(d["st_b"][:, t:t + 1], sqb[:, sl], AX.X) \
                .then_inc(s_dve.h, 1)
        s_dve.n += TT
        e.wait_ge(s_dve.h, s_dve.n)
        e.tensor_scalar(d["st_m"][:, :], d["st_a"][:, :], 1.0 / DM, None,
                        ALU.mult).then_inc(s_dve.h, 1)
        s_dve.n += 1
        e.tensor_scalar(d["st_b"][:, :], d["st_b"][:, :], 1.0 / DM, None,
                        ALU.mult).then_inc(s_dve.h, 1)
        s_dve.n += 1
        e.wait_ge(s_dve.h, s_dve.n)
        e.tensor_mul(d["st_a"][:, :], d["st_m"][:, :], d["st_m"][:, :]) \
            .then_inc(s_dve.h, 1)
        s_dve.n += 1
        e.wait_ge(s_dve.h, s_dve.n)
        e.scalar_tensor_tensor(d["st_b"][:, :], d["st_b"][:, :], EPS,
                               d["st_a"][:, :], ALU.add, ALU.subtract) \
            .then_inc(s_dve.h, 1)
        s_dve.n += 1
    ctx.vector(fv)

    def fs2(e):
        e.wait_ge(s_dve.h, s_dve.n)
        e.activation(d["st_r"][:, :], d["st_b"][:, :], AF.Sqrt).then_inc(s_act.h, 1)
        s_act.n += 1
    ctx.scalar(fs2)

    def fv2(e):
        e.wait_ge(s_act.h, s_act.n)
        e.reciprocal(d["st_r"][:, :], d["st_r"][:, :]).then_inc(s_dve.h, 1)
        s_dve.n += 1
        if want_max:
            e.wait_ge(s_dve.h, s_dve.n)
            e.tensor_sub(d["st_c"][:, :], d["st_c"][:, :], d["st_m"][:, :]) \
                .then_inc(s_dve.h, 1)
            s_dve.n += 1
            e.tensor_sub(d["st_d"][:, :], d["st_m"][:, :], d["st_d"][:, :]) \
                .then_inc(s_dve.h, 1)
            s_dve.n += 1
            e.wait_ge(s_dve.h, s_dve.n)
            e.tensor_max(d["st_c"][:, :], d["st_c"][:, :], d["st_d"][:, :]) \
                .then_inc(s_dve.h, 1)
            s_dve.n += 1
            e.wait_ge(s_dve.h, s_dve.n)
            e.tensor_mul(d["st_c"][:, :], d["st_c"][:, :], d["st_r"][:, :]) \
                .then_inc(s_dve.h, 1)
            s_dve.n += 1
            e.wait_ge(s_dve.h, s_dve.n)
            e.reduce_max(d["lmax_col"][:, :], d["st_c"][:, :], AX.X) \
                .then_inc(s_dve.h, 1)
            s_dve.n += 1
    ctx.vector(fv2)


def emit_maxreduce(ctx, d, slot):
    s_pe, s_dve, s_gld, s_cc, s_gp = d["s_pe"], d["s_dve"], d["s_gld"], d["s_cc"], d["s_gp"]
    pall = d["pall"]

    def ft(e):
        e.wait_ge(s_dve.h, s_dve.n)
        e.transpose(pall[0:1, 7 * 512:7 * 512 + 128], d["lmax_col"][:, :],
                    d["iden_f"][:, :]).then_inc(s_pe.h, 1)
        s_pe.n += 1
    ctx.tensor(ft)

    def fv(e):
        e.wait_ge(s_pe.h, s_pe.n)
        e.reduce_max(d["s_tmp"][:, 0:1], pall[0:1, 7 * 512:7 * 512 + 128], AX.X) \
            .then_inc(s_dve.h, 1)
        s_dve.n += 1
    ctx.vector(fv)

    def fg(e):
        e.memset(d["s_tmp"][:, 1:4], 0.0).then_inc(s_gp.h, 1)
        s_gp.n += 1
        e.wait_ge(s_gp.h, s_gp.n)
        e.wait_ge(s_dve.h, s_dve.n)
        e.dma_start(out=d["mx_in"][:, 0:4], in_=d["s_tmp"][:, 0:4]) \
            .then_inc(s_gld.h, 16)
        s_gld.n += 16
        e.wait_ge(s_gld.h, s_gld.n)
        e.collective_compute(
            "AllReduce", ALU.max, replica_groups=[list(range(NCORES))],
            ins=[d["mx_in"].ap()], outs=[d["mx_out"].ap()],
        ).then_inc(s_cc.h, 1)
        s_cc.n += 1
        e.wait_ge(s_cc.h, s_cc.n)
        e.dma_start(out=d["gmax"][:, slot:slot + 1], in_=d["mx_out"][:, 0:1]) \
            .then_inc(s_gld.h, 16)
        s_gld.n += 16
        e.wait_ge(s_gld.h, s_gld.n)
    ctx.gpsimd(fg)


def emit_scale_chain(ctx, d, slot, sig_idx):
    s_dve, s_pe, s_gld = d["s_dve"], d["s_pe"], d["s_gld"]
    pall = d["pall"]

    def fv(e):
        e.wait_ge(s_gld.h, s_gld.n)
        e.tensor_scalar(d["s_tmp"][:, 4:5], d["gmax"][:, slot:slot + 1],
                        d["ssig"][:, sig_idx:sig_idx + 1], 1.0 / QMAX,
                        ALU.mult, ALU.mult).then_inc(s_dve.h, 1)
        s_dve.n += 1
        e.wait_ge(s_dve.h, s_dve.n)
        e.reciprocal(d["s_tmp"][:, 5:6], d["s_tmp"][:, 4:5]).then_inc(s_dve.h, 1)
        s_dve.n += 1
    ctx.vector(fv)

    def ft(e):
        e.wait_ge(s_dve.h, s_dve.n)
        e.matmul(pall[:, 7 * 512:7 * 512 + 2], d["ones_row"][:, :],
                 d["s_tmp"][0:1, 4:6], start=True, stop=True).then_inc(s_pe.h, 1)
        s_pe.n += 1
    ctx.tensor(ft)

    def fv2(e):
        e.wait_ge(s_pe.h, s_pe.n)
        e.tensor_copy(d["binv"][:, 2 * slot:2 * slot + 1],
                      pall[:, 7 * 512 + 1:7 * 512 + 2])
        e.tensor_copy(d["binv"][:, 2 * slot + 1:2 * slot + 2],
                      pall[:, 7 * 512:7 * 512 + 1]).then_inc(s_dve.h, 1)
        s_dve.n += 1
        if slot % 2 == 1:
            e.wait_ge(s_dve.h, s_dve.n)
            e.tensor_scalar(d["hcol"][:, slot // 2:slot // 2 + 1],
                            d["binv"][:, 2 * slot + 1:2 * slot + 2], 0.5, None,
                            ALU.mult).then_inc(s_dve.h, 1)
            s_dve.n += 1
        if "gmax" in d["dbg"]:
            e.wait_ge(s_dve.h, s_dve.n)

            def fdbg(e2):
                e2.wait_ge(s_dve.h, s_dve.n)
                e2.dma_start(out=d["dbg_out"]["gmax"][:, :], in_=d["gmax"][:, :]) \
                    .then_inc(d["s_st"].h, 16)
                d["s_st"].n += 16
            ctx.gpsimd(fdbg)
    ctx.vector(fv2)


def emit_xq(ctx, d, slot):
    s_dve, s_st, s_sc = d["s_dve"], d["s_st"], d["s_sc"]
    qstage, qout, srs, xq_fm = d["qstage"], d["qout"], d["srs"], d["xq_fm"]
    dve0 = s_dve.n
    p_xs = mk_psems(ctx, d, "xqs%d" % slot)

    def fv(e):
        e.tensor_scalar(srs[:, :], d["st_r"][:, :],
                        d["binv"][:, 2 * slot:2 * slot + 1], None, ALU.mult) \
            .then_inc(s_dve.h, 1)
        e.wait_ge(s_dve.h, dve0 + 1)
        for t in range(TT):
            sl = slice((t % 2) * DM, (t % 2) * DM + DM)
            if t >= 2:
                e.wait_ge(p_xs[t % 2].h, 16 * (t // 2))
            xt = d["xr"][:, t * DM:(t + 1) * DM]
            e.tensor_scalar(qstage[:, sl], xt, d["st_m"][:, t:t + 1],
                            srs[:, t:t + 1], ALU.subtract, ALU.mult)
            e.tensor_scalar(qstage[:, sl], qstage[:, sl], QMAX, -QMAX - 1.0,
                            ALU.min, ALU.max)
            e.tensor_scalar(qout[:, sl], qstage[:, sl], RC, RC, ALU.add,
                            ALU.subtract).then_inc(s_dve.h, 1)
        s_dve.n += 1 + TT
    ctx.vector(fv)

    xq_t = d["xq_nat"].rearrange("(t p) m -> t p m", p=128)
    sc0 = s_sc.n

    def fst(e):
        for t in range(TT):
            sl = slice((t % 2) * DM, (t % 2) * DM + DM)
            e.wait_ge(s_dve.h, dve0 + 2 + t)
            e.dma_start(out=xq_t[t], in_=qout[:, sl]).then_inc(p_xs[t % 2].h, 16)
            p_xs[t % 2].n += 16
        e.wait_ge(p_xs[0].h, p_xs[0].n)
        e.wait_ge(p_xs[1].h, p_xs[1].n)
        for hc in range(HC):
            e.dma_start_transpose(xq_fm[:, hc * TOK:(hc + 1) * TOK],
                                  d["xq_nat"][:, hc * 128:(hc + 1) * 128]) \
                .then_inc(s_sc.h, 16)
        s_sc.n += 16 * HC
        e.wait_ge(s_sc.h, s_sc.n)
        if "xq" in d["dbg"]:
            e.dma_start(out=d["dbg_out"]["xq"][:, :], in_=d["xq_nat"][:, :]) \
                .then_inc(s_st.h, 16)
            s_st.n += 16
            e.wait_ge(s_st.h, s_st.n)
    ctx.sync(fst)


def emit_gateup(ctx, d, scope, slot):
    sb = d["sb"]
    s_ld, s_pe, s_dve, s_act, s_st = d["s_ld"], d["s_pe"], d["s_dve"], d["s_act"], d["s_st"]
    pall, xq_fm = d["pall"], d["xq_fm"]

    wgb = d.get("wgb") or sb("wgb", [128, 2 * HC * 128], BF16, scope)
    wub = d.get("wub") or sb("wub", [128, 2 * HC * 128], BF16, scope)
    sgt = d.get("sgt") or sb("sgt", [128, 2 * 512], F32, scope)
    upt = d.get("upt") or sb("upt", [128, 2 * 512], F32, scope)
    hft = d.get("hft") or sb("hft", [128, 2 * 512], F32, scope)
    hbt = d.get("hbt") or sb("hbt", [128, 2 * 512], F16, scope)
    svg = d.get("svg") or sb("svg", [128, IC], F32, scope)
    svu = d.get("svu") or sb("svu", [128, IC], F32, scope)
    for nm, v in [("wgb", wgb), ("wub", wub), ("sgt", sgt), ("upt", upt),
                  ("hft", hft), ("hbt", hbt), ("svg", svg), ("svu", svu)]:
        d[nm] = v

    NS = 4
    nu = IC * NS
    pe0, act0, dve0 = s_pe.n, s_act.n, s_dve.n
    p_wl = mk_psems(ctx, d, "gwl%d" % slot)
    p_hs = mk_psems(ctx, d, "ghs%d" % slot)
    wg_t = d["wgq_t"].rearrange("(c p) i -> p c i", p=128)
    wu_t = d["wuq_t"].rearrange("(c p) i -> p c i", p=128)

    def fsync(e):
        for ic in range(IC):
            if ic >= 2:
                e.wait_ge(s_pe.h, pe0 + (ic - 1) * NS)
            b = (ic % 2) * HC * 128
            e.dma_start(out=wgb[:, b:b + HC * 128].rearrange("p (c n) -> p c n", c=HC),
                        in_=wg_t[:, :, ic * 128:(ic + 1) * 128]).then_inc(p_wl[ic % 2].h, 16)
            e.dma_start(out=wub[:, b:b + HC * 128].rearrange("p (c n) -> p c n", c=HC),
                        in_=wu_t[:, :, ic * 128:(ic + 1) * 128]).then_inc(p_wl[ic % 2].h, 16)
            p_wl[ic % 2].n += 32
    ctx.sync(fsync)

    def fv0(e):
        e.tensor_scalar(svg[:, :], d["wsc_g"][:, :], d["binv"][:, 2 * slot + 1:2 * slot + 2],
                        1.0 / QMAX, ALU.mult, ALU.mult)
        e.tensor_scalar(svu[:, :], d["wsc_u"][:, :], d["binv"][:, 2 * slot + 1:2 * slot + 2],
                        1.0 / QMAX, ALU.mult, ALU.mult).then_inc(s_dve.h, 1)
        s_dve.n += 1
    ctx.vector(fv0)
    dve_pre = s_dve.n

    def ft(e):
        e.wait_ge(s_dve.h, dve_pre)
        u = 0
        for ic in range(IC):
            e.wait_ge(p_wl[ic % 2].h, 32 * (ic // 2 + 1))
            b = (ic % 2) * HC * 128
            for s in range(NS):
                if u >= 2:
                    e.wait_ge(s_dve.h, dve_pre + (u - 1) * 2)
                    e.wait_ge(s_act.h, act0 + u - 1)
                gb, ub = (u % 2) * 2, (u % 2) * 2 + 1
                for hc in range(HC):
                    e.matmul(pall[:, gb * 512:gb * 512 + 512],
                             wgb[:, b + hc * 128:b + hc * 128 + 128],
                             xq_fm[:, hc * TOK + s * 512:hc * TOK + (s + 1) * 512],
                             start=(hc == 0), stop=(hc == HC - 1))
                for hc in range(HC):
                    inst = e.matmul(
                        pall[:, ub * 512:ub * 512 + 512],
                        wub[:, b + hc * 128:b + hc * 128 + 128],
                        xq_fm[:, hc * TOK + s * 512:hc * TOK + (s + 1) * 512],
                        start=(hc == 0), stop=(hc == HC - 1))
                inst.then_inc(s_pe.h, 1)
                u += 1
        s_pe.n += nu
    ctx.tensor(ft)

    def fs(e):
        u = 0
        for ic in range(IC):
            for s in range(NS):
                e.wait_ge(s_pe.h, pe0 + u + 1)
                if u >= 2:
                    e.wait_ge(s_dve.h, dve_pre + (u - 1) * 2)
                gb = (u % 2) * 2
                e.activation(sgt[:, (u % 2) * 512:(u % 2) * 512 + 512],
                             pall[:, gb * 512:gb * 512 + 512], AF.Silu,
                             scale=svg[:, ic:ic + 1]).then_inc(s_act.h, 1)
                u += 1
        s_act.n += nu
    ctx.scalar(fs)

    def fv(e):
        u = 0
        for ic in range(IC):
            for s in range(NS):
                e.wait_ge(s_pe.h, pe0 + u + 1)
                ub = (u % 2) * 2 + 1
                if u >= 2:
                    e.wait_ge(p_hs[u % 2].h, 16 * (u // 2))
                e.tensor_scalar(upt[:, (u % 2) * 512:(u % 2) * 512 + 512],
                                pall[:, ub * 512:ub * 512 + 512],
                                svu[:, ic:ic + 1], None, ALU.mult)
                e.wait_ge(s_act.h, act0 + u + 1)
                e.tensor_mul(hft[:, (u % 2) * 512:(u % 2) * 512 + 512],
                             sgt[:, (u % 2) * 512:(u % 2) * 512 + 512],
                             upt[:, (u % 2) * 512:(u % 2) * 512 + 512])
                e.tensor_copy(hbt[:, (u % 2) * 512:(u % 2) * 512 + 512],
                              hft[:, (u % 2) * 512:(u % 2) * 512 + 512]) \
                    .then_inc(s_dve.h, 1)
                e.reduce_max(d["hmax_cols"][:, u:u + 1],
                             hft[:, (u % 2) * 512:(u % 2) * 512 + 512], AX.X,
                             apply_absolute_value=True).then_inc(s_dve.h, 1)
                u += 1
        s_dve.n += 2 * nu
    ctx.vector(fv)

    h_t = d["h_nat"].rearrange("(c p) n -> c p n", p=128)

    def fst(e):
        u = 0
        for ic in range(IC):
            for s in range(NS):
                e.wait_ge(s_dve.h, dve_pre + 2 * u + 2)
                e.dma_start(out=h_t[ic, :, s * 512:(s + 1) * 512],
                            in_=hbt[:, (u % 2) * 512:(u % 2) * 512 + 512]) \
                    .then_inc(p_hs[u % 2].h, 16)
                p_hs[u % 2].n += 16
                u += 1
        e.wait_ge(p_hs[0].h, p_hs[0].n)
        e.wait_ge(p_hs[1].h, p_hs[1].n)
        if "h" in d["dbg"]:
            e.dma_start(out=d["dbg_out"]["h"][:, :], in_=d["h_nat"][:, :]) \
                .then_inc(s_st.h, 16)
            s_st.n += 16
            e.wait_ge(s_st.h, s_st.n)
    ctx.gpsimd(fst)


def emit_hmax_finish(ctx, d):
    s_dve = d["s_dve"]

    def fv(e):
        e.reduce_max(d["lmax_col"][:, :], d["hmax_cols"][:, :], AX.X) \
            .then_inc(s_dve.h, 1)
        s_dve.n += 1
    ctx.vector(fv)


def emit_hquant(ctx, d, scope, slot):
    sb = d["sb"]
    s_ld, s_dve, s_act, s_st = d["s_ld"], d["s_dve"], d["s_act"], d["s_st"]
    hst = d.get("hst") or sb("hst", [128, 2 * TOK], F16, scope)
    hqo = d.get("hqo") or sb("hqo", [128, 2 * TOK], BF16, scope)
    hfs = d.get("hfs") or sb("hfs", [128, 2 * TOK], F32, scope)
    d["hst"], d["hqo"], d["hfs"] = hst, hqo, hfs

    h_t = d["h_nat"].rearrange("(c p) n -> c p n", p=128)
    hq_t = d["hq_nat"].rearrange("(c p) n -> c p n", p=128)
    act0, dve0 = s_act.n, s_dve.n
    p_hl = mk_psems(ctx, d, "hql%d" % slot)
    p_qs = mk_psems(ctx, d, "hqs%d" % slot)

    def fsync(e):
        for ic in range(IC):
            if ic >= 2:
                e.wait_ge(p_qs[ic % 2].h, 16 * (ic // 2))
            e.dma_start(out=hst[:, (ic % 2) * TOK:(ic % 2 + 1) * TOK],
                        in_=h_t[ic]).then_inc(p_hl[ic % 2].h, 16)
            p_hl[ic % 2].n += 16
    ctx.sync(fsync)

    def fs(e):
        for ic in range(IC):
            sl = slice((ic % 2) * TOK, (ic % 2) * TOK + TOK)
            e.wait_ge(p_hl[ic % 2].h, 16 * (ic // 2 + 1))
            if ic >= 2:
                e.wait_ge(s_dve.h, dve0 + 2 * (ic - 1))
            e.activation(hfs[:, sl], hst[:, sl], AF.Copy,
                         scale=d["binv"][:, 2 * slot:2 * slot + 1]) \
                .then_inc(s_act.h, 1)
        s_act.n += IC
    ctx.scalar(fs)

    def fv(e):
        for ic in range(IC):
            sl = slice((ic % 2) * TOK, (ic % 2) * TOK + TOK)
            e.wait_ge(s_act.h, act0 + ic + 1)
            if ic >= 2:
                e.wait_ge(p_qs[ic % 2].h, 16 * (ic // 2))
            e.tensor_scalar(hfs[:, sl], hfs[:, sl], QMAX, -QMAX - 1.0,
                            ALU.min, ALU.max)
            e.tensor_scalar(hqo[:, sl], hfs[:, sl], RC, RC, ALU.add,
                            ALU.subtract).then_inc(s_dve.h, 2)
        s_dve.n += 2 * IC
    ctx.vector(fv)

    def fst(e):
        for ic in range(IC):
            sl = slice((ic % 2) * TOK, (ic % 2) * TOK + TOK)
            e.wait_ge(s_dve.h, dve0 + 2 * (ic + 1))
            e.dma_start(out=hq_t[ic], in_=hqo[:, sl]).then_inc(p_qs[ic % 2].h, 16)
            p_qs[ic % 2].n += 16
        e.wait_ge(p_qs[0].h, p_qs[0].n)
        e.wait_ge(p_qs[1].h, p_qs[1].n)
        if "hq" in d["dbg"]:
            e.dma_start(out=d["dbg_out"]["hq"][:, :], in_=d["hq_nat"][:, :]) \
                .then_inc(s_st.h, 16)
            s_st.n += 16
            e.wait_ge(s_st.h, s_st.n)
    ctx.gpsimd(fst)


def emit_bsd(ctx, d):
    s_pe, s_dve = d["s_pe"], d["s_dve"]
    pall = d["pall"]

    for hc in range(HC):
        def ft(e, hc=hc):
            e.wait_ge(s_dve.h, s_dve.n)
            e.transpose(pall[0:1, 6 * 512:6 * 512 + 128], d["wsc_d"][:, hc:hc + 1],
                        d["iden_f"][:, :]).then_inc(s_pe.h, 1)
            s_pe.n += 1
        ctx.tensor(ft)

        def fv(e, hc=hc):
            e.wait_ge(s_pe.h, s_pe.n)
            e.tensor_copy(d["brow"][:, :], pall[0:1, 6 * 512:6 * 512 + 128]) \
                .then_inc(s_dve.h, 1)
            s_dve.n += 1
        ctx.vector(fv)

        def ft2(e, hc=hc):
            e.wait_ge(s_dve.h, s_dve.n)
            e.matmul(pall[:, 7 * 512:7 * 512 + 128], d["ones_row"][:, :],
                     d["brow"][:, :], start=True, stop=True).then_inc(s_pe.h, 1)
            s_pe.n += 1
        ctx.tensor(ft2)

        def fv2(e, hc=hc):
            e.wait_ge(s_pe.h, s_pe.n)
            e.tensor_scalar(d["bsd"][:, hc * 128:(hc + 1) * 128],
                            pall[:, 7 * 512:7 * 512 + 128], 1.0 / QMAX, None,
                            ALU.mult).then_inc(s_dve.h, 1)
            s_dve.n += 1
        ctx.vector(fv2)


def emit_down(ctx, d, scope, app):
    sb = d["sb"]
    s_ld, s_pe, s_dve, s_act = d["s_ld"], d["s_pe"], d["s_dve"], d["s_act"]
    pall = d["pall"]
    slot = 2 * app + 1

    wdr = sb("wdr", [128, IC * DM], BF16, scope)
    hstage = sb("hstage", [128, 2 * IC * 256], F16, scope)
    hfs = sb("hfs", [128, IC * 256], F32, scope)
    hqb = sb("hqb", [128, 2 * IC * 256], BF16, scope)
    dst1 = sb("dst1", [128, DM], F32, scope)

    NTP = TT // 2
    CW = IC * 256
    h_v = d["h_nat"].rearrange("(c p) n -> p c n", p=128)
    wd_v = d["wdq_t"].rearrange("(c p) m -> p c m", p=128)
    p_wd = mk_psems(ctx, d, "dwd%d" % app)
    p_hl = mk_psems(ctx, d, "dhl%d" % app)
    pe0, dve0, act0 = s_pe.n, s_dve.n, s_act.n

    def fw(e):
        e.dma_start(out=wdr[:, :].rearrange("p (c m) -> p c m", c=IC),
                    in_=wd_v).then_inc(p_wd[0].h, 16)
        p_wd[0].n += 16
    ctx.sync(fw)

    def fhl(e):
        for tp in range(NTP):
            if tp >= 2:
                e.wait_ge(s_act.h, act0 + tp - 1)
            e.dma_start(out=hstage[:, (tp % 2) * CW:(tp % 2 + 1) * CW]
                        .rearrange("p (c n) -> p c n", c=IC),
                        in_=h_v[:, :, tp * 256:(tp + 1) * 256]) \
                .then_inc(p_hl[tp % 2].h, 16)
            p_hl[tp % 2].n += 16
    ctx.gpsimd(fhl)

    def fa(e):
        for tp in range(NTP):
            e.wait_ge(p_hl[tp % 2].h, 16 * (tp // 2 + 1))
            if tp >= 1:
                e.wait_ge(s_dve.h, dve0 + 5 * (tp - 1) + 1)
            e.activation(hfs[:, :], hstage[:, (tp % 2) * CW:(tp % 2 + 1) * CW],
                         AF.Copy, scale=d["binv"][:, 2 * slot:2 * slot + 1]) \
                .then_inc(s_act.h, 1)
        s_act.n += NTP
    ctx.scalar(fa)

    cnt_quant = {}
    c = dve0
    for tp in range(NTP):
        c += 1
        cnt_quant[tp] = c
        if tp >= 1:
            c += 4

    def emit_evac(e, tp):
        e.wait_ge(s_pe.h, pe0 + (tp + 1) * IC * 4)
        for ti in range(2):
            t = tp * 2 + ti
            bank0 = (tp % 2) * 4 + ti * 2
            e.tensor_mul(dst1[:, :],
                         pall[:, bank0 * 512:bank0 * 512 + 1024], d["bsd"][:, :])
            e.scalar_tensor_tensor(
                d["xr"][:, t * DM:(t + 1) * DM],
                dst1[:, :],
                d["hcol"][:, app:app + 1],
                d["xr"][:, t * DM:(t + 1) * DM],
                ALU.mult, ALU.add).then_inc(s_dve.h, 2)
            s_dve.n += 2
            d.setdefault("xr_dve", {})[t] = s_dve.n

    def fqv(e):
        for tp in range(NTP):
            e.wait_ge(s_act.h, act0 + tp + 1)
            if tp >= 2:
                e.wait_ge(s_pe.h, pe0 + 88 * (tp - 1))
            e.tensor_scalar(hfs[:, :], hfs[:, :], QMAX, -QMAX - 1.0,
                            ALU.min, ALU.max)
            e.tensor_scalar(hqb[:, (tp % 2) * CW:(tp % 2 + 1) * CW], hfs[:, :],
                            RC, RC, ALU.add, ALU.subtract).then_inc(s_dve.h, 1)
            s_dve.n += 1
            if tp >= 1:
                emit_evac(e, tp - 1)
        emit_evac(e, NTP - 1)
    ctx.vector(fqv)

    def ft(e):
        e.wait_ge(p_wd[0].h, 16)
        for tp in range(NTP):
            e.wait_ge(s_dve.h, cnt_quant[tp])
            for ic in range(IC):
                hb = (tp % 2) * CW
                for ti in range(2):
                    for half in range(2):
                        bank = (tp % 2) * 4 + ti * 2 + half
                        ii = e.matmul(
                            pall[:, bank * 512:bank * 512 + 512],
                            hqb[:, hb + ic * 256 + ti * 128:hb + ic * 256 + ti * 128 + 128],
                            wdr[:, ic * DM + half * 512:ic * DM + half * 512 + 512],
                            start=(ic == 0), stop=(ic == IC - 1))
                ii.then_inc(s_pe.h, 4)
        s_pe.n += 4 * NTP * IC
    ctx.tensor(ft)


PI = 3.14159265358979323846


def emit_attention(ctx, d, scope):
    nc = ctx.nc
    sb = d["sb"]
    s_pe, s_dve, s_act, s_gp = d["s_pe"], d["s_dve"], d["s_act"], d["s_gp"]
    s_ld, s_sc, s_gld, s_st, s_cc = d["s_ld"], d["s_sc"], d["s_gld"], d["s_st"], d["s_cc"]
    pall = d["pall"]

    emit_ln_stats(ctx, d, want_max=False)
    ctx.barrier()

    battn = sb("battn", [128, 4], F32, scope)
    ones_col = sb("ones_col", [128, 1], BF16, scope)
    core_scope = ExitStack()
    av_fm = sb("av_fm", [128, 4 * S], BF16, core_scope)
    mid_scope = ExitStack()
    qf = sb("qf", [128, 4 * S], BF16, mid_scope)
    kf = sb("kf", [128, 4 * S], BF16, mid_scope)

    xst_scope = ExitStack()
    xst = sb("xst", [128, DM], BF16, xst_scope)
    p_xl = mk_psems(ctx, d, "axl")
    dve0 = s_dve.n

    def fv_ln(e):
        e.memset(ones_col[:, :], 1.0)
        for t in range(TT):
            sl = slice(0, DM)
            if t >= 1:
                e.wait_ge(p_xl[0].h, 16 * t)
            e.tensor_scalar(xst[:, sl], d["xr"][:, t * DM:(t + 1) * DM],
                            d["st_m"][:, t:t + 1], d["st_r"][:, t:t + 1],
                            ALU.subtract, ALU.mult).then_inc(s_dve.h, 1)
        s_dve.n += TT
    ctx.vector(fv_ln)

    ag_in_t = d["ag_xln_in"].rearrange("(t p) m -> t p m", p=128)

    def fst_ln(e):
        for t in range(TT):
            e.wait_ge(s_dve.h, dve0 + t + 1)
            e.dma_start(out=ag_in_t[t], in_=xst[:, :]).then_inc(p_xl[0].h, 16)
            p_xl[0].n += 16
    ctx.sync(fst_ln)

    def fcc(e):
        e.wait_ge(p_xl[0].h, p_xl[0].n)
        e.wait_ge(p_xl[1].h, p_xl[1].n)
        e.collective_compute(
            "AllGather", ALU.bypass,
            replica_groups=[[2 * i, 2 * i + 1] for i in range(NCORES // 2)],
            ins=[d["ag_xln_in"].ap().opt()], outs=[d["ag_xln_out"].ap().opt()],
        ).then_inc(s_cc.h, 1)
        s_cc.n += 1
        if "xln" in d["dbg"]:
            e.wait_ge(s_cc.h, s_cc.n)
            e.dma_start(out=d["dbg_out"]["xln"][:, :], in_=d["ag_xln_out"][:, :]) \
                .then_inc(s_gld.h, 16)
            s_gld.n += 16
            e.wait_ge(s_gld.h, s_gld.n)
    ctx.gpsimd(fcc)
    xst_scope.close()

    with ExitStack() as qscope:
        xfs = av_fm[:, 0:2 * HC * 512]
        cos_t = av_fm[:, 2 * HC * 512:2 * HC * 512 + S]
        sin_t = av_fm[:, 2 * HC * 512 + S:2 * HC * 512 + 2 * S]
        tmpf = sb("tmpf", [128, 2 * 512], F32, qscope)
        pio2 = sb("pio2", [128, 1], F32, qscope)
        qraw = sb("qraw", [128, 2 * 512], BF16, qscope)
        posrow = sb("posrow", [1, 512], F32, qscope)
        ivf = sb("ivf", [128, 1], F32, qscope)
        dummy_split_marker = None

        def fv_ivf(e):
            e.memset(pio2[:, :], PI / 2)
            pt = d["iota_p"]
            for step in range(5):
                ops = [
                    lambda: e.tensor_scalar(ivf[:, :], pt[:, :], 1.0 / 64,
                                            -63.0 / 128, ALU.mult, ALU.add),
                    lambda: e.tensor_scalar(ivf[:, :], ivf[:, :], RC, RC,
                                            ALU.add, ALU.subtract),
                    lambda: e.scalar_tensor_tensor(ivf[:, :], ivf[:, :], -64.0,
                                                   pt[:, :], ALU.mult, ALU.add),
                    lambda: e.tensor_scalar(ivf[:, :], ivf[:, :], 0.5, -0.25,
                                            ALU.mult, ALU.add),
                    lambda: e.tensor_scalar(ivf[:, :], ivf[:, :], RC, RC,
                                            ALU.add, ALU.subtract),
                ]
                ops[step]().then_inc(s_dve.h, 1)
                s_dve.n += 1
                e.wait_ge(s_dve.h, s_dve.n)
        ctx.vector(fv_ivf)

        def fs_ivf(e):
            e.wait_ge(s_dve.h, s_dve.n)
            import math
            e.activation(ivf[:, :], ivf[:, :], AF.Exp, scale=-math.log(10000.0) / 32) \
                .then_inc(s_act.h, 1)
            s_act.n += 1
        ctx.scalar(fs_ivf)

        for sp in range(8):
            def fgp(e, sp=sp):
                if sp == 0:
                    e.wait_ge(s_act.h, s_act.n)
                else:
                    e.wait_ge(s_dve.h, s_dve.n)
                e.iota(posrow[:, :], [[1, 512]], base=sp * 512,
                       channel_multiplier=0,
                       allow_small_or_imprecise_dtypes=True).then_inc(s_gp.h, 1)
                s_gp.n += 1
            ctx.gpsimd(fgp)

            def ft(e, sp=sp):
                e.wait_ge(s_gp.h, s_gp.n)
                e.wait_ge(s_dve.h, s_dve.n)
                e.matmul(pall[:, 6 * 512:7 * 512], d["ones_row"][:, :],
                         posrow[:, :], start=True, stop=True).then_inc(s_pe.h, 1)
                s_pe.n += 1
            ctx.tensor(ft)

            def fv(e, sp=sp):
                e.wait_ge(s_pe.h, s_pe.n)
                ang = tmpf[:, 0:512]
                r = tmpf[:, 512:1024]
                e.tensor_scalar(ang[:, :], pall[:, 6 * 512:7 * 512], ivf[:, :],
                                None, ALU.mult).then_inc(s_dve.h, 1)
                s_dve.n += 1
                e.wait_ge(s_dve.h, s_dve.n)
                e.tensor_scalar(r[:, :], ang[:, :], 1.0 / (2 * PI), None,
                                ALU.mult).then_inc(s_dve.h, 1)
                s_dve.n += 1
                e.wait_ge(s_dve.h, s_dve.n)
                e.tensor_scalar(r[:, :], r[:, :], RC, RC, ALU.add, ALU.subtract) \
                    .then_inc(s_dve.h, 1)
                s_dve.n += 1
                e.wait_ge(s_dve.h, s_dve.n)
                e.scalar_tensor_tensor(r[:, :], r[:, :], -2 * PI, ang[:, :],
                                       ALU.mult, ALU.add).then_inc(s_dve.h, 1)
                s_dve.n += 1
            ctx.vector(fv)

            def fs(e, sp=sp):
                e.wait_ge(s_dve.h, s_dve.n)
                e.activation(sin_t[:, sp * 512:(sp + 1) * 512], tmpf[:, 512:1024],
                             AF.Sin).then_inc(s_act.h, 1)
                s_act.n += 1
            ctx.scalar(fs)

            def fv2(e, sp=sp):
                ang = tmpf[:, 0:512]
                r = tmpf[:, 512:1024]
                e.wait_ge(s_act.h, s_act.n)
                e.tensor_scalar(r[:, :], ang[:, :], 1.0 / (2 * PI), 0.25,
                                ALU.mult, ALU.add).then_inc(s_dve.h, 1)
                s_dve.n += 1
                e.wait_ge(s_dve.h, s_dve.n)
                e.tensor_scalar(r[:, :], r[:, :], RC, RC, ALU.add, ALU.subtract) \
                    .then_inc(s_dve.h, 1)
                s_dve.n += 1
                e.wait_ge(s_dve.h, s_dve.n)
                e.scalar_tensor_tensor(r[:, :], r[:, :], -2 * PI, ang[:, :],
                                       ALU.mult, ALU.add).then_inc(s_dve.h, 1)
                s_dve.n += 1
            ctx.vector(fv2)

            def fs2(e, sp=sp):
                e.wait_ge(s_dve.h, s_dve.n)
                e.activation(cos_t[:, sp * 512:(sp + 1) * 512], tmpf[:, 512:1024],
                             AF.Sin, bias=pio2[:, :]).then_inc(s_act.h, 1)
                s_act.n += 1
            ctx.scalar(fs2)

        if "rotb" in d["dbg"]:
            def fdr(e):
                e.dma_start(out=d["dbg_out"]["rotb"][:, :], in_=d["rot_b"][:, :]) \
                    .then_inc(s_gld.h, 16)
                s_gld.n += 16
                e.wait_ge(s_gld.h, s_gld.n)
            ctx.gpsimd(fdr)
        if d["dbg"] & {"cos", "sin", "ivf"}:
            ctx.barrier()
            def fdbg_t(e):
                for nm, buf in (("cos", cos_t), ("sin", sin_t), ("ivf", ivf)):
                    if nm in d["dbg"]:
                        e.dma_start(out=d["dbg_out"][nm][:, :], in_=buf[:, :]) \
                            .then_inc(s_gld.h, 16)
                        s_gld.n += 16
                e.wait_ge(s_gld.h, s_gld.n)
            ctx.gpsimd(fdbg_t)
            ctx.barrier()
        for (w_in, dstf, pname) in ((d["wq_in"], qf, "aq"), (d["wk_in"], kf, "ak")):
            with ExitStack() as wsc:
                wqa = sb("w" + pname, [128, HC * 512], BF16, wsc)
                wq_v = w_in.rearrange("(c p) m -> p c m", p=128)

                def fwl(e, wqa=wqa, wq_v=wq_v):
                    e.wait_ge(s_pe.h, s_pe.n)
                    e.dma_start(out=wqa[:, :].rearrange("p (c m) -> p c m", c=HC),
                                in_=wq_v).then_inc(s_gld.h, 16)
                    s_gld.n += 16
                    e.wait_ge(s_cc.h, s_cc.n)
                ctx.gpsimd(fwl)
                ctx.barrier()

                p_tr = mk_psems(ctx, d, "atr" + pname)
                dve_pass0 = s_dve.n
                pe_pass0 = s_pe.n
                for sp in range(8):
                    def ftr(e, sp=sp, p_tr=p_tr):
                        if sp >= 2:
                            e.wait_ge(s_pe.h, pe_pass0 + 8 * (sp - 1))
                        xb = (sp % 2) * HC * 512
                        for hc in range(HC):
                            e.dma_start_transpose(
                                xfs[:, xb + hc * 512:xb + (hc + 1) * 512],
                                d["ag_xln_out"][sp * 512:(sp + 1) * 512,
                                                hc * 128:(hc + 1) * 128]) \
                                .then_inc(p_tr[sp % 2].h, 16)
                            p_tr[sp % 2].n += 16
                    ctx.sync(ftr)

                    for mc in range(4):
                        n_u = sp * 4 + mc
                        st_ = (n_u % 2) * 4
                        qb_, rq_ = st_, st_ + 2
                        qsl = slice((n_u % 2) * 512, (n_u % 2) * 512 + 512)
                        xb = (sp % 2) * HC * 512

                        def ft(e, sp=sp, mc=mc, qb_=qb_, wqa=wqa, p_tr=p_tr,
                               n_u=n_u, xb=xb):
                            e.wait_ge(p_tr[sp % 2].h, 128 * (sp // 2 + 1))
                            if n_u >= 2:
                                e.wait_ge(s_dve.h, dve_pass0 + 3 * (n_u - 1))
                            for hc in range(HC):
                                inst = e.matmul(
                                    pall[:, qb_ * 512:qb_ * 512 + 512],
                                    wqa[:, hc * 512 + mc * 128:hc * 512 + mc * 128 + 128],
                                    xfs[:, xb + hc * 512:xb + (hc + 1) * 512],
                                    start=(hc == 0), stop=(hc == HC - 1))
                            inst.then_inc(s_pe.h, 1)
                            s_pe.n += 1
                        ctx.tensor(ft)

                        def fv_cp(e, qb_=qb_, qsl=qsl):
                            e.wait_ge(s_pe.h, s_pe.n)
                            e.tensor_copy(qraw[:, qsl],
                                          pall[:, qb_ * 512:qb_ * 512 + 512]) \
                                .then_inc(s_dve.h, 1)
                            s_dve.n += 1
                        ctx.vector(fv_cp)
                        if sp == 0 and mc == 3 and pname == "aq" and \
                                "qraw0" in d["dbg"]:
                            def fdq(e):
                                e.wait_ge(s_dve.h, s_dve.n)
                                e.dma_start(out=d["dbg_out"]["qraw0"][:, :],
                                            in_=qraw[:, :]).then_inc(s_gld.h, 16)
                                s_gld.n += 16
                                e.wait_ge(s_gld.h, s_gld.n)
                            ctx.gpsimd(fdq)
                            def fdx(e):
                                e.dma_start(out=d["dbg_out"]["xfs0"][:, :],
                                            in_=xfs[:, :]).then_inc(s_gld.h, 16)
                                s_gld.n += 16
                                e.wait_ge(s_gld.h, s_gld.n)
                            ctx.gpsimd(fdx)

                        def ft_rot(e, rq_=rq_, qsl=qsl):
                            e.wait_ge(s_dve.h, s_dve.n)
                            e.matmul(pall[:, rq_ * 512:rq_ * 512 + 512],
                                     d["rot_b"][:, :], qraw[:, qsl],
                                     start=True, stop=True).then_inc(s_pe.h, 1)
                            s_pe.n += 1
                        ctx.tensor(ft_rot)

                        def fv_rope(e, sp=sp, mc=mc, rq_=rq_, dstf=dstf,
                                    qsl=qsl):
                            cs = cos_t[:, sp * 512:(sp + 1) * 512]
                            sn = sin_t[:, sp * 512:(sp + 1) * 512]
                            e.tensor_mul(tmpf[:, 0:512], qraw[:, qsl], cs)
                            e.wait_ge(s_pe.h, s_pe.n)
                            e.tensor_mul(tmpf[:, 512:1024],
                                         pall[:, rq_ * 512:rq_ * 512 + 512], sn) \
                                .then_inc(s_dve.h, 1)
                            s_dve.n += 1
                            e.wait_ge(s_dve.h, s_dve.n)
                            e.tensor_add(dstf[:, mc * S + sp * 512:mc * S + sp * 512 + 512],
                                         tmpf[:, 0:512], tmpf[:, 512:1024]) \
                                .then_inc(s_dve.h, 1)
                            s_dve.n += 1
                        ctx.vector(fv_rope)

        dummy_end_qk = None
    ctx.barrier()
    vt = sb("vt", [128, ST * 520], BF16, mid_scope)

    def fv_vt(e):
        e.memset(vt[:, :], 1.0).then_inc(s_dve.h, 1)
        s_dve.n += 1
    ctx.vector(fv_vt)

    with ExitStack() as qscope:
        xfs = av_fm[:, 0:2 * HC * 512]
        qraw = d["sqb"][:, 256:512].bitcast(BF16)
        nrm = d["sqb"][0:1, 0:64]
        nfin = d["sqb"][0:1, 64:72]
        with ExitStack() as wsc:
            wva = sb("wva", [128, HC * 512], BF16, wsc)
            wv_v = d["wv_in"].rearrange("(c p) m -> p c m", p=128)

            def fwl(e):
                e.wait_ge(s_pe.h, s_pe.n)
                e.wait_ge(s_dve.h, s_dve.n)
                e.dma_start(out=wva[:, :].rearrange("p (c m) -> p c m", c=HC),
                            in_=wv_v).then_inc(s_gld.h, 16)
                s_gld.n += 16
            ctx.gpsimd(fwl)
            ctx.barrier()

            p_tr2 = mk_psems(ctx, d, "atr2")
            pe_v0 = s_pe.n
            for sp in range(8):
                def ftr(e, sp=sp):
                    if sp >= 2:
                        e.wait_ge(s_pe.h, pe_v0 + 4 * (sp - 1))
                    xb = (sp % 2) * HC * 512
                    for hc in range(HC):
                        e.dma_start_transpose(
                            xfs[:, xb + hc * 512:xb + (hc + 1) * 512],
                            d["ag_xln_out"][sp * 512:(sp + 1) * 512,
                                            hc * 128:(hc + 1) * 128]) \
                            .then_inc(p_tr2[sp % 2].h, 16)
                        p_tr2[sp % 2].n += 16
                ctx.sync(ftr)

                for ti4 in range(4):
                    st_ = ((sp * 4 + ti4) % 2) * 4
                    xb = (sp % 2) * HC * 512
                    def ft(e, sp=sp, ti4=ti4, st_=st_, xb=xb):
                        e.wait_ge(p_tr2[sp % 2].h, 128 * (sp // 2 + 1))
                        if sp * 4 + ti4 >= 2:
                            e.wait_ge(s_dve.h, s_dve.n - 1)
                        for hc in range(HC):
                            inst = e.matmul(
                                pall[:, st_ * 512:st_ * 512 + 512],
                                xfs[:, xb + hc * 512 + ti4 * 128:xb + hc * 512 + ti4 * 128 + 128],
                                wva[:, hc * 512:(hc + 1) * 512],
                                start=(hc == 0), stop=(hc == HC - 1))
                        inst.then_inc(s_pe.h, 1)
                        s_pe.n += 1
                    ctx.tensor(ft)

                    def fv(e, sp=sp, ti4=ti4, st_=st_):
                        kb = sp * 4 + ti4
                        e.wait_ge(s_pe.h, s_pe.n)
                        e.tensor_copy(
                            vt[:, kb * 520:(kb + 1) * 520]
                            .rearrange("p (h c) -> p h c", h=8)[:, :, 0:64],
                            pall[:, st_ * 512:st_ * 512 + 512]
                            .rearrange("p (h c) -> p h c", h=8)) \
                            .then_inc(s_dve.h, 1)
                        s_dve.n += 1
                    ctx.vector(fv)

            def fvb(e):
                e.memset(battn[:, :], -32.0).then_inc(s_dve.h, 1)
                s_dve.n += 1
            ctx.vector(fvb)
    ctx.barrier()
    if d["dbg"] & {"qf", "kf", "vt", "battn"}:
        def fdbg(e):
            for nm, buf in (("qf", qf), ("kf", kf), ("vt", vt), ("battn", battn)):
                if nm in d["dbg"]:
                    e.dma_start(out=d["dbg_out"][nm][:, :], in_=buf[:, :]) \
                        .then_inc(s_gld.h, 16)
                    s_gld.n += 16
            e.wait_ge(s_gld.h, s_gld.n)
        ctx.gpsimd(fdbg)
        ctx.barrier()

    with ExitStack() as sscope:
        ebAB = sb("ebAB", [128, 3 * 1024], BF16, sscope)
        nrow = d["sqb"][32:33, 0:1]
        bcs = d["sqb"][64:128, 0:256].bitcast(BF16)

        uidx = 0
        evac_end = {}
        pend_bcast = []
        unit_meta = {}
        for hp in range(4):
            for qb in range(8):
                nkb = 4 * qb + 4
                pe0 = s_pe.n
                act0 = s_act.n
                dve_u0 = s_dve.n
                avb = 4 + 2 * (uidx % 2)

                pe_sc = {}
                pe_av = {}
                _c = pe0
                for kb in range(nkb):
                    _c += 1
                    pe_sc[kb] = _c
                    if kb >= 2:
                        _c += 1
                        pe_av[kb - 2] = _c
                _c += 1
                pe_av[nkb - 2] = _c
                _c += 1
                pe_av[nkb - 1] = _c

                def emit_av(e, kb, hp, nkb, act0, dve_u0, qb, avb):
                    e.wait_ge(s_act.h, act0 + kb + 1)
                    if kb >= 4 * qb:
                        e.wait_ge(s_dve.h, dve_u0 + (kb - 4 * qb + 1))
                    ebs = (kb % 3) * 1024
                    e.matmul(pall[:, avb * 512:avb * 512 + 512][0:65, :],
                             vt[:, kb * 520 + (2 * hp) * 65:kb * 520 + (2 * hp) * 65 + 65],
                             ebAB[:, ebs:ebs + 512],
                             start=(kb == 0), stop=(kb == nkb - 1))
                    e.matmul(pall[:, (avb + 1) * 512:(avb + 1) * 512 + 512][0:65, :],
                             vt[:, kb * 520 + (2 * hp + 1) * 65:kb * 520 + (2 * hp + 1) * 65 + 65],
                             ebAB[:, ebs + 512:ebs + 1024],
                             start=(kb == 0), stop=(kb == nkb - 1)) \
                        .then_inc(s_pe.h, 1)

                w_evac = evac_end.get(uidx - 2, 0)
                my_pend = list(pend_bcast)
                pend_bcast.clear()
                nb = len(my_pend)
                if nb:
                    unit_meta[uidx - 1] = unit_meta[uidx - 1][:3] + (s_pe.n + nb,)
                pe0 += nb
                for kk in list(pe_sc):
                    pe_sc[kk] += nb
                for kk in list(pe_av):
                    pe_av[kk] += nb

                def ft(e, hp=hp, qb=qb, nkb=nkb, pe0=pe0, act0=act0,
                       dve_u0=dve_u0, avb=avb, w_evac=w_evac, my_pend=my_pend):
                    for (pavb, hloc, wdve, nsl) in my_pend:
                        e.wait_ge(s_dve.h, wdve)
                        bank = pavb + hloc
                        e.matmul(pall[:, bank * 512:bank * 512 + 512][64:128, :],
                                 d["ones_row"][:, 0:64],
                                 d["sqb"][0:1, nsl], start=True, stop=True) \
                            .then_inc(s_pe.h, 1)
                        s_pe.n += 1
                    if w_evac:
                        e.wait_ge(s_dve.h, w_evac)
                    for kb in range(nkb):
                        scs = (kb % 2) * 2
                        if kb >= 2:
                            e.wait_ge(s_act.h, act0 + kb - 1)
                        e.matmul(pall[:, scs * 512:scs * 512 + 512],
                                 kf[0:64, hp * S + kb * 128:hp * S + kb * 128 + 128],
                                 qf[0:64, hp * S + qb * 512:hp * S + qb * 512 + 512],
                                 start=True, stop=True)
                        e.matmul(pall[:, (scs + 1) * 512:(scs + 1) * 512 + 512],
                                 kf[64:128, hp * S + kb * 128:hp * S + kb * 128 + 128],
                                 qf[64:128, hp * S + qb * 512:hp * S + qb * 512 + 512],
                                 start=True, stop=True).then_inc(s_pe.h, 1)
                        if kb >= 2:
                            emit_av(e, kb - 2, hp, nkb, act0, dve_u0, qb, avb)
                    emit_av(e, nkb - 2, hp, nkb, act0, dve_u0, qb, avb)
                    emit_av(e, nkb - 1, hp, nkb, act0, dve_u0, qb, avb)
                    s_pe.n += 2 * nkb
                ctx.tensor(ft)

                def fs(e, nkb=nkb, pe0=pe0, hp=hp, pe_sc=pe_sc, pe_av=pe_av):
                    for kb in range(nkb):
                        scs = (kb % 2) * 2
                        ebs = (kb % 3) * 1024
                        e.wait_ge(s_pe.h, pe_sc[kb])
                        if kb >= 3:
                            e.wait_ge(s_pe.h, pe_av[kb - 3])
                        e.activation(ebAB[:, ebs:ebs + 1024],
                                     pall[:, scs * 512:scs * 512 + 1024], AF.Exp,
                                     bias=battn[:, hp:hp + 1], scale=0.125) \
                            .then_inc(s_act.h, 1)
                    s_act.n += nkb
                ctx.scalar(fs)

                def fvm(e, qb=qb, nkb=nkb, act0=act0):
                    for kb in range(4 * qb, nkb):
                        off = (kb - 4 * qb) * 128
                        ebs = (kb % 3) * 1024
                        e.wait_ge(s_act.h, act0 + kb + 1)
                        for hof in (0, 512):
                            if off > 0:
                                e.memset(ebAB[:, ebs + hof:ebs + hof + off], 0.0)
                            e.tensor_mul(ebAB[:, ebs + hof + off:ebs + hof + off + 128],
                                         ebAB[:, ebs + hof + off:ebs + hof + off + 128],
                                         d["tri_b"][:, :])
                        e.tensor_scalar(nrow[:, 0:1], nrow[:, 0:1], 1.0, None,
                                        ALU.mult).then_inc(s_dve.h, 1)
                        s_dve.n += 1
                ctx.vector(fvm)

                pe_mm_end = s_pe.n
                for hloc in (0, 1):
                    nsl = slice(hloc * 512, hloc * 512 + 512)

                    def fve1(e, hloc=hloc, avb=avb, pe_mm_end=pe_mm_end,
                             nsl=nsl):
                        e.wait_ge(s_pe.h, pe_mm_end)
                        bank = avb + hloc
                        e.tensor_scalar(d["sqb"][0:1, nsl],
                                        pall[64:65, bank * 512:bank * 512 + 512],
                                        1e-30, None, ALU.max).then_inc(s_dve.h, 1)
                        s_dve.n += 1
                        e.wait_ge(s_dve.h, s_dve.n)
                        e.reciprocal(d["sqb"][0:1, nsl], d["sqb"][0:1, nsl]) \
                            .then_inc(s_dve.h, 1)
                        s_dve.n += 1
                        pend_bcast.append((avb, hloc, s_dve.n, nsl))
                    ctx.vector(fve1)

                if uidx >= 1:
                    phl, pqb, pavb, ppe = unit_meta[uidx - 1]
                    assert ppe is not None

                    def fve2(e, phl=phl, pqb=pqb, pavb=pavb, ppe=ppe):
                        e.wait_ge(s_pe.h, ppe)
                        for hloc in (0, 1):
                            bank = pavb + hloc
                            e.tensor_copy(bcs[0:64, :],
                                          pall[64:128, bank * 512:bank * 512 + 512]) \
                                .then_inc(s_dve.h, 1)
                            s_dve.n += 1
                            e.tensor_mul(
                                av_fm[hloc * 64:hloc * 64 + 64,
                                      phl * S + pqb * 512:phl * S + pqb * 512 + 512],
                                pall[0:64, bank * 512:bank * 512 + 512],
                                bcs[0:64, :]).then_inc(s_dve.h, 1)
                            s_dve.n += 1
                    ctx.vector(fve2)
                    evac_end[uidx - 1] = s_dve.n
                unit_meta[uidx] = (hp, qb, avb, None)
                uidx += 1
        my_pend = list(pend_bcast)
        pend_bcast.clear()
        lhl, lqb, lavb, _ = unit_meta[uidx - 1]
        bcast_done = s_pe.n + len(my_pend)

        def ft_tail(e, my_pend=my_pend):
            for (pavb, hloc, wdve, nsl) in my_pend:
                e.wait_ge(s_dve.h, wdve)
                bank = pavb + hloc
                e.matmul(pall[:, bank * 512:bank * 512 + 512][64:128, :],
                         d["ones_row"][:, 0:64],
                         d["sqb"][0:1, nsl], start=True, stop=True) \
                    .then_inc(s_pe.h, 1)
                s_pe.n += 1
        ctx.tensor(ft_tail)

        def fve2_tail(e, lhl=lhl, lqb=lqb, lavb=lavb, bcast_done=bcast_done):
            e.wait_ge(s_pe.h, bcast_done)
            for hloc in (0, 1):
                bank = lavb + hloc
                e.tensor_copy(bcs[0:64, :],
                              pall[64:128, bank * 512:bank * 512 + 512]) \
                    .then_inc(s_dve.h, 1)
                s_dve.n += 1
                e.tensor_mul(
                    av_fm[hloc * 64:hloc * 64 + 64,
                          lhl * S + lqb * 512:lhl * S + lqb * 512 + 512],
                    pall[0:64, bank * 512:bank * 512 + 512],
                    bcs[0:64, :]).then_inc(s_dve.h, 1)
                s_dve.n += 1
        ctx.vector(fve2_tail)
        evac_end[uidx - 1] = s_dve.n

        pairs = [[2 * i, 2 * i + 1] for i in range(NCORES // 2)]
        hp_done = {h: evac_end[h * 8 + 7] for h in range(4)}

        def fxa(e):
            for h in range(4):
                e.wait_ge(s_dve.h, hp_done[h])
                e.dma_start(out=d[f"ag_avi{h}"][:, :],
                            in_=av_fm[:, h * S:(h + 1) * S]) \
                    .then_inc(s_gld.h, 16)
                s_gld.n += 16
                e.wait_ge(s_gld.h, s_gld.n)
                e.collective_compute(
                    "AllGather", ALU.bypass, replica_groups=pairs,
                    ins=[d[f"ag_avi{h}"].ap().opt()],
                    outs=[d[f"ag_avo{h}"].ap().opt()],
                ).then_inc(s_cc.h, 1)
                s_cc.n += 1
                e.wait_ge(s_cc.h, s_cc.n)
                with e.register(f"rk{h}") as rk:
                    e.cc_rank_ld(rk, replica_groups=pairs)
                    with e.register(f"ofs{h}") as ofs:
                        e.reg_mov(ofs, rk)
                        e.reg_mul(ofs, ofs, TOK)
                        e.dma_start(
                            out=av_fm[:, h * S:(h + 1) * S]
                            .rearrange("p (c n) -> p c n", c=2),
                            in_=bass.AP(d[f"ag_avo{h}"], ofs,
                                        [[S, 128], [128 * S, 2], [1, TOK]])) \
                            .then_inc(s_gld.h, 16)
                        s_gld.n += 16
        ctx.gpsimd(fxa)
        gld_rb = s_gld.n
        act_end = s_act.n
        pe_tail = s_pe.n

    mid_scope.close()
    with ExitStack() as oscope:
        wob = sb("wob", [128, HC * DM], BF16, oscope)
        wo_v = d["wo_in"].rearrange("(c p) m -> p c m", p=128)

        def fol(e):
            e.wait_ge(s_pe.h, pe_tail)
            e.dma_start(out=wob[:, :].rearrange("p (c m) -> p c m", c=HC),
                        in_=wo_v).then_inc(s_gld.h, 16)
            s_gld.n += 16
        ctx.gpsimd(fol)
        wob_ld = s_gld.n

        pe0 = s_pe.n
        dve0 = s_dve.n

        def ft(e):
            e.wait_ge(s_act.h, act_end)
            e.wait_ge(s_gld.h, wob_ld)
            for t in range(TT):
                st_ = (t % 2) * 2
                if t >= 2:
                    e.wait_ge(s_dve.h, dve0 + t - 1)
                for half in range(2):
                    for dc in range(HC):
                        col = (dc % 4) * S + (dc // 4) * TOK + t * 128
                        inst = e.matmul(
                            pall[:, (st_ + half) * 512:(st_ + half) * 512 + 512],
                            av_fm[:, col:col + 128],
                            wob[:, dc * DM + half * 512:dc * DM + half * 512 + 512],
                            start=(dc == 0), stop=(dc == HC - 1))
                inst.then_inc(s_pe.h, 1)
            s_pe.n += TT
        ctx.tensor(ft)

        def fv(e):
            for t in range(TT):
                st_ = (t % 2) * 2
                e.wait_ge(s_pe.h, pe0 + t + 1)
                e.tensor_add(d["xr"][:, t * DM:(t + 1) * DM],
                             pall[:, st_ * 512:st_ * 512 + 1024],
                             d["xr"][:, t * DM:(t + 1) * DM]).then_inc(s_dve.h, 1)
                s_dve.n += 1
                d.setdefault("xr_dve", {})[t] = s_dve.n
        ctx.vector(fv)
    core_scope.close()
    ctx.barrier()


def emit_output(ctx, d):
    s_st, s_dve = d["s_st"], d["s_dve"]
    xr_dve = dict(d.get("xr_dve", {}))
    out_t = d["out"].rearrange("(t p) m -> t p m", p=128)

    def fst(e):
        for t in range(TT):
            if t in xr_dve:
                e.wait_ge(s_dve.h, xr_dve[t])
            e.dma_start(out=out_t[t], in_=d["xr"][:, t * DM:(t + 1) * DM]) \
                .then_inc(s_st.h, 16)
        s_st.n += 16 * TT
        e.wait_ge(s_st.h, s_st.n)
    ctx.sync(fst)



_BUILD_KW = {}


def kernel(L0_attn_x, L0_attn_Wq, L0_attn_Wk, L0_attn_Wv, L0_attn_Wo,
           L0_ffn_clip_x, L0_ffn_clip_down, L0_ffn_w_gate, L0_ffn_w_up,
           L0_ffn_w_down):
    x = np.ascontiguousarray(np.asarray(L0_attn_x, dtype=np.float32))
    wq = np.ascontiguousarray(np.asarray(L0_attn_Wq, dtype=np.float32))
    wk = np.ascontiguousarray(np.asarray(L0_attn_Wk, dtype=np.float32))
    wv = np.ascontiguousarray(np.asarray(L0_attn_Wv, dtype=np.float32))
    wo = np.ascontiguousarray(np.asarray(L0_attn_Wo, dtype=np.float32))
    wg = np.ascontiguousarray(np.asarray(L0_ffn_w_gate, dtype=np.float32))
    wu = np.ascontiguousarray(np.asarray(L0_ffn_w_up, dtype=np.float32))
    wd = np.ascontiguousarray(np.asarray(L0_ffn_w_down, dtype=np.float32))
    clip = np.array([[np.asarray(L0_ffn_clip_x, np.float32).ravel()[0],
                      np.asarray(L0_ffn_clip_down, np.float32).ravel()[0]]],
                    dtype=np.float32)

    nc, dbg_out = build_program(**_BUILD_KW)
    in_maps = []
    for c in range(NCORES):
        b, h2 = c // 2, c % 2
        in_maps.append({
            "x": np.ascontiguousarray(x[b, h2 * TOK:(h2 + 1) * TOK, :]),
            "wq": np.ascontiguousarray(wq[:, h2 * 512:(h2 + 1) * 512]),
            "wk": np.ascontiguousarray(wk[:, h2 * 512:(h2 + 1) * 512]),
            "wv": np.ascontiguousarray(wv[:, h2 * 512:(h2 + 1) * 512]),
            "wo": wo, "wg": wg, "wu": wu, "wd": wd, "clip": clip,
        })
    res = run_bass_kernel_spmd(nc, in_maps, core_ids=list(range(NCORES)))
    out = np.empty((4, S, DM), np.float32)
    for c in range(NCORES):
        b, h2 = c // 2, c % 2
        out[b, h2 * TOK:(h2 + 1) * TOK, :] = res.results[c]["out"]
    if _BUILD_KW.get("debug"):
        kernel.last_results = res
    return out


# version 17 (final)
# speedup vs baseline: 1.0475x; 1.0019x over previous
import numpy as np
from contextlib import ExitStack

import concourse.bass as bass
import concourse.mybir as mybir
from concourse.bass_utils import run_bass_kernel_spmd

F32 = mybir.dt.float32
BF16 = mybir.dt.bfloat16
F16 = mybir.dt.float16
AF = mybir.ActivationFunctionType
ALU = mybir.AluOpType
AX = mybir.AxisListType

NCORES = 8
TOK = 2048
TT = TOK // 128
DM = 1024
HC = DM // 128
DI = 2816
IC = DI // 128
S = 4096
ST = S // 128
QMAX = 127.0
RC = 1.5 * 2.0 ** 23
EPS = 1e-5
DK = 64

ENGS = ("tensor", "vector", "scalar", "gpsimd", "sync")


class Sem:
    def __init__(self, es, nc, name):
        self.h = es.enter_context(nc.semaphore(name))
        self.n = 0


class InstRec:

    def __init__(self, inst):
        self.i = inst
        self.updated = False

    def then_inc(self, sem, k=None):
        self.updated = True
        if k is None:
            return self.i.then_inc(sem)
        return self.i.then_inc(sem, k)


class EngRec:
    def __init__(self, eng, store, key):
        object.__setattr__(self, "_eng", eng)
        object.__setattr__(self, "_store", store)
        object.__setattr__(self, "_key", key)

    def __getattr__(self, name):
        attr = getattr(self._eng, name)
        if not callable(attr):
            return attr

        def wrap(*a, **kw):
            r = attr(*a, **kw)
            if isinstance(r, bass.BassInstruction):
                r = InstRec(r)
                self._store[self._key] = r
            return r
        return wrap


class Ctx:
    def __init__(self, nc, block, es):
        self.nc = nc
        self.block = block
        self.es = es
        self.last = {}
        self.sems = []
        self.s_bar = Sem(es, nc, "bar")

    def sec(self, eng_name, fn):
        def g(eng):
            fn(EngRec(eng, self.last, eng_name))
        getattr(self.block, eng_name)(g)

    def tensor(self, fn): self.sec("tensor", fn)
    def vector(self, fn): self.sec("vector", fn)
    def scalar(self, fn): self.sec("scalar", fn)
    def gpsimd(self, fn): self.sec("gpsimd", fn)
    def sync(self, fn): self.sec("sync", fn)

    def barrier(self):
        target = self.s_bar.n
        for e in ENGS:
            li = self.last.pop(e, None)
            if li is not None and not li.updated:
                try:
                    li.i.then_inc(self.s_bar.h, 1)
                    target += 1
                except AssertionError:
                    li.i.then_inc(self.s_bar.h, 16)
                    target += 16
        waits = [(s.h, s.n) for s in self.sems if s.n > 0]
        if target > self.s_bar.n:
            waits.append((self.s_bar.h, target))

        for e in ENGS:
            def f(eng, ws=tuple(waits)):
                for h, n in ws:
                    eng.wait_ge(h, n)
            self.sec(e, f)
        self.s_bar.n = target
        self.last = {}



def build_program(upto="full", debug=()):
    nc = bass.Bass("TRN2", target_bir_lowering=False)
    dbg = set(debug)
    d = {}
    d["upto"] = upto

    for nm, shape, dt in [
        ("x", [TOK, DM], F32), ("wq", [DM, 512], F32), ("wk", [DM, 512], F32),
        ("wv", [DM, 512], F32), ("wo", [DM, DM], F32), ("wg", [DI, DM], F32),
        ("wu", [DI, DM], F32), ("wd", [DM, DI], F32), ("clip", [1, 2], F32),
    ]:
        d[nm + "_in"] = nc.dram_tensor(nm, shape, dt, kind="ExternalInput").ap()
    d["out"] = nc.dram_tensor("out", [TOK, DM], F32, kind="ExternalOutput").ap()

    d["dbg"] = dbg
    d["dbg_out"] = {}

    def mkdbg(name, shape, dt=F32):
        if name in dbg:
            d["dbg_out"][name] = nc.dram_tensor(
                "dbg_" + name, shape, dt, kind="ExternalOutput").ap()
    d["mkdbg"] = mkdbg
    mkdbg("xq", [TOK, DM], BF16)
    mkdbg("h", [DI, TOK], F16)
    mkdbg("hq", [DI, TOK], BF16)
    mkdbg("gmax", [1, 4], F32)
    mkdbg("wgq", [DI, DM], BF16)
    mkdbg("qf", [128, 4 * S], BF16)
    mkdbg("kf", [128, 4 * S], BF16)
    mkdbg("vt", [128, (S // 128) * 520], BF16)
    mkdbg("avf", [128, 4 * S], BF16)
    mkdbg("xln", [S, DM], BF16)
    mkdbg("battn", [128, 4], F32)
    mkdbg("cos", [128, S], BF16)
    mkdbg("xfs0", [128, HC * 512], BF16)
    mkdbg("rotb", [128, 128], BF16)
    mkdbg("avall", [128, 8 * TOK], BF16)
    mkdbg("qraw0", [128, 512], BF16)
    mkdbg("sin", [128, S], BF16)
    mkdbg("ivf", [128, 1], F32)

    d["wgq_nat"] = nc.dram_tensor("wgq_nat", [DI, DM], BF16)
    d["wuq_nat"] = nc.dram_tensor("wuq_nat", [DI, DM], BF16)
    d["wdq_nat"] = nc.dram_tensor("wdq_nat", [DM, DI], BF16)
    d["wgq_t"] = nc.dram_tensor("wgq_t", [DM, DI], BF16)
    d["wuq_t"] = nc.dram_tensor("wuq_t", [DM, DI], BF16)
    d["wdq_t"] = nc.dram_tensor("wdq_t", [DI, DM], BF16)
    d["xq_nat"] = nc.dram_tensor("xq_nat", [TOK, DM], BF16)
    d["h_nat"] = nc.dram_tensor("h_nat", [DI, TOK], F16)
    d["hq_nat"] = nc.dram_tensor("hq_nat", [DI, TOK], BF16)
    d["mx_in"] = nc.dram_tensor("mx_in", [1, 16], F32)
    d["mx_out"] = nc.dram_tensor("mx_out", [1, 16], F32)
    d["ag_xln_in"] = nc.dram_tensor("ag_xln_in", [TOK, DM], BF16)
    d["ag_xln_out"] = nc.dram_tensor("ag_xln_out", [S, DM], BF16)
    for _h in range(4):
        d[f"ag_avi{_h}"] = nc.dram_tensor(f"ag_avi{_h}", [128, S], BF16)
        d[f"ag_avo{_h}"] = nc.dram_tensor(f"ag_avo{_h}", [256, S], BF16)

    es = ExitStack()
    d["es_root"] = es

    _cnt = [0]

    def sb(name, shape, dt, scope=None):
        _cnt[0] += 1
        return (scope or es).enter_context(
            nc.sbuf_tensor(f"{name}_{_cnt[0]}", shape, dt))

    def psum(name, shape, dt, scope=None):
        return (scope or es).enter_context(nc.psum_tensor(name, shape, dt))
    d["sb"] = sb
    d["psum"] = psum

    d["xr"] = sb("xr", [128, TT * DM], F32)
    d["iden_f"] = sb("iden_f", [128, 128], F32)
    d["tri_b"] = sb("tri_b", [128, 128], BF16)
    d["rot_b"] = sb("rot_b", [128, 128], BF16)
    d["ones_row"] = sb("ones_row", [1, 128], F32)
    d["iota_p"] = sb("iota_p", [128, 1], F32)
    d["sclip"] = sb("sclip", [1, 2], F32)
    d["ssig"] = sb("ssig", [1, 2], F32)
    d["s_tmp"] = sb("s_tmp", [1, 8], F32)
    d["gmax"] = sb("gmax", [1, 4], F32)
    d["binv"] = sb("binv", [128, 8], F32)
    d["hcol"] = sb("hcol", [128, 2], F32)
    d["lmax_col"] = sb("lmax_col", [128, 1], F32)
    for nm in ("st_m", "st_r", "st_a", "st_b", "st_c", "st_d"):
        d[nm] = sb(nm, [128, TT], F32)
    d["wsc_g"] = sb("wsc_g", [128, IC], F32)
    d["wsc_u"] = sb("wsc_u", [128, IC], F32)
    d["wsc_d"] = sb("wsc_d", [128, HC], F32)
    d["sqb"] = sb("sqb", [128, DM], F32)

    d["pall"] = psum("pall", [128, 8 * 512], F32)

    with nc.Block() as block:
        ctx = Ctx(nc, block, es)
        for s in ("s_ld", "s_sc", "s_gld", "s_st", "s_pe", "s_dve",
                  "s_act", "s_gp", "s_cc"):
            d[s] = Sem(es, nc, s)
            ctx.sems.append(d[s])
        emit_all(ctx, d)
    es.close()
    return nc, d["dbg_out"]


def mk_psems(ctx, d, name):
    import itertools
    es = ctx.es
    sems = [Sem(es, ctx.nc, f"{name}{i}") for i in range(2)]
    for s in sems:
        ctx.sems.append(s)
    return sems


def emit_all(ctx, d):
    upto = d["upto"]
    with ExitStack() as pscope:
        d["iota_j"] = d["sb"]("iota_j", [128, 128], F32, pscope)
        d["scr1"] = d["sb"]("scr1", [128, 128], F32, pscope)
        d["scr2"] = d["sb"]("scr2", [128, 128], F32, pscope)
        emit_prologue(ctx, d)
        ctx.barrier()
    with ExitStack() as wscope:
        emit_weight_quant(ctx, d, wscope)
    emit_ln_stats(ctx, d, want_max=True)
    ctx.barrier()
    emit_maxreduce(ctx, d, slot=0)
    emit_scale_chain(ctx, d, slot=0, sig_idx=0)
    ctx.barrier()
    with ExitStack() as fscope:
        emit_ffn(ctx, d, fscope, app=0)
    if upto == "ffn1":
        emit_output(ctx, d)
        return
    with ExitStack() as ascope:
        emit_attention(ctx, d, ascope)
    if upto == "attn":
        emit_output(ctx, d)
        return
    emit_ln_stats(ctx, d, want_max=True)
    ctx.barrier()
    emit_maxreduce(ctx, d, slot=2)
    emit_scale_chain(ctx, d, slot=2, sig_idx=0)
    ctx.barrier()
    with ExitStack() as fscope:
        emit_ffn(ctx, d, fscope, app=1)
    emit_output(ctx, d)


def emit_ffn(ctx, d, scope, app):
    sb = d["sb"]
    nc = ctx.nc
    d["hmax_cols"] = sb("hmax_cols", [128, IC * 4], F32, scope)
    d["qstage"] = sb("qstage", [128, 2 * DM], F32, scope)
    d["qout"] = sb("qout", [128, 2 * DM], BF16, scope)
    d["srs"] = sb("srs", [128, TT], F32, scope)
    d["bsd"] = sb("bsd", [128, DM], F32, scope)
    d["brow"] = sb("brow", [1, 128], F32, scope)
    with ExitStack() as gscope:
        d["xq_fm"] = sb("xq_fm", [128, HC * TOK], BF16, gscope)
        emit_xq(ctx, d, slot=2 * app)
        ctx.barrier()
        emit_gateup(ctx, d, gscope, slot=2 * app)
        ctx.barrier()
    emit_hmax_finish(ctx, d)
    emit_maxreduce(ctx, d, slot=2 * app + 1)
    emit_scale_chain(ctx, d, slot=2 * app + 1, sig_idx=1)
    emit_bsd(ctx, d)
    ctx.barrier()
    with ExitStack() as dscope:
        emit_down(ctx, d, dscope, app=app)
    ctx.barrier()
    for k in ("xq_fm", "hmax_cols", "qstage", "qout", "srs", "wgb", "wub", "sgt", "upt",
              "hft", "hbt", "svg", "svu", "hst", "hqo", "hfs", "hqb", "wdb",
              "wdr", "hstage", "dst1", "bsd", "brow"):
        d.pop(k, None)



def emit_prologue(ctx, d):
    s_ld, s_gp, s_dve, s_act = d["s_ld"], d["s_gp"], d["s_dve"], d["s_act"]
    x_t = d["x_in"].rearrange("(t p) m -> t p m", p=128)
    xr = d["xr"]

    def fsync(e):
        for t in range(TT):
            e.dma_start(out=xr[:, t * DM:(t + 1) * DM], in_=x_t[t]).then_inc(s_ld.h, 16)
        e.dma_start(out=d["sclip"][:, :], in_=d["clip_in"][:, :]).then_inc(s_ld.h, 16)
        s_ld.n += 16 * (TT + 1)
    ctx.sync(fsync)

    def fgp(e):
        e.iota(d["iota_j"][:, :], [[1, 128]], channel_multiplier=0,
               allow_small_or_imprecise_dtypes=True)
        e.iota(d["iota_p"][:, :], [[1, 1]], channel_multiplier=1,
               allow_small_or_imprecise_dtypes=True)
        e.memset(d["ones_row"][:, :], 1.0).then_inc(s_gp.h, 1)
        s_gp.n += 1
    ctx.gpsimd(fgp)

    def fv(e):
        e.wait_ge(s_gp.h, s_gp.n)
        jt, pt = d["iota_j"], d["iota_p"]
        e.tensor_scalar(d["iden_f"][:, :], jt[:, :], pt[:, :], None, ALU.is_equal)
        e.tensor_scalar(d["tri_b"][:, :], jt[:, :], pt[:, :], 0.0,
                        ALU.subtract, ALU.is_ge)
        pcol = d["lmax_col"]
        e.tensor_scalar(pcol[:, :], pt[:, :], 0.5, -0.25, ALU.mult, ALU.add) \
            .then_inc(s_dve.h, 1)
        s_dve.n += 1
        e.wait_ge(s_dve.h, s_dve.n)
        e.tensor_scalar(pcol[:, :], pcol[:, :], RC, RC, ALU.add, ALU.subtract) \
            .then_inc(s_dve.h, 1)
        s_dve.n += 1
        e.wait_ge(s_dve.h, s_dve.n)
        e.scalar_tensor_tensor(pcol[:, :], pcol[:, :], -2.0, pt[:, :],
                               ALU.mult, ALU.add).then_inc(s_dve.h, 1)
        s_dve.n += 1
        e.wait_ge(s_dve.h, s_dve.n)
        scr1, scr2 = d["scr1"], d["scr2"]
        e.tensor_scalar(scr2[:, :], jt[:, :], pt[:, :], 1.0, ALU.subtract,
                        ALU.is_equal).then_inc(s_dve.h, 1)
        s_dve.n += 1
        e.wait_ge(s_dve.h, s_dve.n)
        e.tensor_scalar(scr1[:, :], scr2[:, :], pcol[:, :], None, ALU.mult) \
            .then_inc(s_dve.h, 1)
        s_dve.n += 1
        e.wait_ge(s_dve.h, s_dve.n)
        e.tensor_sub(scr2[:, :], scr2[:, :], scr1[:, :]).then_inc(s_dve.h, 1)
        s_dve.n += 1
        e.wait_ge(s_dve.h, s_dve.n)
        e.tensor_scalar(scr1[:, :], jt[:, :], pt[:, :], -1.0, ALU.subtract,
                        ALU.is_equal).then_inc(s_dve.h, 1)
        s_dve.n += 1
        e.wait_ge(s_dve.h, s_dve.n)
        e.tensor_scalar(scr1[:, :], scr1[:, :], pcol[:, :], None, ALU.mult) \
            .then_inc(s_dve.h, 1)
        s_dve.n += 1
        e.wait_ge(s_dve.h, s_dve.n)
        e.tensor_sub(d["rot_b"][:, :], scr2[:, :], scr1[:, :]).then_inc(s_dve.h, 1)
        s_dve.n += 1
    ctx.vector(fv)

    def fs(e):
        e.wait_ge(s_ld.h, s_ld.n)
        e.activation(d["ssig"][:, :], d["sclip"][:, :], AF.Sigmoid) \
            .then_inc(s_act.h, 1)
        s_act.n += 1
    ctx.scalar(fs)


def emit_weight_quant(ctx, d, scope):
    sb = d["sb"]
    s_ld, s_sc, s_dve, s_act, s_st = d["s_ld"], d["s_sc"], d["s_dve"], d["s_act"], d["s_st"]

    wst = sb("wst", [128, 2 * DI], F32, scope)
    wsq = sb("wsq", [128, 2 * DI], F32, scope)
    wqb = sb("wqb", [128, 2 * DI], BF16, scope)
    rcol = sb("rcol", [128, 4], F32, scope)
    tb = sb("tb", [128, 2 * DI], BF16, scope)

    jobs = []
    for (src, dst, wsc, nch, W) in [
        (d["wg_in"], d["wgq_nat"], d["wsc_g"], IC, DM),
        (d["wu_in"], d["wuq_nat"], d["wsc_u"], IC, DM),
        (d["wd_in"], d["wdq_nat"], d["wsc_d"], HC, DI),
    ]:
        for k in range(nch):
            jobs.append((src, dst, wsc, k, W))
    nj = len(jobs)
    dve0, act0, st0 = s_dve.n, s_act.n, s_st.n
    p_ld = mk_psems(ctx, d, "wql")
    p_st = mk_psems(ctx, d, "wqs")

    def fsync(e):
        for j, (src, dst, wsc, k, W) in enumerate(jobs):
            if j >= 2:
                e.wait_ge(s_act.h, act0 + j - 1)
            srcv = src.rearrange("(c p) m -> c p m", p=128)
            e.dma_start(out=wst[:, (j % 2) * DI:(j % 2) * DI + W],
                        in_=srcv[k]).then_inc(p_ld[j % 2].h, 16)
            p_ld[j % 2].n += 16
    ctx.sync(fsync)

    def fv(e):
        for j, (src, dst, wsc, k, W) in enumerate(jobs):
            sl = slice((j % 2) * DI, (j % 2) * DI + W)
            e.wait_ge(p_ld[j % 2].h, 16 * (j // 2 + 1))
            e.reduce_max(wsc[:, k:k + 1], wst[:, sl], AX.X,
                         apply_absolute_value=True).then_inc(s_dve.h, 1)
            e.wait_ge(s_dve.h, dve0 + 5 * j + 1)
            e.reciprocal(rcol[:, 2 * (j % 2):2 * (j % 2) + 1], wsc[:, k:k + 1]) \
                .then_inc(s_dve.h, 1)
            e.wait_ge(s_dve.h, dve0 + 5 * j + 2)
            e.tensor_scalar(rcol[:, 2 * (j % 2) + 1:2 * (j % 2) + 2],
                            rcol[:, 2 * (j % 2):2 * (j % 2) + 1], QMAX, None,
                            ALU.mult).then_inc(s_dve.h, 1)
            e.wait_ge(s_act.h, act0 + j + 1)
            if j >= 2:
                e.wait_ge(p_st[j % 2].h, 16 * (j // 2))
            e.tensor_scalar(wsq[:, sl], wsq[:, sl], QMAX, -QMAX - 1.0,
                            ALU.min, ALU.max).then_inc(s_dve.h, 1)
            e.wait_ge(s_dve.h, dve0 + 5 * j + 4)
            e.tensor_scalar(wqb[:, sl], wsq[:, sl], RC, RC, ALU.add,
                            ALU.subtract).then_inc(s_dve.h, 1)
        s_dve.n += 5 * nj
    ctx.vector(fv)

    def fs(e):
        for j, (src, dst, wsc, k, W) in enumerate(jobs):
            sl = slice((j % 2) * DI, (j % 2) * DI + W)
            e.wait_ge(s_dve.h, dve0 + 5 * j + 3)
            e.activation(wsq[:, sl], wst[:, sl], AF.Copy,
                         scale=rcol[:, 2 * (j % 2) + 1:2 * (j % 2) + 2]) \
                .then_inc(s_act.h, 1)
        s_act.n += nj
    ctx.scalar(fs)

    def fst(e):
        for j, (src, dst, wsc, k, W) in enumerate(jobs):
            sl = slice((j % 2) * DI, (j % 2) * DI + W)
            e.wait_ge(s_dve.h, dve0 + 5 * (j + 1))
            dstv = dst.rearrange("(c p) m -> c p m", p=128)
            e.dma_start(out=dstv[k], in_=wqb[:, sl]).then_inc(p_st[j % 2].h, 16)
            p_st[j % 2].n += 16
    ctx.gpsimd(fst)

    ctx.barrier()

    p_tr = mk_psems(ctx, d, "bnt")
    p_bw = mk_psems(ctx, d, "bnc")
    bounce_jobs = []
    for (natt, tt_, nch, W) in [
        (d["wgq_nat"], d["wgq_t"], HC, DI),
        (d["wuq_nat"], d["wuq_t"], HC, DI),
        (d["wdq_nat"], d["wdq_t"], IC, DM),
    ]:
        for k in range(nch):
            bounce_jobs.append((natt, tt_, k, W))

    def fb(e):
        for n, (natt, tt_, k, W) in enumerate(bounce_jobs):
            buf = tb[:, (n % 2) * DI:(n % 2) * DI + W]
            if n >= 2:
                e.wait_ge(p_bw[n % 2].h, 16 * (n // 2))
            e.dma_start_transpose(buf, natt[:, k * 128:(k + 1) * 128]) \
                .then_inc(p_tr[n % 2].h, 16)
            p_tr[n % 2].n += 16
    ctx.sync(fb)

    def fbw(e):
        for n, (natt, tt_, k, W) in enumerate(bounce_jobs):
            buf = tb[:, (n % 2) * DI:(n % 2) * DI + W]
            e.wait_ge(p_tr[n % 2].h, 16 * (n // 2 + 1))
            e.dma_start(out=tt_[k * 128:(k + 1) * 128, :], in_=buf) \
                .then_inc(p_bw[n % 2].h, 16)
            p_bw[n % 2].n += 16
        e.wait_ge(p_bw[0].h, p_bw[0].n)
        e.wait_ge(p_bw[1].h, p_bw[1].n)
    ctx.gpsimd(fbw)

    if "wgq" in d["dbg"]:
        s_gld = d["s_gld"]

        def fdbg(e):
            e.wait_ge(s_st.h, s_st.n)
            e.dma_start(out=d["dbg_out"]["wgq"][:, :], in_=d["wgq_nat"][:, :]) \
                .then_inc(s_gld.h, 16)
            s_gld.n += 16
            e.wait_ge(s_gld.h, s_gld.n)
        ctx.gpsimd(fdbg)
    ctx.barrier()


def emit_ln_stats(ctx, d, want_max):
    s_ld, s_dve, s_act = d["s_ld"], d["s_dve"], d["s_act"]
    xr = d["xr"]
    sqb = d["sqb"]
    act0 = s_act.n
    dve0 = s_dve.n

    xr_dve = dict(d.get("xr_dve", {}))

    def fs(e):
        for t in range(TT):
            if t in xr_dve:
                e.wait_ge(s_dve.h, xr_dve[t])
            if t >= 1:
                e.wait_ge(s_dve.h, dve0 + t)
            sl = slice(0, DM)
            e.activation(sqb[:, sl], xr[:, t * DM:(t + 1) * DM], AF.Square) \
                .then_inc(s_act.h, 1)
        s_act.n += TT
    ctx.scalar(fs)

    def fv(e):
        for t in range(TT):
            xt = xr[:, t * DM:(t + 1) * DM]
            sl = slice(0, DM)
            e.reduce_sum(d["st_a"][:, t:t + 1], xt, AX.X)
            if want_max:
                e.reduce_max(d["st_c"][:, t:t + 1], xt, AX.X)
                e.tensor_reduce(d["st_d"][:, t:t + 1], xt, AX.X, ALU.min)
            e.wait_ge(s_act.h, act0 + t + 1)
            e.reduce_sum(d["st_b"][:, t:t + 1], sqb[:, sl], AX.X) \
                .then_inc(s_dve.h, 1)
        s_dve.n += TT
        e.wait_ge(s_dve.h, s_dve.n)
        e.tensor_scalar(d["st_m"][:, :], d["st_a"][:, :], 1.0 / DM, None,
                        ALU.mult).then_inc(s_dve.h, 1)
        s_dve.n += 1
        e.tensor_scalar(d["st_b"][:, :], d["st_b"][:, :], 1.0 / DM, None,
                        ALU.mult).then_inc(s_dve.h, 1)
        s_dve.n += 1
        e.wait_ge(s_dve.h, s_dve.n)
        e.tensor_mul(d["st_a"][:, :], d["st_m"][:, :], d["st_m"][:, :]) \
            .then_inc(s_dve.h, 1)
        s_dve.n += 1
        e.wait_ge(s_dve.h, s_dve.n)
        e.scalar_tensor_tensor(d["st_b"][:, :], d["st_b"][:, :], EPS,
                               d["st_a"][:, :], ALU.add, ALU.subtract) \
            .then_inc(s_dve.h, 1)
        s_dve.n += 1
    ctx.vector(fv)

    def fs2(e):
        e.wait_ge(s_dve.h, s_dve.n)
        e.activation(d["st_r"][:, :], d["st_b"][:, :], AF.Sqrt).then_inc(s_act.h, 1)
        s_act.n += 1
    ctx.scalar(fs2)

    def fv2(e):
        e.wait_ge(s_act.h, s_act.n)
        e.reciprocal(d["st_r"][:, :], d["st_r"][:, :]).then_inc(s_dve.h, 1)
        s_dve.n += 1
        if want_max:
            e.wait_ge(s_dve.h, s_dve.n)
            e.tensor_sub(d["st_c"][:, :], d["st_c"][:, :], d["st_m"][:, :]) \
                .then_inc(s_dve.h, 1)
            s_dve.n += 1
            e.tensor_sub(d["st_d"][:, :], d["st_m"][:, :], d["st_d"][:, :]) \
                .then_inc(s_dve.h, 1)
            s_dve.n += 1
            e.wait_ge(s_dve.h, s_dve.n)
            e.tensor_max(d["st_c"][:, :], d["st_c"][:, :], d["st_d"][:, :]) \
                .then_inc(s_dve.h, 1)
            s_dve.n += 1
            e.wait_ge(s_dve.h, s_dve.n)
            e.tensor_mul(d["st_c"][:, :], d["st_c"][:, :], d["st_r"][:, :]) \
                .then_inc(s_dve.h, 1)
            s_dve.n += 1
            e.wait_ge(s_dve.h, s_dve.n)
            e.reduce_max(d["lmax_col"][:, :], d["st_c"][:, :], AX.X) \
                .then_inc(s_dve.h, 1)
            s_dve.n += 1
    ctx.vector(fv2)


def emit_maxreduce(ctx, d, slot):
    s_pe, s_dve, s_gld, s_cc, s_gp = d["s_pe"], d["s_dve"], d["s_gld"], d["s_cc"], d["s_gp"]
    pall = d["pall"]

    def ft(e):
        e.wait_ge(s_dve.h, s_dve.n)
        e.transpose(pall[0:1, 7 * 512:7 * 512 + 128], d["lmax_col"][:, :],
                    d["iden_f"][:, :]).then_inc(s_pe.h, 1)
        s_pe.n += 1
    ctx.tensor(ft)

    def fv(e):
        e.wait_ge(s_pe.h, s_pe.n)
        e.reduce_max(d["s_tmp"][:, 0:1], pall[0:1, 7 * 512:7 * 512 + 128], AX.X) \
            .then_inc(s_dve.h, 1)
        s_dve.n += 1
    ctx.vector(fv)

    def fg(e):
        e.memset(d["s_tmp"][:, 1:4], 0.0).then_inc(s_gp.h, 1)
        s_gp.n += 1
        e.wait_ge(s_gp.h, s_gp.n)
        e.wait_ge(s_dve.h, s_dve.n)
        e.dma_start(out=d["mx_in"][:, 0:4], in_=d["s_tmp"][:, 0:4]) \
            .then_inc(s_gld.h, 16)
        s_gld.n += 16
        e.wait_ge(s_gld.h, s_gld.n)
        e.collective_compute(
            "AllReduce", ALU.max, replica_groups=[list(range(NCORES))],
            ins=[d["mx_in"].ap()], outs=[d["mx_out"].ap()],
        ).then_inc(s_cc.h, 1)
        s_cc.n += 1
        e.wait_ge(s_cc.h, s_cc.n)
        e.dma_start(out=d["gmax"][:, slot:slot + 1], in_=d["mx_out"][:, 0:1]) \
            .then_inc(s_gld.h, 16)
        s_gld.n += 16
        e.wait_ge(s_gld.h, s_gld.n)
    ctx.gpsimd(fg)


def emit_scale_chain(ctx, d, slot, sig_idx):
    s_dve, s_pe, s_gld = d["s_dve"], d["s_pe"], d["s_gld"]
    pall = d["pall"]

    def fv(e):
        e.wait_ge(s_gld.h, s_gld.n)
        e.tensor_scalar(d["s_tmp"][:, 4:5], d["gmax"][:, slot:slot + 1],
                        d["ssig"][:, sig_idx:sig_idx + 1], 1.0 / QMAX,
                        ALU.mult, ALU.mult).then_inc(s_dve.h, 1)
        s_dve.n += 1
        e.wait_ge(s_dve.h, s_dve.n)
        e.reciprocal(d["s_tmp"][:, 5:6], d["s_tmp"][:, 4:5]).then_inc(s_dve.h, 1)
        s_dve.n += 1
    ctx.vector(fv)

    def ft(e):
        e.wait_ge(s_dve.h, s_dve.n)
        e.matmul(pall[:, 7 * 512:7 * 512 + 2], d["ones_row"][:, :],
                 d["s_tmp"][0:1, 4:6], start=True, stop=True).then_inc(s_pe.h, 1)
        s_pe.n += 1
    ctx.tensor(ft)

    def fv2(e):
        e.wait_ge(s_pe.h, s_pe.n)
        e.tensor_copy(d["binv"][:, 2 * slot:2 * slot + 1],
                      pall[:, 7 * 512 + 1:7 * 512 + 2])
        e.tensor_copy(d["binv"][:, 2 * slot + 1:2 * slot + 2],
                      pall[:, 7 * 512:7 * 512 + 1]).then_inc(s_dve.h, 1)
        s_dve.n += 1
        if slot % 2 == 1:
            e.wait_ge(s_dve.h, s_dve.n)
            e.tensor_scalar(d["hcol"][:, slot // 2:slot // 2 + 1],
                            d["binv"][:, 2 * slot + 1:2 * slot + 2], 0.5, None,
                            ALU.mult).then_inc(s_dve.h, 1)
            s_dve.n += 1
        if "gmax" in d["dbg"]:
            e.wait_ge(s_dve.h, s_dve.n)

            def fdbg(e2):
                e2.wait_ge(s_dve.h, s_dve.n)
                e2.dma_start(out=d["dbg_out"]["gmax"][:, :], in_=d["gmax"][:, :]) \
                    .then_inc(d["s_st"].h, 16)
                d["s_st"].n += 16
            ctx.gpsimd(fdbg)
    ctx.vector(fv2)


def emit_xq(ctx, d, slot):
    s_dve, s_st, s_sc = d["s_dve"], d["s_st"], d["s_sc"]
    qstage, qout, srs, xq_fm = d["qstage"], d["qout"], d["srs"], d["xq_fm"]
    dve0 = s_dve.n
    p_xs = mk_psems(ctx, d, "xqs%d" % slot)

    def fv(e):
        e.tensor_scalar(srs[:, :], d["st_r"][:, :],
                        d["binv"][:, 2 * slot:2 * slot + 1], None, ALU.mult) \
            .then_inc(s_dve.h, 1)
        e.wait_ge(s_dve.h, dve0 + 1)
        for t in range(TT):
            sl = slice((t % 2) * DM, (t % 2) * DM + DM)
            if t >= 2:
                e.wait_ge(p_xs[t % 2].h, 16 * (t // 2))
            xt = d["xr"][:, t * DM:(t + 1) * DM]
            e.tensor_scalar(qstage[:, sl], xt, d["st_m"][:, t:t + 1],
                            srs[:, t:t + 1], ALU.subtract, ALU.mult)
            e.tensor_scalar(qstage[:, sl], qstage[:, sl], QMAX, -QMAX - 1.0,
                            ALU.min, ALU.max)
            e.tensor_scalar(qout[:, sl], qstage[:, sl], RC, RC, ALU.add,
                            ALU.subtract).then_inc(s_dve.h, 1)
        s_dve.n += 1 + TT
    ctx.vector(fv)

    xq_t = d["xq_nat"].rearrange("(t p) m -> t p m", p=128)
    sc0 = s_sc.n

    def fst(e):
        for t in range(TT):
            sl = slice((t % 2) * DM, (t % 2) * DM + DM)
            e.wait_ge(s_dve.h, dve0 + 2 + t)
            e.dma_start(out=xq_t[t], in_=qout[:, sl]).then_inc(p_xs[t % 2].h, 16)
            p_xs[t % 2].n += 16
            if t == 7:
                e.wait_ge(p_xs[0].h, 16 * 4)
                e.wait_ge(p_xs[1].h, 16 * 4)
                for hc in range(HC):
                    e.dma_start_transpose(
                        xq_fm[:, hc * TOK:hc * TOK + TOK // 2],
                        d["xq_nat"][0:TOK // 2, hc * 128:(hc + 1) * 128]) \
                        .then_inc(s_sc.h, 16)
                s_sc.n += 16 * HC
        e.wait_ge(p_xs[0].h, p_xs[0].n)
        e.wait_ge(p_xs[1].h, p_xs[1].n)
        for hc in range(HC):
            e.dma_start_transpose(
                xq_fm[:, hc * TOK + TOK // 2:(hc + 1) * TOK],
                d["xq_nat"][TOK // 2:TOK, hc * 128:(hc + 1) * 128]) \
                .then_inc(s_sc.h, 16)
        s_sc.n += 16 * HC
        e.wait_ge(s_sc.h, s_sc.n)
        if "xq" in d["dbg"]:
            e.dma_start(out=d["dbg_out"]["xq"][:, :], in_=d["xq_nat"][:, :]) \
                .then_inc(s_st.h, 16)
            s_st.n += 16
            e.wait_ge(s_st.h, s_st.n)
    ctx.sync(fst)


def emit_gateup(ctx, d, scope, slot):
    sb = d["sb"]
    s_ld, s_pe, s_dve, s_act, s_st = d["s_ld"], d["s_pe"], d["s_dve"], d["s_act"], d["s_st"]
    pall, xq_fm = d["pall"], d["xq_fm"]

    wgb = d.get("wgb") or sb("wgb", [128, 2 * HC * 128], BF16, scope)
    wub = d.get("wub") or sb("wub", [128, 2 * HC * 128], BF16, scope)
    sgt = d.get("sgt") or sb("sgt", [128, 2 * 512], F32, scope)
    upt = d.get("upt") or sb("upt", [128, 2 * 512], F32, scope)
    hft = d.get("hft") or sb("hft", [128, 2 * 512], F32, scope)
    hbt = d.get("hbt") or sb("hbt", [128, 2 * 512], F16, scope)
    svg = d.get("svg") or sb("svg", [128, IC], F32, scope)
    svu = d.get("svu") or sb("svu", [128, IC], F32, scope)
    for nm, v in [("wgb", wgb), ("wub", wub), ("sgt", sgt), ("upt", upt),
                  ("hft", hft), ("hbt", hbt), ("svg", svg), ("svu", svu)]:
        d[nm] = v

    NS = 4
    nu = IC * NS
    pe0, act0, dve0 = s_pe.n, s_act.n, s_dve.n
    p_wl = mk_psems(ctx, d, "gwl%d" % slot)
    p_hs = mk_psems(ctx, d, "ghs%d" % slot)
    wg_t = d["wgq_t"].rearrange("(c p) i -> p c i", p=128)
    wu_t = d["wuq_t"].rearrange("(c p) i -> p c i", p=128)

    def fsync(e):
        for ic in range(IC):
            if ic >= 2:
                e.wait_ge(s_pe.h, pe0 + (ic - 1) * NS)
            b = (ic % 2) * HC * 128
            e.dma_start(out=wgb[:, b:b + HC * 128].rearrange("p (c n) -> p c n", c=HC),
                        in_=wg_t[:, :, ic * 128:(ic + 1) * 128]).then_inc(p_wl[ic % 2].h, 16)
            e.dma_start(out=wub[:, b:b + HC * 128].rearrange("p (c n) -> p c n", c=HC),
                        in_=wu_t[:, :, ic * 128:(ic + 1) * 128]).then_inc(p_wl[ic % 2].h, 16)
            p_wl[ic % 2].n += 32
    ctx.sync(fsync)

    def fv0(e):
        e.tensor_scalar(svg[:, :], d["wsc_g"][:, :], d["binv"][:, 2 * slot + 1:2 * slot + 2],
                        1.0 / QMAX, ALU.mult, ALU.mult)
        e.tensor_scalar(svu[:, :], d["wsc_u"][:, :], d["binv"][:, 2 * slot + 1:2 * slot + 2],
                        1.0 / QMAX, ALU.mult, ALU.mult).then_inc(s_dve.h, 1)
        s_dve.n += 1
    ctx.vector(fv0)
    dve_pre = s_dve.n

    def ft(e):
        e.wait_ge(s_dve.h, dve_pre)
        u = 0
        for ic in range(IC):
            e.wait_ge(p_wl[ic % 2].h, 32 * (ic // 2 + 1))
            b = (ic % 2) * HC * 128
            for s in range(NS):
                if u >= 2:
                    e.wait_ge(s_dve.h, dve_pre + (u - 1) * 2)
                    e.wait_ge(s_act.h, act0 + u - 1)
                gb, ub = (u % 2) * 2, (u % 2) * 2 + 1
                for hc in range(HC):
                    e.matmul(pall[:, gb * 512:gb * 512 + 512],
                             wgb[:, b + hc * 128:b + hc * 128 + 128],
                             xq_fm[:, hc * TOK + s * 512:hc * TOK + (s + 1) * 512],
                             start=(hc == 0), stop=(hc == HC - 1))
                for hc in range(HC):
                    inst = e.matmul(
                        pall[:, ub * 512:ub * 512 + 512],
                        wub[:, b + hc * 128:b + hc * 128 + 128],
                        xq_fm[:, hc * TOK + s * 512:hc * TOK + (s + 1) * 512],
                        start=(hc == 0), stop=(hc == HC - 1))
                inst.then_inc(s_pe.h, 1)
                u += 1
        s_pe.n += nu
    ctx.tensor(ft)

    def fs(e):
        u = 0
        for ic in range(IC):
            for s in range(NS):
                e.wait_ge(s_pe.h, pe0 + u + 1)
                if u >= 2:
                    e.wait_ge(s_dve.h, dve_pre + (u - 1) * 2)
                gb = (u % 2) * 2
                e.activation(sgt[:, (u % 2) * 512:(u % 2) * 512 + 512],
                             pall[:, gb * 512:gb * 512 + 512], AF.Silu,
                             scale=svg[:, ic:ic + 1]).then_inc(s_act.h, 1)
                u += 1
        s_act.n += nu
    ctx.scalar(fs)

    def fv(e):
        u = 0
        for ic in range(IC):
            for s in range(NS):
                e.wait_ge(s_pe.h, pe0 + u + 1)
                ub = (u % 2) * 2 + 1
                if u >= 2:
                    e.wait_ge(p_hs[u % 2].h, 16 * (u // 2))
                e.tensor_scalar(upt[:, (u % 2) * 512:(u % 2) * 512 + 512],
                                pall[:, ub * 512:ub * 512 + 512],
                                svu[:, ic:ic + 1], None, ALU.mult)
                e.wait_ge(s_act.h, act0 + u + 1)
                e.tensor_mul(hft[:, (u % 2) * 512:(u % 2) * 512 + 512],
                             sgt[:, (u % 2) * 512:(u % 2) * 512 + 512],
                             upt[:, (u % 2) * 512:(u % 2) * 512 + 512])
                e.tensor_copy(hbt[:, (u % 2) * 512:(u % 2) * 512 + 512],
                              hft[:, (u % 2) * 512:(u % 2) * 512 + 512]) \
                    .then_inc(s_dve.h, 1)
                e.reduce_max(d["hmax_cols"][:, u:u + 1],
                             hft[:, (u % 2) * 512:(u % 2) * 512 + 512], AX.X,
                             apply_absolute_value=True).then_inc(s_dve.h, 1)
                u += 1
        s_dve.n += 2 * nu
    ctx.vector(fv)

    h_t = d["h_nat"].rearrange("(c p) n -> c p n", p=128)

    def fst(e):
        u = 0
        for ic in range(IC):
            for s in range(NS):
                e.wait_ge(s_dve.h, dve_pre + 2 * u + 2)
                e.dma_start(out=h_t[ic, :, s * 512:(s + 1) * 512],
                            in_=hbt[:, (u % 2) * 512:(u % 2) * 512 + 512]) \
                    .then_inc(p_hs[u % 2].h, 16)
                p_hs[u % 2].n += 16
                u += 1
        e.wait_ge(p_hs[0].h, p_hs[0].n)
        e.wait_ge(p_hs[1].h, p_hs[1].n)
        if "h" in d["dbg"]:
            e.dma_start(out=d["dbg_out"]["h"][:, :], in_=d["h_nat"][:, :]) \
                .then_inc(s_st.h, 16)
            s_st.n += 16
            e.wait_ge(s_st.h, s_st.n)
    ctx.gpsimd(fst)


def emit_hmax_finish(ctx, d):
    s_dve = d["s_dve"]

    def fv(e):
        e.reduce_max(d["lmax_col"][:, :], d["hmax_cols"][:, :], AX.X) \
            .then_inc(s_dve.h, 1)
        s_dve.n += 1
    ctx.vector(fv)


def emit_hquant(ctx, d, scope, slot):
    sb = d["sb"]
    s_ld, s_dve, s_act, s_st = d["s_ld"], d["s_dve"], d["s_act"], d["s_st"]
    hst = d.get("hst") or sb("hst", [128, 2 * TOK], F16, scope)
    hqo = d.get("hqo") or sb("hqo", [128, 2 * TOK], BF16, scope)
    hfs = d.get("hfs") or sb("hfs", [128, 2 * TOK], F32, scope)
    d["hst"], d["hqo"], d["hfs"] = hst, hqo, hfs

    h_t = d["h_nat"].rearrange("(c p) n -> c p n", p=128)
    hq_t = d["hq_nat"].rearrange("(c p) n -> c p n", p=128)
    act0, dve0 = s_act.n, s_dve.n
    p_hl = mk_psems(ctx, d, "hql%d" % slot)
    p_qs = mk_psems(ctx, d, "hqs%d" % slot)

    def fsync(e):
        for ic in range(IC):
            if ic >= 2:
                e.wait_ge(p_qs[ic % 2].h, 16 * (ic // 2))
            e.dma_start(out=hst[:, (ic % 2) * TOK:(ic % 2 + 1) * TOK],
                        in_=h_t[ic]).then_inc(p_hl[ic % 2].h, 16)
            p_hl[ic % 2].n += 16
    ctx.sync(fsync)

    def fs(e):
        for ic in range(IC):
            sl = slice((ic % 2) * TOK, (ic % 2) * TOK + TOK)
            e.wait_ge(p_hl[ic % 2].h, 16 * (ic // 2 + 1))
            if ic >= 2:
                e.wait_ge(s_dve.h, dve0 + 2 * (ic - 1))
            e.activation(hfs[:, sl], hst[:, sl], AF.Copy,
                         scale=d["binv"][:, 2 * slot:2 * slot + 1]) \
                .then_inc(s_act.h, 1)
        s_act.n += IC
    ctx.scalar(fs)

    def fv(e):
        for ic in range(IC):
            sl = slice((ic % 2) * TOK, (ic % 2) * TOK + TOK)
            e.wait_ge(s_act.h, act0 + ic + 1)
            if ic >= 2:
                e.wait_ge(p_qs[ic % 2].h, 16 * (ic // 2))
            e.tensor_scalar(hfs[:, sl], hfs[:, sl], QMAX, -QMAX - 1.0,
                            ALU.min, ALU.max)
            e.tensor_scalar(hqo[:, sl], hfs[:, sl], RC, RC, ALU.add,
                            ALU.subtract).then_inc(s_dve.h, 2)
        s_dve.n += 2 * IC
    ctx.vector(fv)

    def fst(e):
        for ic in range(IC):
            sl = slice((ic % 2) * TOK, (ic % 2) * TOK + TOK)
            e.wait_ge(s_dve.h, dve0 + 2 * (ic + 1))
            e.dma_start(out=hq_t[ic], in_=hqo[:, sl]).then_inc(p_qs[ic % 2].h, 16)
            p_qs[ic % 2].n += 16
        e.wait_ge(p_qs[0].h, p_qs[0].n)
        e.wait_ge(p_qs[1].h, p_qs[1].n)
        if "hq" in d["dbg"]:
            e.dma_start(out=d["dbg_out"]["hq"][:, :], in_=d["hq_nat"][:, :]) \
                .then_inc(s_st.h, 16)
            s_st.n += 16
            e.wait_ge(s_st.h, s_st.n)
    ctx.gpsimd(fst)


def emit_bsd(ctx, d):
    s_pe, s_dve = d["s_pe"], d["s_dve"]
    pall = d["pall"]

    for hc in range(HC):
        def ft(e, hc=hc):
            e.wait_ge(s_dve.h, s_dve.n)
            e.transpose(pall[0:1, 6 * 512:6 * 512 + 128], d["wsc_d"][:, hc:hc + 1],
                        d["iden_f"][:, :]).then_inc(s_pe.h, 1)
            s_pe.n += 1
        ctx.tensor(ft)

        def fv(e, hc=hc):
            e.wait_ge(s_pe.h, s_pe.n)
            e.tensor_copy(d["brow"][:, :], pall[0:1, 6 * 512:6 * 512 + 128]) \
                .then_inc(s_dve.h, 1)
            s_dve.n += 1
        ctx.vector(fv)

        def ft2(e, hc=hc):
            e.wait_ge(s_dve.h, s_dve.n)
            e.matmul(pall[:, 7 * 512:7 * 512 + 128], d["ones_row"][:, :],
                     d["brow"][:, :], start=True, stop=True).then_inc(s_pe.h, 1)
            s_pe.n += 1
        ctx.tensor(ft2)

        def fv2(e, hc=hc):
            e.wait_ge(s_pe.h, s_pe.n)
            e.tensor_scalar(d["bsd"][:, hc * 128:(hc + 1) * 128],
                            pall[:, 7 * 512:7 * 512 + 128], 1.0 / QMAX, None,
                            ALU.mult).then_inc(s_dve.h, 1)
            s_dve.n += 1
        ctx.vector(fv2)


def emit_down(ctx, d, scope, app):
    sb = d["sb"]
    s_ld, s_pe, s_dve, s_act = d["s_ld"], d["s_pe"], d["s_dve"], d["s_act"]
    pall = d["pall"]
    slot = 2 * app + 1

    wdr = sb("wdr", [128, IC * DM], BF16, scope)
    hstage = sb("hstage", [128, 2 * IC * 256], F16, scope)
    hfs = sb("hfs", [128, IC * 256], F32, scope)
    hqb = sb("hqb", [128, 2 * IC * 256], BF16, scope)
    dst1 = sb("dst1", [128, DM], F32, scope)

    NTP = TT // 2
    CW = IC * 256
    h_v = d["h_nat"].rearrange("(c p) n -> p c n", p=128)
    wd_v = d["wdq_t"].rearrange("(c p) m -> p c m", p=128)
    p_wd = mk_psems(ctx, d, "dwd%d" % app)
    p_hl = mk_psems(ctx, d, "dhl%d" % app)
    pe0, dve0, act0 = s_pe.n, s_dve.n, s_act.n

    def fw(e):
        e.dma_start(out=wdr[:, :].rearrange("p (c m) -> p c m", c=IC),
                    in_=wd_v).then_inc(p_wd[0].h, 16)
        p_wd[0].n += 16
    ctx.sync(fw)

    def fhl(e):
        for tp in range(NTP):
            if tp >= 2:
                e.wait_ge(s_act.h, act0 + tp - 1)
            e.dma_start(out=hstage[:, (tp % 2) * CW:(tp % 2 + 1) * CW]
                        .rearrange("p (c n) -> p c n", c=IC),
                        in_=h_v[:, :, tp * 256:(tp + 1) * 256]) \
                .then_inc(p_hl[tp % 2].h, 16)
            p_hl[tp % 2].n += 16
    ctx.gpsimd(fhl)

    def fa(e):
        for tp in range(NTP):
            e.wait_ge(p_hl[tp % 2].h, 16 * (tp // 2 + 1))
            if tp >= 1:
                e.wait_ge(s_dve.h, dve0 + 5 * (tp - 1) + 1)
            e.activation(hfs[:, :], hstage[:, (tp % 2) * CW:(tp % 2 + 1) * CW],
                         AF.Copy, scale=d["binv"][:, 2 * slot:2 * slot + 1]) \
                .then_inc(s_act.h, 1)
        s_act.n += NTP
    ctx.scalar(fa)

    cnt_quant = {}
    c = dve0
    for tp in range(NTP):
        c += 1
        cnt_quant[tp] = c
        if tp >= 1:
            c += 4

    def emit_evac(e, tp):
        e.wait_ge(s_pe.h, pe0 + (tp + 1) * IC * 4)
        for ti in range(2):
            t = tp * 2 + ti
            bank0 = (tp % 2) * 4 + ti * 2
            e.tensor_mul(dst1[:, :],
                         pall[:, bank0 * 512:bank0 * 512 + 1024], d["bsd"][:, :])
            e.scalar_tensor_tensor(
                d["xr"][:, t * DM:(t + 1) * DM],
                dst1[:, :],
                d["hcol"][:, app:app + 1],
                d["xr"][:, t * DM:(t + 1) * DM],
                ALU.mult, ALU.add).then_inc(s_dve.h, 2)
            s_dve.n += 2
            d.setdefault("xr_dve", {})[t] = s_dve.n

    def fqv(e):
        for tp in range(NTP):
            e.wait_ge(s_act.h, act0 + tp + 1)
            if tp >= 2:
                e.wait_ge(s_pe.h, pe0 + 88 * (tp - 1))
            e.tensor_scalar(hfs[:, :], hfs[:, :], QMAX, -QMAX - 1.0,
                            ALU.min, ALU.max)
            e.tensor_scalar(hqb[:, (tp % 2) * CW:(tp % 2 + 1) * CW], hfs[:, :],
                            RC, RC, ALU.add, ALU.subtract).then_inc(s_dve.h, 1)
            s_dve.n += 1
            if tp >= 1:
                emit_evac(e, tp - 1)
        emit_evac(e, NTP - 1)
    ctx.vector(fqv)

    def ft(e):
        e.wait_ge(p_wd[0].h, 16)
        for tp in range(NTP):
            e.wait_ge(s_dve.h, cnt_quant[tp])
            for ic in range(IC):
                hb = (tp % 2) * CW
                for ti in range(2):
                    for half in range(2):
                        bank = (tp % 2) * 4 + ti * 2 + half
                        ii = e.matmul(
                            pall[:, bank * 512:bank * 512 + 512],
                            hqb[:, hb + ic * 256 + ti * 128:hb + ic * 256 + ti * 128 + 128],
                            wdr[:, ic * DM + half * 512:ic * DM + half * 512 + 512],
                            start=(ic == 0), stop=(ic == IC - 1))
                ii.then_inc(s_pe.h, 4)
        s_pe.n += 4 * NTP * IC
    ctx.tensor(ft)


PI = 3.14159265358979323846


def emit_attention(ctx, d, scope):
    nc = ctx.nc
    sb = d["sb"]
    s_pe, s_dve, s_act, s_gp = d["s_pe"], d["s_dve"], d["s_act"], d["s_gp"]
    s_ld, s_sc, s_gld, s_st, s_cc = d["s_ld"], d["s_sc"], d["s_gld"], d["s_st"], d["s_cc"]
    pall = d["pall"]

    emit_ln_stats(ctx, d, want_max=False)
    ctx.barrier()

    battn = sb("battn", [128, 4], F32, scope)
    ones_col = sb("ones_col", [128, 1], BF16, scope)
    core_scope = ExitStack()
    av_fm = sb("av_fm", [128, 4 * S], BF16, core_scope)
    mid_scope = ExitStack()
    qf = sb("qf", [128, 4 * S], BF16, mid_scope)
    kf = sb("kf", [128, 4 * S], BF16, mid_scope)

    xst_scope = ExitStack()
    xst = sb("xst", [128, DM], BF16, xst_scope)
    p_xl = mk_psems(ctx, d, "axl")
    dve0 = s_dve.n

    def fv_ln(e):
        e.memset(ones_col[:, :], 1.0)
        for t in range(TT):
            sl = slice(0, DM)
            if t >= 1:
                e.wait_ge(p_xl[0].h, 16 * t)
            e.tensor_scalar(xst[:, sl], d["xr"][:, t * DM:(t + 1) * DM],
                            d["st_m"][:, t:t + 1], d["st_r"][:, t:t + 1],
                            ALU.subtract, ALU.mult).then_inc(s_dve.h, 1)
        s_dve.n += TT
    ctx.vector(fv_ln)

    ag_in_t = d["ag_xln_in"].rearrange("(t p) m -> t p m", p=128)

    def fst_ln(e):
        for t in range(TT):
            e.wait_ge(s_dve.h, dve0 + t + 1)
            e.dma_start(out=ag_in_t[t], in_=xst[:, :]).then_inc(p_xl[0].h, 16)
            p_xl[0].n += 16
    ctx.sync(fst_ln)

    def fcc(e):
        e.wait_ge(p_xl[0].h, p_xl[0].n)
        e.wait_ge(p_xl[1].h, p_xl[1].n)
        e.collective_compute(
            "AllGather", ALU.bypass,
            replica_groups=[[2 * i, 2 * i + 1] for i in range(NCORES // 2)],
            ins=[d["ag_xln_in"].ap().opt()], outs=[d["ag_xln_out"].ap().opt()],
        ).then_inc(s_cc.h, 1)
        s_cc.n += 1
        if "xln" in d["dbg"]:
            e.wait_ge(s_cc.h, s_cc.n)
            e.dma_start(out=d["dbg_out"]["xln"][:, :], in_=d["ag_xln_out"][:, :]) \
                .then_inc(s_gld.h, 16)
            s_gld.n += 16
            e.wait_ge(s_gld.h, s_gld.n)
    ctx.gpsimd(fcc)
    xst_scope.close()

    with ExitStack() as qscope:
        xfs = av_fm[:, 0:2 * HC * 512]
        cos_t = av_fm[:, 2 * HC * 512:2 * HC * 512 + S]
        sin_t = av_fm[:, 2 * HC * 512 + S:2 * HC * 512 + 2 * S]
        tmpf = sb("tmpf", [128, 2 * 512], F32, qscope)
        pio2 = sb("pio2", [128, 1], F32, qscope)
        qraw = sb("qraw", [128, 2 * 512], BF16, qscope)
        posrow = sb("posrow", [1, 512], F32, qscope)
        ivf = sb("ivf", [128, 1], F32, qscope)
        dummy_split_marker = None

        def fv_ivf(e):
            e.memset(pio2[:, :], PI / 2)
            pt = d["iota_p"]
            for step in range(5):
                ops = [
                    lambda: e.tensor_scalar(ivf[:, :], pt[:, :], 1.0 / 64,
                                            -63.0 / 128, ALU.mult, ALU.add),
                    lambda: e.tensor_scalar(ivf[:, :], ivf[:, :], RC, RC,
                                            ALU.add, ALU.subtract),
                    lambda: e.scalar_tensor_tensor(ivf[:, :], ivf[:, :], -64.0,
                                                   pt[:, :], ALU.mult, ALU.add),
                    lambda: e.tensor_scalar(ivf[:, :], ivf[:, :], 0.5, -0.25,
                                            ALU.mult, ALU.add),
                    lambda: e.tensor_scalar(ivf[:, :], ivf[:, :], RC, RC,
                                            ALU.add, ALU.subtract),
                ]
                ops[step]().then_inc(s_dve.h, 1)
                s_dve.n += 1
                e.wait_ge(s_dve.h, s_dve.n)
        ctx.vector(fv_ivf)

        def fs_ivf(e):
            e.wait_ge(s_dve.h, s_dve.n)
            import math
            e.activation(ivf[:, :], ivf[:, :], AF.Exp, scale=-math.log(10000.0) / 32) \
                .then_inc(s_act.h, 1)
            s_act.n += 1
        ctx.scalar(fs_ivf)

        for sp in range(8):
            def fgp(e, sp=sp):
                if sp == 0:
                    e.wait_ge(s_act.h, s_act.n)
                else:
                    e.wait_ge(s_dve.h, s_dve.n)
                e.iota(posrow[:, :], [[1, 512]], base=sp * 512,
                       channel_multiplier=0,
                       allow_small_or_imprecise_dtypes=True).then_inc(s_gp.h, 1)
                s_gp.n += 1
            ctx.gpsimd(fgp)

            def ft(e, sp=sp):
                e.wait_ge(s_gp.h, s_gp.n)
                e.wait_ge(s_dve.h, s_dve.n)
                e.matmul(pall[:, 6 * 512:7 * 512], d["ones_row"][:, :],
                         posrow[:, :], start=True, stop=True).then_inc(s_pe.h, 1)
                s_pe.n += 1
            ctx.tensor(ft)

            def fv(e, sp=sp):
                e.wait_ge(s_pe.h, s_pe.n)
                ang = tmpf[:, 0:512]
                r = tmpf[:, 512:1024]
                e.tensor_scalar(ang[:, :], pall[:, 6 * 512:7 * 512], ivf[:, :],
                                None, ALU.mult).then_inc(s_dve.h, 1)
                s_dve.n += 1
                e.wait_ge(s_dve.h, s_dve.n)
                e.tensor_scalar(r[:, :], ang[:, :], 1.0 / (2 * PI), None,
                                ALU.mult).then_inc(s_dve.h, 1)
                s_dve.n += 1
                e.wait_ge(s_dve.h, s_dve.n)
                e.tensor_scalar(r[:, :], r[:, :], RC, RC, ALU.add, ALU.subtract) \
                    .then_inc(s_dve.h, 1)
                s_dve.n += 1
                e.wait_ge(s_dve.h, s_dve.n)
                e.scalar_tensor_tensor(r[:, :], r[:, :], -2 * PI, ang[:, :],
                                       ALU.mult, ALU.add).then_inc(s_dve.h, 1)
                s_dve.n += 1
            ctx.vector(fv)

            def fs(e, sp=sp):
                e.wait_ge(s_dve.h, s_dve.n)
                e.activation(sin_t[:, sp * 512:(sp + 1) * 512], tmpf[:, 512:1024],
                             AF.Sin).then_inc(s_act.h, 1)
                s_act.n += 1
            ctx.scalar(fs)

            def fv2(e, sp=sp):
                ang = tmpf[:, 0:512]
                r = tmpf[:, 512:1024]
                e.wait_ge(s_act.h, s_act.n)
                e.tensor_scalar(r[:, :], ang[:, :], 1.0 / (2 * PI), 0.25,
                                ALU.mult, ALU.add).then_inc(s_dve.h, 1)
                s_dve.n += 1
                e.wait_ge(s_dve.h, s_dve.n)
                e.tensor_scalar(r[:, :], r[:, :], RC, RC, ALU.add, ALU.subtract) \
                    .then_inc(s_dve.h, 1)
                s_dve.n += 1
                e.wait_ge(s_dve.h, s_dve.n)
                e.scalar_tensor_tensor(r[:, :], r[:, :], -2 * PI, ang[:, :],
                                       ALU.mult, ALU.add).then_inc(s_dve.h, 1)
                s_dve.n += 1
            ctx.vector(fv2)

            def fs2(e, sp=sp):
                e.wait_ge(s_dve.h, s_dve.n)
                e.activation(cos_t[:, sp * 512:(sp + 1) * 512], tmpf[:, 512:1024],
                             AF.Sin, bias=pio2[:, :]).then_inc(s_act.h, 1)
                s_act.n += 1
            ctx.scalar(fs2)

        if "rotb" in d["dbg"]:
            def fdr(e):
                e.dma_start(out=d["dbg_out"]["rotb"][:, :], in_=d["rot_b"][:, :]) \
                    .then_inc(s_gld.h, 16)
                s_gld.n += 16
                e.wait_ge(s_gld.h, s_gld.n)
            ctx.gpsimd(fdr)
        if d["dbg"] & {"cos", "sin", "ivf"}:
            ctx.barrier()
            def fdbg_t(e):
                for nm, buf in (("cos", cos_t), ("sin", sin_t), ("ivf", ivf)):
                    if nm in d["dbg"]:
                        e.dma_start(out=d["dbg_out"][nm][:, :], in_=buf[:, :]) \
                            .then_inc(s_gld.h, 16)
                        s_gld.n += 16
                e.wait_ge(s_gld.h, s_gld.n)
            ctx.gpsimd(fdbg_t)
            ctx.barrier()
        for (w_in, dstf, pname) in ((d["wq_in"], qf, "aq"), (d["wk_in"], kf, "ak")):
            with ExitStack() as wsc:
                wqa = sb("w" + pname, [128, HC * 512], BF16, wsc)
                wq_v = w_in.rearrange("(c p) m -> p c m", p=128)

                def fwl(e, wqa=wqa, wq_v=wq_v):
                    e.wait_ge(s_pe.h, s_pe.n)
                    e.dma_start(out=wqa[:, :].rearrange("p (c m) -> p c m", c=HC),
                                in_=wq_v).then_inc(s_gld.h, 16)
                    s_gld.n += 16
                    e.wait_ge(s_cc.h, s_cc.n)
                ctx.gpsimd(fwl)
                ctx.barrier()

                p_tr = mk_psems(ctx, d, "atr" + pname)
                dve_pass0 = s_dve.n
                pe_pass0 = s_pe.n
                for sp in range(8):
                    def ftr(e, sp=sp, p_tr=p_tr):
                        if sp >= 2:
                            e.wait_ge(s_pe.h, pe_pass0 + 8 * (sp - 1))
                        xb = (sp % 2) * HC * 512
                        for hc in range(HC):
                            e.dma_start_transpose(
                                xfs[:, xb + hc * 512:xb + (hc + 1) * 512],
                                d["ag_xln_out"][sp * 512:(sp + 1) * 512,
                                                hc * 128:(hc + 1) * 128]) \
                                .then_inc(p_tr[sp % 2].h, 16)
                            p_tr[sp % 2].n += 16
                    ctx.sync(ftr)

                    for mc in range(4):
                        n_u = sp * 4 + mc
                        st_ = (n_u % 2) * 4
                        qb_, rq_ = st_, st_ + 2
                        qsl = slice((n_u % 2) * 512, (n_u % 2) * 512 + 512)
                        xb = (sp % 2) * HC * 512

                        def ft(e, sp=sp, mc=mc, qb_=qb_, wqa=wqa, p_tr=p_tr,
                               n_u=n_u, xb=xb):
                            e.wait_ge(p_tr[sp % 2].h, 128 * (sp // 2 + 1))
                            if n_u >= 2:
                                e.wait_ge(s_dve.h, dve_pass0 + 3 * (n_u - 1))
                            for hc in range(HC):
                                inst = e.matmul(
                                    pall[:, qb_ * 512:qb_ * 512 + 512],
                                    wqa[:, hc * 512 + mc * 128:hc * 512 + mc * 128 + 128],
                                    xfs[:, xb + hc * 512:xb + (hc + 1) * 512],
                                    start=(hc == 0), stop=(hc == HC - 1))
                            inst.then_inc(s_pe.h, 1)
                            s_pe.n += 1
                        ctx.tensor(ft)

                        def fv_cp(e, qb_=qb_, qsl=qsl):
                            e.wait_ge(s_pe.h, s_pe.n)
                            e.tensor_copy(qraw[:, qsl],
                                          pall[:, qb_ * 512:qb_ * 512 + 512]) \
                                .then_inc(s_dve.h, 1)
                            s_dve.n += 1
                        ctx.vector(fv_cp)
                        if sp == 0 and mc == 3 and pname == "aq" and \
                                "qraw0" in d["dbg"]:
                            def fdq(e):
                                e.wait_ge(s_dve.h, s_dve.n)
                                e.dma_start(out=d["dbg_out"]["qraw0"][:, :],
                                            in_=qraw[:, :]).then_inc(s_gld.h, 16)
                                s_gld.n += 16
                                e.wait_ge(s_gld.h, s_gld.n)
                            ctx.gpsimd(fdq)
                            def fdx(e):
                                e.dma_start(out=d["dbg_out"]["xfs0"][:, :],
                                            in_=xfs[:, :]).then_inc(s_gld.h, 16)
                                s_gld.n += 16
                                e.wait_ge(s_gld.h, s_gld.n)
                            ctx.gpsimd(fdx)

                        def ft_rot(e, rq_=rq_, qsl=qsl):
                            e.wait_ge(s_dve.h, s_dve.n)
                            e.matmul(pall[:, rq_ * 512:rq_ * 512 + 512],
                                     d["rot_b"][:, :], qraw[:, qsl],
                                     start=True, stop=True).then_inc(s_pe.h, 1)
                            s_pe.n += 1
                        ctx.tensor(ft_rot)

                        def fv_rope(e, sp=sp, mc=mc, rq_=rq_, dstf=dstf,
                                    qsl=qsl):
                            cs = cos_t[:, sp * 512:(sp + 1) * 512]
                            sn = sin_t[:, sp * 512:(sp + 1) * 512]
                            e.tensor_mul(tmpf[:, 0:512], qraw[:, qsl], cs)
                            e.wait_ge(s_pe.h, s_pe.n)
                            e.tensor_mul(tmpf[:, 512:1024],
                                         pall[:, rq_ * 512:rq_ * 512 + 512], sn) \
                                .then_inc(s_dve.h, 1)
                            s_dve.n += 1
                            e.wait_ge(s_dve.h, s_dve.n)
                            e.tensor_add(dstf[:, mc * S + sp * 512:mc * S + sp * 512 + 512],
                                         tmpf[:, 0:512], tmpf[:, 512:1024]) \
                                .then_inc(s_dve.h, 1)
                            s_dve.n += 1
                        ctx.vector(fv_rope)

        dummy_end_qk = None
    ctx.barrier()
    vt = sb("vt", [128, ST * 520], BF16, mid_scope)

    def fv_vt(e):
        e.memset(vt[:, :], 1.0).then_inc(s_dve.h, 1)
        s_dve.n += 1
    ctx.vector(fv_vt)

    with ExitStack() as qscope:
        xfs = av_fm[:, 0:2 * HC * 512]
        qraw = d["sqb"][:, 256:512].bitcast(BF16)
        nrm = d["sqb"][0:1, 0:64]
        nfin = d["sqb"][0:1, 64:72]
        with ExitStack() as wsc:
            wva = sb("wva", [128, HC * 512], BF16, wsc)
            wv_v = d["wv_in"].rearrange("(c p) m -> p c m", p=128)

            def fwl(e):
                e.wait_ge(s_pe.h, s_pe.n)
                e.wait_ge(s_dve.h, s_dve.n)
                e.dma_start(out=wva[:, :].rearrange("p (c m) -> p c m", c=HC),
                            in_=wv_v).then_inc(s_gld.h, 16)
                s_gld.n += 16
            ctx.gpsimd(fwl)
            ctx.barrier()

            p_tr2 = mk_psems(ctx, d, "atr2")
            pe_v0 = s_pe.n
            for sp in range(8):
                def ftr(e, sp=sp):
                    if sp >= 2:
                        e.wait_ge(s_pe.h, pe_v0 + 4 * (sp - 1))
                    xb = (sp % 2) * HC * 512
                    for hc in range(HC):
                        e.dma_start_transpose(
                            xfs[:, xb + hc * 512:xb + (hc + 1) * 512],
                            d["ag_xln_out"][sp * 512:(sp + 1) * 512,
                                            hc * 128:(hc + 1) * 128]) \
                            .then_inc(p_tr2[sp % 2].h, 16)
                        p_tr2[sp % 2].n += 16
                ctx.sync(ftr)

                for ti4 in range(4):
                    st_ = ((sp * 4 + ti4) % 2) * 4
                    xb = (sp % 2) * HC * 512
                    def ft(e, sp=sp, ti4=ti4, st_=st_, xb=xb):
                        e.wait_ge(p_tr2[sp % 2].h, 128 * (sp // 2 + 1))
                        if sp * 4 + ti4 >= 2:
                            e.wait_ge(s_dve.h, s_dve.n - 1)
                        for hc in range(HC):
                            inst = e.matmul(
                                pall[:, st_ * 512:st_ * 512 + 512],
                                xfs[:, xb + hc * 512 + ti4 * 128:xb + hc * 512 + ti4 * 128 + 128],
                                wva[:, hc * 512:(hc + 1) * 512],
                                start=(hc == 0), stop=(hc == HC - 1))
                        inst.then_inc(s_pe.h, 1)
                        s_pe.n += 1
                    ctx.tensor(ft)

                    def fv(e, sp=sp, ti4=ti4, st_=st_):
                        kb = sp * 4 + ti4
                        e.wait_ge(s_pe.h, s_pe.n)
                        e.tensor_copy(
                            vt[:, kb * 520:(kb + 1) * 520]
                            .rearrange("p (h c) -> p h c", h=8)[:, :, 0:64],
                            pall[:, st_ * 512:st_ * 512 + 512]
                            .rearrange("p (h c) -> p h c", h=8)) \
                            .then_inc(s_dve.h, 1)
                        s_dve.n += 1
                    ctx.vector(fv)

            def fvb(e):
                e.memset(battn[:, :], -32.0).then_inc(s_dve.h, 1)
                s_dve.n += 1
            ctx.vector(fvb)
    ctx.barrier()
    if d["dbg"] & {"qf", "kf", "vt", "battn"}:
        def fdbg(e):
            for nm, buf in (("qf", qf), ("kf", kf), ("vt", vt), ("battn", battn)):
                if nm in d["dbg"]:
                    e.dma_start(out=d["dbg_out"][nm][:, :], in_=buf[:, :]) \
                        .then_inc(s_gld.h, 16)
                    s_gld.n += 16
            e.wait_ge(s_gld.h, s_gld.n)
        ctx.gpsimd(fdbg)
        ctx.barrier()

    with ExitStack() as sscope:
        ebAB = sb("ebAB", [128, 3 * 1024], BF16, sscope)
        nrow = d["sqb"][32:33, 0:1]
        bcs = d["sqb"][64:128, 0:256].bitcast(BF16)

        uidx = 0
        evac_end = {}
        pend_bcast = []
        unit_meta = {}
        for hp in range(4):
            for qb in range(8):
                nkb = 4 * qb + 4
                pe0 = s_pe.n
                act0 = s_act.n
                dve_u0 = s_dve.n
                avb = 4 + 2 * (uidx % 2)

                pe_sc = {}
                pe_av = {}
                _c = pe0
                for kb in range(nkb):
                    _c += 1
                    pe_sc[kb] = _c
                    if kb >= 2:
                        _c += 1
                        pe_av[kb - 2] = _c
                _c += 1
                pe_av[nkb - 2] = _c
                _c += 1
                pe_av[nkb - 1] = _c

                def emit_av(e, kb, hp, nkb, act0, dve_u0, qb, avb):
                    e.wait_ge(s_act.h, act0 + kb + 1)
                    if kb >= 4 * qb:
                        e.wait_ge(s_dve.h, dve_u0 + (kb - 4 * qb + 1))
                    ebs = (kb % 3) * 1024
                    e.matmul(pall[:, avb * 512:avb * 512 + 512][0:65, :],
                             vt[:, kb * 520 + (2 * hp) * 65:kb * 520 + (2 * hp) * 65 + 65],
                             ebAB[:, ebs:ebs + 512],
                             start=(kb == 0), stop=(kb == nkb - 1))
                    e.matmul(pall[:, (avb + 1) * 512:(avb + 1) * 512 + 512][0:65, :],
                             vt[:, kb * 520 + (2 * hp + 1) * 65:kb * 520 + (2 * hp + 1) * 65 + 65],
                             ebAB[:, ebs + 512:ebs + 1024],
                             start=(kb == 0), stop=(kb == nkb - 1)) \
                        .then_inc(s_pe.h, 1)

                w_evac = evac_end.get(uidx - 2, 0)
                my_pend = list(pend_bcast)
                pend_bcast.clear()
                nb = len(my_pend)
                if nb:
                    unit_meta[uidx - 1] = unit_meta[uidx - 1][:3] + (s_pe.n + nb,)
                pe0 += nb
                for kk in list(pe_sc):
                    pe_sc[kk] += nb
                for kk in list(pe_av):
                    pe_av[kk] += nb

                def ft(e, hp=hp, qb=qb, nkb=nkb, pe0=pe0, act0=act0,
                       dve_u0=dve_u0, avb=avb, w_evac=w_evac, my_pend=my_pend):
                    for (pavb, hloc, wdve, nsl) in my_pend:
                        e.wait_ge(s_dve.h, wdve)
                        bank = pavb + hloc
                        e.matmul(pall[:, bank * 512:bank * 512 + 512][64:128, :],
                                 d["ones_row"][:, 0:64],
                                 d["sqb"][0:1, nsl], start=True, stop=True) \
                            .then_inc(s_pe.h, 1)
                        s_pe.n += 1
                    if w_evac:
                        e.wait_ge(s_dve.h, w_evac)
                    for kb in range(nkb):
                        scs = (kb % 2) * 2
                        if kb >= 2:
                            e.wait_ge(s_act.h, act0 + kb - 1)
                        e.matmul(pall[:, scs * 512:scs * 512 + 512],
                                 kf[0:64, hp * S + kb * 128:hp * S + kb * 128 + 128],
                                 qf[0:64, hp * S + qb * 512:hp * S + qb * 512 + 512],
                                 start=True, stop=True)
                        e.matmul(pall[:, (scs + 1) * 512:(scs + 1) * 512 + 512],
                                 kf[64:128, hp * S + kb * 128:hp * S + kb * 128 + 128],
                                 qf[64:128, hp * S + qb * 512:hp * S + qb * 512 + 512],
                                 start=True, stop=True).then_inc(s_pe.h, 1)
                        if kb >= 2:
                            emit_av(e, kb - 2, hp, nkb, act0, dve_u0, qb, avb)
                    emit_av(e, nkb - 2, hp, nkb, act0, dve_u0, qb, avb)
                    emit_av(e, nkb - 1, hp, nkb, act0, dve_u0, qb, avb)
                    s_pe.n += 2 * nkb
                ctx.tensor(ft)

                def fs(e, nkb=nkb, pe0=pe0, hp=hp, pe_sc=pe_sc, pe_av=pe_av):
                    for kb in range(nkb):
                        scs = (kb % 2) * 2
                        ebs = (kb % 3) * 1024
                        e.wait_ge(s_pe.h, pe_sc[kb])
                        if kb >= 3:
                            e.wait_ge(s_pe.h, pe_av[kb - 3])
                        e.activation(ebAB[:, ebs:ebs + 1024],
                                     pall[:, scs * 512:scs * 512 + 1024], AF.Exp,
                                     bias=battn[:, hp:hp + 1], scale=0.125) \
                            .then_inc(s_act.h, 1)
                    s_act.n += nkb
                ctx.scalar(fs)

                def fvm(e, qb=qb, nkb=nkb, act0=act0):
                    for kb in range(4 * qb, nkb):
                        off = (kb - 4 * qb) * 128
                        ebs = (kb % 3) * 1024
                        e.wait_ge(s_act.h, act0 + kb + 1)
                        for hof in (0, 512):
                            if off > 0:
                                e.memset(ebAB[:, ebs + hof:ebs + hof + off], 0.0)
                            e.tensor_mul(ebAB[:, ebs + hof + off:ebs + hof + off + 128],
                                         ebAB[:, ebs + hof + off:ebs + hof + off + 128],
                                         d["tri_b"][:, :])
                        e.tensor_scalar(nrow[:, 0:1], nrow[:, 0:1], 1.0, None,
                                        ALU.mult).then_inc(s_dve.h, 1)
                        s_dve.n += 1
                ctx.vector(fvm)

                pe_mm_end = s_pe.n
                for hloc in (0, 1):
                    nsl = slice(hloc * 512, hloc * 512 + 512)

                    def fve1(e, hloc=hloc, avb=avb, pe_mm_end=pe_mm_end,
                             nsl=nsl):
                        e.wait_ge(s_pe.h, pe_mm_end)
                        bank = avb + hloc
                        e.tensor_scalar(d["sqb"][0:1, nsl],
                                        pall[64:65, bank * 512:bank * 512 + 512],
                                        1e-30, None, ALU.max).then_inc(s_dve.h, 1)
                        s_dve.n += 1
                        e.wait_ge(s_dve.h, s_dve.n)
                        e.reciprocal(d["sqb"][0:1, nsl], d["sqb"][0:1, nsl]) \
                            .then_inc(s_dve.h, 1)
                        s_dve.n += 1
                        pend_bcast.append((avb, hloc, s_dve.n, nsl))
                    ctx.vector(fve1)

                if uidx >= 1:
                    phl, pqb, pavb, ppe = unit_meta[uidx - 1]
                    assert ppe is not None

                    def fve2(e, phl=phl, pqb=pqb, pavb=pavb, ppe=ppe):
                        e.wait_ge(s_pe.h, ppe)
                        for hloc in (0, 1):
                            bank = pavb + hloc
                            e.tensor_copy(bcs[0:64, :],
                                          pall[64:128, bank * 512:bank * 512 + 512]) \
                                .then_inc(s_dve.h, 1)
                            s_dve.n += 1
                            e.tensor_mul(
                                av_fm[hloc * 64:hloc * 64 + 64,
                                      phl * S + pqb * 512:phl * S + pqb * 512 + 512],
                                pall[0:64, bank * 512:bank * 512 + 512],
                                bcs[0:64, :]).then_inc(s_dve.h, 1)
                            s_dve.n += 1
                    ctx.vector(fve2)
                    evac_end[uidx - 1] = s_dve.n
                unit_meta[uidx] = (hp, qb, avb, None)
                uidx += 1
        my_pend = list(pend_bcast)
        pend_bcast.clear()
        lhl, lqb, lavb, _ = unit_meta[uidx - 1]
        bcast_done = s_pe.n + len(my_pend)

        def ft_tail(e, my_pend=my_pend):
            for (pavb, hloc, wdve, nsl) in my_pend:
                e.wait_ge(s_dve.h, wdve)
                bank = pavb + hloc
                e.matmul(pall[:, bank * 512:bank * 512 + 512][64:128, :],
                         d["ones_row"][:, 0:64],
                         d["sqb"][0:1, nsl], start=True, stop=True) \
                    .then_inc(s_pe.h, 1)
                s_pe.n += 1
        ctx.tensor(ft_tail)

        def fve2_tail(e, lhl=lhl, lqb=lqb, lavb=lavb, bcast_done=bcast_done):
            e.wait_ge(s_pe.h, bcast_done)
            for hloc in (0, 1):
                bank = lavb + hloc
                e.tensor_copy(bcs[0:64, :],
                              pall[64:128, bank * 512:bank * 512 + 512]) \
                    .then_inc(s_dve.h, 1)
                s_dve.n += 1
                e.tensor_mul(
                    av_fm[hloc * 64:hloc * 64 + 64,
                          lhl * S + lqb * 512:lhl * S + lqb * 512 + 512],
                    pall[0:64, bank * 512:bank * 512 + 512],
                    bcs[0:64, :]).then_inc(s_dve.h, 1)
                s_dve.n += 1
        ctx.vector(fve2_tail)
        evac_end[uidx - 1] = s_dve.n

        pairs = [[2 * i, 2 * i + 1] for i in range(NCORES // 2)]
        hp_done = {h: evac_end[h * 8 + 7] for h in range(4)}

        def exch_chain(e, h):
            e.wait_ge(s_dve.h, hp_done[h])
            e.dma_start(out=d[f"ag_avi{h}"][:, :],
                        in_=av_fm[:, h * S:(h + 1) * S]) \
                .then_inc(s_gld.h, 16)
            s_gld.n += 16
            e.wait_ge(s_gld.h, s_gld.n)
            e.collective_compute(
                "AllGather", ALU.bypass, replica_groups=pairs,
                ins=[d[f"ag_avi{h}"].ap().opt()],
                outs=[d[f"ag_avo{h}"].ap().opt()],
            ).then_inc(s_cc.h, 1)
            s_cc.n += 1
            e.wait_ge(s_cc.h, s_cc.n)
            with e.register(f"rk{h}") as rk:
                e.cc_rank_ld(rk, replica_groups=pairs)
                with e.register(f"ofs{h}") as ofs:
                    e.reg_mov(ofs, rk)
                    e.reg_mul(ofs, ofs, TOK)
                    e.dma_start(
                        out=av_fm[:, h * S:(h + 1) * S]
                        .rearrange("p (c n) -> p c n", c=2),
                        in_=bass.AP(d[f"ag_avo{h}"], ofs,
                                    [[S, 128], [128 * S, 2], [1, TOK]])) \
                        .then_inc(s_gld.h, 16)
                    s_gld.n += 16

        def fxa(e):
            for h in range(3):
                exch_chain(e, h)
        ctx.gpsimd(fxa)
        act_end = s_act.n
        pe_tail = s_pe.n

    mid_scope.close()
    with ExitStack() as oscope:
        wob = sb("wob", [128, HC * DM], BF16, oscope)
        wo_v = d["wo_in"].rearrange("(c p) m -> p c m", p=128)

        def fol(e):
            e.wait_ge(s_pe.h, pe_tail)
            e.dma_start(out=wob[:, :].rearrange("p (c m) -> p c m", c=HC),
                        in_=wo_v).then_inc(s_ld.h, 16)
            s_ld.n += 16
            exch_chain(e, 3)
        ctx.gpsimd(fol)
        wob_ld = s_ld.n
        gld_rb = s_gld.n

        pe0 = s_pe.n
        dve0 = s_dve.n

        def ft(e):
            e.wait_ge(s_act.h, act_end)
            e.wait_ge(s_gld.h, gld_rb)
            e.wait_ge(s_ld.h, wob_ld)
            for t in range(TT):
                st_ = (t % 2) * 2
                if t >= 2:
                    e.wait_ge(s_dve.h, dve0 + t - 1)
                for half in range(2):
                    for dc in range(HC):
                        col = (dc % 4) * S + (dc // 4) * TOK + t * 128
                        inst = e.matmul(
                            pall[:, (st_ + half) * 512:(st_ + half) * 512 + 512],
                            av_fm[:, col:col + 128],
                            wob[:, dc * DM + half * 512:dc * DM + half * 512 + 512],
                            start=(dc == 0), stop=(dc == HC - 1))
                inst.then_inc(s_pe.h, 1)
            s_pe.n += TT
        ctx.tensor(ft)

        def fv(e):
            for t in range(TT):
                st_ = (t % 2) * 2
                e.wait_ge(s_pe.h, pe0 + t + 1)
                e.tensor_add(d["xr"][:, t * DM:(t + 1) * DM],
                             pall[:, st_ * 512:st_ * 512 + 1024],
                             d["xr"][:, t * DM:(t + 1) * DM]).then_inc(s_dve.h, 1)
                s_dve.n += 1
                d.setdefault("xr_dve", {})[t] = s_dve.n
        ctx.vector(fv)
    core_scope.close()
    ctx.barrier()


def emit_output(ctx, d):
    s_st, s_dve = d["s_st"], d["s_dve"]
    xr_dve = dict(d.get("xr_dve", {}))
    out_t = d["out"].rearrange("(t p) m -> t p m", p=128)

    def fst(e):
        for t in range(TT):
            if t in xr_dve:
                e.wait_ge(s_dve.h, xr_dve[t])
            e.dma_start(out=out_t[t], in_=d["xr"][:, t * DM:(t + 1) * DM]) \
                .then_inc(s_st.h, 16)
        s_st.n += 16 * TT
        e.wait_ge(s_st.h, s_st.n)
    ctx.sync(fst)



_BUILD_KW = {}


def kernel(L0_attn_x, L0_attn_Wq, L0_attn_Wk, L0_attn_Wv, L0_attn_Wo,
           L0_ffn_clip_x, L0_ffn_clip_down, L0_ffn_w_gate, L0_ffn_w_up,
           L0_ffn_w_down):
    x = np.ascontiguousarray(np.asarray(L0_attn_x, dtype=np.float32))
    wq = np.ascontiguousarray(np.asarray(L0_attn_Wq, dtype=np.float32))
    wk = np.ascontiguousarray(np.asarray(L0_attn_Wk, dtype=np.float32))
    wv = np.ascontiguousarray(np.asarray(L0_attn_Wv, dtype=np.float32))
    wo = np.ascontiguousarray(np.asarray(L0_attn_Wo, dtype=np.float32))
    wg = np.ascontiguousarray(np.asarray(L0_ffn_w_gate, dtype=np.float32))
    wu = np.ascontiguousarray(np.asarray(L0_ffn_w_up, dtype=np.float32))
    wd = np.ascontiguousarray(np.asarray(L0_ffn_w_down, dtype=np.float32))
    clip = np.array([[np.asarray(L0_ffn_clip_x, np.float32).ravel()[0],
                      np.asarray(L0_ffn_clip_down, np.float32).ravel()[0]]],
                    dtype=np.float32)

    nc, dbg_out = build_program(**_BUILD_KW)
    in_maps = []
    for c in range(NCORES):
        b, h2 = c // 2, c % 2
        in_maps.append({
            "x": np.ascontiguousarray(x[b, h2 * TOK:(h2 + 1) * TOK, :]),
            "wq": np.ascontiguousarray(wq[:, h2 * 512:(h2 + 1) * 512]),
            "wk": np.ascontiguousarray(wk[:, h2 * 512:(h2 + 1) * 512]),
            "wv": np.ascontiguousarray(wv[:, h2 * 512:(h2 + 1) * 512]),
            "wo": wo, "wg": wg, "wu": wu, "wd": wd, "clip": clip,
        })
    res = run_bass_kernel_spmd(nc, in_maps, core_ids=list(range(NCORES)))
    out = np.empty((4, S, DM), np.float32)
    for c in range(NCORES):
        b, h2 = c // 2, c % 2
        out[b, h2 * TOK:(h2 + 1) * TOK, :] = res.results[c]["out"]
    if _BUILD_KW.get("debug"):
        kernel.last_results = res
    return out


# version 18
# speedup vs baseline: 1.0653x; 1.0170x over previous
import numpy as np
from contextlib import ExitStack

import concourse.bass as bass
import concourse.mybir as mybir
from concourse.bass_utils import run_bass_kernel_spmd

F32 = mybir.dt.float32
BF16 = mybir.dt.bfloat16
F16 = mybir.dt.float16
AF = mybir.ActivationFunctionType
ALU = mybir.AluOpType
AX = mybir.AxisListType

NCORES = 8
TOK = 2048
TT = TOK // 128
DM = 1024
HC = DM // 128
DI = 2816
IC = DI // 128
S = 4096
ST = S // 128
QMAX = 127.0
RC = 1.5 * 2.0 ** 23
EPS = 1e-5
DK = 64

ENGS = ("tensor", "vector", "scalar", "gpsimd", "sync")


class Sem:
    def __init__(self, es, nc, name):
        self.h = es.enter_context(nc.semaphore(name))
        self.n = 0


class InstRec:

    def __init__(self, inst):
        self.i = inst
        self.updated = False

    def then_inc(self, sem, k=None):
        self.updated = True
        if k is None:
            return self.i.then_inc(sem)
        return self.i.then_inc(sem, k)


class EngRec:
    def __init__(self, eng, store, key):
        object.__setattr__(self, "_eng", eng)
        object.__setattr__(self, "_store", store)
        object.__setattr__(self, "_key", key)

    def __getattr__(self, name):
        attr = getattr(self._eng, name)
        if not callable(attr):
            return attr

        def wrap(*a, **kw):
            r = attr(*a, **kw)
            if isinstance(r, bass.BassInstruction):
                r = InstRec(r)
                self._store[self._key] = r
            return r
        return wrap


class Ctx:
    def __init__(self, nc, block, es):
        self.nc = nc
        self.block = block
        self.es = es
        self.last = {}
        self.sems = []
        self.s_bar = Sem(es, nc, "bar")

    def sec(self, eng_name, fn):
        def g(eng):
            fn(EngRec(eng, self.last, eng_name))
        getattr(self.block, eng_name)(g)

    def tensor(self, fn): self.sec("tensor", fn)
    def vector(self, fn): self.sec("vector", fn)
    def scalar(self, fn): self.sec("scalar", fn)
    def gpsimd(self, fn): self.sec("gpsimd", fn)
    def sync(self, fn): self.sec("sync", fn)

    def barrier(self):
        target = self.s_bar.n
        for e in ENGS:
            li = self.last.pop(e, None)
            if li is not None and not li.updated:
                try:
                    li.i.then_inc(self.s_bar.h, 1)
                    target += 1
                except AssertionError:
                    li.i.then_inc(self.s_bar.h, 16)
                    target += 16
        waits = [(s.h, s.n) for s in self.sems if s.n > 0]
        if target > self.s_bar.n:
            waits.append((self.s_bar.h, target))

        for e in ENGS:
            def f(eng, ws=tuple(waits)):
                for h, n in ws:
                    eng.wait_ge(h, n)
            self.sec(e, f)
        self.s_bar.n = target
        self.last = {}



def build_program(upto="full", debug=()):
    nc = bass.Bass("TRN2", target_bir_lowering=False)
    dbg = set(debug)
    d = {}
    d["upto"] = upto

    for nm, shape, dt in [
        ("x", [TOK, DM], F32), ("wq", [DM, 512], F32), ("wk", [DM, 512], F32),
        ("wv", [DM, 512], F32), ("wo", [DM, DM], F32), ("wg", [DI, DM], F32),
        ("wu", [DI, DM], F32), ("wd", [DM, DI], F32), ("clip", [1, 2], F32),
    ]:
        d[nm + "_in"] = nc.dram_tensor(nm, shape, dt, kind="ExternalInput").ap()
    d["out"] = nc.dram_tensor("out", [TOK, DM], F32, kind="ExternalOutput").ap()

    d["dbg"] = dbg
    d["dbg_out"] = {}

    def mkdbg(name, shape, dt=F32):
        if name in dbg:
            d["dbg_out"][name] = nc.dram_tensor(
                "dbg_" + name, shape, dt, kind="ExternalOutput").ap()
    d["mkdbg"] = mkdbg
    mkdbg("xq", [TOK, DM], BF16)
    mkdbg("h", [DI, TOK], F16)
    mkdbg("hq", [DI, TOK], BF16)
    mkdbg("gmax", [1, 4], F32)
    mkdbg("wgq", [DI, DM], BF16)
    mkdbg("qf", [128, 4 * S], BF16)
    mkdbg("kf", [128, 4 * S], BF16)
    mkdbg("vt", [128, (S // 128) * 520], BF16)
    mkdbg("avf", [128, 4 * S], BF16)
    mkdbg("xln", [S, DM], BF16)
    mkdbg("battn", [128, 4], F32)
    mkdbg("cos", [128, S], BF16)
    mkdbg("xfs0", [128, HC * 512], BF16)
    mkdbg("rotb", [128, 128], BF16)
    mkdbg("avall", [128, 8 * TOK], BF16)
    mkdbg("qraw0", [128, 512], BF16)
    mkdbg("sin", [128, S], BF16)
    mkdbg("ivf", [128, 1], F32)

    d["wgq_nat"] = nc.dram_tensor("wgq_nat", [DI, DM], BF16)
    d["wuq_nat"] = nc.dram_tensor("wuq_nat", [DI, DM], BF16)
    d["wdq_nat"] = nc.dram_tensor("wdq_nat", [DM, DI], BF16)
    d["wgq_t"] = nc.dram_tensor("wgq_t", [DM, DI], BF16)
    d["wuq_t"] = nc.dram_tensor("wuq_t", [DM, DI], BF16)
    d["wdq_t"] = nc.dram_tensor("wdq_t", [DI, DM], BF16)
    d["xq_nat"] = nc.dram_tensor("xq_nat", [TOK, DM], BF16)
    d["h_nat"] = nc.dram_tensor("h_nat", [DI, TOK], F16)
    d["hq_nat"] = nc.dram_tensor("hq_nat", [DI, TOK], BF16)
    d["mx_in"] = nc.dram_tensor("mx_in", [1, 16], F32)
    d["mx_out"] = nc.dram_tensor("mx_out", [1, 16], F32)
    d["ag_xln_in"] = nc.dram_tensor("ag_xln_in", [TOK, DM], BF16)
    d["ag_xln_out"] = nc.dram_tensor("ag_xln_out", [S, DM], BF16)
    for _h in range(4):
        d[f"ag_avi{_h}"] = nc.dram_tensor(f"ag_avi{_h}", [128, S], BF16)
        d[f"ag_avo{_h}"] = nc.dram_tensor(f"ag_avo{_h}", [256, S], BF16)

    es = ExitStack()
    d["es_root"] = es

    _cnt = [0]

    def sb(name, shape, dt, scope=None):
        _cnt[0] += 1
        return (scope or es).enter_context(
            nc.sbuf_tensor(f"{name}_{_cnt[0]}", shape, dt))

    def psum(name, shape, dt, scope=None):
        return (scope or es).enter_context(nc.psum_tensor(name, shape, dt))
    d["sb"] = sb
    d["psum"] = psum

    d["xr"] = sb("xr", [128, TT * DM], F32)
    d["iden_f"] = sb("iden_f", [128, 128], F32)
    d["tri_b"] = sb("tri_b", [128, 128], BF16)
    d["rot_b"] = sb("rot_b", [128, 128], BF16)
    d["ones_row"] = sb("ones_row", [1, 128], F32)
    d["iota_p"] = sb("iota_p", [128, 1], F32)
    d["sclip"] = sb("sclip", [1, 2], F32)
    d["ssig"] = sb("ssig", [1, 2], F32)
    d["s_tmp"] = sb("s_tmp", [1, 8], F32)
    d["gmax"] = sb("gmax", [1, 4], F32)
    d["binv"] = sb("binv", [128, 8], F32)
    d["hcol"] = sb("hcol", [128, 2], F32)
    d["lmax_col"] = sb("lmax_col", [128, 1], F32)
    for nm in ("st_m", "st_r", "st_a", "st_b", "st_c", "st_d"):
        d[nm] = sb(nm, [128, TT], F32)
    d["wsc_g"] = sb("wsc_g", [128, IC], F32)
    d["wsc_u"] = sb("wsc_u", [128, IC], F32)
    d["wsc_d"] = sb("wsc_d", [128, HC], F32)
    d["sqb"] = sb("sqb", [128, DM], F32)

    d["pall"] = psum("pall", [128, 8 * 512], F32)

    with nc.Block() as block:
        ctx = Ctx(nc, block, es)
        for s in ("s_ld", "s_sc", "s_gld", "s_st", "s_pe", "s_dve",
                  "s_act", "s_gp", "s_cc"):
            d[s] = Sem(es, nc, s)
            ctx.sems.append(d[s])
        emit_all(ctx, d)
    es.close()
    return nc, d["dbg_out"]


def mk_psems(ctx, d, name):
    import itertools
    es = ctx.es
    sems = [Sem(es, ctx.nc, f"{name}{i}") for i in range(2)]
    for s in sems:
        ctx.sems.append(s)
    return sems


def emit_all(ctx, d):
    upto = d["upto"]
    with ExitStack() as pscope:
        d["iota_j"] = d["sb"]("iota_j", [128, 128], F32, pscope)
        d["scr1"] = d["sb"]("scr1", [128, 128], F32, pscope)
        d["scr2"] = d["sb"]("scr2", [128, 128], F32, pscope)
        emit_prologue(ctx, d)
        ctx.barrier()
    with ExitStack() as wscope:
        emit_weight_quant(ctx, d, wscope)
    emit_ln_stats(ctx, d, want_max=True)
    ctx.barrier()
    emit_maxreduce(ctx, d, slot=0)
    emit_scale_chain(ctx, d, slot=0, sig_idx=0)
    ctx.barrier()
    with ExitStack() as fscope:
        emit_ffn(ctx, d, fscope, app=0)
    if upto == "ffn1":
        emit_output(ctx, d)
        return
    with ExitStack() as ascope:
        emit_attention(ctx, d, ascope)
    if upto == "attn":
        emit_output(ctx, d)
        return
    emit_ln_stats(ctx, d, want_max=True)
    ctx.barrier()
    emit_maxreduce(ctx, d, slot=2)
    emit_scale_chain(ctx, d, slot=2, sig_idx=0)
    ctx.barrier()
    with ExitStack() as fscope:
        emit_ffn(ctx, d, fscope, app=1)
    emit_output(ctx, d)


def emit_ffn(ctx, d, scope, app):
    sb = d["sb"]
    nc = ctx.nc
    d["hmax_cols"] = sb("hmax_cols", [128, IC * 4], F32, scope)
    d["qstage"] = sb("qstage", [128, 2 * DM], F32, scope)
    d["qout"] = sb("qout", [128, 2 * DM], BF16, scope)
    d["srs"] = sb("srs", [128, TT], F32, scope)
    d["bsd"] = sb("bsd", [128, DM], F32, scope)
    d["brow"] = sb("brow", [1, 128], F32, scope)
    with ExitStack() as gscope:
        d["xq_fm"] = sb("xq_fm", [128, HC * TOK], BF16, gscope)
        emit_xq(ctx, d, slot=2 * app)
        ctx.barrier()
        emit_gateup(ctx, d, gscope, slot=2 * app)
        ctx.barrier()
    emit_hmax_finish(ctx, d)
    emit_maxreduce(ctx, d, slot=2 * app + 1)
    emit_scale_chain(ctx, d, slot=2 * app + 1, sig_idx=1)
    emit_bsd(ctx, d)
    ctx.barrier()
    with ExitStack() as dscope:
        emit_down(ctx, d, dscope, app=app)
    ctx.barrier()
    for k in ("xq_fm", "hmax_cols", "qstage", "qout", "srs", "wgb", "wub", "sgt", "upt",
              "hft", "hbt", "svg", "svu", "hst", "hqo", "hfs", "hqb", "wdb",
              "wdr", "hstage", "dst1", "bsd", "brow"):
        d.pop(k, None)



def emit_prologue(ctx, d):
    s_ld, s_gp, s_dve, s_act = d["s_ld"], d["s_gp"], d["s_dve"], d["s_act"]
    x_t = d["x_in"].rearrange("(t p) m -> t p m", p=128)
    xr = d["xr"]

    def fsync(e):
        for t in range(TT):
            e.dma_start(out=xr[:, t * DM:(t + 1) * DM], in_=x_t[t]).then_inc(s_ld.h, 16)
        e.dma_start(out=d["sclip"][:, :], in_=d["clip_in"][:, :]).then_inc(s_ld.h, 16)
        s_ld.n += 16 * (TT + 1)
    ctx.sync(fsync)

    def fgp(e):
        e.iota(d["iota_j"][:, :], [[1, 128]], channel_multiplier=0,
               allow_small_or_imprecise_dtypes=True)
        e.iota(d["iota_p"][:, :], [[1, 1]], channel_multiplier=1,
               allow_small_or_imprecise_dtypes=True)
        e.memset(d["ones_row"][:, :], 1.0).then_inc(s_gp.h, 1)
        s_gp.n += 1
    ctx.gpsimd(fgp)

    def fv(e):
        e.wait_ge(s_gp.h, s_gp.n)
        jt, pt = d["iota_j"], d["iota_p"]
        e.tensor_scalar(d["iden_f"][:, :], jt[:, :], pt[:, :], None, ALU.is_equal)
        e.tensor_scalar(d["tri_b"][:, :], jt[:, :], pt[:, :], 0.0,
                        ALU.subtract, ALU.is_ge)
        pcol = d["lmax_col"]
        e.tensor_scalar(pcol[:, :], pt[:, :], 0.5, -0.25, ALU.mult, ALU.add) \
            .then_inc(s_dve.h, 1)
        s_dve.n += 1
        e.wait_ge(s_dve.h, s_dve.n)
        e.tensor_scalar(pcol[:, :], pcol[:, :], RC, RC, ALU.add, ALU.subtract) \
            .then_inc(s_dve.h, 1)
        s_dve.n += 1
        e.wait_ge(s_dve.h, s_dve.n)
        e.scalar_tensor_tensor(pcol[:, :], pcol[:, :], -2.0, pt[:, :],
                               ALU.mult, ALU.add).then_inc(s_dve.h, 1)
        s_dve.n += 1
        e.wait_ge(s_dve.h, s_dve.n)
        scr1, scr2 = d["scr1"], d["scr2"]
        e.tensor_scalar(scr2[:, :], jt[:, :], pt[:, :], 1.0, ALU.subtract,
                        ALU.is_equal).then_inc(s_dve.h, 1)
        s_dve.n += 1
        e.wait_ge(s_dve.h, s_dve.n)
        e.tensor_scalar(scr1[:, :], scr2[:, :], pcol[:, :], None, ALU.mult) \
            .then_inc(s_dve.h, 1)
        s_dve.n += 1
        e.wait_ge(s_dve.h, s_dve.n)
        e.tensor_sub(scr2[:, :], scr2[:, :], scr1[:, :]).then_inc(s_dve.h, 1)
        s_dve.n += 1
        e.wait_ge(s_dve.h, s_dve.n)
        e.tensor_scalar(scr1[:, :], jt[:, :], pt[:, :], -1.0, ALU.subtract,
                        ALU.is_equal).then_inc(s_dve.h, 1)
        s_dve.n += 1
        e.wait_ge(s_dve.h, s_dve.n)
        e.tensor_scalar(scr1[:, :], scr1[:, :], pcol[:, :], None, ALU.mult) \
            .then_inc(s_dve.h, 1)
        s_dve.n += 1
        e.wait_ge(s_dve.h, s_dve.n)
        e.tensor_sub(d["rot_b"][:, :], scr2[:, :], scr1[:, :]).then_inc(s_dve.h, 1)
        s_dve.n += 1
    ctx.vector(fv)

    def fs(e):
        e.wait_ge(s_ld.h, s_ld.n)
        e.activation(d["ssig"][:, :], d["sclip"][:, :], AF.Sigmoid) \
            .then_inc(s_act.h, 1)
        s_act.n += 1
    ctx.scalar(fs)


def emit_weight_quant(ctx, d, scope):
    sb = d["sb"]
    s_ld, s_sc, s_dve, s_act, s_st = d["s_ld"], d["s_sc"], d["s_dve"], d["s_act"], d["s_st"]

    wst = sb("wst", [128, 2 * DI], F32, scope)
    wsq = sb("wsq", [128, 2 * DI], F32, scope)
    wqb = sb("wqb", [128, 2 * DI], BF16, scope)
    rcol = sb("rcol", [128, 4], F32, scope)
    tb = sb("tb", [128, 2 * DI], BF16, scope)

    jobs = []
    for (src, dst, wsc, nch, W) in [
        (d["wg_in"], d["wgq_nat"], d["wsc_g"], IC, DM),
        (d["wu_in"], d["wuq_nat"], d["wsc_u"], IC, DM),
        (d["wd_in"], d["wdq_nat"], d["wsc_d"], HC, DI),
    ]:
        for k in range(nch):
            jobs.append((src, dst, wsc, k, W))
    nj = len(jobs)
    dve0, act0, st0 = s_dve.n, s_act.n, s_st.n
    p_ld = mk_psems(ctx, d, "wql")
    p_st = mk_psems(ctx, d, "wqs")

    def fsync(e):
        for j, (src, dst, wsc, k, W) in enumerate(jobs):
            if j >= 2:
                e.wait_ge(s_act.h, act0 + j - 1)
            srcv = src.rearrange("(c p) m -> c p m", p=128)
            e.dma_start(out=wst[:, (j % 2) * DI:(j % 2) * DI + W],
                        in_=srcv[k]).then_inc(p_ld[j % 2].h, 16)
            p_ld[j % 2].n += 16
    ctx.sync(fsync)

    def fv(e):
        for j, (src, dst, wsc, k, W) in enumerate(jobs):
            sl = slice((j % 2) * DI, (j % 2) * DI + W)
            e.wait_ge(p_ld[j % 2].h, 16 * (j // 2 + 1))
            e.reduce_max(wsc[:, k:k + 1], wst[:, sl], AX.X,
                         apply_absolute_value=True).then_inc(s_dve.h, 1)
            e.wait_ge(s_dve.h, dve0 + 5 * j + 1)
            e.reciprocal(rcol[:, 2 * (j % 2):2 * (j % 2) + 1], wsc[:, k:k + 1]) \
                .then_inc(s_dve.h, 1)
            e.wait_ge(s_dve.h, dve0 + 5 * j + 2)
            e.tensor_scalar(rcol[:, 2 * (j % 2) + 1:2 * (j % 2) + 2],
                            rcol[:, 2 * (j % 2):2 * (j % 2) + 1], QMAX, None,
                            ALU.mult).then_inc(s_dve.h, 1)
            e.wait_ge(s_act.h, act0 + j + 1)
            if j >= 2:
                e.wait_ge(p_st[j % 2].h, 16 * (j // 2))
            e.tensor_scalar(wsq[:, sl], wsq[:, sl], QMAX, -QMAX - 1.0,
                            ALU.min, ALU.max).then_inc(s_dve.h, 1)
            e.wait_ge(s_dve.h, dve0 + 5 * j + 4)
            e.tensor_scalar(wqb[:, sl], wsq[:, sl], RC, RC, ALU.add,
                            ALU.subtract).then_inc(s_dve.h, 1)
        s_dve.n += 5 * nj
    ctx.vector(fv)

    def fs(e):
        for j, (src, dst, wsc, k, W) in enumerate(jobs):
            sl = slice((j % 2) * DI, (j % 2) * DI + W)
            e.wait_ge(s_dve.h, dve0 + 5 * j + 3)
            e.activation(wsq[:, sl], wst[:, sl], AF.Copy,
                         scale=rcol[:, 2 * (j % 2) + 1:2 * (j % 2) + 2]) \
                .then_inc(s_act.h, 1)
        s_act.n += nj
    ctx.scalar(fs)

    def fst(e):
        for j, (src, dst, wsc, k, W) in enumerate(jobs):
            sl = slice((j % 2) * DI, (j % 2) * DI + W)
            e.wait_ge(s_dve.h, dve0 + 5 * (j + 1))
            dstv = dst.rearrange("(c p) m -> c p m", p=128)
            e.dma_start(out=dstv[k], in_=wqb[:, sl]).then_inc(p_st[j % 2].h, 16)
            p_st[j % 2].n += 16
    ctx.gpsimd(fst)

    ctx.barrier()

    p_tr = mk_psems(ctx, d, "bnt")
    p_bw = mk_psems(ctx, d, "bnc")
    bounce_jobs = []
    for (natt, tt_, nch, W) in [
        (d["wgq_nat"], d["wgq_t"], HC, DI),
        (d["wuq_nat"], d["wuq_t"], HC, DI),
        (d["wdq_nat"], d["wdq_t"], IC, DM),
    ]:
        for k in range(nch):
            bounce_jobs.append((natt, tt_, k, W))

    def fb(e):
        for n, (natt, tt_, k, W) in enumerate(bounce_jobs):
            buf = tb[:, (n % 2) * DI:(n % 2) * DI + W]
            if n >= 2:
                e.wait_ge(p_bw[n % 2].h, 16 * (n // 2))
            e.dma_start_transpose(buf, natt[:, k * 128:(k + 1) * 128]) \
                .then_inc(p_tr[n % 2].h, 16)
            p_tr[n % 2].n += 16
    ctx.sync(fb)

    def fbw(e):
        for n, (natt, tt_, k, W) in enumerate(bounce_jobs):
            buf = tb[:, (n % 2) * DI:(n % 2) * DI + W]
            e.wait_ge(p_tr[n % 2].h, 16 * (n // 2 + 1))
            e.dma_start(out=tt_[k * 128:(k + 1) * 128, :], in_=buf) \
                .then_inc(p_bw[n % 2].h, 16)
            p_bw[n % 2].n += 16
        e.wait_ge(p_bw[0].h, p_bw[0].n)
        e.wait_ge(p_bw[1].h, p_bw[1].n)
    ctx.gpsimd(fbw)

    if "wgq" in d["dbg"]:
        s_gld = d["s_gld"]

        def fdbg(e):
            e.wait_ge(s_st.h, s_st.n)
            e.dma_start(out=d["dbg_out"]["wgq"][:, :], in_=d["wgq_nat"][:, :]) \
                .then_inc(s_gld.h, 16)
            s_gld.n += 16
            e.wait_ge(s_gld.h, s_gld.n)
        ctx.gpsimd(fdbg)
    ctx.barrier()


def emit_ln_stats(ctx, d, want_max):
    s_ld, s_dve, s_act = d["s_ld"], d["s_dve"], d["s_act"]
    xr = d["xr"]
    sqb = d["sqb"]
    act0 = s_act.n
    dve0 = s_dve.n

    xr_dve = dict(d.get("xr_dve", {}))

    def fs(e):
        for t in range(TT):
            if t in xr_dve:
                e.wait_ge(s_dve.h, xr_dve[t])
            if t >= 1:
                e.wait_ge(s_dve.h, dve0 + t)
            sl = slice(0, DM)
            e.activation(sqb[:, sl], xr[:, t * DM:(t + 1) * DM], AF.Square) \
                .then_inc(s_act.h, 1)
        s_act.n += TT
    ctx.scalar(fs)

    def fv(e):
        for t in range(TT):
            xt = xr[:, t * DM:(t + 1) * DM]
            sl = slice(0, DM)
            e.reduce_sum(d["st_a"][:, t:t + 1], xt, AX.X)
            if want_max:
                e.reduce_max(d["st_c"][:, t:t + 1], xt, AX.X)
                e.tensor_reduce(d["st_d"][:, t:t + 1], xt, AX.X, ALU.min)
            e.wait_ge(s_act.h, act0 + t + 1)
            e.reduce_sum(d["st_b"][:, t:t + 1], sqb[:, sl], AX.X) \
                .then_inc(s_dve.h, 1)
        s_dve.n += TT
        e.wait_ge(s_dve.h, s_dve.n)
        e.tensor_scalar(d["st_m"][:, :], d["st_a"][:, :], 1.0 / DM, None,
                        ALU.mult).then_inc(s_dve.h, 1)
        s_dve.n += 1
        e.tensor_scalar(d["st_b"][:, :], d["st_b"][:, :], 1.0 / DM, None,
                        ALU.mult).then_inc(s_dve.h, 1)
        s_dve.n += 1
        e.wait_ge(s_dve.h, s_dve.n)
        e.tensor_mul(d["st_a"][:, :], d["st_m"][:, :], d["st_m"][:, :]) \
            .then_inc(s_dve.h, 1)
        s_dve.n += 1
        e.wait_ge(s_dve.h, s_dve.n)
        e.scalar_tensor_tensor(d["st_b"][:, :], d["st_b"][:, :], EPS,
                               d["st_a"][:, :], ALU.add, ALU.subtract) \
            .then_inc(s_dve.h, 1)
        s_dve.n += 1
    ctx.vector(fv)

    def fs2(e):
        e.wait_ge(s_dve.h, s_dve.n)
        e.activation(d["st_r"][:, :], d["st_b"][:, :], AF.Sqrt).then_inc(s_act.h, 1)
        s_act.n += 1
    ctx.scalar(fs2)

    def fv2(e):
        e.wait_ge(s_act.h, s_act.n)
        e.reciprocal(d["st_r"][:, :], d["st_r"][:, :]).then_inc(s_dve.h, 1)
        s_dve.n += 1
        if want_max:
            e.wait_ge(s_dve.h, s_dve.n)
            e.tensor_sub(d["st_c"][:, :], d["st_c"][:, :], d["st_m"][:, :]) \
                .then_inc(s_dve.h, 1)
            s_dve.n += 1
            e.tensor_sub(d["st_d"][:, :], d["st_m"][:, :], d["st_d"][:, :]) \
                .then_inc(s_dve.h, 1)
            s_dve.n += 1
            e.wait_ge(s_dve.h, s_dve.n)
            e.tensor_max(d["st_c"][:, :], d["st_c"][:, :], d["st_d"][:, :]) \
                .then_inc(s_dve.h, 1)
            s_dve.n += 1
            e.wait_ge(s_dve.h, s_dve.n)
            e.tensor_mul(d["st_c"][:, :], d["st_c"][:, :], d["st_r"][:, :]) \
                .then_inc(s_dve.h, 1)
            s_dve.n += 1
            e.wait_ge(s_dve.h, s_dve.n)
            e.reduce_max(d["lmax_col"][:, :], d["st_c"][:, :], AX.X) \
                .then_inc(s_dve.h, 1)
            s_dve.n += 1
    ctx.vector(fv2)


def emit_maxreduce(ctx, d, slot):
    s_pe, s_dve, s_gld, s_cc, s_gp = d["s_pe"], d["s_dve"], d["s_gld"], d["s_cc"], d["s_gp"]
    pall = d["pall"]

    def ft(e):
        e.wait_ge(s_dve.h, s_dve.n)
        e.transpose(pall[0:1, 7 * 512:7 * 512 + 128], d["lmax_col"][:, :],
                    d["iden_f"][:, :]).then_inc(s_pe.h, 1)
        s_pe.n += 1
    ctx.tensor(ft)

    def fv(e):
        e.wait_ge(s_pe.h, s_pe.n)
        e.reduce_max(d["s_tmp"][:, 0:1], pall[0:1, 7 * 512:7 * 512 + 128], AX.X) \
            .then_inc(s_dve.h, 1)
        s_dve.n += 1
    ctx.vector(fv)

    def fg(e):
        e.memset(d["s_tmp"][:, 1:4], 0.0).then_inc(s_gp.h, 1)
        s_gp.n += 1
        e.wait_ge(s_gp.h, s_gp.n)
        e.wait_ge(s_dve.h, s_dve.n)
        e.dma_start(out=d["mx_in"][:, 0:4], in_=d["s_tmp"][:, 0:4]) \
            .then_inc(s_gld.h, 16)
        s_gld.n += 16
        e.wait_ge(s_gld.h, s_gld.n)
        e.collective_compute(
            "AllReduce", ALU.max, replica_groups=[list(range(NCORES))],
            ins=[d["mx_in"].ap()], outs=[d["mx_out"].ap()],
        ).then_inc(s_cc.h, 1)
        s_cc.n += 1
        e.wait_ge(s_cc.h, s_cc.n)
        e.dma_start(out=d["gmax"][:, slot:slot + 1], in_=d["mx_out"][:, 0:1]) \
            .then_inc(s_gld.h, 16)
        s_gld.n += 16
        e.wait_ge(s_gld.h, s_gld.n)
    ctx.gpsimd(fg)


def emit_scale_chain(ctx, d, slot, sig_idx):
    s_dve, s_pe, s_gld = d["s_dve"], d["s_pe"], d["s_gld"]
    pall = d["pall"]

    def fv(e):
        e.wait_ge(s_gld.h, s_gld.n)
        e.tensor_scalar(d["s_tmp"][:, 4:5], d["gmax"][:, slot:slot + 1],
                        d["ssig"][:, sig_idx:sig_idx + 1], 1.0 / QMAX,
                        ALU.mult, ALU.mult).then_inc(s_dve.h, 1)
        s_dve.n += 1
        e.wait_ge(s_dve.h, s_dve.n)
        e.reciprocal(d["s_tmp"][:, 5:6], d["s_tmp"][:, 4:5]).then_inc(s_dve.h, 1)
        s_dve.n += 1
    ctx.vector(fv)

    def ft(e):
        e.wait_ge(s_dve.h, s_dve.n)
        e.matmul(pall[:, 7 * 512:7 * 512 + 2], d["ones_row"][:, :],
                 d["s_tmp"][0:1, 4:6], start=True, stop=True).then_inc(s_pe.h, 1)
        s_pe.n += 1
    ctx.tensor(ft)

    def fv2(e):
        e.wait_ge(s_pe.h, s_pe.n)
        e.tensor_copy(d["binv"][:, 2 * slot:2 * slot + 1],
                      pall[:, 7 * 512 + 1:7 * 512 + 2])
        e.tensor_copy(d["binv"][:, 2 * slot + 1:2 * slot + 2],
                      pall[:, 7 * 512:7 * 512 + 1]).then_inc(s_dve.h, 1)
        s_dve.n += 1
        if slot % 2 == 1:
            e.wait_ge(s_dve.h, s_dve.n)
            e.tensor_scalar(d["hcol"][:, slot // 2:slot // 2 + 1],
                            d["binv"][:, 2 * slot + 1:2 * slot + 2], 0.5, None,
                            ALU.mult).then_inc(s_dve.h, 1)
            s_dve.n += 1
        if "gmax" in d["dbg"]:
            e.wait_ge(s_dve.h, s_dve.n)

            def fdbg(e2):
                e2.wait_ge(s_dve.h, s_dve.n)
                e2.dma_start(out=d["dbg_out"]["gmax"][:, :], in_=d["gmax"][:, :]) \
                    .then_inc(d["s_st"].h, 16)
                d["s_st"].n += 16
            ctx.gpsimd(fdbg)
    ctx.vector(fv2)


def emit_xq(ctx, d, slot):
    s_dve, s_st, s_sc = d["s_dve"], d["s_st"], d["s_sc"]
    qstage, qout, srs, xq_fm = d["qstage"], d["qout"], d["srs"], d["xq_fm"]
    dve0 = s_dve.n
    p_xs = mk_psems(ctx, d, "xqs%d" % slot)

    def fv(e):
        e.tensor_scalar(srs[:, :], d["st_r"][:, :],
                        d["binv"][:, 2 * slot:2 * slot + 1], None, ALU.mult) \
            .then_inc(s_dve.h, 1)
        e.wait_ge(s_dve.h, dve0 + 1)
        for t in range(TT):
            sl = slice((t % 2) * DM, (t % 2) * DM + DM)
            if t >= 2:
                e.wait_ge(p_xs[t % 2].h, 16 * (t // 2))
            xt = d["xr"][:, t * DM:(t + 1) * DM]
            e.tensor_scalar(qstage[:, sl], xt, d["st_m"][:, t:t + 1],
                            srs[:, t:t + 1], ALU.subtract, ALU.mult)
            e.tensor_scalar(qstage[:, sl], qstage[:, sl], QMAX, -QMAX - 1.0,
                            ALU.min, ALU.max)
            e.tensor_scalar(qout[:, sl], qstage[:, sl], RC, RC, ALU.add,
                            ALU.subtract).then_inc(s_dve.h, 1)
        s_dve.n += 1 + TT
    ctx.vector(fv)

    xq_t = d["xq_nat"].rearrange("(t p) m -> t p m", p=128)
    sc0 = s_sc.n

    def fst(e):
        for t in range(TT):
            sl = slice((t % 2) * DM, (t % 2) * DM + DM)
            e.wait_ge(s_dve.h, dve0 + 2 + t)
            e.dma_start(out=xq_t[t], in_=qout[:, sl]).then_inc(p_xs[t % 2].h, 16)
            p_xs[t % 2].n += 16
            if t == 7:
                e.wait_ge(p_xs[0].h, 16 * 4)
                e.wait_ge(p_xs[1].h, 16 * 4)
                for hc in range(HC):
                    e.dma_start_transpose(
                        xq_fm[:, hc * TOK:hc * TOK + TOK // 2],
                        d["xq_nat"][0:TOK // 2, hc * 128:(hc + 1) * 128]) \
                        .then_inc(s_sc.h, 16)
                s_sc.n += 16 * HC
        e.wait_ge(p_xs[0].h, p_xs[0].n)
        e.wait_ge(p_xs[1].h, p_xs[1].n)
        for hc in range(HC):
            e.dma_start_transpose(
                xq_fm[:, hc * TOK + TOK // 2:(hc + 1) * TOK],
                d["xq_nat"][TOK // 2:TOK, hc * 128:(hc + 1) * 128]) \
                .then_inc(s_sc.h, 16)
        s_sc.n += 16 * HC
        e.wait_ge(s_sc.h, s_sc.n)
        if "xq" in d["dbg"]:
            e.dma_start(out=d["dbg_out"]["xq"][:, :], in_=d["xq_nat"][:, :]) \
                .then_inc(s_st.h, 16)
            s_st.n += 16
            e.wait_ge(s_st.h, s_st.n)
    ctx.sync(fst)


def emit_gateup(ctx, d, scope, slot):
    sb = d["sb"]
    s_ld, s_pe, s_dve, s_act, s_st = d["s_ld"], d["s_pe"], d["s_dve"], d["s_act"], d["s_st"]
    pall, xq_fm = d["pall"], d["xq_fm"]

    wgb = d.get("wgb") or sb("wgb", [128, 2 * HC * 128], BF16, scope)
    wub = d.get("wub") or sb("wub", [128, 2 * HC * 128], BF16, scope)
    sgt = d.get("sgt") or sb("sgt", [128, 2 * 512], F32, scope)
    upt = d.get("upt") or sb("upt", [128, 2 * 512], F32, scope)
    hft = d.get("hft") or sb("hft", [128, 2 * 512], F32, scope)
    hbt = d.get("hbt") or sb("hbt", [128, 2 * 512], F16, scope)
    svg = d.get("svg") or sb("svg", [128, IC], F32, scope)
    svu = d.get("svu") or sb("svu", [128, IC], F32, scope)
    for nm, v in [("wgb", wgb), ("wub", wub), ("sgt", sgt), ("upt", upt),
                  ("hft", hft), ("hbt", hbt), ("svg", svg), ("svu", svu)]:
        d[nm] = v

    NS = 4
    nu = IC * NS
    pe0, act0, dve0 = s_pe.n, s_act.n, s_dve.n
    p_wl = mk_psems(ctx, d, "gwl%d" % slot)
    p_hs = mk_psems(ctx, d, "ghs%d" % slot)
    wg_t = d["wgq_t"].rearrange("(c p) i -> p c i", p=128)
    wu_t = d["wuq_t"].rearrange("(c p) i -> p c i", p=128)

    def fsync(e):
        for ic in range(IC):
            if ic >= 2:
                e.wait_ge(s_pe.h, pe0 + (ic - 1) * NS)
            b = (ic % 2) * HC * 128
            e.dma_start(out=wgb[:, b:b + HC * 128].rearrange("p (c n) -> p c n", c=HC),
                        in_=wg_t[:, :, ic * 128:(ic + 1) * 128]).then_inc(p_wl[ic % 2].h, 16)
            e.dma_start(out=wub[:, b:b + HC * 128].rearrange("p (c n) -> p c n", c=HC),
                        in_=wu_t[:, :, ic * 128:(ic + 1) * 128]).then_inc(p_wl[ic % 2].h, 16)
            p_wl[ic % 2].n += 32
    ctx.sync(fsync)

    def fv0(e):
        e.tensor_scalar(svg[:, :], d["wsc_g"][:, :], d["binv"][:, 2 * slot + 1:2 * slot + 2],
                        1.0 / QMAX, ALU.mult, ALU.mult)
        e.tensor_scalar(svu[:, :], d["wsc_u"][:, :], d["binv"][:, 2 * slot + 1:2 * slot + 2],
                        1.0 / QMAX, ALU.mult, ALU.mult).then_inc(s_dve.h, 1)
        s_dve.n += 1
    ctx.vector(fv0)
    dve_pre = s_dve.n

    def ft(e):
        e.wait_ge(s_dve.h, dve_pre)
        u = 0
        for ic in range(IC):
            e.wait_ge(p_wl[ic % 2].h, 32 * (ic // 2 + 1))
            b = (ic % 2) * HC * 128
            for s in range(NS):
                if u >= 2:
                    e.wait_ge(s_dve.h, dve_pre + (u - 1) * 2)
                    e.wait_ge(s_act.h, act0 + u - 1)
                gb, ub = (u % 2) * 2, (u % 2) * 2 + 1
                for hc in range(HC):
                    e.matmul(pall[:, gb * 512:gb * 512 + 512],
                             wgb[:, b + hc * 128:b + hc * 128 + 128],
                             xq_fm[:, hc * TOK + s * 512:hc * TOK + (s + 1) * 512],
                             start=(hc == 0), stop=(hc == HC - 1))
                for hc in range(HC):
                    inst = e.matmul(
                        pall[:, ub * 512:ub * 512 + 512],
                        wub[:, b + hc * 128:b + hc * 128 + 128],
                        xq_fm[:, hc * TOK + s * 512:hc * TOK + (s + 1) * 512],
                        start=(hc == 0), stop=(hc == HC - 1))
                inst.then_inc(s_pe.h, 1)
                u += 1
        s_pe.n += nu
    ctx.tensor(ft)

    def fs(e):
        u = 0
        for ic in range(IC):
            for s in range(NS):
                e.wait_ge(s_pe.h, pe0 + u + 1)
                if u >= 2:
                    e.wait_ge(s_dve.h, dve_pre + (u - 1) * 2)
                gb = (u % 2) * 2
                e.activation(sgt[:, (u % 2) * 512:(u % 2) * 512 + 512],
                             pall[:, gb * 512:gb * 512 + 512], AF.Silu,
                             scale=svg[:, ic:ic + 1]).then_inc(s_act.h, 1)
                u += 1
        s_act.n += nu
    ctx.scalar(fs)

    def fv(e):
        u = 0
        for ic in range(IC):
            for s in range(NS):
                e.wait_ge(s_pe.h, pe0 + u + 1)
                ub = (u % 2) * 2 + 1
                if u >= 2:
                    e.wait_ge(p_hs[u % 2].h, 16 * (u // 2))
                e.tensor_scalar(upt[:, (u % 2) * 512:(u % 2) * 512 + 512],
                                pall[:, ub * 512:ub * 512 + 512],
                                svu[:, ic:ic + 1], None, ALU.mult)
                e.wait_ge(s_act.h, act0 + u + 1)
                e.tensor_mul(hft[:, (u % 2) * 512:(u % 2) * 512 + 512],
                             sgt[:, (u % 2) * 512:(u % 2) * 512 + 512],
                             upt[:, (u % 2) * 512:(u % 2) * 512 + 512])
                e.tensor_copy(hbt[:, (u % 2) * 512:(u % 2) * 512 + 512],
                              hft[:, (u % 2) * 512:(u % 2) * 512 + 512]) \
                    .then_inc(s_dve.h, 1)
                e.reduce_max(d["hmax_cols"][:, u:u + 1],
                             hft[:, (u % 2) * 512:(u % 2) * 512 + 512], AX.X,
                             apply_absolute_value=True).then_inc(s_dve.h, 1)
                u += 1
        s_dve.n += 2 * nu
    ctx.vector(fv)

    h_t = d["h_nat"].rearrange("(c p) n -> c p n", p=128)

    def fst(e):
        u = 0
        for ic in range(IC):
            for s in range(NS):
                e.wait_ge(s_dve.h, dve_pre + 2 * u + 2)
                e.dma_start(out=h_t[ic, :, s * 512:(s + 1) * 512],
                            in_=hbt[:, (u % 2) * 512:(u % 2) * 512 + 512]) \
                    .then_inc(p_hs[u % 2].h, 16)
                p_hs[u % 2].n += 16
                u += 1
        e.wait_ge(p_hs[0].h, p_hs[0].n)
        e.wait_ge(p_hs[1].h, p_hs[1].n)
        if "h" in d["dbg"]:
            e.dma_start(out=d["dbg_out"]["h"][:, :], in_=d["h_nat"][:, :]) \
                .then_inc(s_st.h, 16)
            s_st.n += 16
            e.wait_ge(s_st.h, s_st.n)
    ctx.gpsimd(fst)


def emit_hmax_finish(ctx, d):
    s_dve = d["s_dve"]

    def fv(e):
        e.reduce_max(d["lmax_col"][:, :], d["hmax_cols"][:, :], AX.X) \
            .then_inc(s_dve.h, 1)
        s_dve.n += 1
    ctx.vector(fv)


def emit_hquant(ctx, d, scope, slot):
    sb = d["sb"]
    s_ld, s_dve, s_act, s_st = d["s_ld"], d["s_dve"], d["s_act"], d["s_st"]
    hst = d.get("hst") or sb("hst", [128, 2 * TOK], F16, scope)
    hqo = d.get("hqo") or sb("hqo", [128, 2 * TOK], BF16, scope)
    hfs = d.get("hfs") or sb("hfs", [128, 2 * TOK], F32, scope)
    d["hst"], d["hqo"], d["hfs"] = hst, hqo, hfs

    h_t = d["h_nat"].rearrange("(c p) n -> c p n", p=128)
    hq_t = d["hq_nat"].rearrange("(c p) n -> c p n", p=128)
    act0, dve0 = s_act.n, s_dve.n
    p_hl = mk_psems(ctx, d, "hql%d" % slot)
    p_qs = mk_psems(ctx, d, "hqs%d" % slot)

    def fsync(e):
        for ic in range(IC):
            if ic >= 2:
                e.wait_ge(p_qs[ic % 2].h, 16 * (ic // 2))
            e.dma_start(out=hst[:, (ic % 2) * TOK:(ic % 2 + 1) * TOK],
                        in_=h_t[ic]).then_inc(p_hl[ic % 2].h, 16)
            p_hl[ic % 2].n += 16
    ctx.sync(fsync)

    def fs(e):
        for ic in range(IC):
            sl = slice((ic % 2) * TOK, (ic % 2) * TOK + TOK)
            e.wait_ge(p_hl[ic % 2].h, 16 * (ic // 2 + 1))
            if ic >= 2:
                e.wait_ge(s_dve.h, dve0 + 2 * (ic - 1))
            e.activation(hfs[:, sl], hst[:, sl], AF.Copy,
                         scale=d["binv"][:, 2 * slot:2 * slot + 1]) \
                .then_inc(s_act.h, 1)
        s_act.n += IC
    ctx.scalar(fs)

    def fv(e):
        for ic in range(IC):
            sl = slice((ic % 2) * TOK, (ic % 2) * TOK + TOK)
            e.wait_ge(s_act.h, act0 + ic + 1)
            if ic >= 2:
                e.wait_ge(p_qs[ic % 2].h, 16 * (ic // 2))
            e.tensor_scalar(hfs[:, sl], hfs[:, sl], QMAX, -QMAX - 1.0,
                            ALU.min, ALU.max)
            e.tensor_scalar(hqo[:, sl], hfs[:, sl], RC, RC, ALU.add,
                            ALU.subtract).then_inc(s_dve.h, 2)
        s_dve.n += 2 * IC
    ctx.vector(fv)

    def fst(e):
        for ic in range(IC):
            sl = slice((ic % 2) * TOK, (ic % 2) * TOK + TOK)
            e.wait_ge(s_dve.h, dve0 + 2 * (ic + 1))
            e.dma_start(out=hq_t[ic], in_=hqo[:, sl]).then_inc(p_qs[ic % 2].h, 16)
            p_qs[ic % 2].n += 16
        e.wait_ge(p_qs[0].h, p_qs[0].n)
        e.wait_ge(p_qs[1].h, p_qs[1].n)
        if "hq" in d["dbg"]:
            e.dma_start(out=d["dbg_out"]["hq"][:, :], in_=d["hq_nat"][:, :]) \
                .then_inc(s_st.h, 16)
            s_st.n += 16
            e.wait_ge(s_st.h, s_st.n)
    ctx.gpsimd(fst)


def emit_bsd(ctx, d):
    s_pe, s_dve = d["s_pe"], d["s_dve"]
    pall = d["pall"]

    for hc in range(HC):
        def ft(e, hc=hc):
            e.wait_ge(s_dve.h, s_dve.n)
            e.transpose(pall[0:1, 6 * 512:6 * 512 + 128], d["wsc_d"][:, hc:hc + 1],
                        d["iden_f"][:, :]).then_inc(s_pe.h, 1)
            s_pe.n += 1
        ctx.tensor(ft)

        def fv(e, hc=hc):
            e.wait_ge(s_pe.h, s_pe.n)
            e.tensor_copy(d["brow"][:, :], pall[0:1, 6 * 512:6 * 512 + 128]) \
                .then_inc(s_dve.h, 1)
            s_dve.n += 1
        ctx.vector(fv)

        def ft2(e, hc=hc):
            e.wait_ge(s_dve.h, s_dve.n)
            e.matmul(pall[:, 7 * 512:7 * 512 + 128], d["ones_row"][:, :],
                     d["brow"][:, :], start=True, stop=True).then_inc(s_pe.h, 1)
            s_pe.n += 1
        ctx.tensor(ft2)

        def fv2(e, hc=hc):
            e.wait_ge(s_pe.h, s_pe.n)
            e.tensor_scalar(d["bsd"][:, hc * 128:(hc + 1) * 128],
                            pall[:, 7 * 512:7 * 512 + 128], 1.0 / QMAX, None,
                            ALU.mult).then_inc(s_dve.h, 1)
            s_dve.n += 1
        ctx.vector(fv2)


def emit_down(ctx, d, scope, app):
    sb = d["sb"]
    s_ld, s_pe, s_dve, s_act = d["s_ld"], d["s_pe"], d["s_dve"], d["s_act"]
    pall = d["pall"]
    slot = 2 * app + 1

    wdr = sb("wdr", [128, IC * DM], BF16, scope)
    hstage = sb("hstage", [128, 2 * IC * 256], F16, scope)
    hfs = sb("hfs", [128, IC * 256], F32, scope)
    hqb = sb("hqb", [128, 2 * IC * 256], BF16, scope)
    dst1 = sb("dst1", [128, DM], F32, scope)

    NTP = TT // 2
    CW = IC * 256
    h_v = d["h_nat"].rearrange("(c p) n -> p c n", p=128)
    wd_v = d["wdq_t"].rearrange("(c p) m -> p c m", p=128)
    p_wd = mk_psems(ctx, d, "dwd%d" % app)
    p_hl = mk_psems(ctx, d, "dhl%d" % app)
    pe0, dve0, act0 = s_pe.n, s_dve.n, s_act.n

    def fw(e):
        e.dma_start(out=wdr[:, :].rearrange("p (c m) -> p c m", c=IC),
                    in_=wd_v).then_inc(p_wd[0].h, 16)
        p_wd[0].n += 16
    ctx.sync(fw)

    def fhl(e):
        for tp in range(NTP):
            if tp >= 2:
                e.wait_ge(s_act.h, act0 + tp - 1)
            e.dma_start(out=hstage[:, (tp % 2) * CW:(tp % 2 + 1) * CW]
                        .rearrange("p (c n) -> p c n", c=IC),
                        in_=h_v[:, :, tp * 256:(tp + 1) * 256]) \
                .then_inc(p_hl[tp % 2].h, 16)
            p_hl[tp % 2].n += 16
    ctx.gpsimd(fhl)

    def fa(e):
        for tp in range(NTP):
            e.wait_ge(p_hl[tp % 2].h, 16 * (tp // 2 + 1))
            if tp >= 1:
                e.wait_ge(s_dve.h, dve0 + 5 * (tp - 1) + 1)
            e.activation(hfs[:, :], hstage[:, (tp % 2) * CW:(tp % 2 + 1) * CW],
                         AF.Copy, scale=d["binv"][:, 2 * slot:2 * slot + 1]) \
                .then_inc(s_act.h, 1)
        s_act.n += NTP
    ctx.scalar(fa)

    cnt_quant = {}
    c = dve0
    for tp in range(NTP):
        c += 1
        cnt_quant[tp] = c
        if tp >= 1:
            c += 4

    def emit_evac(e, tp):
        e.wait_ge(s_pe.h, pe0 + (tp + 1) * IC * 4)
        for ti in range(2):
            t = tp * 2 + ti
            bank0 = (tp % 2) * 4 + ti * 2
            e.tensor_mul(dst1[:, :],
                         pall[:, bank0 * 512:bank0 * 512 + 1024], d["bsd"][:, :])
            e.scalar_tensor_tensor(
                d["xr"][:, t * DM:(t + 1) * DM],
                dst1[:, :],
                d["hcol"][:, app:app + 1],
                d["xr"][:, t * DM:(t + 1) * DM],
                ALU.mult, ALU.add).then_inc(s_dve.h, 2)
            s_dve.n += 2
            d.setdefault("xr_dve", {})[t] = s_dve.n

    def fqv(e):
        for tp in range(NTP):
            e.wait_ge(s_act.h, act0 + tp + 1)
            if tp >= 2:
                e.wait_ge(s_pe.h, pe0 + 88 * (tp - 1))
            e.tensor_scalar(hfs[:, :], hfs[:, :], QMAX, -QMAX - 1.0,
                            ALU.min, ALU.max)
            e.tensor_scalar(hqb[:, (tp % 2) * CW:(tp % 2 + 1) * CW], hfs[:, :],
                            RC, RC, ALU.add, ALU.subtract).then_inc(s_dve.h, 1)
            s_dve.n += 1
            if tp >= 1:
                emit_evac(e, tp - 1)
        emit_evac(e, NTP - 1)
    ctx.vector(fqv)

    def ft(e):
        e.wait_ge(p_wd[0].h, 16)
        for tp in range(NTP):
            e.wait_ge(s_dve.h, cnt_quant[tp])
            for ic in range(IC):
                hb = (tp % 2) * CW
                for ti in range(2):
                    for half in range(2):
                        bank = (tp % 2) * 4 + ti * 2 + half
                        ii = e.matmul(
                            pall[:, bank * 512:bank * 512 + 512],
                            hqb[:, hb + ic * 256 + ti * 128:hb + ic * 256 + ti * 128 + 128],
                            wdr[:, ic * DM + half * 512:ic * DM + half * 512 + 512],
                            start=(ic == 0), stop=(ic == IC - 1))
                ii.then_inc(s_pe.h, 4)
        s_pe.n += 4 * NTP * IC
    ctx.tensor(ft)


PI = 3.14159265358979323846


def emit_attention(ctx, d, scope):
    nc = ctx.nc
    sb = d["sb"]
    s_pe, s_dve, s_act, s_gp = d["s_pe"], d["s_dve"], d["s_act"], d["s_gp"]
    s_ld, s_sc, s_gld, s_st, s_cc = d["s_ld"], d["s_sc"], d["s_gld"], d["s_st"], d["s_cc"]
    pall = d["pall"]

    emit_ln_stats(ctx, d, want_max=False)
    ctx.barrier()

    battn = sb("battn", [128, 4], F32, scope)
    ones_col = sb("ones_col", [128, 1], BF16, scope)
    core_scope = ExitStack()
    av_fm = sb("av_fm", [128, 4 * S], BF16, core_scope)
    mid_scope = ExitStack()
    qf = sb("qf", [128, 4 * S], BF16, mid_scope)
    kf = sb("kf", [128, 4 * S], BF16, mid_scope)

    xst_scope = ExitStack()
    xst = sb("xst", [128, DM], BF16, xst_scope)
    p_xl = mk_psems(ctx, d, "axl")
    dve0 = s_dve.n

    def fv_ln(e):
        e.memset(ones_col[:, :], 1.0)
        for t in range(TT):
            sl = slice(0, DM)
            if t >= 1:
                e.wait_ge(p_xl[0].h, 16 * t)
            e.tensor_scalar(xst[:, sl], d["xr"][:, t * DM:(t + 1) * DM],
                            d["st_m"][:, t:t + 1], d["st_r"][:, t:t + 1],
                            ALU.subtract, ALU.mult).then_inc(s_dve.h, 1)
        s_dve.n += TT
    ctx.vector(fv_ln)

    ag_in_t = d["ag_xln_in"].rearrange("(t p) m -> t p m", p=128)

    def fst_ln(e):
        for t in range(TT):
            e.wait_ge(s_dve.h, dve0 + t + 1)
            e.dma_start(out=ag_in_t[t], in_=xst[:, :]).then_inc(p_xl[0].h, 16)
            p_xl[0].n += 16
    ctx.sync(fst_ln)

    def fcc(e):
        e.wait_ge(p_xl[0].h, p_xl[0].n)
        e.wait_ge(p_xl[1].h, p_xl[1].n)
        e.collective_compute(
            "AllGather", ALU.bypass,
            replica_groups=[[2 * i, 2 * i + 1] for i in range(NCORES // 2)],
            ins=[d["ag_xln_in"].ap().opt()], outs=[d["ag_xln_out"].ap().opt()],
        ).then_inc(s_cc.h, 1)
        s_cc.n += 1
        if "xln" in d["dbg"]:
            e.wait_ge(s_cc.h, s_cc.n)
            e.dma_start(out=d["dbg_out"]["xln"][:, :], in_=d["ag_xln_out"][:, :]) \
                .then_inc(s_gld.h, 16)
            s_gld.n += 16
            e.wait_ge(s_gld.h, s_gld.n)
    ctx.gpsimd(fcc)
    xst_scope.close()

    with ExitStack() as qscope:
        xfs = av_fm[:, 0:2 * HC * 512]
        cos_t = av_fm[:, 2 * HC * 512:2 * HC * 512 + S]
        sin_t = av_fm[:, 2 * HC * 512 + S:2 * HC * 512 + 2 * S]
        tmpf = sb("tmpf", [128, 2 * 512], F32, qscope)
        pio2 = sb("pio2", [128, 1], F32, qscope)
        qraw = sb("qraw", [128, 2 * 512], BF16, qscope)
        posrow = sb("posrow", [1, 512], F32, qscope)
        ivf = sb("ivf", [128, 1], F32, qscope)
        dummy_split_marker = None

        def fv_ivf(e):
            e.memset(pio2[:, :], PI / 2)
            pt = d["iota_p"]
            for step in range(5):
                ops = [
                    lambda: e.tensor_scalar(ivf[:, :], pt[:, :], 1.0 / 64,
                                            -63.0 / 128, ALU.mult, ALU.add),
                    lambda: e.tensor_scalar(ivf[:, :], ivf[:, :], RC, RC,
                                            ALU.add, ALU.subtract),
                    lambda: e.scalar_tensor_tensor(ivf[:, :], ivf[:, :], -64.0,
                                                   pt[:, :], ALU.mult, ALU.add),
                    lambda: e.tensor_scalar(ivf[:, :], ivf[:, :], 0.5, -0.25,
                                            ALU.mult, ALU.add),
                    lambda: e.tensor_scalar(ivf[:, :], ivf[:, :], RC, RC,
                                            ALU.add, ALU.subtract),
                ]
                ops[step]().then_inc(s_dve.h, 1)
                s_dve.n += 1
                e.wait_ge(s_dve.h, s_dve.n)
        ctx.vector(fv_ivf)

        def fs_ivf(e):
            e.wait_ge(s_dve.h, s_dve.n)
            import math
            e.activation(ivf[:, :], ivf[:, :], AF.Exp, scale=-math.log(10000.0) / 32) \
                .then_inc(s_act.h, 1)
            s_act.n += 1
        ctx.scalar(fs_ivf)

        for sp in range(8):
            def fgp(e, sp=sp):
                if sp == 0:
                    e.wait_ge(s_act.h, s_act.n)
                else:
                    e.wait_ge(s_dve.h, s_dve.n)
                e.iota(posrow[:, :], [[1, 512]], base=sp * 512,
                       channel_multiplier=0,
                       allow_small_or_imprecise_dtypes=True).then_inc(s_gp.h, 1)
                s_gp.n += 1
            ctx.gpsimd(fgp)

            def ft(e, sp=sp):
                e.wait_ge(s_gp.h, s_gp.n)
                e.wait_ge(s_dve.h, s_dve.n)
                e.matmul(pall[:, 6 * 512:7 * 512], d["ones_row"][:, :],
                         posrow[:, :], start=True, stop=True).then_inc(s_pe.h, 1)
                s_pe.n += 1
            ctx.tensor(ft)

            def fv(e, sp=sp):
                e.wait_ge(s_pe.h, s_pe.n)
                ang = tmpf[:, 0:512]
                r = tmpf[:, 512:1024]
                e.tensor_scalar(ang[:, :], pall[:, 6 * 512:7 * 512], ivf[:, :],
                                None, ALU.mult).then_inc(s_dve.h, 1)
                s_dve.n += 1
                e.wait_ge(s_dve.h, s_dve.n)
                e.tensor_scalar(r[:, :], ang[:, :], 1.0 / (2 * PI), None,
                                ALU.mult).then_inc(s_dve.h, 1)
                s_dve.n += 1
                e.wait_ge(s_dve.h, s_dve.n)
                e.tensor_scalar(r[:, :], r[:, :], RC, RC, ALU.add, ALU.subtract) \
                    .then_inc(s_dve.h, 1)
                s_dve.n += 1
                e.wait_ge(s_dve.h, s_dve.n)
                e.scalar_tensor_tensor(r[:, :], r[:, :], -2 * PI, ang[:, :],
                                       ALU.mult, ALU.add).then_inc(s_dve.h, 1)
                s_dve.n += 1
            ctx.vector(fv)

            def fs(e, sp=sp):
                e.wait_ge(s_dve.h, s_dve.n)
                e.activation(sin_t[:, sp * 512:(sp + 1) * 512], tmpf[:, 512:1024],
                             AF.Sin).then_inc(s_act.h, 1)
                s_act.n += 1
            ctx.scalar(fs)

            def fv2(e, sp=sp):
                ang = tmpf[:, 0:512]
                r = tmpf[:, 512:1024]
                e.wait_ge(s_act.h, s_act.n)
                e.tensor_scalar(r[:, :], ang[:, :], 1.0 / (2 * PI), 0.25,
                                ALU.mult, ALU.add).then_inc(s_dve.h, 1)
                s_dve.n += 1
                e.wait_ge(s_dve.h, s_dve.n)
                e.tensor_scalar(r[:, :], r[:, :], RC, RC, ALU.add, ALU.subtract) \
                    .then_inc(s_dve.h, 1)
                s_dve.n += 1
                e.wait_ge(s_dve.h, s_dve.n)
                e.scalar_tensor_tensor(r[:, :], r[:, :], -2 * PI, ang[:, :],
                                       ALU.mult, ALU.add).then_inc(s_dve.h, 1)
                s_dve.n += 1
            ctx.vector(fv2)

            def fs2(e, sp=sp):
                e.wait_ge(s_dve.h, s_dve.n)
                e.activation(cos_t[:, sp * 512:(sp + 1) * 512], tmpf[:, 512:1024],
                             AF.Sin, bias=pio2[:, :]).then_inc(s_act.h, 1)
                s_act.n += 1
            ctx.scalar(fs2)

        if "rotb" in d["dbg"]:
            def fdr(e):
                e.dma_start(out=d["dbg_out"]["rotb"][:, :], in_=d["rot_b"][:, :]) \
                    .then_inc(s_gld.h, 16)
                s_gld.n += 16
                e.wait_ge(s_gld.h, s_gld.n)
            ctx.gpsimd(fdr)
        if d["dbg"] & {"cos", "sin", "ivf"}:
            ctx.barrier()
            def fdbg_t(e):
                for nm, buf in (("cos", cos_t), ("sin", sin_t), ("ivf", ivf)):
                    if nm in d["dbg"]:
                        e.dma_start(out=d["dbg_out"][nm][:, :], in_=buf[:, :]) \
                            .then_inc(s_gld.h, 16)
                        s_gld.n += 16
                e.wait_ge(s_gld.h, s_gld.n)
            ctx.gpsimd(fdbg_t)
            ctx.barrier()
        for (w_in, dstf, pname) in ((d["wq_in"], qf, "aq"), (d["wk_in"], kf, "ak")):
            with ExitStack() as wsc:
                wqa = sb("w" + pname, [128, HC * 512], BF16, wsc)
                wq_v = w_in.rearrange("(c p) m -> p c m", p=128)

                def fwl(e, wqa=wqa, wq_v=wq_v):
                    e.wait_ge(s_pe.h, s_pe.n)
                    e.dma_start(out=wqa[:, :].rearrange("p (c m) -> p c m", c=HC),
                                in_=wq_v).then_inc(s_gld.h, 16)
                    s_gld.n += 16
                    e.wait_ge(s_cc.h, s_cc.n)
                ctx.gpsimd(fwl)
                ctx.barrier()

                p_tr = mk_psems(ctx, d, "atr" + pname)
                dve_pass0 = s_dve.n
                pe_pass0 = s_pe.n
                for sp in range(8):
                    def ftr(e, sp=sp, p_tr=p_tr):
                        if sp >= 2:
                            e.wait_ge(s_pe.h, pe_pass0 + 8 * (sp - 1))
                        xb = (sp % 2) * HC * 512
                        for hc in range(HC):
                            e.dma_start_transpose(
                                xfs[:, xb + hc * 512:xb + (hc + 1) * 512],
                                d["ag_xln_out"][sp * 512:(sp + 1) * 512,
                                                hc * 128:(hc + 1) * 128]) \
                                .then_inc(p_tr[sp % 2].h, 16)
                            p_tr[sp % 2].n += 16
                    ctx.sync(ftr)

                    for mc in range(4):
                        n_u = sp * 4 + mc
                        st_ = (n_u % 2) * 4
                        qb_, rq_ = st_, st_ + 2
                        qsl = slice((n_u % 2) * 512, (n_u % 2) * 512 + 512)
                        xb = (sp % 2) * HC * 512

                        def ft(e, sp=sp, mc=mc, qb_=qb_, wqa=wqa, p_tr=p_tr,
                               n_u=n_u, xb=xb):
                            e.wait_ge(p_tr[sp % 2].h, 128 * (sp // 2 + 1))
                            if n_u >= 2:
                                e.wait_ge(s_dve.h, dve_pass0 + 3 * (n_u - 1))
                            for hc in range(HC):
                                inst = e.matmul(
                                    pall[:, qb_ * 512:qb_ * 512 + 512],
                                    wqa[:, hc * 512 + mc * 128:hc * 512 + mc * 128 + 128],
                                    xfs[:, xb + hc * 512:xb + (hc + 1) * 512],
                                    start=(hc == 0), stop=(hc == HC - 1))
                            inst.then_inc(s_pe.h, 1)
                            s_pe.n += 1
                        ctx.tensor(ft)

                        def fv_cp(e, qb_=qb_, qsl=qsl):
                            e.wait_ge(s_pe.h, s_pe.n)
                            e.tensor_copy(qraw[:, qsl],
                                          pall[:, qb_ * 512:qb_ * 512 + 512]) \
                                .then_inc(s_dve.h, 1)
                            s_dve.n += 1
                        ctx.vector(fv_cp)
                        if sp == 0 and mc == 3 and pname == "aq" and \
                                "qraw0" in d["dbg"]:
                            def fdq(e):
                                e.wait_ge(s_dve.h, s_dve.n)
                                e.dma_start(out=d["dbg_out"]["qraw0"][:, :],
                                            in_=qraw[:, :]).then_inc(s_gld.h, 16)
                                s_gld.n += 16
                                e.wait_ge(s_gld.h, s_gld.n)
                            ctx.gpsimd(fdq)
                            def fdx(e):
                                e.dma_start(out=d["dbg_out"]["xfs0"][:, :],
                                            in_=xfs[:, :]).then_inc(s_gld.h, 16)
                                s_gld.n += 16
                                e.wait_ge(s_gld.h, s_gld.n)
                            ctx.gpsimd(fdx)

                        def ft_rot(e, rq_=rq_, qsl=qsl):
                            e.wait_ge(s_dve.h, s_dve.n)
                            e.matmul(pall[:, rq_ * 512:rq_ * 512 + 512],
                                     d["rot_b"][:, :], qraw[:, qsl],
                                     start=True, stop=True).then_inc(s_pe.h, 1)
                            s_pe.n += 1
                        ctx.tensor(ft_rot)

                        def fv_rope(e, sp=sp, mc=mc, rq_=rq_, dstf=dstf,
                                    qsl=qsl):
                            cs = cos_t[:, sp * 512:(sp + 1) * 512]
                            sn = sin_t[:, sp * 512:(sp + 1) * 512]
                            e.tensor_mul(tmpf[:, 0:512], qraw[:, qsl], cs)
                            e.wait_ge(s_pe.h, s_pe.n)
                            e.tensor_mul(tmpf[:, 512:1024],
                                         pall[:, rq_ * 512:rq_ * 512 + 512], sn) \
                                .then_inc(s_dve.h, 1)
                            s_dve.n += 1
                            e.wait_ge(s_dve.h, s_dve.n)
                            e.tensor_add(dstf[:, mc * S + sp * 512:mc * S + sp * 512 + 512],
                                         tmpf[:, 0:512], tmpf[:, 512:1024]) \
                                .then_inc(s_dve.h, 1)
                            s_dve.n += 1
                        ctx.vector(fv_rope)

        dummy_end_qk = None
    ctx.barrier()
    vt = sb("vt", [128, ST * 520], BF16, mid_scope)

    def fv_vt(e):
        e.memset(vt[:, :], 1.0).then_inc(s_dve.h, 1)
        s_dve.n += 1
    ctx.vector(fv_vt)

    with ExitStack() as qscope:
        xfs = av_fm[:, 0:2 * HC * 512]
        qraw = d["sqb"][:, 256:512].bitcast(BF16)
        nrm = d["sqb"][0:1, 0:64]
        nfin = d["sqb"][0:1, 64:72]
        with ExitStack() as wsc:
            wva = sb("wva", [128, HC * 512], BF16, wsc)
            wv_v = d["wv_in"].rearrange("(c p) m -> p c m", p=128)

            def fwl(e):
                e.wait_ge(s_pe.h, s_pe.n)
                e.wait_ge(s_dve.h, s_dve.n)
                e.dma_start(out=wva[:, :].rearrange("p (c m) -> p c m", c=HC),
                            in_=wv_v).then_inc(s_gld.h, 16)
                s_gld.n += 16
            ctx.gpsimd(fwl)
            ctx.barrier()

            p_tr2 = mk_psems(ctx, d, "atr2")
            pe_v0 = s_pe.n
            for sp in range(8):
                def ftr(e, sp=sp):
                    if sp >= 2:
                        e.wait_ge(s_pe.h, pe_v0 + 4 * (sp - 1))
                    xb = (sp % 2) * HC * 512
                    for hc in range(HC):
                        e.dma_start_transpose(
                            xfs[:, xb + hc * 512:xb + (hc + 1) * 512],
                            d["ag_xln_out"][sp * 512:(sp + 1) * 512,
                                            hc * 128:(hc + 1) * 128]) \
                            .then_inc(p_tr2[sp % 2].h, 16)
                        p_tr2[sp % 2].n += 16
                ctx.sync(ftr)

                for ti4 in range(4):
                    st_ = ((sp * 4 + ti4) % 2) * 4
                    xb = (sp % 2) * HC * 512
                    def ft(e, sp=sp, ti4=ti4, st_=st_, xb=xb):
                        e.wait_ge(p_tr2[sp % 2].h, 128 * (sp // 2 + 1))
                        if sp * 4 + ti4 >= 2:
                            e.wait_ge(s_dve.h, s_dve.n - 1)
                        for hc in range(HC):
                            inst = e.matmul(
                                pall[:, st_ * 512:st_ * 512 + 512],
                                xfs[:, xb + hc * 512 + ti4 * 128:xb + hc * 512 + ti4 * 128 + 128],
                                wva[:, hc * 512:(hc + 1) * 512],
                                start=(hc == 0), stop=(hc == HC - 1))
                        inst.then_inc(s_pe.h, 1)
                        s_pe.n += 1
                    ctx.tensor(ft)

                    def fv(e, sp=sp, ti4=ti4, st_=st_):
                        kb = sp * 4 + ti4
                        e.wait_ge(s_pe.h, s_pe.n)
                        e.tensor_copy(
                            vt[:, kb * 520:(kb + 1) * 520]
                            .rearrange("p (h c) -> p h c", h=8)[:, :, 0:64],
                            pall[:, st_ * 512:st_ * 512 + 512]
                            .rearrange("p (h c) -> p h c", h=8)) \
                            .then_inc(s_dve.h, 1)
                        s_dve.n += 1
                    ctx.vector(fv)

            def fvb(e):
                e.memset(battn[:, :], -32.0).then_inc(s_dve.h, 1)
                s_dve.n += 1
            ctx.vector(fvb)
    ctx.barrier()
    if d["dbg"] & {"qf", "kf", "vt", "battn"}:
        def fdbg(e):
            for nm, buf in (("qf", qf), ("kf", kf), ("vt", vt), ("battn", battn)):
                if nm in d["dbg"]:
                    e.dma_start(out=d["dbg_out"][nm][:, :], in_=buf[:, :]) \
                        .then_inc(s_gld.h, 16)
                    s_gld.n += 16
            e.wait_ge(s_gld.h, s_gld.n)
        ctx.gpsimd(fdbg)
        ctx.barrier()

    with ExitStack() as sscope:
        ebAB = sb("ebAB", [128, 3 * 1024], BF16, sscope)
        nrow = d["sqb"][32:33, 0:1]
        bcs = d["sqb"][64:128, 0:256].bitcast(BF16)

        uidx = 0
        evac_end = {}
        pend_bcast = []
        unit_meta = {}
        for hp in range(4):
            for qb in range(8):
                nkb = 4 * qb + 4
                pe0 = s_pe.n
                act0 = s_act.n
                dve_u0 = s_dve.n
                avb = 4 + 2 * (uidx % 2)

                pe_sc = {}
                pe_av = {}
                _c = pe0
                for kb in range(nkb):
                    _c += 1
                    pe_sc[kb] = _c
                    if kb >= 2:
                        _c += 1
                        pe_av[kb - 2] = _c
                _c += 1
                pe_av[nkb - 2] = _c
                _c += 1
                pe_av[nkb - 1] = _c

                def emit_av(e, kb, hp, nkb, act0, dve_u0, qb, avb):
                    e.wait_ge(s_act.h, act0 + kb + 1)
                    if kb >= 4 * qb:
                        e.wait_ge(s_dve.h, dve_u0 + (kb - 4 * qb + 1))
                    ebs = (kb % 3) * 1024
                    e.matmul(pall[:, avb * 512:avb * 512 + 512][0:65, :],
                             vt[:, kb * 520 + (2 * hp) * 65:kb * 520 + (2 * hp) * 65 + 65],
                             ebAB[:, ebs:ebs + 512],
                             start=(kb == 0), stop=(kb == nkb - 1))
                    e.matmul(pall[:, (avb + 1) * 512:(avb + 1) * 512 + 512][0:65, :],
                             vt[:, kb * 520 + (2 * hp + 1) * 65:kb * 520 + (2 * hp + 1) * 65 + 65],
                             ebAB[:, ebs + 512:ebs + 1024],
                             start=(kb == 0), stop=(kb == nkb - 1)) \
                        .then_inc(s_pe.h, 1)

                w_evac = evac_end.get(uidx - 2, 0)
                my_pend = list(pend_bcast)
                pend_bcast.clear()
                nb = len(my_pend)
                if nb:
                    unit_meta[uidx - 1] = unit_meta[uidx - 1][:3] + (s_pe.n + nb,)
                pe0 += nb
                for kk in list(pe_sc):
                    pe_sc[kk] += nb
                for kk in list(pe_av):
                    pe_av[kk] += nb

                def ft(e, hp=hp, qb=qb, nkb=nkb, pe0=pe0, act0=act0,
                       dve_u0=dve_u0, avb=avb, w_evac=w_evac, my_pend=my_pend):
                    for (pavb, hloc, wdve, nsl) in my_pend:
                        e.wait_ge(s_dve.h, wdve)
                        bank = pavb + hloc
                        e.matmul(pall[:, bank * 512:bank * 512 + 512][64:128, :],
                                 d["ones_row"][:, 0:64],
                                 d["sqb"][0:1, nsl], start=True, stop=True) \
                            .then_inc(s_pe.h, 1)
                        s_pe.n += 1
                    if w_evac:
                        e.wait_ge(s_dve.h, w_evac)
                    for kb in range(nkb):
                        scs = (kb % 2) * 2
                        if kb >= 2:
                            e.wait_ge(s_act.h, act0 + kb - 1)
                        e.matmul(pall[:, scs * 512:scs * 512 + 512],
                                 kf[0:64, hp * S + kb * 128:hp * S + kb * 128 + 128],
                                 qf[0:64, hp * S + qb * 512:hp * S + qb * 512 + 512],
                                 start=True, stop=True)
                        e.matmul(pall[:, (scs + 1) * 512:(scs + 1) * 512 + 512],
                                 kf[64:128, hp * S + kb * 128:hp * S + kb * 128 + 128],
                                 qf[64:128, hp * S + qb * 512:hp * S + qb * 512 + 512],
                                 start=True, stop=True).then_inc(s_pe.h, 1)
                        if kb >= 2:
                            emit_av(e, kb - 2, hp, nkb, act0, dve_u0, qb, avb)
                    emit_av(e, nkb - 2, hp, nkb, act0, dve_u0, qb, avb)
                    emit_av(e, nkb - 1, hp, nkb, act0, dve_u0, qb, avb)
                    s_pe.n += 2 * nkb
                ctx.tensor(ft)

                def fs(e, nkb=nkb, pe0=pe0, hp=hp, pe_sc=pe_sc, pe_av=pe_av):
                    for kb in range(nkb):
                        scs = (kb % 2) * 2
                        ebs = (kb % 3) * 1024
                        e.wait_ge(s_pe.h, pe_sc[kb])
                        if kb >= 3:
                            e.wait_ge(s_pe.h, pe_av[kb - 3])
                        e.activation(ebAB[:, ebs:ebs + 1024],
                                     pall[:, scs * 512:scs * 512 + 1024], AF.Exp,
                                     bias=battn[:, hp:hp + 1], scale=0.125) \
                            .then_inc(s_act.h, 1)
                    s_act.n += nkb
                ctx.scalar(fs)

                def fvm(e, qb=qb, nkb=nkb, act0=act0):
                    for kb in range(4 * qb, nkb):
                        off = (kb - 4 * qb) * 128
                        ebs = (kb % 3) * 1024
                        e.wait_ge(s_act.h, act0 + kb + 1)
                        for hof in (0, 512):
                            if off > 0:
                                e.memset(ebAB[:, ebs + hof:ebs + hof + off], 0.0)
                            e.tensor_mul(ebAB[:, ebs + hof + off:ebs + hof + off + 128],
                                         ebAB[:, ebs + hof + off:ebs + hof + off + 128],
                                         d["tri_b"][:, :])
                        e.tensor_scalar(nrow[:, 0:1], nrow[:, 0:1], 1.0, None,
                                        ALU.mult).then_inc(s_dve.h, 1)
                        s_dve.n += 1
                ctx.vector(fvm)

                pe_mm_end = s_pe.n
                for hloc in (0, 1):
                    nsl = slice(hloc * 512, hloc * 512 + 512)

                    def fve1(e, hloc=hloc, avb=avb, pe_mm_end=pe_mm_end,
                             nsl=nsl):
                        e.wait_ge(s_pe.h, pe_mm_end)
                        bank = avb + hloc
                        e.tensor_scalar(d["sqb"][0:1, nsl],
                                        pall[64:65, bank * 512:bank * 512 + 512],
                                        1e-30, None, ALU.max).then_inc(s_dve.h, 1)
                        s_dve.n += 1
                        e.wait_ge(s_dve.h, s_dve.n)
                        e.reciprocal(d["sqb"][0:1, nsl], d["sqb"][0:1, nsl]) \
                            .then_inc(s_dve.h, 1)
                        s_dve.n += 1
                        pend_bcast.append((avb, hloc, s_dve.n, nsl))
                    ctx.vector(fve1)

                if uidx >= 1:
                    phl, pqb, pavb, ppe = unit_meta[uidx - 1]
                    assert ppe is not None

                    def fve2(e, phl=phl, pqb=pqb, pavb=pavb, ppe=ppe):
                        e.wait_ge(s_pe.h, ppe)
                        for hloc in (0, 1):
                            bank = pavb + hloc
                            e.tensor_copy(bcs[0:64, :],
                                          pall[64:128, bank * 512:bank * 512 + 512]) \
                                .then_inc(s_dve.h, 1)
                            s_dve.n += 1
                            e.tensor_mul(
                                av_fm[hloc * 64:hloc * 64 + 64,
                                      phl * S + pqb * 512:phl * S + pqb * 512 + 512],
                                pall[0:64, bank * 512:bank * 512 + 512],
                                bcs[0:64, :]).then_inc(s_dve.h, 1)
                            s_dve.n += 1
                    ctx.vector(fve2)
                    evac_end[uidx - 1] = s_dve.n
                unit_meta[uidx] = (hp, qb, avb, None)
                uidx += 1
        my_pend = list(pend_bcast)
        pend_bcast.clear()
        lhl, lqb, lavb, _ = unit_meta[uidx - 1]
        bcast_done = s_pe.n + len(my_pend)

        def ft_tail(e, my_pend=my_pend):
            for (pavb, hloc, wdve, nsl) in my_pend:
                e.wait_ge(s_dve.h, wdve)
                bank = pavb + hloc
                e.matmul(pall[:, bank * 512:bank * 512 + 512][64:128, :],
                         d["ones_row"][:, 0:64],
                         d["sqb"][0:1, nsl], start=True, stop=True) \
                    .then_inc(s_pe.h, 1)
                s_pe.n += 1
        ctx.tensor(ft_tail)

        def fve2_tail(e, lhl=lhl, lqb=lqb, lavb=lavb, bcast_done=bcast_done):
            e.wait_ge(s_pe.h, bcast_done)
            for hloc in (0, 1):
                bank = lavb + hloc
                e.tensor_copy(bcs[0:64, :],
                              pall[64:128, bank * 512:bank * 512 + 512]) \
                    .then_inc(s_dve.h, 1)
                s_dve.n += 1
                e.tensor_mul(
                    av_fm[hloc * 64:hloc * 64 + 64,
                          lhl * S + lqb * 512:lhl * S + lqb * 512 + 512],
                    pall[0:64, bank * 512:bank * 512 + 512],
                    bcs[0:64, :]).then_inc(s_dve.h, 1)
                s_dve.n += 1
        ctx.vector(fve2_tail)
        evac_end[uidx - 1] = s_dve.n

        pairs = [[2 * i, 2 * i + 1] for i in range(NCORES // 2)]
        hp_done = {h: evac_end[h * 8 + 7] for h in range(4)}

        def exch_chain(e, h):
            e.wait_ge(s_dve.h, hp_done[h])
            e.dma_start(out=d[f"ag_avi{h}"][:, :],
                        in_=av_fm[:, h * S:(h + 1) * S]) \
                .then_inc(s_gld.h, 16)
            s_gld.n += 16
            e.wait_ge(s_gld.h, s_gld.n)
            e.collective_compute(
                "AllGather", ALU.bypass, replica_groups=pairs,
                ins=[d[f"ag_avi{h}"].ap().opt()],
                outs=[d[f"ag_avo{h}"].ap().opt()],
            ).then_inc(s_cc.h, 1)
            s_cc.n += 1
            e.wait_ge(s_cc.h, s_cc.n)
            with e.register(f"rk{h}") as rk:
                e.cc_rank_ld(rk, replica_groups=pairs)
                with e.register(f"ofs{h}") as ofs:
                    e.reg_mov(ofs, rk)
                    e.reg_mul(ofs, ofs, TOK)
                    e.dma_start(
                        out=av_fm[:, h * S:(h + 1) * S]
                        .rearrange("p (c n) -> p c n", c=2),
                        in_=bass.AP(d[f"ag_avo{h}"], ofs,
                                    [[S, 128], [128 * S, 2], [1, TOK]])) \
                        .then_inc(s_gld.h, 16)
                    s_gld.n += 16

        def fxa(e):
            for h in range(3):
                exch_chain(e, h)
        ctx.gpsimd(fxa)
        act_end = s_act.n
        pe_tail = s_pe.n

    mid_scope.close()
    with ExitStack() as oscope:
        wob = sb("wob", [128, HC * DM], BF16, oscope)
        wo_v = d["wo_in"].rearrange("(c p) m -> p c m", p=128)

        def fol(e):
            e.wait_ge(s_pe.h, pe_tail)
            e.dma_start(out=wob[:, :].rearrange("p (c m) -> p c m", c=HC),
                        in_=wo_v).then_inc(s_ld.h, 16)
            s_ld.n += 16
            exch_chain(e, 3)
        ctx.gpsimd(fol)
        wob_ld = s_ld.n
        gld_rb = s_gld.n

        pe0 = s_pe.n
        dve0 = s_dve.n

        def ft(e):
            e.wait_ge(s_act.h, act_end)
            e.wait_ge(s_gld.h, gld_rb)
            e.wait_ge(s_ld.h, wob_ld)
            for t in range(TT):
                st_ = (t % 2) * 2
                if t >= 2:
                    e.wait_ge(s_dve.h, dve0 + t - 1)
                for half in range(2):
                    for dc in range(HC):
                        col = (dc % 4) * S + (dc // 4) * TOK + t * 128
                        inst = e.matmul(
                            pall[:, (st_ + half) * 512:(st_ + half) * 512 + 512],
                            av_fm[:, col:col + 128],
                            wob[:, dc * DM + half * 512:dc * DM + half * 512 + 512],
                            start=(dc == 0), stop=(dc == HC - 1))
                inst.then_inc(s_pe.h, 1)
            s_pe.n += TT
        ctx.tensor(ft)

        def fv(e):
            for t in range(TT):
                st_ = (t % 2) * 2
                e.wait_ge(s_pe.h, pe0 + t + 1)
                e.tensor_add(d["xr"][:, t * DM:(t + 1) * DM],
                             pall[:, st_ * 512:st_ * 512 + 1024],
                             d["xr"][:, t * DM:(t + 1) * DM]).then_inc(s_dve.h, 1)
                s_dve.n += 1
                d.setdefault("xr_dve", {})[t] = s_dve.n
        ctx.vector(fv)
    core_scope.close()
    ctx.barrier()


def emit_output(ctx, d):
    s_st, s_dve, s_gp = d["s_st"], d["s_dve"], d["s_gp"]
    xr_dve = dict(d.get("xr_dve", {}))
    out_t = d["out"].rearrange("(t p) m -> t p m", p=128)

    def fst(e):
        for t in range(0, TT, 2):
            if t in xr_dve:
                e.wait_ge(s_dve.h, xr_dve[t])
            e.dma_start(out=out_t[t], in_=d["xr"][:, t * DM:(t + 1) * DM]) \
                .then_inc(s_st.h, 16)
        s_st.n += 16 * (TT // 2)
        e.wait_ge(s_st.h, s_st.n)
    ctx.sync(fst)

    def fgp(e):
        for t in range(1, TT, 2):
            if t in xr_dve:
                e.wait_ge(s_dve.h, xr_dve[t])
            e.dma_start(out=out_t[t], in_=d["xr"][:, t * DM:(t + 1) * DM]) \
                .then_inc(s_gp.h, 16)
        s_gp.n += 16 * (TT // 2)
        e.wait_ge(s_gp.h, s_gp.n)
    ctx.gpsimd(fgp)



_BUILD_KW = {}


def kernel(L0_attn_x, L0_attn_Wq, L0_attn_Wk, L0_attn_Wv, L0_attn_Wo,
           L0_ffn_clip_x, L0_ffn_clip_down, L0_ffn_w_gate, L0_ffn_w_up,
           L0_ffn_w_down):
    x = np.ascontiguousarray(np.asarray(L0_attn_x, dtype=np.float32))
    wq = np.ascontiguousarray(np.asarray(L0_attn_Wq, dtype=np.float32))
    wk = np.ascontiguousarray(np.asarray(L0_attn_Wk, dtype=np.float32))
    wv = np.ascontiguousarray(np.asarray(L0_attn_Wv, dtype=np.float32))
    wo = np.ascontiguousarray(np.asarray(L0_attn_Wo, dtype=np.float32))
    wg = np.ascontiguousarray(np.asarray(L0_ffn_w_gate, dtype=np.float32))
    wu = np.ascontiguousarray(np.asarray(L0_ffn_w_up, dtype=np.float32))
    wd = np.ascontiguousarray(np.asarray(L0_ffn_w_down, dtype=np.float32))
    clip = np.array([[np.asarray(L0_ffn_clip_x, np.float32).ravel()[0],
                      np.asarray(L0_ffn_clip_down, np.float32).ravel()[0]]],
                    dtype=np.float32)

    nc, dbg_out = build_program(**_BUILD_KW)
    in_maps = []
    for c in range(NCORES):
        b, h2 = c // 2, c % 2
        in_maps.append({
            "x": np.ascontiguousarray(x[b, h2 * TOK:(h2 + 1) * TOK, :]),
            "wq": np.ascontiguousarray(wq[:, h2 * 512:(h2 + 1) * 512]),
            "wk": np.ascontiguousarray(wk[:, h2 * 512:(h2 + 1) * 512]),
            "wv": np.ascontiguousarray(wv[:, h2 * 512:(h2 + 1) * 512]),
            "wo": wo, "wg": wg, "wu": wu, "wd": wd, "clip": clip,
        })
    res = run_bass_kernel_spmd(nc, in_maps, core_ids=list(range(NCORES)))
    out = np.empty((4, S, DM), np.float32)
    for c in range(NCORES):
        b, h2 = c // 2, c % 2
        out[b, h2 * TOK:(h2 + 1) * TOK, :] = res.results[c]["out"]
    if _BUILD_KW.get("debug"):
        kernel.last_results = res
    return out
